# version 06
# baseline (speedup 1.0000x reference)
import sys
sys.path.insert(0, "/opt/trn_rl_repo")
import numpy as np
import ml_dtypes

import concourse.bass as bass
import concourse.mybir as mybir
from concourse.bass_utils import run_bass_kernel_spmd

F32 = mybir.dt.float32
BF16 = mybir.dt.bfloat16
AF = mybir.ActivationFunctionType
ALU = mybir.AluOpType
AX = mybir.AxisListType

B, L, D = 16, 2048, 1024
NCORES = 8
SEQ = 2
DI = 2048
DST = 16
DFF = 2816
TC = 256
TOK = SEQ * TC
NT = L // TC
NDC = DI // 128
NKD = D // 128
NKF = DFF // 128
EPS = 1e-5
MID = [0.25, 0.75, 1.25, 1.75, 2.5, 3.5, 5.0]
MIDW = [0.5, 0.5, 0.5, 0.5, 1.0, 1.0, 2.0]

CFG_NT = NT


class Plan:
    ENGS = ("v", "s", "t", "g", "q")

    def __init__(self):
        self.ops = {e: [] for e in self.ENGS}
        self.count = {}
        self.semnames = set("S" + e for e in self.ENGS)
        self.sems = {}

    def dma_sem(self, name):
        self.semnames.add(name)

    def op(self, eng, method, *args, waits=(), dma=None, **kw):
        import traceback
        fr = traceback.extract_stack(limit=4)
        tbs = ";".join(f"{f.lineno}" for f in fr[:-1])
        w = tuple(t for t in waits if t is not None)
        if dma is not None:
            self.count[dma] = self.count.get(dma, 0) + 16
            tok = (dma, self.count[dma])
        else:
            s = "S" + eng
            self.count[s] = self.count.get(s, 0) + 1
            tok = (s, self.count[s])
        self.ops[eng].append((method, args, kw, w, tok, tbs))
        return tok

    def emit(self, block):
        plan = self
        secs = {"v": block.vector, "s": block.scalar, "t": block.tensor,
                "g": block.gpsimd, "q": block.sync}

        def make_body(e):
            own = "S" + e

            def body(engine):
                waited = {}
                for (method, args, kw, waits, tok, src) in plan.ops[e]:
                    need = {}
                    for (sname, val) in waits:
                        if val > need.get(sname, -1):
                            need[sname] = val
                    for sname in sorted(need):
                        val = need[sname]
                        if waited.get(sname, -1) >= val:
                            continue
                        engine.wait_ge(plan.sems[sname], val)
                        waited[sname] = val
                    try:
                        instr = getattr(engine, method)(*args, **kw)
                    except Exception as ex:
                        raise RuntimeError(f"emit failed: {method} planned at {src}: {ex}") from ex
                    sname, _ = tok
                    instr.then_inc(plan.sems[sname], 1 if sname == own else 16)
            return body

        for e in self.ENGS:
            if self.ops[e]:
                secs[e](make_body(e))


def col(ap, *idx):
    sl = tuple([slice(None)] + [slice(i, i + 1) for i in idx])
    return ap[sl]


def cdiv(a, b):
    return (a + b - 1) // b


def build_program():
    nc = bass.Bass("TRN2", target_bir_lowering=False)
    P = Plan()
    V, S, T, G, Q = "v", "s", "t", "g", "q"

    def din(name, shape, dt=F32):
        return nc.dram_tensor(name, shape, dt, kind="ExternalInput").ap()

    x_in = din("x", [SEQ, L, D])
    ln_g = din("ln_g", [D]); ln_b = din("ln_b", [D])
    w_in = din("w_in", [2 * DI, D])
    conv_w = din("conv_w", [DI, 4]); conv_b = din("conv_b", [DI])
    w_xp = din("w_xp", [96, DI])
    w_dt = din("w_dt", [DI, 64]); b_dt = din("b_dt", [DI])
    a_log = din("a_log", [DI, DST]); d_vec = din("d_vec", [DI])
    w_out = din("w_out", [D, DI])
    wg_in = din("wg", [DFF, D]); wu_in = din("wu", [DFF, D])
    wd_in = din("wd", [D, DFF])
    eye_b = din("eye_b", [128, 128], BF16)
    ones1f = din("ones1f", [1, 128], F32)
    sel_in = din("sel", [16, 16 * 128], BF16)
    csts = din("csts", [128, 8], F32)
    C_ONE, C_EPS, C_LNS, C_LNB, C_NEGH = 0, 1, 2, 3, 4

    out_t = nc.dram_tensor("out", [SEQ, L, D], F32, kind="ExternalOutput").ap()

    def scr(name, shape, dt=BF16):
        return nc.dram_tensor(name, shape, dt, kind="Internal").ap()

    winT_sc = scr("winT_sc", [32, 128, NKD, 128])
    woutT_sc = scr("woutT_sc", [NDC, 128, D])
    wgT_sc = scr("wgT_sc", [NKD, 128, DFF])
    wuT_sc = scr("wuT_sc", [NKD, 128, DFF])
    wdT_sc = scr("wdT_sc", [NKF, 128, D])

    from contextlib import ExitStack
    stack = ExitStack()

    def sb(name, shape, dt=F32):
        return stack.enter_context(nc.sbuf_tensor(name, shape, dt))

    def psum(name, shape, dt=F32):
        return stack.enter_context(nc.psum_tensor(name, shape, dt))

    cst = sb("cst", [128, 8])
    eyeb = sb("eyeb", [128, 128], BF16)
    onesf = sb("onesf", [1, 128])
    selb = sb("selb", [16, 16 * 128], BF16)
    A_sc = sb("A_sc", [128, NDC, DST])
    A_scb = sb("A_scb", [128, NDC, DST], BF16)
    D_sc = sb("D_sc", [128, NDC])
    bdt_sc = sb("bdt_sc", [128, NDC])
    cvb_sc = sb("cvb_sc", [128, NDC])
    cvw_sc = sb("cvw_sc", [128, NDC, 4])
    g_bc = sb("g_bc", [128, D])
    b_bc = sb("b_bc", [128, D])
    wxpT = sb("wxpT", [128, NDC, 96], BF16)
    wdtT = sb("wdtT", [64, DI], BF16)
    hcar = sb("hcar", [128, NDC, DST, SEQ])
    ctail = sb("ctail", [128, NDC, SEQ, 3])
    bnb = sb("bnb", [128, 4, 12])
    mv = sb("mv", [128, 4, 2])
    rr = sb("rr", [128, 4])
    rho = sb("rho", [128, 4])
    rat = sb("rat", [128, 4])
    lnt = sb("lnt", [128, 4])
    qm = sb("qm", [128, 64])
    qs = sb("qs", [128, 64])
    qr = sb("qr", [128, 64])
    qsb = sb("qsb", [128, 64], BF16)
    sdt = sb("sdt", [128, DST])
    uhead = sb("uhead", [128, NDC, SEQ, 8])
    dl_b = sb("dl_b", [64, TOK], BF16)
    bcrB = sb("bcrB", [16, TOK], BF16)
    bcrC = sb("bcrC", [16, TOK], BF16)

    acc = sb("acc", [128, 4, D])
    szb = sb("szb", [128, NDC, TOK], BF16)
    ucb = sb("ucb", [128, NDC, TOK], BF16)

    SCANR = sb("SCANR", [128, 8192])
    BbR = sb("BbR", [128, 8192], BF16)
    CbR = sb("CbR", [128, 8192], BF16)
    DtR = sb("DtR", [128, 8192], BF16)
    hqTR = sb("hqTR", [128, 11264], BF16)

    winb = [sb(f"winb{i}", [128, NKD, 128], BF16) for i in range(2)]
    wgb = sb("wgb", [128, NKD, 512], BF16)
    wub = sb("wub", [128, NKD, 512], BF16)
    wdb = [sb(f"wdb{i}", [128, D], BF16) for i in range(2)]
    wob = [sb(f"wob{i}", [128, D], BF16) for i in range(2)]

    SCB = SCANR[:].bitcast(BF16)
    u_v = SCANR[:].rearrange("p (c t) -> p c t", c=NDC)
    pp32 = [SCANR[:, 0:4096], SCANR[:, 4096:8192]]
    qv_v = SCANR[:, 0:1024]
    qa2_v = SCANR[:, 1024:2048]
    qABCD = [SCB[:, 4096 + i * 1024: 4096 + (i + 1) * 1024] for i in range(4)]
    p1s = SCANR[:, 0:1024]
    p2s = SCANR[:, 1024:2048]
    argb = SCB[:, 0:4096].rearrange("p (n t) -> p n t", n=DST)
    dAb = SCB[:, 4096:8192].rearrange("p (n t) -> p n t", n=DST)
    dBub = SCB[:, 8192:12288].rearrange("p (n t) -> p n t", n=DST)
    hb_v = SCB[:, 12288:16384].rearrange("p (n t) -> p n t", n=DST)
    tr8 = SCB[:, 0:2048].rearrange("p (n t) -> p n t", n=8)
    tr4 = SCB[:, 2048:3072].rearrange("p (n t) -> p n t", n=4)
    tr2 = SCB[:, 3072:3584].rearrange("p (n t) -> p n t", n=2)
    ygb_v = SCB[:, 0:8192].rearrange("p (c t) -> p c t", c=NDC)

    BbF = BbR[:].bitcast(F32)
    xqb_v = BbR[:, 0:4096].rearrange("p (j d) -> p j d", j=4)
    xn2b_v = BbR[:, 4096:8192].rearrange("p (j d) -> p j d", j=4)
    ca_lo = BbF[:].rearrange("p (c t) -> p c t", c=8)
    Bb_v = BbR[:].rearrange("p (n t) -> p n t", n=DST)
    ppb = [BbR[:, 0:4096], BbR[:, 4096:8192]]
    hslp = [BbF[:, 0:512], BbF[:, 512:1024]]
    hsl = hslp[0]
    hqlp = [BbR[:, 2048:2560], BbR[:, 2560:3072]]

    CbF = CbR[:].bitcast(F32)
    xn_v = CbF[:].rearrange("p (j d) -> p j d", j=4)
    xqT_v = CbR[:, 0:4096].rearrange("p (k t) -> p k t", k=NKD)
    Cb_v = CbR[:].rearrange("p (n t) -> p n t", n=DST)
    ppq = [CbR[:, 0:4096], CbR[:, 4096:8192]]

    DtF = DtR[:].bitcast(F32)
    xt_v = DtF[:].rearrange("p (j d) -> p j d", j=4)
    ca_hi = DtF[:].rearrange("p (c t) -> p c t", c=8)
    dtb_v = DtR[:].rearrange("p (c t) -> p c t", c=NDC)

    hqF = hqTR[:].bitcast(F32)
    xn2T_v = hqTR[:, 0:4096].rearrange("p (k t) -> p k t", k=NKD)
    hqT_v = hqTR[:].rearrange("p (k t) -> p k t", k=NKF)
    yb_v = hqTR[:, 0:8192].rearrange("p (c t) -> p c t", c=NDC)
    dtusl = hqTR[:, 8192:8704]
    pqv = hqF[:, 0:1024]
    pqa2 = hqF[:, 1024:2048]
    pqABCD = [hqTR[:, 4096 + i * 1024: 4096 + (i + 1) * 1024] for i in range(4)]

    ps_a0 = psum("ps_a0", [128, 512]); ps_a1 = psum("ps_a1", [128, 512])
    ps_w0 = psum("ps_w0", [128, 1024]); ps_w1 = psum("ps_w1", [128, 1024])
    ps_t0 = psum("ps_t0", [128, 1024], BF16)
    ps_t1 = psum("ps_t1", [128, 1024], BF16)
    ps_ts = [ps_t0, ps_t1]

    for s in ["dld0", "dld1", "dsp0", "dsp1", "dx", "dso", "dwin0", "dwin1",
              "dwg", "dwu", "dwd0", "dwd1", "dwo0", "dwo1"]:
        P.dma_sem(s)

    st = {"tp": 0, "tp_ev": {}, "paf": {0: None, 1: None}, "pw": {0: None, 1: None}, "pab": 0, "qprev": None}

    def transpose_group(blocks, evacs, wt):
        k = st["tp"]
        pst = ps_ts[k % 2]
        w = list(wt)
        if (k - 2) in st["tp_ev"]:
            w.append(st["tp_ev"][k - 2])
        mm = None
        for (j, src) in blocks:
            mm = P.op(T, "transpose", pst[:, j * 128:(j + 1) * 128], src,
                      eyeb[:], waits=(w + [ld_all]) if mm is None else ())
        ev = None
        for (dst, c0, c1) in evacs:
            ev = P.op(S, "copy", out=dst, in_=pst[:, c0:c1], waits=[mm])
        st["tp_ev"][k] = ev
        st["tp"] = k + 1
        return ev

    def quant(src_ap, dst_ap, nblk, wtoks, scratch):
        qvv, qaa, (qa, qb_, qc, qd) = scratch
        n = nblk * 16
        mm_ = qm[:, 0:nblk]; ss = qs[:, 0:nblk]; rs = qr[:, 0:nblk]
        sbv = qsb[:, 0:nblk]
        vv = qvv[:, 0:n]; aa = qaa[:, 0:n]
        qa = qa[:, 0:n]; qb_ = qb_[:, 0:n]; qc = qc[:, 0:n]; qd = qd[:, 0:n]
        w0 = list(wtoks) + ([st["qprev"]] if st["qprev"] else [])
        src3 = src_ap.rearrange("p (b j) -> p b j", j=16)
        t_m = P.op(V, "tensor_reduce", out=mm_, in_=src3, axis=AX.X, op=ALU.max,
                   apply_absolute_value=True, waits=w0)
        t_s = P.op(V, "tensor_scalar", out=ss, in0=mm_, scalar1=1.0 / 6.0,
                   scalar2=1e-12, op0=ALU.mult, op1=ALU.max, waits=[t_m])
        t_r = P.op(V, "reciprocal", out=rs, in_=ss, waits=[t_s])
        t_sb = P.op(V, "tensor_copy", out=sbv, in_=ss, waits=[t_s])
        rb = rs.unsqueeze(2).broadcast_to([128, nblk, 16])
        t_v = P.op(V, "tensor_tensor", out=vv.rearrange("p (b j) -> p b j", j=16),
                   in0=src3, in1=rb, op=ALU.mult, waits=[t_r])
        t_a = P.op(S, "activation", out=aa, in_=vv, func=AF.Abs,
                   waits=[t_v])
        def cmp(dst, i, w):
            return P.op(V, "tensor_scalar", out=dst, in0=aa, scalar1=MID[i],
                        scalar2=MIDW[i], op0=ALU.is_gt, op1=ALU.mult, waits=w)
        c0 = cmp(qa, 0, [t_a]); c1 = cmp(qb_, 1, [t_a])
        s01 = P.op(V, "tensor_tensor", out=qc, in0=qa, in1=qb_, op=ALU.add,
                   waits=[c0, c1])
        c2 = cmp(qa, 2, [s01]); c3 = cmp(qb_, 3, [s01])
        s23 = P.op(V, "tensor_tensor", out=qd, in0=qa, in1=qb_, op=ALU.add,
                   waits=[c2, c3])
        s03 = P.op(V, "tensor_tensor", out=qc, in0=qc, in1=qd, op=ALU.add,
                   waits=[s23])
        c4 = cmp(qa, 4, [s03]); c5 = cmp(qb_, 5, [s03])
        s45 = P.op(V, "tensor_tensor", out=qd, in0=qa, in1=qb_, op=ALU.add,
                   waits=[c4, c5])
        c6 = cmp(qa, 6, [s45])
        s46 = P.op(V, "tensor_tensor", out=qd, in0=qd, in1=qa, op=ALU.add,
                   waits=[c6])
        lvl = P.op(V, "tensor_tensor", out=qc, in0=qc, in1=qd, op=ALU.add,
                   waits=[s46])
        sg = P.op(V, "tensor_scalar", out=qb_, in0=vv, scalar1=0.0,
                  scalar2=None, op0=ALU.is_ge, waits=[lvl])
        sg2 = P.op(V, "tensor_scalar", out=qb_, in0=qb_, scalar1=2.0,
                   scalar2=-1.0, op0=ALU.mult, op1=ALU.add, waits=[sg])
        tw = P.op(V, "tensor_tensor", out=qa, in0=qc, in1=qb_, op=ALU.mult,
                  waits=[sg2])
        sbb = sbv.unsqueeze(2).broadcast_to([128, nblk, 16])
        tq = P.op(V, "tensor_tensor",
                  out=dst_ap.rearrange("p (b j) -> p b j", j=16),
                  in0=qa.rearrange("p (b j) -> p b j", j=16), in1=sbb,
                  op=ALU.mult, waits=[tw, t_sb])
        st["qprev"] = tq
        return tq

    MAIN_SCR = (qv_v.unsqueeze(0)[0:0] if False else None)
    main_scr = (SCANR[:, 0:1024], SCANR[:, 1024:2048], qABCD)
    prep_scr = (pqv, pqa2, pqABCD)

    dl = lambda **kw: P.op(Q, "dma_start", **kw)
    dl(out=cst[:], in_=csts, dma="dld0")
    dl(out=eyeb[:], in_=eye_b, dma="dld0")
    dl(out=onesf[:], in_=ones1f, dma="dld0")
    dl(out=selb[:], in_=sel_in, dma="dld0")
    dl(out=A_sc[:], in_=a_log.rearrange("(c p) n -> p c n", p=128), dma="dld0")
    dl(out=D_sc[:], in_=d_vec.rearrange("(c p) -> p c", p=128), dma="dld0", allow_slow_non_contiguous=True)
    dl(out=bdt_sc[:], in_=b_dt.rearrange("(c p) -> p c", p=128), dma="dld0", allow_slow_non_contiguous=True)
    dl(out=cvb_sc[:], in_=conv_b.rearrange("(c p) -> p c", p=128), dma="dld0", allow_slow_non_contiguous=True)
    dl(out=cvw_sc[:], in_=conv_w.rearrange("(c p) k -> p c k", p=128),
       dma="dld0")
    dl(out=g_bc[0:1, :], in_=ln_g.unsqueeze(0), dma="dld0")
    ld_all = dl(out=b_bc[0:1, :], in_=ln_b.unsqueeze(0), dma="dld0")

    t_exp = P.op(S, "activation", out=A_sc[:], in_=A_sc[:], func=AF.Exp,
                 waits=[ld_all])
    t_negA = P.op(V, "tensor_scalar_mul", out=A_sc[:], in0=A_sc[:],
                  scalar1=-1.0, waits=[t_exp])
    t_negAb = P.op(V, "tensor_copy", out=A_scb[:], in_=A_sc[:], waits=[t_negA])

    brc = []
    for half in range(2):
        brc.append(P.op(T, "matmul", ps_w0[:, half * 512:(half + 1) * 512],
                        onesf[:], g_bc[0:1, half * 512:(half + 1) * 512],
                        start=True, stop=True, waits=[ld_all]))
        brc.append(P.op(T, "matmul", ps_w1[:, half * 512:(half + 1) * 512],
                        onesf[:], b_bc[0:1, half * 512:(half + 1) * 512],
                        start=True, stop=True, waits=[ld_all]))
    t_gbc = P.op(V, "tensor_copy", out=g_bc[:], in_=ps_w0[:], waits=[brc[-1]])
    t_bbc = P.op(V, "tensor_copy", out=b_bc[:], in_=ps_w1[:], waits=[brc[-1]])

    t_hc = P.op(V, "memset", hcar[:], 0.0)
    t_ct = P.op(V, "memset", ctail[:], 0.0)

    pst8 = {"lc": [None, None], "bc": [None, None], "qc": [None, None], "i": 0}

    def prep_slab(src_dram, n_kblk, quant_nblk, stores, part=128):
        i = pst8["i"] % 2
        n = n_kblk * 128
        tl = P.op(Q, "dma_start", out=pp32[i][0:part, 0:n], in_=src_dram,
                  dma=f"dld{i}", waits=[pst8["lc"][i], ld_all])
        if quant_nblk:
            tc_ = None
            for q0 in range(0, n, 1024):
                q1 = min(q0 + 1024, n)
                tc_ = quant(pp32[i][0:part, q0:q1], ppb[i][0:part, q0:q1],
                            (q1 - q0) // 16, [tl, pst8["bc"][i]], prep_scr)
        else:
            tc_ = P.op(V, "tensor_copy", out=ppb[i][0:part, 0:n],
                       in_=pp32[i][0:part, 0:n], waits=[tl, pst8["bc"][i]])
        pst8["lc"][i] = tc_
        evs = []
        lastmm = None
        for g0 in range(0, n_kblk, 4):
            gsz = min(4, n_kblk - g0)
            k = st["tp"]
            pst = ps_ts[k % 2]
            w = [tc_, pst8["qc"][i]]
            if (k - 2) in st["tp_ev"]:
                w.append(st["tp_ev"][k - 2])
            mm = None
            for j in range(gsz):
                mm = P.op(T, "transpose", pst[:, j * 128:(j + 1) * 128],
                          ppb[i][0:part, (g0 + j) * 128:(g0 + j + 1) * 128],
                          eyeb[:],
                          waits=([t for t in w if t] + [ld_all])
                          if mm is None else ())
            ev = P.op(V, "tensor_copy",
                      out=ppq[i][:, g0 * 128:(g0 + gsz) * 128],
                      in_=pst[:, 0:gsz * 128], waits=[mm])
            st["tp_ev"][k] = ev
            st["tp"] = k + 1
            evs.append(ev)
            lastmm = mm
        pst8["bc"][i] = lastmm
        stok = None
        for (dst, c0, c1) in stores:
            stok = P.op(Q, "dma_start", out=dst, in_=ppq[i][:, c0:c1],
                        dma=f"dsp{i}", waits=evs)
        pst8["qc"][i] = stok
        pst8["i"] += 1
        return stok

    for ec in range(32):
        prep_slab(w_in[ec * 128:(ec + 1) * 128, :], NKD, 0,
                  [(winT_sc[ec].rearrange("p c e -> p (c e)"), 0, NKD * 128)])
    for oc in range(8):
        prep_slab(w_out[oc * 128:(oc + 1) * 128, :], NDC, 0,
                  [(woutT_sc[dc][:, oc * 128:(oc + 1) * 128],
                    dc * 128, (dc + 1) * 128) for dc in range(NDC)])
    for (w_dram, dstsc) in ((wg_in, wgT_sc), (wu_in, wuT_sc)):
        for fc in range(NKF):
            prep_slab(w_dram[fc * 128:(fc + 1) * 128, :], NKD, 64,
                      [(dstsc[kc][:, fc * 128:(fc + 1) * 128],
                        kc * 128, (kc + 1) * 128) for kc in range(NKD)])
    for oc in range(8):
        prep_slab(wd_in[oc * 128:(oc + 1) * 128, :], NKF, NKF * 8,
                  [(wdT_sc[kc][:, oc * 128:(oc + 1) * 128],
                    kc * 128, (kc + 1) * 128) for kc in range(NKF)])

    i = pst8["i"] % 2
    tl = P.op(Q, "dma_start", out=pp32[i][0:96, 0:DI], in_=w_xp,
              dma=f"dld{i}", waits=[pst8["lc"][i]])
    tcc = P.op(V, "tensor_copy", out=ppb[i][0:96, 0:DI],
               in_=pp32[i][0:96, 0:DI], waits=[tl, pst8["bc"][i]])
    t_wxp = None
    for g0 in range(0, NDC, 4):
        k = st["tp"]; pst = ps_ts[k % 2]
        w = [tcc] + ([st["tp_ev"][k - 2]] if (k - 2) in st["tp_ev"] else [])
        mm = None
        for j in range(4):
            dc = g0 + j
            mm = P.op(T, "transpose", pst[:, j * 128:j * 128 + 96],
                      ppb[i][0:96, dc * 128:(dc + 1) * 128], eyeb[0:96, 0:96],
                      waits=(w + [ld_all]) if mm is None else ())
        for j in range(4):
            dc = g0 + j
            t_wxp = P.op(V, "tensor_copy", out=wxpT[:, dc, :],
                         in_=pst[:, j * 128:j * 128 + 96], waits=[mm])
        st["tp_ev"][k] = t_wxp
        st["tp"] = k + 1
    pst8["lc"][i] = tcc
    pst8["bc"][i] = mm
    pst8["i"] += 1

    i = pst8["i"] % 2
    tl = P.op(Q, "dma_start",
              out=pp32[i][:, 0:NDC * 64].rearrange("p (c r) -> p c r", c=NDC),
              in_=w_dt.rearrange("(c p) r -> p c r", p=128),
              dma=f"dld{i}", waits=[pst8["lc"][i]])
    tcc = P.op(V, "tensor_copy", out=ppb[i][:, 0:NDC * 64],
               in_=pp32[i][:, 0:NDC * 64], waits=[tl, pst8["bc"][i]])
    t_wdt = None
    for g0 in range(0, NDC, 4):
        k = st["tp"]; pst = ps_ts[k % 2]
        w = [tcc] + ([st["tp_ev"][k - 2]] if (k - 2) in st["tp_ev"] else [])
        mm = None
        for j in range(4):
            dc = g0 + j
            mm = P.op(T, "transpose", pst[0:64, j * 128:(j + 1) * 128],
                      ppb[i][:, dc * 64:(dc + 1) * 64], eyeb[:],
                      waits=(w + [ld_all]) if mm is None else ())
        t_wdt = P.op(V, "tensor_copy",
                     out=wdtT[:, g0 * 128:(g0 + 4) * 128],
                     in_=pst[0:64, 0:512], waits=[mm])
        st["tp_ev"][k] = t_wdt
        st["tp"] = k + 1
    pst8["lc"][i] = tcc
    pst8["bc"][i] = mm
    pst8["i"] += 1

    prep_sp = [("dsp0", P.count.get("dsp0", 0)), ("dsp1", P.count.get("dsp1", 0))]
    prep_v = ("Sv", P.count["Sv"])

    prev = {}
    ffn_pa_free = [t_gbc, t_bbc]
    psw_free = [t_gbc, t_bbc]
    win_free = [None, None]
    wg_free = None; wu_free = None
    wd_free = [None, None]
    wo_free = [None, None]
    sc_prev = {}

    for ti in range(CFG_NT):
        t0 = ti * TC
        xw = list(prep_sp) + [prev.get("DtR_free")]
        xl = None
        for j in range(4):
            s_, hf = j // 2, j % 2
            xl = P.op(Q, "dma_start", out=xt_v[:, j, :],
                      in_=x_in[s_, t0 + hf * 128: t0 + (hf + 1) * 128, :],
                      dma="dx", waits=xw if j == 0 else ())
        t_x = xl

        bn = []
        for j in range(4):
            bn.append(P.op(V, "bn_stats", out=bnb[:, j, 0:6],
                           in_=xt_v[:, j, 0:512],
                           waits=[t_x, prev.get("bn_free"),
                                  prep_v if ti == 0 else None]))
            bn.append(P.op(V, "bn_stats", out=bnb[:, j, 6:12],
                           in_=xt_v[:, j, 512:1024], waits=[]))
        t_mv = None
        for j in range(4):
            t_mv = P.op(V, "bn_aggr", out=mv[:, j, :], in_=bnb[:, j, :],
                        waits=[bn[-1]] if j == 0 else ())
        t_ln1 = P.op(S, "activation", out=lnt[:], in_=mv[:, :, 1], func=AF.Ln,
                     bias=col(cst, C_EPS),
                     waits=[t_mv, ld_all, prev.get("ln_free")])
        t_r = P.op(S, "activation", out=rr[:], in_=lnt[:], func=AF.Exp,
                   scale=col(cst, C_NEGH), waits=[t_ln1])
        t_ln2 = P.op(S, "activation", out=lnt[:], in_=mv[:, :, 1], func=AF.Ln,
                     scale=col(cst, C_LNS), bias=col(cst, C_LNB), waits=[t_r])
        t_rho = P.op(S, "activation", out=rho[:], in_=lnt[:], func=AF.Exp,
                     scale=col(cst, C_NEGH), waits=[t_ln2])
        t_rci = P.op(V, "reciprocal", out=lnt[:], in_=rr[:],
                     waits=[t_rho, t_mv])
        t_rat = P.op(V, "tensor_tensor", out=rat[:], in0=rho[:], in1=lnt[:],
                     op=ALU.mult, waits=[t_rci])

        t_xn = []
        t_xn2 = []
        for j in range(4):
            a = P.op(V, "tensor_scalar", out=xn_v[:, j, :], in0=xt_v[:, j, :],
                     scalar1=col(mv, j, 0), scalar2=col(rr, j),
                     op0=ALU.subtract, op1=ALU.mult,
                     waits=[t_r, prev.get("CbR_free")])
            t_xn.append(a)
            b_ = P.op(V, "tensor_tensor", out=acc[:, j, :], in0=xt_v[:, j, :],
                      in1=xn_v[:, j, :], op=ALU.add,
                      waits=[a, prev.get("acc_free")])
            c_ = P.op(V, "tensor_scalar", out=p1s, in0=xn_v[:, j, :],
                      scalar1=col(rat, j), scalar2=None, op0=ALU.mult,
                      waits=[t_rat, a, prev.get("SCANR_free") if (ti and j == 0) else None])
            d_ = P.op(V, "tensor_tensor", out=p2s, in0=p1s, in1=g_bc[:],
                      op=ALU.mult, waits=[c_, t_gbc])
            e_ = P.op(V, "tensor_tensor", out=xn2b_v[:, j, :], in0=p2s,
                      in1=b_bc[:], op=ALU.add,
                      waits=[d_, t_bbc, prev.get("BbR_free") if j == 0 else None])
            t_xn2.append(e_)
        t_acc0 = b_

        t_xq = None
        for j in range(4):
            t_xq = quant(xn_v[:, j, :], xqb_v[:, j, :], 64,
                         [t_xn[j], t_xn2[-1] if j == 0 else None], main_scr)

        t_xn2T = None
        for kc in range(NKD):
            t_xn2T = transpose_group(
                [(j, xn2b_v[:, j, kc * 128:(kc + 1) * 128]) for j in range(4)],
                [(xn2T_v[:, kc, :], 0, 512)],
                [t_xn2[-1], prev.get("hqTR_free") if kc == 0 else None])
        t_xqT = None
        for kc in range(NKD):
            t_xqT = transpose_group(
                [(j, xqb_v[:, j, kc * 128:(kc + 1) * 128]) for j in range(4)],
                [(xqT_v[:, kc, :], 0, 512)],
                [t_xq, t_xn[3] if kc == 0 else None])

        t_u = {}
        t_sz = {}
        for eci, ec in enumerate(list(range(16, 32)) + list(range(16))):
            slot = eci % 2
            wl = P.op(Q, "dma_start", out=winb[slot][:],
                      in_=winT_sc[ec].rearrange("p c e -> p (c e)"),
                      dma=f"dwin{slot}",
                      waits=[win_free[slot]] if win_free[slot] else prep_sp)
            bk = eci % 2
            ps = (ps_a0, ps_a1)[bk]
            wt = [wl, t_xn2T,
                  ffn_pa_free[bk] if (ti == 0 and eci < 2) else None,
                  st["paf"][bk]]
            mm = None
            for kc in range(NKD):
                mm = P.op(T, "matmul", ps[:], winb[slot][:, kc, :],
                          xn2T_v[:, kc, :], start=(kc == 0),
                          stop=(kc == NKD - 1), waits=wt if kc == 0 else ())
            if ec < 16:
                ev = P.op(S, "copy", out=u_v[:, ec, :], in_=ps[:],
                          waits=[mm, t_xq, prev.get("SCANR_free")])
                t_u[ec] = ev
            else:
                ev = P.op(S, "activation", out=szb[:, ec - 16, :], in_=ps[:],
                          func=AF.Silu, waits=[mm, prev.get("szb_free")])
                t_sz[ec - 16] = ev
            st["paf"][bk] = ev
            win_free[slot] = mm

        u4 = u_v.rearrange("p c (s t) -> p c s t", s=SEQ)
        th1 = P.op(V, "tensor_copy", out=uhead[:, :, :, 0:3], in_=ctail[:],
                   waits=[t_ct if ti == 0 else prev.get("ctail_w"),
                          prev.get("uhead_free")] + [t_u[ec] for ec in range(16)])
        th2 = P.op(V, "tensor_copy", out=uhead[:, :, :, 3:8],
                   in_=u4[:, :, :, 0:5], waits=[th1])
        t_tailw = P.op(V, "tensor_copy", out=ctail[:],
                       in_=u4[:, :, :, 253:256], waits=[th2])
        prev["ctail_w"] = t_tailw
        t_conv = {}
        for dc in range(NDC):
            ca = (ca_lo if dc < 8 else ca_hi)[:, dc % 8, :] \
                .rearrange("p (s t) -> p s t", s=SEQ)
            w0 = [t_tailw]
            if dc == 0:
                w0 += [prev.get("BbR_free2"), t_xn2[-1]]
            if dc == 8:
                w0 += [prev.get("DtR_free2"), t_xn[-1]]
            base = None
            for k in range(4):
                srcm = u4[:, dc, :, max(0, 2 + k - 2): 253 + k]
            for k in range(4):
                srcm = u4[:, dc, :, 2 + k: 253 + k]
                if k == 0:
                    base = P.op(V, "tensor_scalar", out=ca[:, :, 5:256],
                                in0=srcm, scalar1=col(cvw_sc, dc, 0),
                                scalar2=None, op0=ALU.mult, waits=w0)
                else:
                    base = P.op(V, "scalar_tensor_tensor", out=ca[:, :, 5:256],
                                in0=srcm, scalar=col(cvw_sc, dc, k),
                                in1=ca[:, :, 5:256], op0=ALU.mult, op1=ALU.add,
                                waits=[base])
            for k in range(4):
                srch = uhead[:, dc, :, k:k + 5]
                if k == 0:
                    hh = P.op(V, "tensor_scalar", out=ca[:, :, 0:5], in0=srch,
                              scalar1=col(cvw_sc, dc, 0), scalar2=None,
                              op0=ALU.mult, waits=[base])
                else:
                    hh = P.op(V, "scalar_tensor_tensor", out=ca[:, :, 0:5],
                              in0=srch, scalar=col(cvw_sc, dc, k),
                              in1=ca[:, :, 0:5], op0=ALU.mult, op1=ALU.add,
                              waits=[hh])
            sl = P.op(S, "activation", out=ucb[:, dc, :],
                      in_=ca.rearrange("p s t -> p (s t)"), func=AF.Silu,
                      bias=col(cvb_sc, dc),
                      waits=[hh, prev.get("ucb_free") if dc == 0 else None])
            t_conv[dc] = sl
        t_convlast = t_conv[NDC - 1]
        prev["uhead_free"] = hh

        wt = [t_convlast, t_wxp, st["paf"][0]]
        mm = None
        for dc in range(NDC):
            mm = P.op(T, "matmul", ps_a0[0:64, :], wxpT[:, dc, 0:64],
                      ucb[:, dc, :], start=(dc == 0), stop=(dc == NDC - 1),
                      waits=wt if dc == 0 else ())
        t_dl = P.op(V, "tensor_copy", out=dl_b[:], in_=ps_a0[0:64, :],
                    waits=[mm])
        st["paf"][0] = t_dl
        t_bc2 = {}
        for half, dst in ((0, bcrB), (1, bcrC)):
            bk = (half + 1) % 2
            ps = (ps_a0, ps_a1)[bk]
            wt = [t_convlast, st["paf"][bk],
                  ffn_pa_free[bk] if ti == 0 else None]
            mm = None
            for dc in range(NDC):
                mm = P.op(T, "matmul", ps[0:16, :],
                          wxpT[:, dc, 64 + half * 16:80 + half * 16],
                          ucb[:, dc, :], start=(dc == 0), stop=(dc == NDC - 1),
                          waits=wt if dc == 0 else ())
            t_bc2[half] = P.op(V, "tensor_copy", out=dst[:], in_=ps[0:16, :],
                               waits=[mm])
            st["paf"][bk] = t_bc2[half]

        t_dt = {}
        for dc in range(NDC):
            bk = dc % 2
            ps = (ps_a0, ps_a1)[bk]
            wt = [t_dl, t_wdt, st["paf"][bk]]
            mm = P.op(T, "matmul", ps[:], wdtT[:, dc * 128:(dc + 1) * 128],
                      dl_b[:], start=True, stop=True, waits=wt)
            e1 = P.op(S, "activation", out=hsl, in_=ps[:], func=AF.Exp,
                      bias=col(bdt_sc, dc),
                      waits=[mm, t_convlast,
                             prev.get("hsl_free") if dc == 0 else t_dt[dc - 1]])
            e2 = P.op(S, "activation", out=dtb_v[:, dc, :], in_=hsl,
                      func=AF.Ln, bias=col(cst, C_ONE), waits=[e1])
            t_dt[dc] = e2
            st["paf"][bk] = e2

        t_hqT = {}
        last_f2 = None
        kff = 0
        for fsi in range(6):
            f0 = fsi * 512
            f1 = min(f0 + 512, DFF)
            fn = f1 - f0
            for kc in range(NKD):
                wlg = P.op(Q, "dma_start", out=wgb[:, kc, 0:fn],
                           in_=wgT_sc[kc][:, f0:f1], dma="dwg",
                           waits=([wg_free] if wg_free else list(prep_sp))
                           if kc == 0 else ())
            for kc in range(NKD):
                wlu = P.op(Q, "dma_start", out=wub[:, kc, 0:fn],
                           in_=wuT_sc[kc][:, f0:f1], dma="dwu",
                           waits=([wu_free] if wu_free else list(prep_sp))
                           if kc == 0 else ())
            for tc in range(4):
                par = kff % 2
                gps = (ps_a0[:, 0:fn], ps_w0[:, 0:fn])[par]
                ups = (ps_a1[:, 0:fn], ps_w0[:, 512:512 + fn])[par]
                gkey = ("paf", 0) if par == 0 else ("pw", 0)
                ukey = ("paf", 1) if par == 0 else ("pw", 1)
                g_mm = None
                wt0 = [wlg, t_xqT, st[gkey[0]][gkey[1]],
                       psw_free[0] if (ti == 0 and kff < 2) else None]
                for kc in range(NKD):
                    g_mm = P.op(T, "matmul", gps,
                                xqT_v[:, kc, tc * 128:(tc + 1) * 128],
                                wgb[:, kc, 0:fn], start=(kc == 0),
                                stop=(kc == NKD - 1),
                                waits=wt0 if kc == 0 else ())
                u_mm = None
                wt1 = [wlu, st[ukey[0]][ukey[1]]]
                for kc in range(NKD):
                    u_mm = P.op(T, "matmul", ups,
                                xqT_v[:, kc, tc * 128:(tc + 1) * 128],
                                wub[:, kc, 0:fn], start=(kc == 0),
                                stop=(kc == NKD - 1),
                                waits=wt1 if kc == 0 else ())
                hsl_k = hslp[par]
                hql_k = hqlp[par]
                sg = P.op(S, "activation", out=hsl_k[:, 0:fn], in_=gps,
                          func=AF.Silu, waits=[g_mm, t_dt[NDC - 1],
                                               st.get("hslv%d" % par)])
                hm = P.op(V, "tensor_tensor", out=hsl_k[:, 0:fn],
                          in0=hsl_k[:, 0:fn], in1=ups, op=ALU.mult,
                          waits=[sg, u_mm])
                st[gkey[0]][gkey[1]] = hm
                st[ukey[0]][ukey[1]] = hm
                tq = quant(hsl_k[:, 0:fn], hql_k[:, 0:fn], fn // 16,
                           [hm, st.get("hqlt%d" % par),
                            prev.get("SCANR_free") if (ti and kff == 0) else None,
                            t_u[15] if kff == 0 else None],
                           main_scr)
                st["hslv%d" % par] = tq
                nkb = fn // 128
                ev = transpose_group(
                    [(j, hql_k[:, j * 128:(j + 1) * 128]) for j in range(nkb)],
                    [(hqT_v[:, (f0 // 128) + j, tc * 128:(tc + 1) * 128],
                      j * 128, (j + 1) * 128) for j in range(nkb)],
                    [tq, t_xn2T if kff == 0 else None,
                     prev.get("hqTR_free2") if kff == 0 else None])
                st["hqlt%d" % par] = st["tp_ev"][st["tp"] - 1]
                for j in range(nkb):
                    t_hqT[((f0 // 128) + j, tc)] = ev
                last_f2 = u_mm
                kff += 1
            wg_free = g_mm
            wu_free = u_mm
        t_hsl_last = st["qprev"]
        prev["hsl_free"] = t_hsl_last

        t_f3 = {}
        for tp in range(2):
            mm = None
            for kc in range(NKF):
                slot = kc % 2
                wl = P.op(Q, "dma_start", out=wdb[slot][:], in_=wdT_sc[kc],
                          dma=f"dwd{slot}",
                          waits=[wd_free[slot]] if wd_free[slot] else prep_sp)
                for tcin in range(2):
                    tc = tp * 2 + tcin
                    ps = (ps_w0, ps_w1)[tcin]
                    if kc == 0:
                        wt = [wl, t_hqT[(NKF - 1, tc)],
                              st["pw"][0], st["pw"][1], st.get("hslv1"),
                              psw_free[tcin] if (ti == 0 and tp == 0) else
                              (t_f3[(0, tcin)] if tp == 1 else prev.get("psw_free%d" % tcin))]
                    else:
                        wt = [wl]
                    for nh in range(2):
                        mm = P.op(T, "matmul", ps[:, nh * 512:(nh + 1) * 512],
                                  hqT_v[:, kc, tc * 128:(tc + 1) * 128],
                                  wdb[slot][:, nh * 512:(nh + 1) * 512],
                                  start=(kc == 0), stop=(kc == NKF - 1),
                                  waits=[w for w in wt if w] if nh == 0 else ())
                wd_free[slot] = mm
            for tcin in range(2):
                tc = tp * 2 + tcin
                ps = (ps_w0, ps_w1)[tcin]
                ev = P.op(V, "tensor_tensor", out=acc[:, tc, :],
                          in0=acc[:, tc, :], in1=ps[:], op=ALU.add,
                          waits=[mm, t_acc0])
                t_f3[(tp, tcin)] = ev

        t_Bb = None
        t_Cb = None
        for n in range(2 * DST):
            bk = n % 2
            ps = (ps_a0, ps_a1)[bk]
            wt = [t_bc2[0], t_bc2[1], st["paf"][bk]]
            if n == 0:
                wt += [t_xn2[-1], t_xq, st["qprev"]]
            if n == DST:
                wt += [last_f2]
            src = bcrB if n < DST else bcrC
            mm = P.op(T, "matmul", ps[:], selb[:, (n % DST) * 128:(n % DST + 1) * 128],
                      src[:], start=True, stop=True, waits=wt)
            if n < DST:
                t_Bb = P.op(V, "tensor_copy", out=Bb_v[:, n, :], in_=ps[:],
                            waits=[mm])
                st["paf"][bk] = t_Bb
            else:
                t_Cb = P.op(V, "tensor_copy", out=Cb_v[:, n - DST, :],
                            in_=ps[:], waits=[mm])
                st["paf"][bk] = t_Cb

        abufs = [argb, dAb]
        iters = [(dc, s_) for dc in range(NDC) for s_ in range(SEQ)]
        t_y = {}
        t_du = {}

        def emit_args(k):
            dc, s_ = iters[k]
            buf = abufs[k % 2]
            w = [t_dt[dc], t_negAb if (ti == 0 and k == 0) else None,
                 sc_prev.get("tree%d" % (k % 2)),
                 sc_prev.get("exp%d" % (k % 2))]
            if ti == 0 and k == 0:
                w.append(t_xq)
            if ti == 0 and k <= 1:
                w.append(t_hsl_last)
            if k <= 1:
                w.append(t_f3[(1, 1)])
                w.append(prev.get("SCANR_free"))
            ta = P.op(V, "tensor_tensor", out=buf[:],
                      in0=dtb_v[:, dc, s_ * 256:(s_ + 1) * 256]
                      .unsqueeze(1).broadcast_to([128, DST, 256]),
                      in1=A_scb[:, dc, :].unsqueeze(2)
                      .broadcast_to([128, DST, 256]),
                      op=ALU.mult, waits=w)
            te = P.op(S, "activation", out=buf.rearrange("p n t -> p (n t)"),
                      in_=buf.rearrange("p n t -> p (n t)"), func=AF.Exp,
                      waits=[ta, sc_prev.get("ldt%d" % (k % 2))])
            sc_prev["exp%d" % (k % 2)] = te
            sc_prev["argt%d" % (k % 2)] = ta
            return te

        t_exp_k = {}
        t_exp_k[0] = emit_args(0)
        for k, (dc, s_) in enumerate(iters):
            buf = abufs[k % 2]
            if s_ == 0:
                t_du[dc] = P.op(V, "tensor_tensor", out=dtusl,
                                in0=dtb_v[:, dc, :], in1=ucb[:, dc, :],
                                op=ALU.mult,
                                waits=[t_dt[dc], t_conv[dc],
                                       sc_prev.get("dbu"),
                                       t_hsl_last if dc == 0 else None])
            tdbu = P.op(V, "tensor_tensor", out=dBub[:],
                        in0=dtusl[:, s_ * 256:(s_ + 1) * 256]
                        .unsqueeze(1).broadcast_to([128, DST, 256]),
                        in1=Bb_v[:, :, s_ * 256:(s_ + 1) * 256],
                        op=ALU.mult,
                        waits=[t_du[dc], t_Bb, sc_prev.get("hC"),
                               sc_prev.get("scan")])
            sc_prev["dbu"] = tdbu
            tsd = P.op(V, "tensor_tensor", out=sdt[:],
                       in0=buf[:, :, 0], in1=hcar[:, dc, :, s_],
                       op=ALU.mult,
                       waits=[t_exp_k[k], t_hc if ti == 0 else None,
                              sc_prev.get("carry")])
            tsd2 = P.op(V, "tensor_tensor", out=dBub[:, :, 0],
                        in0=dBub[:, :, 0], in1=sdt[:], op=ALU.add,
                        waits=[tsd, tdbu])
            tz = P.op(V, "memset", buf[:, :, 0:1], 0.0, waits=[tsd])
            sc_prev["ldt%d" % (k % 2)] = tz
            if k + 1 < len(iters):
                t_exp_k[k + 1] = emit_args(k + 1)
            tscan = P.op(V, "tensor_tensor_scan",
                         out=hb_v.rearrange("p n t -> p (n t)"),
                         data0=buf.rearrange("p n t -> p (n t)"),
                         data1=dBub.rearrange("p n t -> p (n t)"),
                         initial=0.0, op0=ALU.mult, op1=ALU.add,
                         waits=[tsd2, tz, sc_prev.get("hC"),
                                sc_prev.get("carry")])
            sc_prev["ldt%d" % (k % 2)] = tscan
            tcar = P.op(V, "tensor_copy", out=hcar[:, dc, :, s_],
                        in_=hb_v[:, :, 255], waits=[tscan])
            thc = P.op(V, "tensor_tensor", out=dBub[:], in0=hb_v[:],
                       in1=Cb_v[:, :, s_ * 256:(s_ + 1) * 256],
                       op=ALU.mult, waits=[tscan, t_Cb])
            b8 = buf[:, 0:8, :]
            b4 = buf[:, 8:12, :]
            b2 = buf[:, 12:14, :]
            tt8_ = P.op(V, "tensor_tensor", out=b8, in0=dBub[:, 0:8, :],
                        in1=dBub[:, 8:16, :], op=ALU.add,
                        waits=[thc, tscan])
            tt4_ = P.op(V, "tensor_tensor", out=b4, in0=b8[:, 0:4, :],
                        in1=b8[:, 4:8, :], op=ALU.add, waits=[tt8_])
            tt2_ = P.op(V, "tensor_tensor", out=b2, in0=b4[:, 0:2, :],
                        in1=b4[:, 2:4, :], op=ALU.add, waits=[tt4_])
            ty = P.op(V, "tensor_tensor",
                      out=yb_v[:, dc, s_ * 256:(s_ + 1) * 256],
                      in0=b2[:, 0, :], in1=b2[:, 1, :], op=ALU.add,
                      waits=[tt2_,
                             (t_f3[(1, 1)] if k == 0 else None)])
            t_y[dc] = ty
            sc_prev["tree%d" % (k % 2)] = ty
            sc_prev["scan"] = tscan
            sc_prev["carry"] = tcar
            sc_prev["hC"] = thc
            sc_prev["tree"] = ty
            sc_prev["exp"] = t_exp_k[k]
        t_yg = {}
        for dc in range(NDC):
            g1 = P.op(V, "scalar_tensor_tensor", out=ygb_v[:, dc, :],
                      in0=ucb[:, dc, :], scalar=col(D_sc, dc),
                      in1=yb_v[:, dc, :], op0=ALU.mult, op1=ALU.add,
                      waits=[t_y[dc]] + ([sc_prev["exp"], sc_prev["tree"],
                                          prev.get("SCB_m4")] if dc == 0 else []))
            g2 = P.op(V, "tensor_tensor", out=ygb_v[:, dc, :],
                      in0=ygb_v[:, dc, :], in1=szb[:, dc, :], op=ALU.mult,
                      waits=[g1, t_sz[dc]])
            t_yg[dc] = g2

        t_m4 = {}
        for tp in range(2):
            mm = None
            for dc in range(NDC):
                slot = dc % 2
                wl = P.op(Q, "dma_start", out=wob[slot][:], in_=woutT_sc[dc],
                          dma=f"dwo{slot}",
                          waits=[wo_free[slot]] if wo_free[slot] else prep_sp)
                for tcin in range(2):
                    tc = tp * 2 + tcin
                    ps = (ps_w0, ps_w1)[tcin]
                    if dc == 0:
                        wt = [wl, t_yg[NDC - 1],
                              t_f3[(1, tcin)] if tp == 0 else t_m4[(0, tcin)]]
                    else:
                        wt = [wl]
                    for nh in range(2):
                        mm = P.op(T, "matmul", ps[:, nh * 512:(nh + 1) * 512],
                                  ygb_v[:, dc, tc * 128:(tc + 1) * 128],
                                  wob[slot][:, nh * 512:(nh + 1) * 512],
                                  start=(dc == 0), stop=(dc == NDC - 1),
                                  waits=wt if nh == 0 else ())
                wo_free[slot] = mm
            for tcin in range(2):
                tc = tp * 2 + tcin
                ps = (ps_w0, ps_w1)[tcin]
                ev = P.op(V, "tensor_tensor", out=acc[:, tc, :],
                          in0=acc[:, tc, :], in1=ps[:], op=ALU.add,
                          waits=[mm, t_f3[(tc // 2, tc % 2)]])
                t_m4[(tp, tcin)] = ev
                prev["psw_free%d" % tcin] = ev

        m4all = [t_m4[(0, 0)], t_m4[(0, 1)], t_m4[(1, 0)], t_m4[(1, 1)]]
        stt = None
        for j in range(4):
            s_, hf = j // 2, j % 2
            stt = P.op(G, "dma_start",
                       out=out_t[s_, t0 + hf * 128: t0 + (hf + 1) * 128, :],
                       in_=acc[:, j, :], dma="dso",
                       waits=m4all if j == 0 else ())

        prev["acc_free"] = ("dso", P.count["dso"])
        prev["DtR_free"] = sc_prev["argt%d" % ((len(iters) - 1) % 2)]
        prev["DtR_free2"] = t_xq
        prev["BbR_free"] = sc_prev["dbu"]
        prev["BbR_free2"] = t_xq
        prev["CbR_free"] = sc_prev["hC"]
        prev["SCANR_free"] = wo_free[0]
        prev["SCB_m4"] = None
        prev["hqTR_free"] = t_yg[NDC - 1]
        prev["hqTR_free2"] = None
        prev["szb_free"] = t_yg[NDC - 1]
        prev["ucb_free"] = t_yg[NDC - 1]
        prev["bn_free"] = t_mv
        prev["ln_free"] = t_rat
        prev["uhead_free"] = prev["uhead_free"]

    P.op(G, "engine_nop", waits=[("dso", P.count.get("dso", 0))])

    for name in sorted(P.semnames):
        P.sems[name] = stack.enter_context(nc.semaphore(name))
    block = stack.enter_context(nc.Block())
    P.emit(block)
    stack.close()
    return nc


def _consts_array():
    c = np.zeros((128, 8), np.float32)
    c[:, 0] = 1.0
    c[:, 1] = EPS
    c[:, 2] = 1.0 + EPS
    c[:, 3] = EPS * EPS
    c[:, 4] = -0.5
    return c


def _sel_array():
    s = np.zeros((16, 16 * 128), ml_dtypes.bfloat16)
    for n in range(16):
        s[n, n * 128:(n + 1) * 128] = 1
    return s


def _base_inputs(inputs):
    return {
        "ln_g": inputs["L0_attn_ln_g"], "ln_b": inputs["L0_attn_ln_b"],
        "w_in": inputs["L0_attn_W_in"], "conv_w": inputs["L0_attn_conv_w"],
        "conv_b": inputs["L0_attn_conv_b"], "w_xp": inputs["L0_attn_W_xp"],
        "w_dt": inputs["L0_attn_W_dt"], "b_dt": inputs["L0_attn_b_dt"],
        "a_log": inputs["L0_attn_A_log"], "d_vec": inputs["L0_attn_D"],
        "w_out": inputs["L0_attn_W_out"], "wg": inputs["L0_ffn_w_gate"],
        "wu": inputs["L0_ffn_w_up"], "wd": inputs["L0_ffn_w_down"],
        "eye_b": np.eye(128, dtype=ml_dtypes.bfloat16),
        "ones1f": np.ones((1, 128), np.float32),
        "sel": _sel_array(),
        "csts": _consts_array(),
    }


_NC_CACHE = {}


def kernel(**inputs):
    if "prog" not in _NC_CACHE:
        _NC_CACHE["prog"] = build_program()
    nc = _NC_CACHE["prog"]
    x = np.ascontiguousarray(inputs["L0_attn_x"])
    base = _base_inputs(inputs)
    in_maps = []
    for c in range(NCORES):
        m = dict(base)
        m["x"] = x[c * SEQ:(c + 1) * SEQ]
        in_maps.append(m)
    res = run_bass_kernel_spmd(nc, in_maps, core_ids=list(range(NCORES)))
    out = np.concatenate([r["out"] for r in res.results], axis=0)
    return out.astype(np.float32)


# version 07
# speedup vs baseline: 1.0107x; 1.0107x over previous
import sys
sys.path.insert(0, "/opt/trn_rl_repo")
import numpy as np
import ml_dtypes

import concourse.bass as bass
import concourse.mybir as mybir
from concourse.bass_utils import run_bass_kernel_spmd

F32 = mybir.dt.float32
BF16 = mybir.dt.bfloat16
AF = mybir.ActivationFunctionType
ALU = mybir.AluOpType
AX = mybir.AxisListType

B, L, D = 16, 2048, 1024
NCORES = 8
SEQ = 2
DI = 2048
DST = 16
DFF = 2816
TC = 256
TOK = SEQ * TC
NT = L // TC
NDC = DI // 128
NKD = D // 128
NKF = DFF // 128
EPS = 1e-5
MID = [0.25, 0.75, 1.25, 1.75, 2.5, 3.5, 5.0]
MIDW = [0.5, 0.5, 0.5, 0.5, 1.0, 1.0, 2.0]

CFG_NT = NT


class Plan:
    ENGS = ("v", "s", "t", "g", "q")

    def __init__(self):
        self.ops = {e: [] for e in self.ENGS}
        self.count = {}
        self.semnames = set("S" + e for e in self.ENGS)
        self.sems = {}

    def dma_sem(self, name):
        self.semnames.add(name)

    def op(self, eng, method, *args, waits=(), dma=None, **kw):
        import traceback
        fr = traceback.extract_stack(limit=4)
        tbs = ";".join(f"{f.lineno}" for f in fr[:-1])
        w = tuple(t for t in waits if t is not None)
        if dma is not None:
            self.count[dma] = self.count.get(dma, 0) + 16
            tok = (dma, self.count[dma])
        else:
            s = "S" + eng
            self.count[s] = self.count.get(s, 0) + 1
            tok = (s, self.count[s])
        self.ops[eng].append((method, args, kw, w, tok, tbs))
        return tok

    def emit(self, block):
        plan = self
        secs = {"v": block.vector, "s": block.scalar, "t": block.tensor,
                "g": block.gpsimd, "q": block.sync}

        def make_body(e):
            own = "S" + e

            def body(engine):
                waited = {}
                for (method, args, kw, waits, tok, src) in plan.ops[e]:
                    need = {}
                    for (sname, val) in waits:
                        if val > need.get(sname, -1):
                            need[sname] = val
                    for sname in sorted(need):
                        val = need[sname]
                        if waited.get(sname, -1) >= val:
                            continue
                        engine.wait_ge(plan.sems[sname], val)
                        waited[sname] = val
                    try:
                        instr = getattr(engine, method)(*args, **kw)
                    except Exception as ex:
                        raise RuntimeError(f"emit failed: {method} planned at {src}: {ex}") from ex
                    sname, _ = tok
                    instr.then_inc(plan.sems[sname], 1 if sname == own else 16)
            return body

        for e in self.ENGS:
            if self.ops[e]:
                secs[e](make_body(e))


def col(ap, *idx):
    sl = tuple([slice(None)] + [slice(i, i + 1) for i in idx])
    return ap[sl]


def cdiv(a, b):
    return (a + b - 1) // b


def build_program():
    nc = bass.Bass("TRN2", target_bir_lowering=False)
    P = Plan()
    V, S, T, G, Q = "v", "s", "t", "g", "q"

    def din(name, shape, dt=F32):
        return nc.dram_tensor(name, shape, dt, kind="ExternalInput").ap()

    x_in = din("x", [SEQ, L, D])
    ln_g = din("ln_g", [D]); ln_b = din("ln_b", [D])
    w_in = din("w_in", [2 * DI, D])
    conv_w = din("conv_w", [DI, 4]); conv_b = din("conv_b", [DI])
    w_xp = din("w_xp", [96, DI])
    w_dt = din("w_dt", [DI, 64]); b_dt = din("b_dt", [DI])
    a_log = din("a_log", [DI, DST]); d_vec = din("d_vec", [DI])
    w_out = din("w_out", [D, DI])
    wg_in = din("wg", [DFF, D]); wu_in = din("wu", [DFF, D])
    wd_in = din("wd", [D, DFF])
    eye_b = din("eye_b", [128, 128], BF16)
    ones1f = din("ones1f", [1, 128], F32)
    sel_in = din("sel", [16, 16 * 128], BF16)
    csts = din("csts", [128, 8], F32)
    C_ONE, C_EPS, C_LNS, C_LNB, C_NEGH = 0, 1, 2, 3, 4

    out_t = nc.dram_tensor("out", [SEQ, L, D], F32, kind="ExternalOutput").ap()

    def scr(name, shape, dt=BF16):
        return nc.dram_tensor(name, shape, dt, kind="Internal").ap()

    winT_sc = scr("winT_sc", [32, 128, NKD, 128])
    woutT_sc = scr("woutT_sc", [NDC, 128, D])
    wgT_sc = scr("wgT_sc", [NKD, 128, DFF])
    wuT_sc = scr("wuT_sc", [NKD, 128, DFF])
    wdT_sc = scr("wdT_sc", [NKF, 128, D])

    from contextlib import ExitStack
    stack = ExitStack()

    def sb(name, shape, dt=F32):
        return stack.enter_context(nc.sbuf_tensor(name, shape, dt))

    def psum(name, shape, dt=F32):
        return stack.enter_context(nc.psum_tensor(name, shape, dt))

    cst = sb("cst", [128, 8])
    eyeb = sb("eyeb", [128, 128], BF16)
    onesf = sb("onesf", [1, 128])
    selb = sb("selb", [16, 16 * 128], BF16)
    A_sc = sb("A_sc", [128, NDC, DST])
    A_scb = sb("A_scb", [128, NDC, DST], BF16)
    D_sc = sb("D_sc", [128, NDC])
    bdt_sc = sb("bdt_sc", [128, NDC])
    cvb_sc = sb("cvb_sc", [128, NDC])
    cvw_sc = sb("cvw_sc", [128, NDC, 4])
    g_bc = sb("g_bc", [128, D])
    b_bc = sb("b_bc", [128, D])
    wxpT = sb("wxpT", [128, NDC, 96], BF16)
    wdtT = sb("wdtT", [64, DI], BF16)
    hcar = sb("hcar", [128, NDC, DST, SEQ])
    ctail = sb("ctail", [128, NDC, SEQ, 3])
    bnb = sb("bnb", [128, 4, 12])
    mv = sb("mv", [128, 4, 2])
    rr = sb("rr", [128, 4])
    rho = sb("rho", [128, 4])
    rat = sb("rat", [128, 4])
    lnt = sb("lnt", [128, 4])
    qm = sb("qm", [128, 64])
    qs = sb("qs", [128, 64])
    qr = sb("qr", [128, 64])
    qsb = sb("qsb", [128, 64], BF16)
    sdt = sb("sdt", [128, DST])
    uhead = sb("uhead", [128, NDC, SEQ, 8])
    dl_b = sb("dl_b", [64, TOK], BF16)
    bcrB = sb("bcrB", [16, TOK], BF16)
    bcrC = sb("bcrC", [16, TOK], BF16)

    acc = sb("acc", [128, 4, D])
    szb = sb("szb", [128, NDC, TOK], BF16)
    ucb = sb("ucb", [128, NDC, TOK], BF16)

    SCANR = sb("SCANR", [128, 8192])
    BbR = sb("BbR", [128, 8192], BF16)
    CbR = sb("CbR", [128, 8192], BF16)
    DtR = sb("DtR", [128, 8192], BF16)
    hqTR = sb("hqTR", [128, 11264], BF16)

    winb = [sb(f"winb{i}", [128, NKD, 128], BF16) for i in range(2)]
    wgb = sb("wgb", [128, NKD, 512], BF16)
    wub = sb("wub", [128, NKD, 512], BF16)
    wdb = [sb(f"wdb{i}", [128, D], BF16) for i in range(2)]
    wob = [sb(f"wob{i}", [128, D], BF16) for i in range(2)]

    SCB = SCANR[:].bitcast(BF16)
    u_v = SCANR[:].rearrange("p (c t) -> p c t", c=NDC)
    pp32 = [SCANR[:, 0:4096], SCANR[:, 4096:8192]]
    qv_v = SCANR[:, 0:1024]
    qa2_v = SCANR[:, 1024:2048]
    qABCD = [SCB[:, 4096 + i * 1024: 4096 + (i + 1) * 1024] for i in range(4)]
    p1s = SCANR[:, 0:1024]
    p2s = SCANR[:, 1024:2048]
    argb = SCB[:, 0:4096].rearrange("p (n t) -> p n t", n=DST)
    dAb = SCB[:, 4096:8192].rearrange("p (n t) -> p n t", n=DST)
    dBub = SCB[:, 8192:12288].rearrange("p (n t) -> p n t", n=DST)
    hb_v = SCB[:, 12288:16384].rearrange("p (n t) -> p n t", n=DST)
    tr8 = SCB[:, 0:2048].rearrange("p (n t) -> p n t", n=8)
    tr4 = SCB[:, 2048:3072].rearrange("p (n t) -> p n t", n=4)
    tr2 = SCB[:, 3072:3584].rearrange("p (n t) -> p n t", n=2)
    ygb_v = SCB[:, 0:8192].rearrange("p (c t) -> p c t", c=NDC)

    BbF = BbR[:].bitcast(F32)
    xqb_v = BbR[:, 0:4096].rearrange("p (j d) -> p j d", j=4)
    xn2b_v = BbR[:, 4096:8192].rearrange("p (j d) -> p j d", j=4)
    ca_lo = BbF[:].rearrange("p (c t) -> p c t", c=8)
    Bb_v = BbR[:].rearrange("p (n t) -> p n t", n=DST)
    ppb = [BbR[:, 0:4096], BbR[:, 4096:8192]]
    hslp = [BbF[:, 0:512], BbF[:, 512:1024]]
    hsl = hslp[0]
    hqlp = [BbR[:, 2048:2560], BbR[:, 2560:3072]]

    CbF = CbR[:].bitcast(F32)
    xn_v = CbF[:].rearrange("p (j d) -> p j d", j=4)
    xqT_v = CbR[:, 0:4096].rearrange("p (k t) -> p k t", k=NKD)
    Cb_v = CbR[:].rearrange("p (n t) -> p n t", n=DST)
    ppq = [CbR[:, 0:4096], CbR[:, 4096:8192]]

    DtF = DtR[:].bitcast(F32)
    xt_v = DtF[:].rearrange("p (j d) -> p j d", j=4)
    ca_hi = DtF[:].rearrange("p (c t) -> p c t", c=8)
    dtb_v = DtR[:].rearrange("p (c t) -> p c t", c=NDC)

    hqF = hqTR[:].bitcast(F32)
    xn2T_v = hqTR[:, 0:4096].rearrange("p (k t) -> p k t", k=NKD)
    hqT_v = hqTR[:].rearrange("p (k t) -> p k t", k=NKF)
    yb_v = hqTR[:, 0:8192].rearrange("p (c t) -> p c t", c=NDC)
    dtusl = hqTR[:, 8192:8704]
    pqv = hqF[:, 0:1024]
    pqa2 = hqF[:, 1024:2048]
    pqABCD = [hqTR[:, 4096 + i * 1024: 4096 + (i + 1) * 1024] for i in range(4)]

    ps_a0 = psum("ps_a0", [128, 512]); ps_a1 = psum("ps_a1", [128, 512])
    ps_w0 = psum("ps_w0", [128, 1024]); ps_w1 = psum("ps_w1", [128, 1024])
    ps_t0 = psum("ps_t0", [128, 1024], BF16)
    ps_t1 = psum("ps_t1", [128, 1024], BF16)
    ps_ts = [ps_t0, ps_t1]

    for s in ["dld0", "dld1", "dsp0", "dsp1", "dx", "dso", "dwin0", "dwin1",
              "dwg", "dwu", "dwd0", "dwd1", "dwo0", "dwo1"]:
        P.dma_sem(s)

    st = {"tp": 0, "tp_ev": {}, "paf": {0: None, 1: None}, "pw": {0: None, 1: None}, "pab": 0, "qprev": None}

    def transpose_group(blocks, evacs, wt):
        k = st["tp"]
        pst = ps_ts[k % 2]
        w = list(wt)
        if (k - 2) in st["tp_ev"]:
            w.append(st["tp_ev"][k - 2])
        mm = None
        for (j, src) in blocks:
            mm = P.op(T, "transpose", pst[:, j * 128:(j + 1) * 128], src,
                      eyeb[:], waits=(w + [ld_all]) if mm is None else ())
        ev = None
        for (dst, c0, c1) in evacs:
            ev = P.op(S, "copy", out=dst, in_=pst[:, c0:c1], waits=[mm])
        st["tp_ev"][k] = ev
        st["tp"] = k + 1
        return ev

    def quant(src_ap, dst_ap, nblk, wtoks, scratch):
        qvv, qaa, (qa, qb_, qc, qd) = scratch
        n = nblk * 16
        mm_ = qm[:, 0:nblk]; ss = qs[:, 0:nblk]; rs = qr[:, 0:nblk]
        sbv = qsb[:, 0:nblk]
        vv = qvv[:, 0:n]; aa = qaa[:, 0:n]
        qa = qa[:, 0:n]; qb_ = qb_[:, 0:n]; qc = qc[:, 0:n]; qd = qd[:, 0:n]
        w0 = list(wtoks) + ([st["qprev"]] if st["qprev"] else [])
        src3 = src_ap.rearrange("p (b j) -> p b j", j=16)
        t_m = P.op(V, "tensor_reduce", out=mm_, in_=src3, axis=AX.X, op=ALU.max,
                   apply_absolute_value=True, waits=w0)
        t_s = P.op(V, "tensor_scalar", out=ss, in0=mm_, scalar1=1.0 / 6.0,
                   scalar2=1e-12, op0=ALU.mult, op1=ALU.max, waits=[t_m])
        t_r = P.op(V, "reciprocal", out=rs, in_=ss, waits=[t_s])
        t_sb = P.op(V, "tensor_copy", out=sbv, in_=ss, waits=[t_s])
        rb = rs.unsqueeze(2).broadcast_to([128, nblk, 16])
        t_v = P.op(V, "tensor_tensor", out=vv.rearrange("p (b j) -> p b j", j=16),
                   in0=src3, in1=rb, op=ALU.mult, waits=[t_r])
        t_a = P.op(S, "activation", out=aa, in_=vv, func=AF.Abs,
                   waits=[t_v])
        def cmp(dst, i, w):
            return P.op(V, "tensor_scalar", out=dst, in0=aa, scalar1=MID[i],
                        scalar2=MIDW[i], op0=ALU.is_gt, op1=ALU.mult, waits=w)
        c0 = cmp(qa, 0, [t_a]); c1 = cmp(qb_, 1, [t_a])
        s01 = P.op(V, "tensor_tensor", out=qc, in0=qa, in1=qb_, op=ALU.add,
                   waits=[c0, c1])
        c2 = cmp(qa, 2, [s01]); c3 = cmp(qb_, 3, [s01])
        s23 = P.op(V, "tensor_tensor", out=qd, in0=qa, in1=qb_, op=ALU.add,
                   waits=[c2, c3])
        s03 = P.op(V, "tensor_tensor", out=qc, in0=qc, in1=qd, op=ALU.add,
                   waits=[s23])
        c4 = cmp(qa, 4, [s03]); c5 = cmp(qb_, 5, [s03])
        s45 = P.op(V, "tensor_tensor", out=qd, in0=qa, in1=qb_, op=ALU.add,
                   waits=[c4, c5])
        c6 = cmp(qa, 6, [s45])
        s46 = P.op(V, "tensor_tensor", out=qd, in0=qd, in1=qa, op=ALU.add,
                   waits=[c6])
        lvl = P.op(V, "tensor_tensor", out=qc, in0=qc, in1=qd, op=ALU.add,
                   waits=[s46])
        sg = P.op(V, "tensor_scalar", out=qb_, in0=vv, scalar1=0.0,
                  scalar2=None, op0=ALU.is_ge, waits=[lvl])
        sg2 = P.op(V, "tensor_scalar", out=qb_, in0=qb_, scalar1=2.0,
                   scalar2=-1.0, op0=ALU.mult, op1=ALU.add, waits=[sg])
        tw = P.op(V, "tensor_tensor", out=qa, in0=qc, in1=qb_, op=ALU.mult,
                  waits=[sg2])
        sbb = sbv.unsqueeze(2).broadcast_to([128, nblk, 16])
        tq = P.op(V, "tensor_tensor",
                  out=dst_ap.rearrange("p (b j) -> p b j", j=16),
                  in0=qa.rearrange("p (b j) -> p b j", j=16), in1=sbb,
                  op=ALU.mult, waits=[tw, t_sb])
        st["qprev"] = tq
        return tq

    MAIN_SCR = (qv_v.unsqueeze(0)[0:0] if False else None)
    main_scr = (SCANR[:, 0:1024], SCANR[:, 1024:2048], qABCD)
    prep_scr = (pqv, pqa2, pqABCD)

    dl = lambda **kw: P.op(Q, "dma_start", **kw)
    dl(out=cst[:], in_=csts, dma="dld0")
    dl(out=eyeb[:], in_=eye_b, dma="dld0")
    dl(out=onesf[:], in_=ones1f, dma="dld0")
    dl(out=selb[:], in_=sel_in, dma="dld0")
    dl(out=A_sc[:], in_=a_log.rearrange("(c p) n -> p c n", p=128), dma="dld0")
    dl(out=D_sc[:], in_=d_vec.rearrange("(c p) -> p c", p=128), dma="dld0", allow_slow_non_contiguous=True)
    dl(out=bdt_sc[:], in_=b_dt.rearrange("(c p) -> p c", p=128), dma="dld0", allow_slow_non_contiguous=True)
    dl(out=cvb_sc[:], in_=conv_b.rearrange("(c p) -> p c", p=128), dma="dld0", allow_slow_non_contiguous=True)
    dl(out=cvw_sc[:], in_=conv_w.rearrange("(c p) k -> p c k", p=128),
       dma="dld0")
    dl(out=g_bc[0:1, :], in_=ln_g.unsqueeze(0), dma="dld0")
    ld_all = dl(out=b_bc[0:1, :], in_=ln_b.unsqueeze(0), dma="dld0")

    t_exp = P.op(S, "activation", out=A_sc[:], in_=A_sc[:], func=AF.Exp,
                 waits=[ld_all])
    t_negA = P.op(V, "tensor_scalar_mul", out=A_sc[:], in0=A_sc[:],
                  scalar1=-1.0, waits=[t_exp])
    t_negAb = P.op(V, "tensor_copy", out=A_scb[:], in_=A_sc[:], waits=[t_negA])

    brc = []
    for half in range(2):
        brc.append(P.op(T, "matmul", ps_w0[:, half * 512:(half + 1) * 512],
                        onesf[:], g_bc[0:1, half * 512:(half + 1) * 512],
                        start=True, stop=True, waits=[ld_all]))
        brc.append(P.op(T, "matmul", ps_w1[:, half * 512:(half + 1) * 512],
                        onesf[:], b_bc[0:1, half * 512:(half + 1) * 512],
                        start=True, stop=True, waits=[ld_all]))
    t_gbc = P.op(V, "tensor_copy", out=g_bc[:], in_=ps_w0[:], waits=[brc[-1]])
    t_bbc = P.op(V, "tensor_copy", out=b_bc[:], in_=ps_w1[:], waits=[brc[-1]])

    t_hc = P.op(V, "memset", hcar[:], 0.0)
    t_ct = P.op(V, "memset", ctail[:], 0.0)

    pst8 = {"lc": [None, None], "bc": [None, None], "qc": [None, None], "i": 0}

    def prep_slab(src_dram, n_kblk, quant_nblk, stores, part=128):
        i = pst8["i"] % 2
        n = n_kblk * 128
        tl = P.op(Q, "dma_start", out=pp32[i][0:part, 0:n], in_=src_dram,
                  dma=f"dld{i}", waits=[pst8["lc"][i], ld_all])
        if quant_nblk:
            tc_ = None
            for q0 in range(0, n, 1024):
                q1 = min(q0 + 1024, n)
                tc_ = quant(pp32[i][0:part, q0:q1], ppb[i][0:part, q0:q1],
                            (q1 - q0) // 16, [tl, pst8["bc"][i]], prep_scr)
        else:
            tc_ = P.op(V, "tensor_copy", out=ppb[i][0:part, 0:n],
                       in_=pp32[i][0:part, 0:n], waits=[tl, pst8["bc"][i]])
        pst8["lc"][i] = tc_
        evs = []
        lastmm = None
        for g0 in range(0, n_kblk, 4):
            gsz = min(4, n_kblk - g0)
            k = st["tp"]
            pst = ps_ts[k % 2]
            w = [tc_, pst8["qc"][i]]
            if (k - 2) in st["tp_ev"]:
                w.append(st["tp_ev"][k - 2])
            mm = None
            for j in range(gsz):
                mm = P.op(T, "transpose", pst[:, j * 128:(j + 1) * 128],
                          ppb[i][0:part, (g0 + j) * 128:(g0 + j + 1) * 128],
                          eyeb[:],
                          waits=([t for t in w if t] + [ld_all])
                          if mm is None else ())
            ev = P.op(V, "tensor_copy",
                      out=ppq[i][:, g0 * 128:(g0 + gsz) * 128],
                      in_=pst[:, 0:gsz * 128], waits=[mm])
            st["tp_ev"][k] = ev
            st["tp"] = k + 1
            evs.append(ev)
            lastmm = mm
        pst8["bc"][i] = lastmm
        stok = None
        for (dst, c0, c1) in stores:
            src = ppq[i][:, c0:c1]
            if dst.ndim == 3:
                src = src.rearrange("p (k e) -> p k e", e=128)
            stok = P.op(Q, "dma_start", out=dst, in_=src,
                        dma=f"dsp{i}", waits=evs)
        pst8["qc"][i] = stok
        pst8["i"] += 1
        return stok

    for ec in range(32):
        prep_slab(w_in[ec * 128:(ec + 1) * 128, :], NKD, 0,
                  [(winT_sc[ec].rearrange("p c e -> p (c e)"), 0, NKD * 128)])
    for oc in range(8):
        prep_slab(w_out[oc * 128:(oc + 1) * 128, :], NDC, 0,
                  [(woutT_sc.rearrange("c p f -> p c f")[:, :, oc * 128:(oc + 1) * 128], 0, NDC * 128)])
    for (w_dram, dstsc) in ((wg_in, wgT_sc), (wu_in, wuT_sc)):
        for fc in range(NKF):
            prep_slab(w_dram[fc * 128:(fc + 1) * 128, :], NKD, 64,
                      [(dstsc.rearrange("k p f -> p k f")[:, :, fc * 128:(fc + 1) * 128], 0, NKD * 128)])
    for oc in range(8):
        prep_slab(wd_in[oc * 128:(oc + 1) * 128, :], NKF, NKF * 8,
                  [(wdT_sc.rearrange("k p f -> p k f")[:, :, oc * 128:(oc + 1) * 128], 0, NKF * 128)])

    i = pst8["i"] % 2
    tl = P.op(Q, "dma_start", out=pp32[i][0:96, 0:DI], in_=w_xp,
              dma=f"dld{i}", waits=[pst8["lc"][i]])
    tcc = P.op(V, "tensor_copy", out=ppb[i][0:96, 0:DI],
               in_=pp32[i][0:96, 0:DI], waits=[tl, pst8["bc"][i]])
    t_wxp = None
    for g0 in range(0, NDC, 4):
        k = st["tp"]; pst = ps_ts[k % 2]
        w = [tcc] + ([st["tp_ev"][k - 2]] if (k - 2) in st["tp_ev"] else [])
        mm = None
        for j in range(4):
            dc = g0 + j
            mm = P.op(T, "transpose", pst[:, j * 128:j * 128 + 96],
                      ppb[i][0:96, dc * 128:(dc + 1) * 128], eyeb[0:96, 0:96],
                      waits=(w + [ld_all]) if mm is None else ())
        for j in range(4):
            dc = g0 + j
            t_wxp = P.op(V, "tensor_copy", out=wxpT[:, dc, :],
                         in_=pst[:, j * 128:j * 128 + 96], waits=[mm])
        st["tp_ev"][k] = t_wxp
        st["tp"] = k + 1
    pst8["lc"][i] = tcc
    pst8["bc"][i] = mm
    pst8["i"] += 1

    i = pst8["i"] % 2
    tl = P.op(Q, "dma_start",
              out=pp32[i][:, 0:NDC * 64].rearrange("p (c r) -> p c r", c=NDC),
              in_=w_dt.rearrange("(c p) r -> p c r", p=128),
              dma=f"dld{i}", waits=[pst8["lc"][i]])
    tcc = P.op(V, "tensor_copy", out=ppb[i][:, 0:NDC * 64],
               in_=pp32[i][:, 0:NDC * 64], waits=[tl, pst8["bc"][i]])
    t_wdt = None
    for g0 in range(0, NDC, 4):
        k = st["tp"]; pst = ps_ts[k % 2]
        w = [tcc] + ([st["tp_ev"][k - 2]] if (k - 2) in st["tp_ev"] else [])
        mm = None
        for j in range(4):
            dc = g0 + j
            mm = P.op(T, "transpose", pst[0:64, j * 128:(j + 1) * 128],
                      ppb[i][:, dc * 64:(dc + 1) * 64], eyeb[:],
                      waits=(w + [ld_all]) if mm is None else ())
        t_wdt = P.op(V, "tensor_copy",
                     out=wdtT[:, g0 * 128:(g0 + 4) * 128],
                     in_=pst[0:64, 0:512], waits=[mm])
        st["tp_ev"][k] = t_wdt
        st["tp"] = k + 1
    pst8["lc"][i] = tcc
    pst8["bc"][i] = mm
    pst8["i"] += 1

    prep_sp = [("dsp0", P.count.get("dsp0", 0)), ("dsp1", P.count.get("dsp1", 0))]
    prep_v = ("Sv", P.count["Sv"])

    prev = {}
    ffn_pa_free = [t_gbc, t_bbc]
    psw_free = [t_gbc, t_bbc]
    win_free = [None, None]
    wg_free = None; wu_free = None
    wd_free = [None, None]
    wo_free = [None, None]
    sc_prev = {}

    for ti in range(CFG_NT):
        t0 = ti * TC
        xw = list(prep_sp) + [prev.get("DtR_free")]
        t_x = None
        for s_ in range(SEQ):
            t_x = P.op(Q, "dma_start",
                       out=xt_v[:, 2 * s_:2 * s_ + 2, :],
                       in_=x_in.rearrange("s (a p) d -> s p a d", p=128)
                       [s_][:, 2 * ti:2 * ti + 2, :],
                       dma="dx", waits=xw if s_ == 0 else ())

        bn = []
        for j in range(4):
            bn.append(P.op(V, "bn_stats", out=bnb[:, j, 0:6],
                           in_=xt_v[:, j, 0:512],
                           waits=[t_x, prev.get("bn_free"),
                                  prep_v if ti == 0 else None]))
            bn.append(P.op(V, "bn_stats", out=bnb[:, j, 6:12],
                           in_=xt_v[:, j, 512:1024], waits=[]))
        t_mv = None
        for j in range(4):
            t_mv = P.op(V, "bn_aggr", out=mv[:, j, :], in_=bnb[:, j, :],
                        waits=[bn[-1]] if j == 0 else ())
        t_ln1 = P.op(S, "activation", out=lnt[:], in_=mv[:, :, 1], func=AF.Ln,
                     bias=col(cst, C_EPS),
                     waits=[t_mv, ld_all, prev.get("ln_free")])
        t_r = P.op(S, "activation", out=rr[:], in_=lnt[:], func=AF.Exp,
                   scale=col(cst, C_NEGH), waits=[t_ln1])
        t_ln2 = P.op(S, "activation", out=lnt[:], in_=mv[:, :, 1], func=AF.Ln,
                     scale=col(cst, C_LNS), bias=col(cst, C_LNB), waits=[t_r])
        t_rho = P.op(S, "activation", out=rho[:], in_=lnt[:], func=AF.Exp,
                     scale=col(cst, C_NEGH), waits=[t_ln2])
        t_rci = P.op(V, "reciprocal", out=lnt[:], in_=rr[:],
                     waits=[t_rho, t_mv])
        t_rat = P.op(V, "tensor_tensor", out=rat[:], in0=rho[:], in1=lnt[:],
                     op=ALU.mult, waits=[t_rci])

        t_xn = []
        t_xn2 = []
        for j in range(4):
            a = P.op(V, "tensor_scalar", out=xn_v[:, j, :], in0=xt_v[:, j, :],
                     scalar1=col(mv, j, 0), scalar2=col(rr, j),
                     op0=ALU.subtract, op1=ALU.mult,
                     waits=[t_r, prev.get("CbR_free")])
            t_xn.append(a)
            b_ = P.op(V, "tensor_tensor", out=acc[:, j, :], in0=xt_v[:, j, :],
                      in1=xn_v[:, j, :], op=ALU.add,
                      waits=[a, prev.get("acc_free")])
            c_ = P.op(V, "tensor_scalar", out=p1s, in0=xn_v[:, j, :],
                      scalar1=col(rat, j), scalar2=None, op0=ALU.mult,
                      waits=[t_rat, a, prev.get("SCANR_free") if (ti and j == 0) else None])
            d_ = P.op(V, "tensor_tensor", out=p2s, in0=p1s, in1=g_bc[:],
                      op=ALU.mult, waits=[c_, t_gbc])
            e_ = P.op(V, "tensor_tensor", out=xn2b_v[:, j, :], in0=p2s,
                      in1=b_bc[:], op=ALU.add,
                      waits=[d_, t_bbc, prev.get("BbR_free") if j == 0 else None])
            t_xn2.append(e_)
        t_acc0 = b_

        t_xq = None
        for j in range(4):
            t_xq = quant(xn_v[:, j, :], xqb_v[:, j, :], 64,
                         [t_xn[j], t_xn2[-1] if j == 0 else None], main_scr)

        t_xn2T = None
        for kc in range(NKD):
            t_xn2T = transpose_group(
                [(j, xn2b_v[:, j, kc * 128:(kc + 1) * 128]) for j in range(4)],
                [(xn2T_v[:, kc, :], 0, 512)],
                [t_xn2[-1], prev.get("hqTR_free") if kc == 0 else None])
        t_xqT = None
        for kc in range(NKD):
            t_xqT = transpose_group(
                [(j, xqb_v[:, j, kc * 128:(kc + 1) * 128]) for j in range(4)],
                [(xqT_v[:, kc, :], 0, 512)],
                [t_xq, t_xn[3] if kc == 0 else None])

        t_u = {}
        t_sz = {}
        ec_order = list(range(16, 32)) + list(range(16))
        for eci, ec in enumerate(ec_order):
            slot = eci % 2
            wl = P.op(Q, "dma_start",
                      out=winb[slot][:],
                      in_=winT_sc[ec].rearrange("p c e -> p (c e)"),
                      dma=f"dwin{slot}",
                      waits=[win_free[slot]] if win_free[slot] else prep_sp)
            bk = eci % 2
            ps = (ps_a0, ps_a1)[bk]
            wt = [wl, t_xn2T,
                  ffn_pa_free[bk] if (ti == 0 and eci < 2) else None,
                  st["paf"][bk]]
            mm = None
            for kc in range(NKD):
                mm = P.op(T, "matmul", ps[:], winb[slot][:, kc, :],
                          xn2T_v[:, kc, :], start=(kc == 0),
                          stop=(kc == NKD - 1), waits=wt if kc == 0 else ())
            if ec < 16:
                ev = P.op(S, "copy", out=u_v[:, ec, :], in_=ps[:],
                          waits=[mm, t_xq, prev.get("SCANR_free")])
                t_u[ec] = ev
            else:
                ev = P.op(S, "activation", out=szb[:, ec - 16, :], in_=ps[:],
                          func=AF.Silu, waits=[mm, prev.get("szb_free")])
                t_sz[ec - 16] = ev
            st["paf"][bk] = ev
            win_free[slot] = mm

        u4 = u_v.rearrange("p c (s t) -> p c s t", s=SEQ)
        th1 = P.op(V, "tensor_copy", out=uhead[:, :, :, 0:3], in_=ctail[:],
                   waits=[t_ct if ti == 0 else prev.get("ctail_w"),
                          prev.get("uhead_free")] + [t_u[ec] for ec in range(16)])
        th2 = P.op(V, "tensor_copy", out=uhead[:, :, :, 3:8],
                   in_=u4[:, :, :, 0:5], waits=[th1])
        t_tailw = P.op(V, "tensor_copy", out=ctail[:],
                       in_=u4[:, :, :, 253:256], waits=[th2])
        prev["ctail_w"] = t_tailw
        t_conv = {}
        for dc in range(NDC):
            ca = (ca_lo if dc < 8 else ca_hi)[:, dc % 8, :] \
                .rearrange("p (s t) -> p s t", s=SEQ)
            w0 = [t_tailw]
            if dc == 0:
                w0 += [prev.get("BbR_free2"), t_xn2[-1]]
            if dc == 8:
                w0 += [prev.get("DtR_free2"), t_xn[-1]]
            base = None
            for k in range(4):
                srcm = u4[:, dc, :, max(0, 2 + k - 2): 253 + k]
            for k in range(4):
                srcm = u4[:, dc, :, 2 + k: 253 + k]
                if k == 0:
                    base = P.op(V, "tensor_scalar", out=ca[:, :, 5:256],
                                in0=srcm, scalar1=col(cvw_sc, dc, 0),
                                scalar2=None, op0=ALU.mult, waits=w0)
                else:
                    base = P.op(V, "scalar_tensor_tensor", out=ca[:, :, 5:256],
                                in0=srcm, scalar=col(cvw_sc, dc, k),
                                in1=ca[:, :, 5:256], op0=ALU.mult, op1=ALU.add,
                                waits=[base])
            for k in range(4):
                srch = uhead[:, dc, :, k:k + 5]
                if k == 0:
                    hh = P.op(V, "tensor_scalar", out=ca[:, :, 0:5], in0=srch,
                              scalar1=col(cvw_sc, dc, 0), scalar2=None,
                              op0=ALU.mult, waits=[base])
                else:
                    hh = P.op(V, "scalar_tensor_tensor", out=ca[:, :, 0:5],
                              in0=srch, scalar=col(cvw_sc, dc, k),
                              in1=ca[:, :, 0:5], op0=ALU.mult, op1=ALU.add,
                              waits=[hh])
            sl = P.op(S, "activation", out=ucb[:, dc, :],
                      in_=ca.rearrange("p s t -> p (s t)"), func=AF.Silu,
                      bias=col(cvb_sc, dc),
                      waits=[hh, prev.get("ucb_free") if dc == 0 else None])
            t_conv[dc] = sl
        t_convlast = t_conv[NDC - 1]
        prev["uhead_free"] = hh

        wt = [t_convlast, t_wxp, st["paf"][0]]
        mm = None
        for dc in range(NDC):
            mm = P.op(T, "matmul", ps_a0[0:64, :], wxpT[:, dc, 0:64],
                      ucb[:, dc, :], start=(dc == 0), stop=(dc == NDC - 1),
                      waits=wt if dc == 0 else ())
        t_dl = P.op(V, "tensor_copy", out=dl_b[:], in_=ps_a0[0:64, :],
                    waits=[mm])
        st["paf"][0] = t_dl
        t_bc2 = {}
        for half, dst in ((0, bcrB), (1, bcrC)):
            bk = (half + 1) % 2
            ps = (ps_a0, ps_a1)[bk]
            wt = [t_convlast, st["paf"][bk],
                  ffn_pa_free[bk] if ti == 0 else None]
            mm = None
            for dc in range(NDC):
                mm = P.op(T, "matmul", ps[0:16, :],
                          wxpT[:, dc, 64 + half * 16:80 + half * 16],
                          ucb[:, dc, :], start=(dc == 0), stop=(dc == NDC - 1),
                          waits=wt if dc == 0 else ())
            t_bc2[half] = P.op(V, "tensor_copy", out=dst[:], in_=ps[0:16, :],
                               waits=[mm])
            st["paf"][bk] = t_bc2[half]

        t_dt = {}
        for dc in range(NDC):
            bk = dc % 2
            ps = (ps_a0, ps_a1)[bk]
            wt = [t_dl, t_wdt, st["paf"][bk]]
            mm = P.op(T, "matmul", ps[:], wdtT[:, dc * 128:(dc + 1) * 128],
                      dl_b[:], start=True, stop=True, waits=wt)
            e1 = P.op(S, "activation", out=hsl, in_=ps[:], func=AF.Exp,
                      bias=col(bdt_sc, dc),
                      waits=[mm, t_convlast,
                             prev.get("hsl_free") if dc == 0 else t_dt[dc - 1]])
            e2 = P.op(S, "activation", out=dtb_v[:, dc, :], in_=hsl,
                      func=AF.Ln, bias=col(cst, C_ONE), waits=[e1])
            t_dt[dc] = e2
            st["paf"][bk] = e2

        t_hqT = {}
        last_f2 = None
        kff = 0
        for fsi in range(6):
            f0 = fsi * 512
            f1 = min(f0 + 512, DFF)
            fn = f1 - f0
            wlg = P.op(Q, "dma_start", out=wgb[:, :, 0:fn],
                       in_=wgT_sc.rearrange("k p f -> p k f")[:, :, f0:f1],
                       dma="dwg",
                       waits=[wg_free] if wg_free else list(prep_sp))
            wlu = P.op(Q, "dma_start", out=wub[:, :, 0:fn],
                       in_=wuT_sc.rearrange("k p f -> p k f")[:, :, f0:f1],
                       dma="dwu",
                       waits=[wu_free] if wu_free else list(prep_sp))
            for tc in range(4):
                par = kff % 2
                gps = (ps_a0[:, 0:fn], ps_w0[:, 0:fn])[par]
                ups = (ps_a1[:, 0:fn], ps_w0[:, 512:512 + fn])[par]
                gkey = ("paf", 0) if par == 0 else ("pw", 0)
                ukey = ("paf", 1) if par == 0 else ("pw", 1)
                g_mm = None
                wt0 = [wlg, t_xqT, st[gkey[0]][gkey[1]],
                       psw_free[0] if (ti == 0 and kff < 2) else None]
                for kc in range(NKD):
                    g_mm = P.op(T, "matmul", gps,
                                xqT_v[:, kc, tc * 128:(tc + 1) * 128],
                                wgb[:, kc, 0:fn], start=(kc == 0),
                                stop=(kc == NKD - 1),
                                waits=wt0 if kc == 0 else ())
                u_mm = None
                wt1 = [wlu, st[ukey[0]][ukey[1]]]
                for kc in range(NKD):
                    u_mm = P.op(T, "matmul", ups,
                                xqT_v[:, kc, tc * 128:(tc + 1) * 128],
                                wub[:, kc, 0:fn], start=(kc == 0),
                                stop=(kc == NKD - 1),
                                waits=wt1 if kc == 0 else ())
                hsl_k = hslp[par]
                hql_k = hqlp[par]
                sg = P.op(S, "activation", out=hsl_k[:, 0:fn], in_=gps,
                          func=AF.Silu, waits=[g_mm, t_dt[NDC - 1],
                                               st.get("hslv%d" % par)])
                hm = P.op(V, "tensor_tensor", out=hsl_k[:, 0:fn],
                          in0=hsl_k[:, 0:fn], in1=ups, op=ALU.mult,
                          waits=[sg, u_mm])
                st[gkey[0]][gkey[1]] = hm
                st[ukey[0]][ukey[1]] = hm
                tq = quant(hsl_k[:, 0:fn], hql_k[:, 0:fn], fn // 16,
                           [hm, st.get("hqlt%d" % par),
                            prev.get("SCANR_free") if (ti and kff == 0) else None,
                            t_u[15] if kff == 0 else None],
                           main_scr)
                st["hslv%d" % par] = tq
                nkb = fn // 128
                ev = transpose_group(
                    [(j, hql_k[:, j * 128:(j + 1) * 128]) for j in range(nkb)],
                    [(hqT_v[:, (f0 // 128) + j, tc * 128:(tc + 1) * 128],
                      j * 128, (j + 1) * 128) for j in range(nkb)],
                    [tq, t_xn2T if kff == 0 else None,
                     prev.get("hqTR_free2") if kff == 0 else None])
                st["hqlt%d" % par] = st["tp_ev"][st["tp"] - 1]
                for j in range(nkb):
                    t_hqT[((f0 // 128) + j, tc)] = ev
                last_f2 = u_mm
                kff += 1
            wg_free = g_mm
            wu_free = u_mm
        t_hsl_last = st["qprev"]
        prev["hsl_free"] = t_hsl_last

        t_f3 = {}
        for tp in range(2):
            mm = None
            for kc in range(NKF):
                slot = kc % 2
                wl = P.op(G, "dma_start", out=wdb[slot][:], in_=wdT_sc[kc],
                          dma=f"dwd{slot}",
                          waits=[wd_free[slot]] if wd_free[slot] else prep_sp)
                for tcin in range(2):
                    tc = tp * 2 + tcin
                    ps = (ps_w0, ps_w1)[tcin]
                    if kc == 0:
                        wt = [wl, t_hqT[(NKF - 1, tc)],
                              st["pw"][0], st["pw"][1], st.get("hslv1"),
                              psw_free[tcin] if (ti == 0 and tp == 0) else
                              (t_f3[(0, tcin)] if tp == 1 else prev.get("psw_free%d" % tcin))]
                    else:
                        wt = [wl]
                    for nh in range(2):
                        mm = P.op(T, "matmul", ps[:, nh * 512:(nh + 1) * 512],
                                  hqT_v[:, kc, tc * 128:(tc + 1) * 128],
                                  wdb[slot][:, nh * 512:(nh + 1) * 512],
                                  start=(kc == 0), stop=(kc == NKF - 1),
                                  waits=[w for w in wt if w] if nh == 0 else ())
                wd_free[slot] = mm
            for tcin in range(2):
                tc = tp * 2 + tcin
                ps = (ps_w0, ps_w1)[tcin]
                ev = P.op(V, "tensor_tensor", out=acc[:, tc, :],
                          in0=acc[:, tc, :], in1=ps[:], op=ALU.add,
                          waits=[mm, t_acc0])
                t_f3[(tp, tcin)] = ev

        t_Bb = None
        t_Cb = None
        for n in range(2 * DST):
            bk = n % 2
            ps = (ps_a0, ps_a1)[bk]
            wt = [t_bc2[0], t_bc2[1], st["paf"][bk]]
            if n == 0:
                wt += [t_xn2[-1], t_xq, st["qprev"]]
            if n == DST:
                wt += [last_f2]
            src = bcrB if n < DST else bcrC
            mm = P.op(T, "matmul", ps[:], selb[:, (n % DST) * 128:(n % DST + 1) * 128],
                      src[:], start=True, stop=True, waits=wt)
            if n < DST:
                t_Bb = P.op(V, "tensor_copy", out=Bb_v[:, n, :], in_=ps[:],
                            waits=[mm])
                st["paf"][bk] = t_Bb
            else:
                t_Cb = P.op(V, "tensor_copy", out=Cb_v[:, n - DST, :],
                            in_=ps[:], waits=[mm])
                st["paf"][bk] = t_Cb

        abufs = [argb, dAb]
        iters = [(dc, s_) for dc in range(NDC) for s_ in range(SEQ)]
        t_y = {}
        t_du = {}

        def emit_args(k):
            dc, s_ = iters[k]
            buf = abufs[k % 2]
            w = [t_dt[dc], t_negAb if (ti == 0 and k == 0) else None,
                 sc_prev.get("tree%d" % (k % 2)),
                 sc_prev.get("exp%d" % (k % 2))]
            if ti == 0 and k == 0:
                w.append(t_xq)
            if ti == 0 and k <= 1:
                w.append(t_hsl_last)
            if k <= 1:
                w.append(t_f3[(1, 1)])
                w.append(prev.get("SCANR_free"))
            ta = P.op(V, "tensor_tensor", out=buf[:],
                      in0=dtb_v[:, dc, s_ * 256:(s_ + 1) * 256]
                      .unsqueeze(1).broadcast_to([128, DST, 256]),
                      in1=A_scb[:, dc, :].unsqueeze(2)
                      .broadcast_to([128, DST, 256]),
                      op=ALU.mult, waits=w)
            te = P.op(S, "activation", out=buf.rearrange("p n t -> p (n t)"),
                      in_=buf.rearrange("p n t -> p (n t)"), func=AF.Exp,
                      waits=[ta, sc_prev.get("ldt%d" % (k % 2))])
            sc_prev["exp%d" % (k % 2)] = te
            sc_prev["argt%d" % (k % 2)] = ta
            return te

        t_exp_k = {}
        t_exp_k[0] = emit_args(0)
        for k, (dc, s_) in enumerate(iters):
            buf = abufs[k % 2]
            if s_ == 0:
                t_du[dc] = P.op(V, "tensor_tensor", out=dtusl,
                                in0=dtb_v[:, dc, :], in1=ucb[:, dc, :],
                                op=ALU.mult,
                                waits=[t_dt[dc], t_conv[dc],
                                       sc_prev.get("dbu"),
                                       t_hsl_last if dc == 0 else None])
            tdbu = P.op(V, "tensor_tensor", out=dBub[:],
                        in0=dtusl[:, s_ * 256:(s_ + 1) * 256]
                        .unsqueeze(1).broadcast_to([128, DST, 256]),
                        in1=Bb_v[:, :, s_ * 256:(s_ + 1) * 256],
                        op=ALU.mult,
                        waits=[t_du[dc], t_Bb, sc_prev.get("hC"),
                               sc_prev.get("scan")])
            sc_prev["dbu"] = tdbu
            tsd = P.op(V, "tensor_tensor", out=sdt[:],
                       in0=buf[:, :, 0], in1=hcar[:, dc, :, s_],
                       op=ALU.mult,
                       waits=[t_exp_k[k], t_hc if ti == 0 else None,
                              sc_prev.get("carry")])
            tsd2 = P.op(V, "tensor_tensor", out=dBub[:, :, 0],
                        in0=dBub[:, :, 0], in1=sdt[:], op=ALU.add,
                        waits=[tsd, tdbu])
            tz = P.op(V, "memset", buf[:, :, 0:1], 0.0, waits=[tsd])
            sc_prev["ldt%d" % (k % 2)] = tz
            if k + 1 < len(iters):
                t_exp_k[k + 1] = emit_args(k + 1)
            tscan = P.op(V, "tensor_tensor_scan",
                         out=hb_v.rearrange("p n t -> p (n t)"),
                         data0=buf.rearrange("p n t -> p (n t)"),
                         data1=dBub.rearrange("p n t -> p (n t)"),
                         initial=0.0, op0=ALU.mult, op1=ALU.add,
                         waits=[tsd2, tz, sc_prev.get("hC"),
                                sc_prev.get("carry")])
            sc_prev["ldt%d" % (k % 2)] = tscan
            tcar = P.op(V, "tensor_copy", out=hcar[:, dc, :, s_],
                        in_=hb_v[:, :, 255], waits=[tscan])
            thc = P.op(V, "tensor_tensor", out=dBub[:], in0=hb_v[:],
                       in1=Cb_v[:, :, s_ * 256:(s_ + 1) * 256],
                       op=ALU.mult, waits=[tscan, t_Cb])
            b8 = buf[:, 0:8, :]
            b4 = buf[:, 8:12, :]
            b2 = buf[:, 12:14, :]
            tt8_ = P.op(V, "tensor_tensor", out=b8, in0=dBub[:, 0:8, :],
                        in1=dBub[:, 8:16, :], op=ALU.add,
                        waits=[thc, tscan])
            tt4_ = P.op(V, "tensor_tensor", out=b4, in0=b8[:, 0:4, :],
                        in1=b8[:, 4:8, :], op=ALU.add, waits=[tt8_])
            tt2_ = P.op(V, "tensor_tensor", out=b2, in0=b4[:, 0:2, :],
                        in1=b4[:, 2:4, :], op=ALU.add, waits=[tt4_])
            ty = P.op(V, "tensor_tensor",
                      out=yb_v[:, dc, s_ * 256:(s_ + 1) * 256],
                      in0=b2[:, 0, :], in1=b2[:, 1, :], op=ALU.add,
                      waits=[tt2_,
                             (t_f3[(1, 1)] if k == 0 else None)])
            t_y[dc] = ty
            sc_prev["tree%d" % (k % 2)] = ty
            sc_prev["scan"] = tscan
            sc_prev["carry"] = tcar
            sc_prev["hC"] = thc
            sc_prev["tree"] = ty
            sc_prev["exp"] = t_exp_k[k]
        t_yg = {}
        for dc in range(NDC):
            g1 = P.op(V, "scalar_tensor_tensor", out=ygb_v[:, dc, :],
                      in0=ucb[:, dc, :], scalar=col(D_sc, dc),
                      in1=yb_v[:, dc, :], op0=ALU.mult, op1=ALU.add,
                      waits=[t_y[dc]] + ([sc_prev["exp"], sc_prev["tree"],
                                          prev.get("SCB_m4")] if dc == 0 else []))
            g2 = P.op(V, "tensor_tensor", out=ygb_v[:, dc, :],
                      in0=ygb_v[:, dc, :], in1=szb[:, dc, :], op=ALU.mult,
                      waits=[g1, t_sz[dc]])
            t_yg[dc] = g2

        t_m4 = {}
        for tp in range(2):
            mm = None
            for dc in range(NDC):
                slot = dc % 2
                wl = P.op(G, "dma_start", out=wob[slot][:], in_=woutT_sc[dc],
                          dma=f"dwo{slot}",
                          waits=[wo_free[slot]] if wo_free[slot] else prep_sp)
                for tcin in range(2):
                    tc = tp * 2 + tcin
                    ps = (ps_w0, ps_w1)[tcin]
                    if dc == 0:
                        wt = [wl, t_yg[NDC - 1],
                              t_f3[(1, tcin)] if tp == 0 else t_m4[(0, tcin)]]
                    else:
                        wt = [wl]
                    for nh in range(2):
                        mm = P.op(T, "matmul", ps[:, nh * 512:(nh + 1) * 512],
                                  ygb_v[:, dc, tc * 128:(tc + 1) * 128],
                                  wob[slot][:, nh * 512:(nh + 1) * 512],
                                  start=(dc == 0), stop=(dc == NDC - 1),
                                  waits=wt if nh == 0 else ())
                wo_free[slot] = mm
            for tcin in range(2):
                tc = tp * 2 + tcin
                ps = (ps_w0, ps_w1)[tcin]
                ev = P.op(V, "tensor_tensor", out=acc[:, tc, :],
                          in0=acc[:, tc, :], in1=ps[:], op=ALU.add,
                          waits=[mm, t_f3[(tc // 2, tc % 2)]])
                t_m4[(tp, tcin)] = ev
                prev["psw_free%d" % tcin] = ev

        m4all = [t_m4[(0, 0)], t_m4[(0, 1)], t_m4[(1, 0)], t_m4[(1, 1)]]
        stt = None
        for s_ in range(SEQ):
            stt = P.op(G, "dma_start",
                       out=out_t.rearrange("s (a p) d -> s p a d", p=128)
                       [s_][:, 2 * ti:2 * ti + 2, :],
                       in_=acc[:, 2 * s_:2 * s_ + 2, :],
                       dma="dso", waits=m4all if s_ == 0 else ())

        prev["acc_free"] = ("dso", P.count["dso"])
        prev["DtR_free"] = sc_prev["argt%d" % ((len(iters) - 1) % 2)]
        prev["DtR_free2"] = t_xq
        prev["BbR_free"] = sc_prev["dbu"]
        prev["BbR_free2"] = t_xq
        prev["CbR_free"] = sc_prev["hC"]
        prev["SCANR_free"] = wo_free[0]
        prev["SCB_m4"] = None
        prev["hqTR_free"] = t_yg[NDC - 1]
        prev["hqTR_free2"] = None
        prev["szb_free"] = t_yg[NDC - 1]
        prev["ucb_free"] = t_yg[NDC - 1]
        prev["bn_free"] = t_mv
        prev["ln_free"] = t_rat
        prev["uhead_free"] = prev["uhead_free"]

    P.op(G, "engine_nop", waits=[("dso", P.count.get("dso", 0))])

    for name in sorted(P.semnames):
        P.sems[name] = stack.enter_context(nc.semaphore(name))
    block = stack.enter_context(nc.Block())
    P.emit(block)
    stack.close()
    return nc


def _consts_array():
    c = np.zeros((128, 8), np.float32)
    c[:, 0] = 1.0
    c[:, 1] = EPS
    c[:, 2] = 1.0 + EPS
    c[:, 3] = EPS * EPS
    c[:, 4] = -0.5
    return c


def _sel_array():
    s = np.zeros((16, 16 * 128), ml_dtypes.bfloat16)
    for n in range(16):
        s[n, n * 128:(n + 1) * 128] = 1
    return s


def _base_inputs(inputs):
    return {
        "ln_g": inputs["L0_attn_ln_g"], "ln_b": inputs["L0_attn_ln_b"],
        "w_in": inputs["L0_attn_W_in"], "conv_w": inputs["L0_attn_conv_w"],
        "conv_b": inputs["L0_attn_conv_b"], "w_xp": inputs["L0_attn_W_xp"],
        "w_dt": inputs["L0_attn_W_dt"], "b_dt": inputs["L0_attn_b_dt"],
        "a_log": inputs["L0_attn_A_log"], "d_vec": inputs["L0_attn_D"],
        "w_out": inputs["L0_attn_W_out"], "wg": inputs["L0_ffn_w_gate"],
        "wu": inputs["L0_ffn_w_up"], "wd": inputs["L0_ffn_w_down"],
        "eye_b": np.eye(128, dtype=ml_dtypes.bfloat16),
        "ones1f": np.ones((1, 128), np.float32),
        "sel": _sel_array(),
        "csts": _consts_array(),
    }


_NC_CACHE = {}


def kernel(**inputs):
    if "prog" not in _NC_CACHE:
        _NC_CACHE["prog"] = build_program()
    nc = _NC_CACHE["prog"]
    x = np.ascontiguousarray(inputs["L0_attn_x"])
    base = _base_inputs(inputs)
    in_maps = []
    for c in range(NCORES):
        m = dict(base)
        m["x"] = x[c * SEQ:(c + 1) * SEQ]
        in_maps.append(m)
    res = run_bass_kernel_spmd(nc, in_maps, core_ids=list(range(NCORES)))
    out = np.concatenate([r["out"] for r in res.results], axis=0)
    return out.astype(np.float32)


# version 08
# speedup vs baseline: 1.0481x; 1.0370x over previous
import sys
sys.path.insert(0, "/opt/trn_rl_repo")
import numpy as np
import ml_dtypes

import concourse.bass as bass
import concourse.mybir as mybir
from concourse.bass_utils import run_bass_kernel_spmd

F32 = mybir.dt.float32
BF16 = mybir.dt.bfloat16
AF = mybir.ActivationFunctionType
ALU = mybir.AluOpType
AX = mybir.AxisListType

B, L, D = 16, 2048, 1024
NCORES = 8
SEQ = 2
DI = 2048
DST = 16
DFF = 2816
TC = 256
TOK = SEQ * TC
NT = L // TC
NDC = DI // 128
NKD = D // 128
NKF = DFF // 128
EPS = 1e-5
MID = [0.25, 0.75, 1.25, 1.75, 2.5, 3.5, 5.0]
MIDW = [0.5, 0.5, 0.5, 0.5, 1.0, 1.0, 2.0]

CFG_NT = NT


class Plan:
    ENGS = ("v", "s", "t", "g", "q")

    def __init__(self):
        self.ops = {e: [] for e in self.ENGS}
        self.count = {}
        self.semnames = set("S" + e for e in self.ENGS)
        self.sems = {}

    def dma_sem(self, name):
        self.semnames.add(name)

    def op(self, eng, method, *args, waits=(), dma=None, **kw):
        import traceback
        fr = traceback.extract_stack(limit=4)
        tbs = ";".join(f"{f.lineno}" for f in fr[:-1])
        w = tuple(t for t in waits if t is not None)
        if dma is not None:
            self.count[dma] = self.count.get(dma, 0) + 16
            tok = (dma, self.count[dma])
        else:
            s = "S" + eng
            self.count[s] = self.count.get(s, 0) + 1
            tok = (s, self.count[s])
        self.ops[eng].append((method, args, kw, w, tok, tbs))
        return tok

    def emit(self, block):
        plan = self
        secs = {"v": block.vector, "s": block.scalar, "t": block.tensor,
                "g": block.gpsimd, "q": block.sync}

        def make_body(e):
            own = "S" + e

            def body(engine):
                waited = {}
                for (method, args, kw, waits, tok, src) in plan.ops[e]:
                    need = {}
                    for (sname, val) in waits:
                        if val > need.get(sname, -1):
                            need[sname] = val
                    for sname in sorted(need):
                        val = need[sname]
                        if waited.get(sname, -1) >= val:
                            continue
                        engine.wait_ge(plan.sems[sname], val)
                        waited[sname] = val
                    try:
                        instr = getattr(engine, method)(*args, **kw)
                    except Exception as ex:
                        raise RuntimeError(f"emit failed: {method} planned at {src}: {ex}") from ex
                    sname, _ = tok
                    instr.then_inc(plan.sems[sname], 1 if sname == own else 16)
            return body

        for e in self.ENGS:
            if self.ops[e]:
                secs[e](make_body(e))


def col(ap, *idx):
    sl = tuple([slice(None)] + [slice(i, i + 1) for i in idx])
    return ap[sl]


def cdiv(a, b):
    return (a + b - 1) // b


def build_program():
    nc = bass.Bass("TRN2", target_bir_lowering=False)
    P = Plan()
    V, S, T, G, Q = "v", "s", "t", "g", "q"

    def din(name, shape, dt=F32):
        return nc.dram_tensor(name, shape, dt, kind="ExternalInput").ap()

    x_in = din("x", [SEQ, L, D])
    ln_g = din("ln_g", [D]); ln_b = din("ln_b", [D])
    w_in = din("w_in", [2 * DI, D])
    conv_w = din("conv_w", [DI, 4]); conv_b = din("conv_b", [DI])
    w_xp = din("w_xp", [96, DI])
    w_dt = din("w_dt", [DI, 64]); b_dt = din("b_dt", [DI])
    a_log = din("a_log", [DI, DST]); d_vec = din("d_vec", [DI])
    w_out = din("w_out", [D, DI])
    wg_in = din("wg", [DFF, D]); wu_in = din("wu", [DFF, D])
    wd_in = din("wd", [D, DFF])
    eye_b = din("eye_b", [128, 128], BF16)
    ones1f = din("ones1f", [1, 128], F32)
    sel_in = din("sel", [16, 16 * 128], BF16)
    csts = din("csts", [128, 8], F32)
    C_ONE, C_EPS, C_LNS, C_LNB, C_NEGH = 0, 1, 2, 3, 4

    out_t = nc.dram_tensor("out", [SEQ, L, D], F32, kind="ExternalOutput").ap()

    def scr(name, shape, dt=BF16):
        return nc.dram_tensor(name, shape, dt, kind="Internal").ap()

    winT_sc = scr("winT_sc", [32, 128, NKD, 128])
    woutT_sc = scr("woutT_sc", [NDC, 128, D])
    wgT_sc = scr("wgT_sc", [NKD, 128, DFF])
    wuT_sc = scr("wuT_sc", [NKD, 128, DFF])
    wdT_sc = scr("wdT_sc", [NKF, 128, D])

    from contextlib import ExitStack
    stack = ExitStack()

    def sb(name, shape, dt=F32):
        return stack.enter_context(nc.sbuf_tensor(name, shape, dt))

    def psum(name, shape, dt=F32):
        return stack.enter_context(nc.psum_tensor(name, shape, dt))

    cst = sb("cst", [128, 8])
    eyeb = sb("eyeb", [128, 128], BF16)
    onesf = sb("onesf", [1, 128])
    selb = sb("selb", [16, 16 * 128], BF16)
    A_sc = sb("A_sc", [128, NDC, DST])
    A_scb = sb("A_scb", [128, NDC, DST], BF16)
    D_sc = sb("D_sc", [128, NDC])
    bdt_sc = sb("bdt_sc", [128, NDC])
    cvb_sc = sb("cvb_sc", [128, NDC])
    cvw_sc = sb("cvw_sc", [128, NDC, 4])
    g_bc = sb("g_bc", [128, D])
    b_bc = sb("b_bc", [128, D])
    wxpT = sb("wxpT", [128, NDC, 96], BF16)
    wdtT = sb("wdtT", [64, DI], BF16)
    hcar = sb("hcar", [128, NDC, DST, SEQ])
    ctail = sb("ctail", [128, NDC, SEQ, 3])
    bnb = sb("bnb", [128, 4, 12])
    mv = sb("mv", [128, 4, 2])
    rr = sb("rr", [128, 4])
    rho = sb("rho", [128, 4])
    rat = sb("rat", [128, 4])
    lnt = sb("lnt", [128, 4])
    qm = sb("qm", [128, 64])
    qs = sb("qs", [128, 64])
    qr = sb("qr", [128, 64])
    qsb = sb("qsb", [128, 64], BF16)
    sdt = sb("sdt", [128, DST])
    uhead = sb("uhead", [128, NDC, SEQ, 8])
    uheadb = None
    dl_b = sb("dl_b", [64, TOK], BF16)
    bcrB = sb("bcrB", [16, TOK], BF16)
    bcrC = sb("bcrC", [16, TOK], BF16)

    acc = sb("acc", [128, 4, D])
    szb = sb("szb", [128, NDC, TOK], BF16)
    ucb = sb("ucb", [128, NDC, TOK], BF16)

    SCANR = sb("SCANR", [128, 8192])
    BbR = sb("BbR", [128, 8192], BF16)
    CbR = sb("CbR", [128, 8192], BF16)
    DtR = sb("DtR", [128, 8192], BF16)
    hqTR = sb("hqTR", [128, 11264], BF16)

    winb = [sb(f"winb{i}", [128, NKD, 128], BF16) for i in range(2)]
    wgb = sb("wgb", [128, NKD, 512], BF16)
    wub = sb("wub", [128, NKD, 512], BF16)
    wdb = [sb(f"wdb{i}", [128, D], BF16) for i in range(2)]
    wob = [sb(f"wob{i}", [128, D], BF16) for i in range(2)]

    SCB = SCANR[:].bitcast(BF16)
    u_v = SCANR[:].rearrange("p (c t) -> p c t", c=NDC)
    pp32 = [SCANR[:, 0:4096], SCANR[:, 4096:8192]]
    qv_v = SCANR[:, 0:1024]
    qa2_v = SCANR[:, 1024:2048]
    qABCD = [SCB[:, 4096 + i * 1024: 4096 + (i + 1) * 1024] for i in range(4)]
    argb = SCB[:, 0:4096].rearrange("p (n t) -> p n t", n=DST)
    dAb = SCB[:, 4096:8192].rearrange("p (n t) -> p n t", n=DST)
    dBub = SCB[:, 8192:12288].rearrange("p (n t) -> p n t", n=DST)
    hb_v = SCB[:, 12288:16384].rearrange("p (n t) -> p n t", n=DST)
    tr8 = SCB[:, 0:2048].rearrange("p (n t) -> p n t", n=8)
    tr4 = SCB[:, 2048:3072].rearrange("p (n t) -> p n t", n=4)
    tr2 = SCB[:, 3072:3584].rearrange("p (n t) -> p n t", n=2)
    ygb_v = SCB[:, 0:8192].rearrange("p (c t) -> p c t", c=NDC)

    BbF = BbR[:].bitcast(F32)
    xqb_v = BbR[:, 0:4096].rearrange("p (j d) -> p j d", j=4)
    xn2b_v = BbR[:, 4096:8192].rearrange("p (j d) -> p j d", j=4)
    ca_lo = BbF[:].rearrange("p (c t) -> p c t", c=8)
    Bb_v = BbR[:].rearrange("p (n t) -> p n t", n=DST)
    ppb = [BbR[:, 0:4096], BbR[:, 4096:8192]]
    hslp = [BbF[:, 0:512], BbF[:, 512:1024]]
    hsl = hslp[0]
    hqlp = [BbR[:, 2048:2560], BbR[:, 2560:3072]]

    CbF = CbR[:].bitcast(F32)
    xn_v = CbF[:].rearrange("p (j d) -> p j d", j=4)
    xqT_v = CbR[:, 0:4096].rearrange("p (k t) -> p k t", k=NKD)
    Cb_v = CbR[:].rearrange("p (n t) -> p n t", n=DST)
    ppq = [CbR[:, 0:4096], CbR[:, 4096:8192]]

    DtF = DtR[:].bitcast(F32)
    xt_v = DtF[:].rearrange("p (j d) -> p j d", j=4)
    ca_hi = DtF[:].rearrange("p (c t) -> p c t", c=8)
    dtb_v = DtR[:].rearrange("p (c t) -> p c t", c=NDC)

    hqF = hqTR[:].bitcast(F32)
    xn2T_v = hqTR[:, 0:4096].rearrange("p (k t) -> p k t", k=NKD)
    hqT_v = hqTR[:].rearrange("p (k t) -> p k t", k=NKF)
    yb_v = hqTR[:, 0:8192].rearrange("p (c t) -> p c t", c=NDC)
    dtusl = uhead[:].rearrange("p a b c -> p (a b c)").bitcast(BF16)[:, 0:512]
    pqv = hqF[:, 0:1024]
    pqa2 = hqF[:, 1024:2048]
    pqABCD = [hqTR[:, 4096 + i * 1024: 4096 + (i + 1) * 1024] for i in range(4)]
    x1qv = hqF[:, 2048:2560]
    x1qa2 = hqF[:, 2560:3072]
    x1ABCD = [hqTR[:, 6144 + i * 512: 6144 + (i + 1) * 512] for i in range(4)]
    p1s = hqF[:, 2048:3072]
    p2s = hqF[:, 3072:4096]

    ps_a0 = psum("ps_a0", [128, 512]); ps_a1 = psum("ps_a1", [128, 512])
    ps_w0 = psum("ps_w0", [128, 1024]); ps_w1 = psum("ps_w1", [128, 1024])
    ps_t0 = psum("ps_t0", [128, 1024], BF16)
    ps_t1 = psum("ps_t1", [128, 1024], BF16)
    ps_ts = [ps_t0, ps_t1]

    for s in ["dld0", "dld1", "dsp0", "dsp1", "dx", "dso", "dwin0", "dwin1",
              "dwg", "dwu", "dwd0", "dwd1", "dwo0", "dwo1"]:
        P.dma_sem(s)

    st = {"tp": 0, "tp_ev": {}, "paf": {0: None, 1: None}, "pw": {0: None, 1: None}, "pab": 0, "qprev": None}

    def transpose_group(blocks, evacs, wt):
        k = st["tp"]
        pst = ps_ts[k % 2]
        w = list(wt)
        if (k - 2) in st["tp_ev"]:
            w.append(st["tp_ev"][k - 2])
        mm = None
        for (j, src) in blocks:
            mm = P.op(T, "transpose", pst[:, j * 128:(j + 1) * 128], src,
                      eyeb[:], waits=(w + [ld_all]) if mm is None else ())
        ev = None
        for (dst, c0, c1) in evacs:
            ev = P.op(S, "copy", out=dst, in_=pst[:, c0:c1], waits=[mm])
        st["tp_ev"][k] = ev
        st["tp"] = k + 1
        return ev

    def quant(src_ap, dst_ap, nblk, wtoks, scratch):
        qvv, qaa, (qa, qb_, qc, qd) = scratch
        n = nblk * 16
        mm_ = qm[:, 0:nblk]; ss = qs[:, 0:nblk]; rs = qr[:, 0:nblk]
        sbv = qsb[:, 0:nblk]
        vv = qvv[:, 0:n]; aa = qaa[:, 0:n]
        qa = qa[:, 0:n]; qb_ = qb_[:, 0:n]; qc = qc[:, 0:n]; qd = qd[:, 0:n]
        w0 = list(wtoks) + ([st["qprev"]] if st["qprev"] else [])
        src3 = src_ap.rearrange("p (b j) -> p b j", j=16)
        t_m = P.op(V, "tensor_reduce", out=mm_, in_=src3, axis=AX.X, op=ALU.max,
                   apply_absolute_value=True, waits=w0)
        t_s = P.op(V, "tensor_scalar", out=ss, in0=mm_, scalar1=1.0 / 6.0,
                   scalar2=1e-12, op0=ALU.mult, op1=ALU.max, waits=[t_m])
        t_r = P.op(V, "reciprocal", out=rs, in_=ss, waits=[t_s])
        t_sb = P.op(V, "tensor_copy", out=sbv, in_=ss, waits=[t_s])
        rb = rs.unsqueeze(2).broadcast_to([128, nblk, 16])
        t_v = P.op(V, "tensor_tensor", out=vv.rearrange("p (b j) -> p b j", j=16),
                   in0=src3, in1=rb, op=ALU.mult, waits=[t_r])
        t_a = P.op(S, "activation", out=aa, in_=vv, func=AF.Abs,
                   waits=[t_v])
        def cmp(dst, i, w):
            return P.op(V, "tensor_scalar", out=dst, in0=aa, scalar1=MID[i],
                        scalar2=MIDW[i], op0=ALU.is_gt, op1=ALU.mult, waits=w)
        c0 = cmp(qa, 0, [t_a]); c1 = cmp(qb_, 1, [t_a])
        s01 = P.op(V, "tensor_tensor", out=qc, in0=qa, in1=qb_, op=ALU.add,
                   waits=[c0, c1])
        c2 = cmp(qa, 2, [s01]); c3 = cmp(qb_, 3, [s01])
        s23 = P.op(V, "tensor_tensor", out=qd, in0=qa, in1=qb_, op=ALU.add,
                   waits=[c2, c3])
        s03 = P.op(V, "tensor_tensor", out=qc, in0=qc, in1=qd, op=ALU.add,
                   waits=[s23])
        c4 = cmp(qa, 4, [s03]); c5 = cmp(qb_, 5, [s03])
        s45 = P.op(V, "tensor_tensor", out=qd, in0=qa, in1=qb_, op=ALU.add,
                   waits=[c4, c5])
        c6 = cmp(qa, 6, [s45])
        s46 = P.op(V, "tensor_tensor", out=qd, in0=qd, in1=qa, op=ALU.add,
                   waits=[c6])
        lvl = P.op(V, "tensor_tensor", out=qc, in0=qc, in1=qd, op=ALU.add,
                   waits=[s46])
        sg = P.op(V, "tensor_scalar", out=qb_, in0=vv, scalar1=0.0,
                  scalar2=None, op0=ALU.is_ge, waits=[lvl])
        sg2 = P.op(V, "tensor_scalar", out=qb_, in0=qb_, scalar1=2.0,
                   scalar2=-1.0, op0=ALU.mult, op1=ALU.add, waits=[sg])
        tw = P.op(V, "tensor_tensor", out=qa, in0=qc, in1=qb_, op=ALU.mult,
                  waits=[sg2])
        sbb = sbv.unsqueeze(2).broadcast_to([128, nblk, 16])
        tq = P.op(V, "tensor_tensor",
                  out=dst_ap.rearrange("p (b j) -> p b j", j=16),
                  in0=qa.rearrange("p (b j) -> p b j", j=16), in1=sbb,
                  op=ALU.mult, waits=[tw, t_sb])
        st["qprev"] = tq
        return tq

    MAIN_SCR = (qv_v.unsqueeze(0)[0:0] if False else None)
    main_scr = (SCANR[:, 0:1024], SCANR[:, 1024:2048], qABCD)
    prep_scr = (pqv, pqa2, pqABCD)
    p1_scr = (x1qv, x1qa2, x1ABCD)

    dl = lambda **kw: P.op(Q, "dma_start", **kw)
    dl(out=cst[:], in_=csts, dma="dld0")
    dl(out=eyeb[:], in_=eye_b, dma="dld0")
    dl(out=onesf[:], in_=ones1f, dma="dld0")
    dl(out=selb[:], in_=sel_in, dma="dld0")
    dl(out=A_sc[:], in_=a_log.rearrange("(c p) n -> p c n", p=128), dma="dld0")
    dl(out=D_sc[:], in_=d_vec.rearrange("(c p) -> p c", p=128), dma="dld0", allow_slow_non_contiguous=True)
    dl(out=bdt_sc[:], in_=b_dt.rearrange("(c p) -> p c", p=128), dma="dld0", allow_slow_non_contiguous=True)
    dl(out=cvb_sc[:], in_=conv_b.rearrange("(c p) -> p c", p=128), dma="dld0", allow_slow_non_contiguous=True)
    dl(out=cvw_sc[:], in_=conv_w.rearrange("(c p) k -> p c k", p=128),
       dma="dld0")
    dl(out=g_bc[0:1, :], in_=ln_g.unsqueeze(0), dma="dld0")
    ld_all = dl(out=b_bc[0:1, :], in_=ln_b.unsqueeze(0), dma="dld0")

    t_exp = P.op(S, "activation", out=A_sc[:], in_=A_sc[:], func=AF.Exp,
                 waits=[ld_all])
    t_negA = P.op(V, "tensor_scalar_mul", out=A_sc[:], in0=A_sc[:],
                  scalar1=-1.0, waits=[t_exp])
    t_negAb = P.op(V, "tensor_copy", out=A_scb[:], in_=A_sc[:], waits=[t_negA])

    brc = []
    for half in range(2):
        brc.append(P.op(T, "matmul", ps_w0[:, half * 512:(half + 1) * 512],
                        onesf[:], g_bc[0:1, half * 512:(half + 1) * 512],
                        start=True, stop=True, waits=[ld_all]))
        brc.append(P.op(T, "matmul", ps_w1[:, half * 512:(half + 1) * 512],
                        onesf[:], b_bc[0:1, half * 512:(half + 1) * 512],
                        start=True, stop=True, waits=[ld_all]))
    t_gbc = P.op(V, "tensor_copy", out=g_bc[:], in_=ps_w0[:], waits=[brc[-1]])
    t_bbc = P.op(V, "tensor_copy", out=b_bc[:], in_=ps_w1[:], waits=[brc[-1]])

    t_hc = P.op(V, "memset", hcar[:], 0.0)
    t_ct = P.op(V, "memset", ctail[:], 0.0)

    pst8 = {"lc": [None, None], "bc": [None, None], "qc": [None, None], "i": 0}

    def prep_slab(src_dram, n_kblk, quant_nblk, stores, part=128):
        i = pst8["i"] % 2
        n = n_kblk * 128
        tl = P.op(Q, "dma_start", out=pp32[i][0:part, 0:n], in_=src_dram,
                  dma=f"dld{i}", waits=[pst8["lc"][i], ld_all])
        if quant_nblk:
            tc_ = None
            for q0 in range(0, n, 1024):
                q1 = min(q0 + 1024, n)
                tc_ = quant(pp32[i][0:part, q0:q1], ppb[i][0:part, q0:q1],
                            (q1 - q0) // 16, [tl, pst8["bc"][i]], prep_scr)
        else:
            tc_ = P.op(V, "tensor_copy", out=ppb[i][0:part, 0:n],
                       in_=pp32[i][0:part, 0:n], waits=[tl, pst8["bc"][i]])
        pst8["lc"][i] = tc_
        evs = []
        lastmm = None
        for g0 in range(0, n_kblk, 4):
            gsz = min(4, n_kblk - g0)
            k = st["tp"]
            pst = ps_ts[k % 2]
            w = [tc_, pst8["qc"][i]]
            if (k - 2) in st["tp_ev"]:
                w.append(st["tp_ev"][k - 2])
            mm = None
            for j in range(gsz):
                mm = P.op(T, "transpose", pst[:, j * 128:(j + 1) * 128],
                          ppb[i][0:part, (g0 + j) * 128:(g0 + j + 1) * 128],
                          eyeb[:],
                          waits=([t for t in w if t] + [ld_all])
                          if mm is None else ())
            ev = P.op(V, "tensor_copy",
                      out=ppq[i][:, g0 * 128:(g0 + gsz) * 128],
                      in_=pst[:, 0:gsz * 128], waits=[mm])
            st["tp_ev"][k] = ev
            st["tp"] = k + 1
            evs.append(ev)
            lastmm = mm
        pst8["bc"][i] = lastmm
        stok = None
        for (dst, c0, c1) in stores:
            src = ppq[i][:, c0:c1]
            if dst.ndim == 3:
                src = src.rearrange("p (k e) -> p k e", e=128)
            stok = P.op(Q, "dma_start", out=dst, in_=src,
                        dma=f"dsp{i}", waits=evs)
        pst8["qc"][i] = stok
        pst8["i"] += 1
        return stok

    for ec in range(32):
        prep_slab(w_in[ec * 128:(ec + 1) * 128, :], NKD, 0,
                  [(winT_sc[ec].rearrange("p c e -> p (c e)"), 0, NKD * 128)])
    for oc in range(8):
        prep_slab(w_out[oc * 128:(oc + 1) * 128, :], NDC, 0,
                  [(woutT_sc.rearrange("c p f -> p c f")[:, :, oc * 128:(oc + 1) * 128], 0, NDC * 128)])
    for (w_dram, dstsc) in ((wg_in, wgT_sc), (wu_in, wuT_sc)):
        for fc in range(NKF):
            prep_slab(w_dram[fc * 128:(fc + 1) * 128, :], NKD, 64,
                      [(dstsc.rearrange("k p f -> p k f")[:, :, fc * 128:(fc + 1) * 128], 0, NKD * 128)])
    for oc in range(8):
        prep_slab(wd_in[oc * 128:(oc + 1) * 128, :], NKF, NKF * 8,
                  [(wdT_sc.rearrange("k p f -> p k f")[:, :, oc * 128:(oc + 1) * 128], 0, NKF * 128)])

    i = pst8["i"] % 2
    tl = P.op(Q, "dma_start", out=pp32[i][0:96, 0:DI], in_=w_xp,
              dma=f"dld{i}", waits=[pst8["lc"][i]])
    tcc = P.op(V, "tensor_copy", out=ppb[i][0:96, 0:DI],
               in_=pp32[i][0:96, 0:DI], waits=[tl, pst8["bc"][i]])
    t_wxp = None
    for g0 in range(0, NDC, 4):
        k = st["tp"]; pst = ps_ts[k % 2]
        w = [tcc] + ([st["tp_ev"][k - 2]] if (k - 2) in st["tp_ev"] else [])
        mm = None
        for j in range(4):
            dc = g0 + j
            mm = P.op(T, "transpose", pst[:, j * 128:j * 128 + 96],
                      ppb[i][0:96, dc * 128:(dc + 1) * 128], eyeb[0:96, 0:96],
                      waits=(w + [ld_all]) if mm is None else ())
        for j in range(4):
            dc = g0 + j
            t_wxp = P.op(V, "tensor_copy", out=wxpT[:, dc, :],
                         in_=pst[:, j * 128:j * 128 + 96], waits=[mm])
        st["tp_ev"][k] = t_wxp
        st["tp"] = k + 1
    pst8["lc"][i] = tcc
    pst8["bc"][i] = mm
    pst8["i"] += 1

    i = pst8["i"] % 2
    tl = P.op(Q, "dma_start",
              out=pp32[i][:, 0:NDC * 64].rearrange("p (c r) -> p c r", c=NDC),
              in_=w_dt.rearrange("(c p) r -> p c r", p=128),
              dma=f"dld{i}", waits=[pst8["lc"][i]])
    tcc = P.op(V, "tensor_copy", out=ppb[i][:, 0:NDC * 64],
               in_=pp32[i][:, 0:NDC * 64], waits=[tl, pst8["bc"][i]])
    t_wdt = None
    for g0 in range(0, NDC, 4):
        k = st["tp"]; pst = ps_ts[k % 2]
        w = [tcc] + ([st["tp_ev"][k - 2]] if (k - 2) in st["tp_ev"] else [])
        mm = None
        for j in range(4):
            dc = g0 + j
            mm = P.op(T, "transpose", pst[0:64, j * 128:(j + 1) * 128],
                      ppb[i][:, dc * 64:(dc + 1) * 64], eyeb[:],
                      waits=(w + [ld_all]) if mm is None else ())
        t_wdt = P.op(V, "tensor_copy",
                     out=wdtT[:, g0 * 128:(g0 + 4) * 128],
                     in_=pst[0:64, 0:512], waits=[mm])
        st["tp_ev"][k] = t_wdt
        st["tp"] = k + 1
    pst8["lc"][i] = tcc
    pst8["bc"][i] = mm
    pst8["i"] += 1

    prep_sp = [("dsp0", P.count.get("dsp0", 0)), ("dsp1", P.count.get("dsp1", 0))]
    prep_v = ("Sv", P.count["Sv"])

    prev = {}
    ffn_pa_free = [t_gbc, t_bbc]
    psw_free = [t_gbc, t_bbc]
    win_free = [None, None]
    wg_free = None; wu_free = None
    wd_free = [None, None]
    wo_free = [None, None]
    sc_prev = {}
    pending_m4 = None

    for ti in range(CFG_NT):
        t0 = ti * TC
        xw = list(prep_sp) + [prev.get("DtR_free")]
        t_x = None
        for s_ in range(SEQ):
            t_x = P.op(Q, "dma_start",
                       out=xt_v[:, 2 * s_:2 * s_ + 2, :],
                       in_=x_in.rearrange("s (a p) d -> s p a d", p=128)
                       [s_][:, 2 * ti:2 * ti + 2, :],
                       dma="dx", waits=xw if s_ == 0 else ())

        bn = []
        for j in range(4):
            bn.append(P.op(V, "bn_stats", out=bnb[:, j, 0:6],
                           in_=xt_v[:, j, 0:512],
                           waits=[t_x, prev.get("bn_free"),
                                  prep_v if ti == 0 else None]))
            bn.append(P.op(V, "bn_stats", out=bnb[:, j, 6:12],
                           in_=xt_v[:, j, 512:1024], waits=[]))
        t_mv = None
        for j in range(4):
            t_mv = P.op(V, "bn_aggr", out=mv[:, j, :], in_=bnb[:, j, :],
                        waits=[bn[-1]] if j == 0 else ())
        t_ln1 = P.op(S, "activation", out=lnt[:], in_=mv[:, :, 1], func=AF.Ln,
                     bias=col(cst, C_EPS),
                     waits=[t_mv, ld_all, prev.get("ln_free")])
        t_r = P.op(S, "activation", out=rr[:], in_=lnt[:], func=AF.Exp,
                   scale=col(cst, C_NEGH), waits=[t_ln1])
        t_ln2 = P.op(S, "activation", out=lnt[:], in_=mv[:, :, 1], func=AF.Ln,
                     scale=col(cst, C_LNS), bias=col(cst, C_LNB), waits=[t_r])
        t_rho = P.op(S, "activation", out=rho[:], in_=lnt[:], func=AF.Exp,
                     scale=col(cst, C_NEGH), waits=[t_ln2])
        t_rci = P.op(V, "reciprocal", out=lnt[:], in_=rr[:],
                     waits=[t_rho, t_mv])
        t_rat = P.op(V, "tensor_tensor", out=rat[:], in0=rho[:], in1=lnt[:],
                     op=ALU.mult, waits=[t_rci])

        t_xn = []
        t_xn2 = []
        for j in range(4):
            a = P.op(V, "tensor_scalar", out=xn_v[:, j, :], in0=xt_v[:, j, :],
                     scalar1=col(mv, j, 0), scalar2=col(rr, j),
                     op0=ALU.subtract, op1=ALU.mult,
                     waits=[t_r, prev.get("CbR_free")])
            t_xn.append(a)
            pass
            c_ = P.op(V, "tensor_scalar", out=p1s, in0=xn_v[:, j, :],
                      scalar1=col(rat, j), scalar2=None, op0=ALU.mult,
                      waits=[t_rat, a, prev.get("hqTR_free") if (ti and j == 0) else None])
            d_ = P.op(V, "tensor_tensor", out=p2s, in0=p1s, in1=g_bc[:],
                      op=ALU.mult, waits=[c_, t_gbc])
            e_ = P.op(V, "tensor_tensor", out=xn2b_v[:, j, :], in0=p2s,
                      in1=b_bc[:], op=ALU.add,
                      waits=[d_, t_bbc, prev.get("BbR_free") if j == 0 else None])
            t_xn2.append(e_)


        t_xq = None
        for j in range(4):
            for hh in range(2):
                t_xq = quant(xn_v[:, j, hh * 512:(hh + 1) * 512],
                             xqb_v[:, j, hh * 512:(hh + 1) * 512], 32,
                             [t_xn[j],
                              t_xn2[-1] if (j == 0 and hh == 0) else None,
                              prev.get("hqTR_free") if (j == 0 and hh == 0) else None],
                             p1_scr)

        if pending_m4 is not None:
            t_m4p = {}
            for tcin in range(2):
                ps = (ps_w0, ps_w1)[tcin]
                ev = P.op(V, "tensor_tensor", out=acc[:, tcin, :],
                          in0=acc[:, tcin, :], in1=ps[:], op=ALU.add,
                          waits=[pending_m4["mm"][(0, tcin)]])
                t_m4p[(0, tcin)] = ev
            mm = None
            tp1mm = {}
            for dc in range(NDC):
                slot = dc % 2
                wl = P.op(G, "dma_start", out=wob[slot][:], in_=woutT_sc[dc],
                          dma=f"dwo{slot}",
                          waits=[wo_free[slot]] if wo_free[slot] else prep_sp)
                for tcin in range(2):
                    tc = 2 + tcin
                    ps = (ps_w0, ps_w1)[tcin]
                    if dc == 0:
                        wt = [wl, pending_m4["yg"], t_m4p[(0, tcin)]]
                    else:
                        wt = [wl]
                    for nh in range(2):
                        mm = P.op(T, "matmul", ps[:, nh * 512:(nh + 1) * 512],
                                  ygb_v[:, dc, tc * 128:(tc + 1) * 128],
                                  wob[slot][:, nh * 512:(nh + 1) * 512],
                                  start=(dc == 0), stop=(dc == NDC - 1),
                                  waits=wt if nh == 0 else ())
                    tp1mm[tcin] = mm
                wo_free[slot] = mm
            pending_m4["evp0"] = t_m4p
            pending_m4["tp1mm"] = tp1mm
            prev["SCANR_free"] = mm
            prev["ygb_free"] = mm

        t_xn2T = None
        for kc in range(NKD):
            t_xn2T = transpose_group(
                [(j, xn2b_v[:, j, kc * 128:(kc + 1) * 128]) for j in range(4)],
                [(xn2T_v[:, kc, :], 0, 512)],
                [t_xn2[-1], prev.get("hqTR_free") if kc == 0 else None])
        t_u = {}
        t_sz = {}
        ec_order = list(range(16, 32)) + list(range(16))
        for eci, ec in enumerate(ec_order):
            slot = eci % 2
            wl = P.op(Q, "dma_start",
                      out=winb[slot][:],
                      in_=winT_sc[ec].rearrange("p c e -> p (c e)"),
                      dma=f"dwin{slot}",
                      waits=[win_free[slot]] if win_free[slot] else prep_sp)
            bk = eci % 2
            ps = (ps_a0, ps_a1)[bk]
            wt = [wl, t_xn2T,
                  ffn_pa_free[bk] if (ti == 0 and eci < 2) else None,
                  st["paf"][bk]]
            mm = None
            for kc in range(NKD):
                mm = P.op(T, "matmul", ps[:], winb[slot][:, kc, :],
                          xn2T_v[:, kc, :], start=(kc == 0),
                          stop=(kc == NKD - 1), waits=wt if kc == 0 else ())
            if ec < 16:
                ev = P.op(S, "copy", out=u_v[:, ec, :], in_=ps[:],
                          waits=[mm, prev.get("SCANR_free")])
                t_u[ec] = ev
            else:
                ev = P.op(S, "activation", out=szb[:, ec - 16, :], in_=ps[:],
                          func=AF.Silu, waits=[mm, prev.get("szb_free")])
                t_sz[ec - 16] = ev
            st["paf"][bk] = ev
            win_free[slot] = mm

        if pending_m4 is not None:
            pi = pending_m4["ti"]
            t_m4p = pending_m4["evp0"]
            evs = [t_m4p[(0, 0)], t_m4p[(0, 1)]]
            for tcin in range(2):
                tc = 2 + tcin
                ps = (ps_w0, ps_w1)[tcin]
                ev = P.op(V, "tensor_tensor", out=acc[:, tc, :],
                          in0=acc[:, tc, :], in1=ps[:], op=ALU.add,
                          waits=[pending_m4["tp1mm"][tcin]])
                evs.append(ev)
                prev["psw_free%d" % tcin] = ev
                st["pw"][tcin] = ev
            for s_ in range(SEQ):
                P.op(G, "dma_start",
                     out=out_t.rearrange("s (a p) d -> s p a d", p=128)
                     [s_][:, 2 * pi:2 * pi + 2, :],
                     in_=acc[:, 2 * s_:2 * s_ + 2, :],
                     dma="dso", waits=evs if s_ == 0 else ())
            prev["acc_free"] = ("dso", P.count["dso"])
            pending_m4 = None

        t_acc0 = None
        for j in range(4):
            t_acc0 = P.op(V, "tensor_tensor", out=acc[:, j, :],
                          in0=xt_v[:, j, :], in1=xn_v[:, j, :], op=ALU.add,
                          waits=[t_xn[j], prev.get("acc_free") if j == 0 else None])

        t_xqT = None
        for kc in range(NKD):
            t_xqT = transpose_group(
                [(j, xqb_v[:, j, kc * 128:(kc + 1) * 128]) for j in range(4)],
                [(xqT_v[:, kc, :], 0, 512)],
                [t_xq, t_acc0 if kc == 0 else None,
                 t_xn[3] if kc == 0 else None])

        u4 = u_v.rearrange("p c (s t) -> p c s t", s=SEQ)
        th1 = P.op(V, "tensor_copy", out=uhead[:, :, :, 0:3], in_=ctail[:],
                   waits=[t_ct if ti == 0 else prev.get("ctail_w"),
                          prev.get("uhead_free")] + [t_u[ec] for ec in range(16)])
        th2 = P.op(V, "tensor_copy", out=uhead[:, :, :, 3:8],
                   in_=u4[:, :, :, 0:5], waits=[th1])
        t_tailw = P.op(V, "tensor_copy", out=ctail[:],
                       in_=u4[:, :, :, 253:256], waits=[th2])
        prev["ctail_w"] = t_tailw
        t_conv = {}
        for dc in range(NDC):
            ca = (ca_lo if dc < 8 else ca_hi)[:, dc % 8, :] \
                .rearrange("p (s t) -> p s t", s=SEQ)
            w0 = [t_tailw]
            if dc == 0:
                w0 += [prev.get("BbR_free2"), t_xn2[-1], t_xqT]
            if dc == 8:
                w0 += [prev.get("DtR_free2"), t_xn[-1]]
            base = None
            for k in range(4):
                srcm = u4[:, dc, :, max(0, 2 + k - 2): 253 + k]
            for k in range(4):
                srcm = u4[:, dc, :, 2 + k: 253 + k]
                if k == 0:
                    base = P.op(V, "tensor_scalar", out=ca[:, :, 5:256],
                                in0=srcm, scalar1=col(cvw_sc, dc, 0),
                                scalar2=None, op0=ALU.mult, waits=w0)
                else:
                    base = P.op(V, "scalar_tensor_tensor", out=ca[:, :, 5:256],
                                in0=srcm, scalar=col(cvw_sc, dc, k),
                                in1=ca[:, :, 5:256], op0=ALU.mult, op1=ALU.add,
                                waits=[base])
            for k in range(4):
                srch = uhead[:, dc, :, k:k + 5]
                if k == 0:
                    hh = P.op(V, "tensor_scalar", out=ca[:, :, 0:5], in0=srch,
                              scalar1=col(cvw_sc, dc, 0), scalar2=None,
                              op0=ALU.mult, waits=[base])
                else:
                    hh = P.op(V, "scalar_tensor_tensor", out=ca[:, :, 0:5],
                              in0=srch, scalar=col(cvw_sc, dc, k),
                              in1=ca[:, :, 0:5], op0=ALU.mult, op1=ALU.add,
                              waits=[hh])
            sl = P.op(S, "activation", out=ucb[:, dc, :],
                      in_=ca.rearrange("p s t -> p (s t)"), func=AF.Silu,
                      bias=col(cvb_sc, dc),
                      waits=[hh, prev.get("ucb_free") if dc == 0 else None])
            t_conv[dc] = sl
        t_convlast = t_conv[NDC - 1]
        prev["uhead_free"] = hh

        wt = [t_convlast, t_wxp, st["paf"][0]]
        mm = None
        for dc in range(NDC):
            mm = P.op(T, "matmul", ps_a0[0:64, :], wxpT[:, dc, 0:64],
                      ucb[:, dc, :], start=(dc == 0), stop=(dc == NDC - 1),
                      waits=wt if dc == 0 else ())
        t_dl = P.op(V, "tensor_copy", out=dl_b[:], in_=ps_a0[0:64, :],
                    waits=[mm])
        st["paf"][0] = t_dl
        t_bc2 = {}
        for half, dst in ((0, bcrB), (1, bcrC)):
            bk = (half + 1) % 2
            ps = (ps_a0, ps_a1)[bk]
            wt = [t_convlast, st["paf"][bk],
                  ffn_pa_free[bk] if ti == 0 else None]
            mm = None
            for dc in range(NDC):
                mm = P.op(T, "matmul", ps[0:16, :],
                          wxpT[:, dc, 64 + half * 16:80 + half * 16],
                          ucb[:, dc, :], start=(dc == 0), stop=(dc == NDC - 1),
                          waits=wt if dc == 0 else ())
            t_bc2[half] = P.op(V, "tensor_copy", out=dst[:], in_=ps[0:16, :],
                               waits=[mm])
            st["paf"][bk] = t_bc2[half]

        t_dt = {}
        for dc in range(NDC):
            bk = dc % 2
            ps = (ps_a0, ps_a1)[bk]
            wt = [t_dl, t_wdt, st["paf"][bk]]
            mm = P.op(T, "matmul", ps[:], wdtT[:, dc * 128:(dc + 1) * 128],
                      dl_b[:], start=True, stop=True, waits=wt)
            e1 = P.op(S, "activation", out=hsl, in_=ps[:], func=AF.Exp,
                      bias=col(bdt_sc, dc),
                      waits=[mm, t_convlast,
                             prev.get("hsl_free") if dc == 0 else t_dt[dc - 1]])
            e2 = P.op(S, "activation", out=dtb_v[:, dc, :], in_=hsl,
                      func=AF.Ln, bias=col(cst, C_ONE), waits=[e1])
            t_dt[dc] = e2
            st["paf"][bk] = e2

        t_hqT = {}
        last_f2 = None
        kff = 0
        for fsi in range(6):
            f0 = fsi * 512
            f1 = min(f0 + 512, DFF)
            fn = f1 - f0
            wlg = P.op(Q, "dma_start", out=wgb[:, :, 0:fn],
                       in_=wgT_sc.rearrange("k p f -> p k f")[:, :, f0:f1],
                       dma="dwg",
                       waits=[wg_free] if wg_free else list(prep_sp))
            wlu = P.op(Q, "dma_start", out=wub[:, :, 0:fn],
                       in_=wuT_sc.rearrange("k p f -> p k f")[:, :, f0:f1],
                       dma="dwu",
                       waits=[wu_free] if wu_free else list(prep_sp))
            for tc in range(4):
                par = kff % 2
                gps = (ps_a0[:, 0:fn], ps_w0[:, 0:fn])[par]
                ups = (ps_a1[:, 0:fn], ps_w0[:, 512:512 + fn])[par]
                gkey = ("paf", 0) if par == 0 else ("pw", 0)
                ukey = ("paf", 1) if par == 0 else ("pw", 1)
                g_mm = None
                wt0 = [wlg, t_xqT, st[gkey[0]][gkey[1]],
                       psw_free[0] if (ti == 0 and kff < 2) else None]
                for kc in range(NKD):
                    g_mm = P.op(T, "matmul", gps,
                                xqT_v[:, kc, tc * 128:(tc + 1) * 128],
                                wgb[:, kc, 0:fn], start=(kc == 0),
                                stop=(kc == NKD - 1),
                                waits=wt0 if kc == 0 else ())
                u_mm = None
                wt1 = [wlu, st[ukey[0]][ukey[1]]]
                for kc in range(NKD):
                    u_mm = P.op(T, "matmul", ups,
                                xqT_v[:, kc, tc * 128:(tc + 1) * 128],
                                wub[:, kc, 0:fn], start=(kc == 0),
                                stop=(kc == NKD - 1),
                                waits=wt1 if kc == 0 else ())
                hsl_k = hslp[par]
                hql_k = hqlp[par]
                sg = P.op(S, "activation", out=hsl_k[:, 0:fn], in_=gps,
                          func=AF.Silu, waits=[g_mm, t_dt[NDC - 1],
                                               st.get("hslv%d" % par)])
                hm = P.op(V, "tensor_tensor", out=hsl_k[:, 0:fn],
                          in0=hsl_k[:, 0:fn], in1=ups, op=ALU.mult,
                          waits=[sg, u_mm])
                st[gkey[0]][gkey[1]] = hm
                st[ukey[0]][ukey[1]] = hm
                tq = quant(hsl_k[:, 0:fn], hql_k[:, 0:fn], fn // 16,
                           [hm, st.get("hqlt%d" % par),
                            prev.get("SCANR_free") if (ti and kff == 0) else None,
                            t_convlast if kff == 0 else None],
                           main_scr)
                st["hslv%d" % par] = tq
                nkb = fn // 128
                ev = transpose_group(
                    [(j, hql_k[:, j * 128:(j + 1) * 128]) for j in range(nkb)],
                    [(hqT_v[:, (f0 // 128) + j, tc * 128:(tc + 1) * 128],
                      j * 128, (j + 1) * 128) for j in range(nkb)],
                    [tq, t_xn2T if kff == 0 else None,
                     prev.get("hqTR_free2") if kff == 0 else None])
                st["hqlt%d" % par] = st["tp_ev"][st["tp"] - 1]
                for j in range(nkb):
                    t_hqT[((f0 // 128) + j, tc)] = ev
                last_f2 = u_mm
                kff += 1
            wg_free = g_mm
            wu_free = u_mm
        t_hsl_last = st["qprev"]
        prev["hsl_free"] = t_hsl_last

        t_Bb = None
        t_Cb = None
        for n in range(2 * DST):
            bk = n % 2
            ps = (ps_a0, ps_a1)[bk]
            wt = [t_bc2[0], t_bc2[1], st["paf"][bk]]
            if n == 0:
                wt += [t_xn2[-1], t_xq, st["qprev"]]
            if n == DST:
                wt += [last_f2]
            src = bcrB if n < DST else bcrC
            mm = P.op(T, "matmul", ps[:], selb[:, (n % DST) * 128:(n % DST + 1) * 128],
                      src[:], start=True, stop=True, waits=wt)
            if n < DST:
                t_Bb = P.op(V, "tensor_copy", out=Bb_v[:, n, :], in_=ps[:],
                            waits=[mm])
                st["paf"][bk] = t_Bb
            else:
                t_Cb = P.op(V, "tensor_copy", out=Cb_v[:, n - DST, :],
                            in_=ps[:], waits=[mm])
                st["paf"][bk] = t_Cb

        f3_kc = {}
        f3_last_mm = {}
        t_f3_p0 = {}
        for tp in range(2):
            mm = None
            for kc in range(NKF):
                slot = kc % 2
                wl = P.op(G, "dma_start", out=wdb[slot][:], in_=wdT_sc[kc],
                          dma=f"dwd{slot}",
                          waits=[wd_free[slot]] if wd_free[slot] else prep_sp)
                for tcin in range(2):
                    tc = tp * 2 + tcin
                    ps = (ps_w0, ps_w1)[tcin]
                    if kc == 0:
                        wt = [wl, t_hqT[(NKF - 1, tc)],
                              st["pw"][0], st["pw"][1], st.get("hslv1"),
                              psw_free[tcin] if (ti == 0 and tp == 0) else
                              (t_f3_p0[tcin] if tp == 1 else prev.get("psw_free%d" % tcin))]
                    else:
                        wt = [wl]
                    for nh in range(2):
                        mm = P.op(T, "matmul", ps[:, nh * 512:(nh + 1) * 512],
                                  hqT_v[:, kc, tc * 128:(tc + 1) * 128],
                                  wdb[slot][:, nh * 512:(nh + 1) * 512],
                                  start=(kc == 0), stop=(kc == NKF - 1),
                                  waits=[w for w in wt if w] if nh == 0 else ())
                    f3_last_mm[(tp, tcin)] = mm
                if tp == 1:
                    f3_kc[kc] = mm
                wd_free[slot] = mm
            if tp == 0:
                for tcin in range(2):
                    ps = (ps_w0, ps_w1)[tcin]
                    t_f3_p0[tcin] = P.op(V, "tensor_tensor", out=acc[:, tcin, :],
                                         in0=acc[:, tcin, :], in1=ps[:],
                                         op=ALU.add,
                                         waits=[f3_last_mm[(0, tcin)], t_acc0])

        abufs = [argb, dAb]
        iters = [(dc, s_) for dc in range(NDC) for s_ in range(SEQ)]
        t_y = {}
        t_du = {}

        def emit_args(k):
            dc, s_ = iters[k]
            buf = abufs[k % 2]
            w = [t_dt[dc], t_negAb if (ti == 0 and k == 0) else None,
                 sc_prev.get("tree%d" % (k % 2)),
                 sc_prev.get("exp%d" % (k % 2))]
            if ti == 0 and k == 0:
                w.append(t_xq)
            if ti == 0 and k <= 1:
                w.append(t_hsl_last)
            if k <= 1:
                w.append(prev.get("SCANR_free"))
            ta = P.op(V, "tensor_tensor", out=buf[:],
                      in0=dtb_v[:, dc, s_ * 256:(s_ + 1) * 256]
                      .unsqueeze(1).broadcast_to([128, DST, 256]),
                      in1=A_scb[:, dc, :].unsqueeze(2)
                      .broadcast_to([128, DST, 256]),
                      op=ALU.mult, waits=w)
            te = P.op(S, "activation", out=buf.rearrange("p n t -> p (n t)"),
                      in_=buf.rearrange("p n t -> p (n t)"), func=AF.Exp,
                      waits=[ta, sc_prev.get("ldt%d" % (k % 2))])
            sc_prev["exp%d" % (k % 2)] = te
            sc_prev["argt%d" % (k % 2)] = ta
            return te

        t_exp_k = {}
        t_exp_k[0] = emit_args(0)
        for k, (dc, s_) in enumerate(iters):
            buf = abufs[k % 2]
            if s_ == 0:
                t_du[dc] = P.op(V, "tensor_tensor", out=dtusl,
                                in0=dtb_v[:, dc, :], in1=ucb[:, dc, :],
                                op=ALU.mult,
                                waits=[t_dt[dc], t_conv[dc],
                                       sc_prev.get("dbu"),
                                       t_hsl_last if dc == 0 else None])
            tdbu = P.op(V, "tensor_tensor", out=dBub[:],
                        in0=dtusl[:, s_ * 256:(s_ + 1) * 256]
                        .unsqueeze(1).broadcast_to([128, DST, 256]),
                        in1=Bb_v[:, :, s_ * 256:(s_ + 1) * 256],
                        op=ALU.mult,
                        waits=[t_du[dc], t_Bb, sc_prev.get("hC"),
                               sc_prev.get("scan")])
            sc_prev["dbu"] = tdbu
            tsd = P.op(V, "tensor_tensor", out=sdt[:],
                       in0=buf[:, :, 0], in1=hcar[:, dc, :, s_],
                       op=ALU.mult,
                       waits=[t_exp_k[k], t_hc if ti == 0 else None,
                              sc_prev.get("carry")])
            tsd2 = P.op(V, "tensor_tensor", out=dBub[:, :, 0],
                        in0=dBub[:, :, 0], in1=sdt[:], op=ALU.add,
                        waits=[tsd, tdbu])
            tz = P.op(V, "memset", buf[:, :, 0:1], 0.0, waits=[tsd])
            sc_prev["ldt%d" % (k % 2)] = tz
            if k + 1 < len(iters):
                t_exp_k[k + 1] = emit_args(k + 1)
            tscan = P.op(V, "tensor_tensor_scan",
                         out=hb_v.rearrange("p n t -> p (n t)"),
                         data0=buf.rearrange("p n t -> p (n t)"),
                         data1=dBub.rearrange("p n t -> p (n t)"),
                         initial=0.0, op0=ALU.mult, op1=ALU.add,
                         waits=[tsd2, tz, sc_prev.get("hC"),
                                sc_prev.get("carry")])
            sc_prev["ldt%d" % (k % 2)] = tscan
            tcar = P.op(V, "tensor_copy", out=hcar[:, dc, :, s_],
                        in_=hb_v[:, :, 255], waits=[tscan])
            thc = P.op(V, "tensor_tensor", out=dBub[:], in0=hb_v[:],
                       in1=Cb_v[:, :, s_ * 256:(s_ + 1) * 256],
                       op=ALU.mult, waits=[tscan, t_Cb])
            b8 = buf[:, 0:8, :]
            b4 = buf[:, 8:12, :]
            b2 = buf[:, 12:14, :]
            tt8_ = P.op(V, "tensor_tensor", out=b8, in0=dBub[:, 0:8, :],
                        in1=dBub[:, 8:16, :], op=ALU.add,
                        waits=[thc, tscan])
            tt4_ = P.op(V, "tensor_tensor", out=b4, in0=b8[:, 0:4, :],
                        in1=b8[:, 4:8, :], op=ALU.add, waits=[tt8_])
            tt2_ = P.op(V, "tensor_tensor", out=b2, in0=b4[:, 0:2, :],
                        in1=b4[:, 2:4, :], op=ALU.add, waits=[tt4_])
            ty = P.op(V, "tensor_tensor",
                      out=yb_v[:, dc, s_ * 256:(s_ + 1) * 256],
                      in0=b2[:, 0, :], in1=b2[:, 1, :], op=ALU.add,
                      waits=[tt2_, f3_kc[dc]])
            t_y[dc] = ty
            sc_prev["tree%d" % (k % 2)] = ty
            sc_prev["scan"] = tscan
            sc_prev["carry"] = tcar
            sc_prev["hC"] = thc
            sc_prev["tree"] = ty
            sc_prev["exp"] = t_exp_k[k]
        t_f3 = {(0, 0): t_f3_p0[0], (0, 1): t_f3_p0[1]}
        for tcin in range(2):
            tc = 2 + tcin
            ps = (ps_w0, ps_w1)[tcin]
            ev = P.op(V, "tensor_tensor", out=acc[:, tc, :],
                      in0=acc[:, tc, :], in1=ps[:], op=ALU.add,
                      waits=[f3_last_mm[(1, tcin)], t_acc0])
            t_f3[(1, tcin)] = ev

        t_yg = {}
        for dc in range(NDC):
            g1 = P.op(V, "scalar_tensor_tensor", out=ygb_v[:, dc, :],
                      in0=ucb[:, dc, :], scalar=col(D_sc, dc),
                      in1=yb_v[:, dc, :], op0=ALU.mult, op1=ALU.add,
                      waits=[t_y[dc]] + ([sc_prev["exp"], sc_prev["tree"],
                                          prev.get("SCB_m4")] if dc == 0 else []))
            g2 = P.op(V, "tensor_tensor", out=ygb_v[:, dc, :],
                      in0=ygb_v[:, dc, :], in1=szb[:, dc, :], op=ALU.mult,
                      waits=[g1, t_sz[dc]])
            t_yg[dc] = g2

        m4_last_mm = {}
        mm = None
        for dc in range(NDC):
            slot = dc % 2
            wl = P.op(G, "dma_start", out=wob[slot][:], in_=woutT_sc[dc],
                      dma=f"dwo{slot}",
                      waits=[wo_free[slot]] if wo_free[slot] else prep_sp)
            for tcin in range(2):
                tc = tcin
                ps = (ps_w0, ps_w1)[tcin]
                if dc == 0:
                    wt = [wl, t_yg[NDC - 1], t_f3[(1, tcin)]]
                else:
                    wt = [wl]
                for nh in range(2):
                    mm = P.op(T, "matmul", ps[:, nh * 512:(nh + 1) * 512],
                              ygb_v[:, dc, tc * 128:(tc + 1) * 128],
                              wob[slot][:, nh * 512:(nh + 1) * 512],
                              start=(dc == 0), stop=(dc == NDC - 1),
                              waits=wt if nh == 0 else ())
                m4_last_mm[(0, tcin)] = mm
            wo_free[slot] = mm
        pending_m4 = {"mm": dict(m4_last_mm), "ti": ti, "yg": t_yg[NDC - 1]}

        prev["DtR_free"] = sc_prev["argt%d" % ((len(iters) - 1) % 2)]
        prev["DtR_free2"] = t_xq
        prev["BbR_free"] = sc_prev["dbu"]
        prev["BbR_free2"] = t_xq
        prev["CbR_free"] = sc_prev["hC"]
        prev["SCANR_free"] = wo_free[0]
        prev["SCB_m4"] = None
        prev["hqTR_free"] = t_yg[NDC - 1]
        prev["hqTR_free2"] = None
        prev["szb_free"] = t_yg[NDC - 1]
        prev["ucb_free"] = t_yg[NDC - 1]
        prev["bn_free"] = t_mv
        prev["ln_free"] = t_rat
        prev["uhead_free"] = prev["uhead_free"]

    if pending_m4 is not None:
        pi = pending_m4["ti"]
        m4all = []
        for tcin in range(2):
            ps = (ps_w0, ps_w1)[tcin]
            ev = P.op(V, "tensor_tensor", out=acc[:, tcin, :],
                      in0=acc[:, tcin, :], in1=ps[:], op=ALU.add,
                      waits=[pending_m4["mm"][(0, tcin)]])
            m4all.append(ev)
        mm = None
        tp1mm = {}
        for dc in range(NDC):
            slot = dc % 2
            wl = P.op(G, "dma_start", out=wob[slot][:], in_=woutT_sc[dc],
                      dma=f"dwo{slot}",
                      waits=[wo_free[slot]] if wo_free[slot] else prep_sp)
            for tcin in range(2):
                tc = 2 + tcin
                ps = (ps_w0, ps_w1)[tcin]
                if dc == 0:
                    wt = [wl, pending_m4["yg"], m4all[tcin]]
                else:
                    wt = [wl]
                for nh in range(2):
                    mm = P.op(T, "matmul", ps[:, nh * 512:(nh + 1) * 512],
                              ygb_v[:, dc, tc * 128:(tc + 1) * 128],
                              wob[slot][:, nh * 512:(nh + 1) * 512],
                              start=(dc == 0), stop=(dc == NDC - 1),
                              waits=wt if nh == 0 else ())
                tp1mm[tcin] = mm
            wo_free[slot] = mm
        for tcin in range(2):
            tc = 2 + tcin
            ps = (ps_w0, ps_w1)[tcin]
            ev = P.op(V, "tensor_tensor", out=acc[:, tc, :],
                      in0=acc[:, tc, :], in1=ps[:], op=ALU.add,
                      waits=[tp1mm[tcin]])
            m4all.append(ev)
        for s_ in range(SEQ):
            P.op(G, "dma_start",
                 out=out_t.rearrange("s (a p) d -> s p a d", p=128)
                 [s_][:, 2 * pi:2 * pi + 2, :],
                 in_=acc[:, 2 * s_:2 * s_ + 2, :],
                 dma="dso", waits=m4all if s_ == 0 else ())
        pending_m4 = None

    P.op(G, "engine_nop", waits=[("dso", P.count.get("dso", 0))])

    for name in sorted(P.semnames):
        P.sems[name] = stack.enter_context(nc.semaphore(name))
    block = stack.enter_context(nc.Block())
    P.emit(block)
    stack.close()
    return nc


def _consts_array():
    c = np.zeros((128, 8), np.float32)
    c[:, 0] = 1.0
    c[:, 1] = EPS
    c[:, 2] = 1.0 + EPS
    c[:, 3] = EPS * EPS
    c[:, 4] = -0.5
    return c


def _sel_array():
    s = np.zeros((16, 16 * 128), ml_dtypes.bfloat16)
    for n in range(16):
        s[n, n * 128:(n + 1) * 128] = 1
    return s


def _base_inputs(inputs):
    return {
        "ln_g": inputs["L0_attn_ln_g"], "ln_b": inputs["L0_attn_ln_b"],
        "w_in": inputs["L0_attn_W_in"], "conv_w": inputs["L0_attn_conv_w"],
        "conv_b": inputs["L0_attn_conv_b"], "w_xp": inputs["L0_attn_W_xp"],
        "w_dt": inputs["L0_attn_W_dt"], "b_dt": inputs["L0_attn_b_dt"],
        "a_log": inputs["L0_attn_A_log"], "d_vec": inputs["L0_attn_D"],
        "w_out": inputs["L0_attn_W_out"], "wg": inputs["L0_ffn_w_gate"],
        "wu": inputs["L0_ffn_w_up"], "wd": inputs["L0_ffn_w_down"],
        "eye_b": np.eye(128, dtype=ml_dtypes.bfloat16),
        "ones1f": np.ones((1, 128), np.float32),
        "sel": _sel_array(),
        "csts": _consts_array(),
    }


_NC_CACHE = {}


def kernel(**inputs):
    if "prog" not in _NC_CACHE:
        _NC_CACHE["prog"] = build_program()
    nc = _NC_CACHE["prog"]
    x = np.ascontiguousarray(inputs["L0_attn_x"])
    base = _base_inputs(inputs)
    in_maps = []
    for c in range(NCORES):
        m = dict(base)
        m["x"] = x[c * SEQ:(c + 1) * SEQ]
        in_maps.append(m)
    res = run_bass_kernel_spmd(nc, in_maps, core_ids=list(range(NCORES)))
    out = np.concatenate([r["out"] for r in res.results], axis=0)
    return out.astype(np.float32)


# version 09
# speedup vs baseline: 1.0937x; 1.0435x over previous
import sys
sys.path.insert(0, "/opt/trn_rl_repo")
import numpy as np
import ml_dtypes

import concourse.bass as bass
import concourse.mybir as mybir
from concourse.bass_utils import run_bass_kernel_spmd

F32 = mybir.dt.float32
BF16 = mybir.dt.bfloat16
AF = mybir.ActivationFunctionType
ALU = mybir.AluOpType
AX = mybir.AxisListType

B, L, D = 16, 2048, 1024
NCORES = 8
SEQ = 2
DI = 2048
DST = 16
DFF = 2816
TC = 256
TOK = SEQ * TC
NT = L // TC
NDC = DI // 128
NKD = D // 128
NKF = DFF // 128
EPS = 1e-5
MID = [0.25, 0.75, 1.25, 1.75, 2.5, 3.5, 5.0]
MIDSQ = [m * m for m in MID]
MIDW = [0.5, 0.5, 0.5, 0.5, 1.0, 1.0, 2.0]

CFG_NT = NT


class Plan:
    ENGS = ("v", "s", "t", "g", "q")

    def __init__(self):
        self.ops = {e: [] for e in self.ENGS}
        self.count = {}
        self.semnames = set("S" + e for e in self.ENGS)
        self.sems = {}

    def dma_sem(self, name):
        self.semnames.add(name)

    def op(self, eng, method, *args, waits=(), dma=None, **kw):
        import traceback
        fr = traceback.extract_stack(limit=4)
        tbs = ";".join(f"{f.lineno}" for f in fr[:-1])
        w = tuple(t for t in waits if t is not None)
        if dma is not None:
            self.count[dma] = self.count.get(dma, 0) + 16
            tok = (dma, self.count[dma])
        else:
            s = "S" + eng
            self.count[s] = self.count.get(s, 0) + 1
            tok = (s, self.count[s])
        self.ops[eng].append((method, args, kw, w, tok, tbs))
        return tok

    def emit(self, block):
        plan = self
        secs = {"v": block.vector, "s": block.scalar, "t": block.tensor,
                "g": block.gpsimd, "q": block.sync}

        def make_body(e):
            own = "S" + e

            def body(engine):
                waited = {}
                for (method, args, kw, waits, tok, src) in plan.ops[e]:
                    need = {}
                    for (sname, val) in waits:
                        if val > need.get(sname, -1):
                            need[sname] = val
                    for sname in sorted(need):
                        val = need[sname]
                        if waited.get(sname, -1) >= val:
                            continue
                        engine.wait_ge(plan.sems[sname], val)
                        waited[sname] = val
                    try:
                        instr = getattr(engine, method)(*args, **kw)
                    except Exception as ex:
                        raise RuntimeError(f"emit failed: {method} planned at {src}: {ex}") from ex
                    sname, _ = tok
                    instr.then_inc(plan.sems[sname], 1 if sname == own else 16)
            return body

        for e in self.ENGS:
            if self.ops[e]:
                secs[e](make_body(e))


def col(ap, *idx):
    sl = tuple([slice(None)] + [slice(i, i + 1) for i in idx])
    return ap[sl]


def cdiv(a, b):
    return (a + b - 1) // b


def build_program():
    nc = bass.Bass("TRN2", target_bir_lowering=False)
    P = Plan()
    V, S, T, G, Q = "v", "s", "t", "g", "q"

    def din(name, shape, dt=F32):
        return nc.dram_tensor(name, shape, dt, kind="ExternalInput").ap()

    x_in = din("x", [SEQ, L, D])
    ln_g = din("ln_g", [D]); ln_b = din("ln_b", [D])
    w_in = din("w_in", [2 * DI, D])
    conv_w = din("conv_w", [DI, 4]); conv_b = din("conv_b", [DI])
    w_xp = din("w_xp", [96, DI])
    w_dt = din("w_dt", [DI, 64]); b_dt = din("b_dt", [DI])
    a_log = din("a_log", [DI, DST]); d_vec = din("d_vec", [DI])
    w_out = din("w_out", [D, DI])
    wg_in = din("wg", [DFF, D]); wu_in = din("wu", [DFF, D])
    wd_in = din("wd", [D, DFF])
    eye_b = din("eye_b", [128, 128], BF16)
    ones1f = din("ones1f", [1, 128], F32)
    sel_in = din("sel", [16, 16 * 128], BF16)
    csts = din("csts", [128, 8], F32)
    C_ONE, C_EPS, C_LNS, C_LNB, C_NEGH = 0, 1, 2, 3, 4

    out_t = nc.dram_tensor("out", [SEQ, L, D], F32, kind="ExternalOutput").ap()

    def scr(name, shape, dt=BF16):
        return nc.dram_tensor(name, shape, dt, kind="Internal").ap()

    winT_sc = scr("winT_sc", [32, 128, NKD, 128])
    woutT_sc = scr("woutT_sc", [NDC, 128, D])
    wgT_sc = scr("wgT_sc", [NKD, 128, DFF])
    wuT_sc = scr("wuT_sc", [NKD, 128, DFF])
    wdT_sc = scr("wdT_sc", [NKF, 128, D])

    from contextlib import ExitStack
    stack = ExitStack()

    def sb(name, shape, dt=F32):
        return stack.enter_context(nc.sbuf_tensor(name, shape, dt))

    def psum(name, shape, dt=F32):
        return stack.enter_context(nc.psum_tensor(name, shape, dt))

    cst = sb("cst", [128, 8])
    eyeb = sb("eyeb", [128, 128], BF16)
    onesf = sb("onesf", [1, 128])
    selb = sb("selb", [16, 16 * 128], BF16)
    A_sc = sb("A_sc", [128, NDC, DST])
    A_scb = sb("A_scb", [128, NDC, DST], BF16)
    D_sc = sb("D_sc", [128, NDC])
    bdt_sc = sb("bdt_sc", [128, NDC])
    cvb_sc = sb("cvb_sc", [128, NDC])
    cvw_sc = sb("cvw_sc", [128, NDC, 4])
    g_bc = sb("g_bc", [128, D])
    b_bc = sb("b_bc", [128, D])
    wxpT = sb("wxpT", [128, NDC, 96], BF16)
    wdtT = sb("wdtT", [64, DI], BF16)
    hcar = sb("hcar", [128, NDC, DST, SEQ])
    ctail = sb("ctail", [128, NDC, SEQ, 3])
    bnb = sb("bnb", [128, 4, 12])
    mv = sb("mv", [128, 4, 2])
    rr = sb("rr", [128, 4])
    rho = sb("rho", [128, 4])
    rat = sb("rat", [128, 4])
    lnt = sb("lnt", [128, 4])
    qm = sb("qm", [128, 64])
    qs = sb("qs", [128, 64])
    qr = sb("qr", [128, 64])
    qsb = sb("qsb", [128, 64], BF16)
    sdt = sb("sdt", [128, DST])
    uhead = sb("uhead", [128, NDC, SEQ, 8])
    uheadb = None
    dl_b = sb("dl_b", [64, TOK], BF16)
    bcrB = sb("bcrB", [16, TOK], BF16)
    bcrC = sb("bcrC", [16, TOK], BF16)

    acc = sb("acc", [128, 4, D])
    szb = sb("szb", [128, NDC, TOK], BF16)
    ucb = sb("ucb", [128, NDC, TOK], BF16)

    SCANR = sb("SCANR", [128, 8192])
    BbR = sb("BbR", [128, 8192], BF16)
    CbR = sb("CbR", [128, 8192], BF16)
    DtR = sb("DtR", [128, 8192], BF16)
    hqTR = sb("hqTR", [128, 11264], BF16)

    winb = [sb(f"winb{i}", [128, NKD, 128], BF16) for i in range(2)]
    wgb = sb("wgb", [128, NKD, 512], BF16)
    wub = sb("wub", [128, NKD, 512], BF16)
    wdb = [sb(f"wdb{i}", [128, D], BF16) for i in range(2)]
    wob = [sb(f"wob{i}", [128, D], BF16) for i in range(2)]

    SCB = SCANR[:].bitcast(BF16)
    u_v = SCANR[:].rearrange("p (c t) -> p c t", c=NDC)
    pp32 = [SCANR[:, 0:4096], SCANR[:, 4096:8192]]
    qv_v = SCANR[:, 0:1024]
    qa2_v = SCANR[:, 1024:2048]
    qABCD = [SCB[:, 4096 + i * 1024: 4096 + (i + 1) * 1024] for i in range(4)]
    argb = SCB[:, 0:4096].rearrange("p (n t) -> p n t", n=DST)
    dAb = SCB[:, 4096:8192].rearrange("p (n t) -> p n t", n=DST)
    dBub = SCB[:, 8192:12288].rearrange("p (n t) -> p n t", n=DST)
    hb_v = SCB[:, 12288:16384].rearrange("p (n t) -> p n t", n=DST)
    tr8 = SCB[:, 0:2048].rearrange("p (n t) -> p n t", n=8)
    tr4 = SCB[:, 2048:3072].rearrange("p (n t) -> p n t", n=4)
    tr2 = SCB[:, 3072:3584].rearrange("p (n t) -> p n t", n=2)
    ygb_v = SCB[:, 0:8192].rearrange("p (c t) -> p c t", c=NDC)

    BbF = BbR[:].bitcast(F32)
    xqb_v = BbR[:, 0:4096].rearrange("p (j d) -> p j d", j=4)
    xn2b_v = BbR[:, 4096:8192].rearrange("p (j d) -> p j d", j=4)
    ca_lo = BbF[:].rearrange("p (c t) -> p c t", c=8)
    Bb_v = BbR[:].rearrange("p (n t) -> p n t", n=DST)
    ppb = [BbR[:, 0:4096], BbR[:, 4096:8192]]
    hslp = [BbF[:, 0:512], BbF[:, 512:1024]]
    hsl = hslp[0]
    hqlp = [BbR[:, 2048:2560], BbR[:, 2560:3072]]

    CbF = CbR[:].bitcast(F32)
    xn_v = CbF[:].rearrange("p (j d) -> p j d", j=4)
    xqT_v = CbR[:, 0:4096].rearrange("p (k t) -> p k t", k=NKD)
    Cb_v = CbR[:].rearrange("p (n t) -> p n t", n=DST)
    ppq = [CbR[:, 0:4096], CbR[:, 4096:8192]]

    DtF = DtR[:].bitcast(F32)
    xt_v = DtF[:].rearrange("p (j d) -> p j d", j=4)
    ca_hi = DtF[:].rearrange("p (c t) -> p c t", c=8)
    dtb_v = DtR[:].rearrange("p (c t) -> p c t", c=NDC)

    hqF = hqTR[:].bitcast(F32)
    xn2T_v = hqTR[:, 0:4096].rearrange("p (k t) -> p k t", k=NKD)
    hqT_v = hqTR[:].rearrange("p (k t) -> p k t", k=NKF)
    yb_v = hqTR[:, 0:8192].rearrange("p (c t) -> p c t", c=NDC)
    dtusl = uhead[:].rearrange("p a b c -> p (a b c)").bitcast(BF16)[:, 0:512]
    pqv = hqF[:, 0:1024]
    pqa2 = hqF[:, 1024:2048]
    pqABCD = [hqTR[:, 4096 + i * 1024: 4096 + (i + 1) * 1024] for i in range(4)]
    x1qv = hqF[:, 2048:2560]
    x1qa2 = hqF[:, 2560:3072]
    x1ABCD = [hqTR[:, 6144 + i * 512: 6144 + (i + 1) * 512] for i in range(4)]
    p1s = hqF[:, 2048:3072]
    p2s = hqF[:, 3072:4096]

    ps_a0 = psum("ps_a0", [128, 512]); ps_a1 = psum("ps_a1", [128, 512])
    ps_w0 = psum("ps_w0", [128, 1024]); ps_w1 = psum("ps_w1", [128, 1024])
    ps_t0 = psum("ps_t0", [128, 1024], BF16)
    ps_t1 = psum("ps_t1", [128, 1024], BF16)
    ps_ts = [ps_t0, ps_t1]

    for s in ["dld0", "dld1", "dsp0", "dsp1", "dx", "dso", "dwin0", "dwin1",
              "dwg", "dwu", "dwd0", "dwd1", "dwo0", "dwo1", "dwoq0"]:
        P.dma_sem(s)

    st = {"tp": 0, "tp_ev": {}, "paf": {0: None, 1: None}, "pw": {0: None, 1: None}, "pab": 0, "qprev": None}

    def transpose_group(blocks, evacs, wt):
        k = st["tp"]
        pst = ps_ts[k % 2]
        w = list(wt)
        if (k - 2) in st["tp_ev"]:
            w.append(st["tp_ev"][k - 2])
        mm = None
        for (j, src) in blocks:
            mm = P.op(T, "transpose", pst[:, j * 128:(j + 1) * 128], src,
                      eyeb[:], waits=(w + [ld_all]) if mm is None else ())
        ev = None
        for (dst, c0, c1) in evacs:
            ev = P.op(S, "copy", out=dst, in_=pst[:, c0:c1], waits=[mm])
        st["tp_ev"][k] = ev
        st["tp"] = k + 1
        return ev

    def quant(src_ap, dst_ap, nblk, wtoks, scratch, sq=False):
        qvv, qaa, (qa, qb_, qc, qd) = scratch
        n = nblk * 16
        mm_ = qm[:, 0:nblk]; ss = qs[:, 0:nblk]; rs = qr[:, 0:nblk]
        sbv = qsb[:, 0:nblk]
        vv = qvv[:, 0:n]; aa = qaa[:, 0:n]
        qa = qa[:, 0:n]; qb_ = qb_[:, 0:n]; qc = qc[:, 0:n]; qd = qd[:, 0:n]
        w0 = list(wtoks) + ([st["qprev"]] if st["qprev"] else [])
        src3 = src_ap.rearrange("p (b j) -> p b j", j=16)
        t_m = P.op(V, "tensor_reduce", out=mm_, in_=src3, axis=AX.X, op=ALU.max,
                   apply_absolute_value=True, waits=w0)
        t_s = P.op(V, "tensor_scalar", out=ss, in0=mm_, scalar1=1.0 / 6.0,
                   scalar2=1e-12, op0=ALU.mult, op1=ALU.max, waits=[t_m])
        t_r = P.op(V, "reciprocal", out=rs, in_=ss, waits=[t_s])
        t_sb = P.op(V, "tensor_copy", out=sbv, in_=ss, waits=[t_s])
        rb = rs.unsqueeze(2).broadcast_to([128, nblk, 16])
        t_v = P.op(V, "tensor_tensor", out=vv.rearrange("p (b j) -> p b j", j=16),
                   in0=src3, in1=rb, op=ALU.mult, waits=[t_r])
        if sq:
            t_a = P.op(V, "tensor_tensor", out=aa, in0=vv, in1=vv,
                       op=ALU.mult, waits=[t_v])
        else:
            t_a = P.op(S, "activation", out=aa, in_=vv, func=AF.Abs,
                       waits=[t_v])
        thr = MIDSQ if sq else MID
        def cmp(dst, i, w):
            return P.op(V, "tensor_scalar", out=dst, in0=aa, scalar1=thr[i],
                        scalar2=MIDW[i], op0=ALU.is_gt, op1=ALU.mult, waits=w)
        c0 = cmp(qa, 0, [t_a]); c1 = cmp(qb_, 1, [t_a])
        s01 = P.op(V, "tensor_tensor", out=qc, in0=qa, in1=qb_, op=ALU.add,
                   waits=[c0, c1])
        c2 = cmp(qa, 2, [s01]); c3 = cmp(qb_, 3, [s01])
        s23 = P.op(V, "tensor_tensor", out=qd, in0=qa, in1=qb_, op=ALU.add,
                   waits=[c2, c3])
        s03 = P.op(V, "tensor_tensor", out=qc, in0=qc, in1=qd, op=ALU.add,
                   waits=[s23])
        c4 = cmp(qa, 4, [s03]); c5 = cmp(qb_, 5, [s03])
        s45 = P.op(V, "tensor_tensor", out=qd, in0=qa, in1=qb_, op=ALU.add,
                   waits=[c4, c5])
        c6 = cmp(qa, 6, [s45])
        s46 = P.op(V, "tensor_tensor", out=qd, in0=qd, in1=qa, op=ALU.add,
                   waits=[c6])
        lvl = P.op(V, "tensor_tensor", out=qc, in0=qc, in1=qd, op=ALU.add,
                   waits=[s46])
        sg = P.op(V, "tensor_scalar", out=qb_, in0=vv, scalar1=0.0,
                  scalar2=None, op0=ALU.is_ge, waits=[lvl])
        sg2 = P.op(V, "tensor_scalar", out=qb_, in0=qb_, scalar1=2.0,
                   scalar2=-1.0, op0=ALU.mult, op1=ALU.add, waits=[sg])
        tw = P.op(V, "tensor_tensor", out=qa, in0=qc, in1=qb_, op=ALU.mult,
                  waits=[sg2])
        sbb = sbv.unsqueeze(2).broadcast_to([128, nblk, 16])
        tq = P.op(V, "tensor_tensor",
                  out=dst_ap.rearrange("p (b j) -> p b j", j=16),
                  in0=qa.rearrange("p (b j) -> p b j", j=16), in1=sbb,
                  op=ALU.mult, waits=[tw, t_sb])
        st["qprev"] = tq
        return tq

    MAIN_SCR = (qv_v.unsqueeze(0)[0:0] if False else None)
    main_scr = (SCANR[:, 0:1024], SCANR[:, 1024:2048], qABCD)
    prep_scr = (pqv, pqa2, pqABCD)
    p1_scr = (x1qv, x1qa2, x1ABCD)

    dl = lambda **kw: P.op(Q, "dma_start", **kw)
    dl(out=cst[:], in_=csts, dma="dld0")
    dl(out=eyeb[:], in_=eye_b, dma="dld0")
    dl(out=onesf[:], in_=ones1f, dma="dld0")
    dl(out=selb[:], in_=sel_in, dma="dld0")
    dl(out=A_sc[:], in_=a_log.rearrange("(c p) n -> p c n", p=128), dma="dld0")
    dl(out=D_sc[:], in_=d_vec.rearrange("(c p) -> p c", p=128), dma="dld0", allow_slow_non_contiguous=True)
    dl(out=bdt_sc[:], in_=b_dt.rearrange("(c p) -> p c", p=128), dma="dld0", allow_slow_non_contiguous=True)
    dl(out=cvb_sc[:], in_=conv_b.rearrange("(c p) -> p c", p=128), dma="dld0", allow_slow_non_contiguous=True)
    dl(out=cvw_sc[:], in_=conv_w.rearrange("(c p) k -> p c k", p=128),
       dma="dld0")
    dl(out=g_bc[0:1, :], in_=ln_g.unsqueeze(0), dma="dld0")
    ld_all = dl(out=b_bc[0:1, :], in_=ln_b.unsqueeze(0), dma="dld0")

    t_exp = P.op(S, "activation", out=A_sc[:], in_=A_sc[:], func=AF.Exp,
                 waits=[ld_all])
    t_negA = P.op(V, "tensor_scalar_mul", out=A_sc[:], in0=A_sc[:],
                  scalar1=-1.0, waits=[t_exp])
    t_negAb = P.op(V, "tensor_copy", out=A_scb[:], in_=A_sc[:], waits=[t_negA])

    brc = []
    for half in range(2):
        brc.append(P.op(T, "matmul", ps_w0[:, half * 512:(half + 1) * 512],
                        onesf[:], g_bc[0:1, half * 512:(half + 1) * 512],
                        start=True, stop=True, waits=[ld_all]))
        brc.append(P.op(T, "matmul", ps_w1[:, half * 512:(half + 1) * 512],
                        onesf[:], b_bc[0:1, half * 512:(half + 1) * 512],
                        start=True, stop=True, waits=[ld_all]))
    t_gbc = P.op(V, "tensor_copy", out=g_bc[:], in_=ps_w0[:], waits=[brc[-1]])
    t_bbc = P.op(V, "tensor_copy", out=b_bc[:], in_=ps_w1[:], waits=[brc[-1]])

    t_hc = P.op(V, "memset", hcar[:], 0.0)
    t_ct = P.op(V, "memset", ctail[:], 0.0)

    pst8 = {"lc": [None, None], "bc": [None, None], "qc": [None, None], "i": 0}

    def prep_slab(src_dram, n_kblk, quant_nblk, stores, part=128):
        i = pst8["i"] % 2
        n = n_kblk * 128
        tl = P.op(Q, "dma_start", out=pp32[i][0:part, 0:n], in_=src_dram,
                  dma=f"dld{i}", waits=[pst8["lc"][i], ld_all])
        if quant_nblk:
            tc_ = None
            for q0 in range(0, n, 1024):
                q1 = min(q0 + 1024, n)
                tc_ = quant(pp32[i][0:part, q0:q1], ppb[i][0:part, q0:q1],
                            (q1 - q0) // 16, [tl, pst8["bc"][i]], prep_scr)
        else:
            tc_ = P.op(V, "tensor_copy", out=ppb[i][0:part, 0:n],
                       in_=pp32[i][0:part, 0:n], waits=[tl, pst8["bc"][i]])
        pst8["lc"][i] = tc_
        evs = []
        lastmm = None
        for g0 in range(0, n_kblk, 4):
            gsz = min(4, n_kblk - g0)
            k = st["tp"]
            pst = ps_ts[k % 2]
            w = [tc_, pst8["qc"][i]]
            if (k - 2) in st["tp_ev"]:
                w.append(st["tp_ev"][k - 2])
            mm = None
            for j in range(gsz):
                mm = P.op(T, "transpose", pst[:, j * 128:(j + 1) * 128],
                          ppb[i][0:part, (g0 + j) * 128:(g0 + j + 1) * 128],
                          eyeb[:],
                          waits=([t for t in w if t] + [ld_all])
                          if mm is None else ())
            ev = P.op(V, "tensor_copy",
                      out=ppq[i][:, g0 * 128:(g0 + gsz) * 128],
                      in_=pst[:, 0:gsz * 128], waits=[mm])
            st["tp_ev"][k] = ev
            st["tp"] = k + 1
            evs.append(ev)
            lastmm = mm
        pst8["bc"][i] = lastmm
        stok = None
        for (dst, c0, c1) in stores:
            src = ppq[i][:, c0:c1]
            if dst.ndim == 3:
                src = src.rearrange("p (k e) -> p k e", e=128)
            stok = P.op(Q, "dma_start", out=dst, in_=src,
                        dma=f"dsp{i}", waits=evs)
        pst8["qc"][i] = stok
        pst8["i"] += 1
        return stok

    for ec in range(32):
        prep_slab(w_in[ec * 128:(ec + 1) * 128, :], NKD, 0,
                  [(winT_sc[ec].rearrange("p c e -> p (c e)"), 0, NKD * 128)])
    for oc in range(8):
        prep_slab(w_out[oc * 128:(oc + 1) * 128, :], NDC, 0,
                  [(woutT_sc.rearrange("c p f -> p c f")[:, :, oc * 128:(oc + 1) * 128], 0, NDC * 128)])
    for (w_dram, dstsc) in ((wg_in, wgT_sc), (wu_in, wuT_sc)):
        for fc in range(NKF):
            prep_slab(w_dram[fc * 128:(fc + 1) * 128, :], NKD, 64,
                      [(dstsc.rearrange("k p f -> p k f")[:, :, fc * 128:(fc + 1) * 128], 0, NKD * 128)])
    for oc in range(8):
        prep_slab(wd_in[oc * 128:(oc + 1) * 128, :], NKF, NKF * 8,
                  [(wdT_sc.rearrange("k p f -> p k f")[:, :, oc * 128:(oc + 1) * 128], 0, NKF * 128)])

    i = pst8["i"] % 2
    tl = P.op(Q, "dma_start", out=pp32[i][0:96, 0:DI], in_=w_xp,
              dma=f"dld{i}", waits=[pst8["lc"][i]])
    tcc = P.op(V, "tensor_copy", out=ppb[i][0:96, 0:DI],
               in_=pp32[i][0:96, 0:DI], waits=[tl, pst8["bc"][i]])
    t_wxp = None
    for g0 in range(0, NDC, 4):
        k = st["tp"]; pst = ps_ts[k % 2]
        w = [tcc] + ([st["tp_ev"][k - 2]] if (k - 2) in st["tp_ev"] else [])
        mm = None
        for j in range(4):
            dc = g0 + j
            mm = P.op(T, "transpose", pst[:, j * 128:j * 128 + 96],
                      ppb[i][0:96, dc * 128:(dc + 1) * 128], eyeb[0:96, 0:96],
                      waits=(w + [ld_all]) if mm is None else ())
        for j in range(4):
            dc = g0 + j
            t_wxp = P.op(V, "tensor_copy", out=wxpT[:, dc, :],
                         in_=pst[:, j * 128:j * 128 + 96], waits=[mm])
        st["tp_ev"][k] = t_wxp
        st["tp"] = k + 1
    pst8["lc"][i] = tcc
    pst8["bc"][i] = mm
    pst8["i"] += 1

    i = pst8["i"] % 2
    tl = P.op(Q, "dma_start",
              out=pp32[i][:, 0:NDC * 64].rearrange("p (c r) -> p c r", c=NDC),
              in_=w_dt.rearrange("(c p) r -> p c r", p=128),
              dma=f"dld{i}", waits=[pst8["lc"][i]])
    tcc = P.op(V, "tensor_copy", out=ppb[i][:, 0:NDC * 64],
               in_=pp32[i][:, 0:NDC * 64], waits=[tl, pst8["bc"][i]])
    t_wdt = None
    for g0 in range(0, NDC, 4):
        k = st["tp"]; pst = ps_ts[k % 2]
        w = [tcc] + ([st["tp_ev"][k - 2]] if (k - 2) in st["tp_ev"] else [])
        mm = None
        for j in range(4):
            dc = g0 + j
            mm = P.op(T, "transpose", pst[0:64, j * 128:(j + 1) * 128],
                      ppb[i][:, dc * 64:(dc + 1) * 64], eyeb[:],
                      waits=(w + [ld_all]) if mm is None else ())
        t_wdt = P.op(V, "tensor_copy",
                     out=wdtT[:, g0 * 128:(g0 + 4) * 128],
                     in_=pst[0:64, 0:512], waits=[mm])
        st["tp_ev"][k] = t_wdt
        st["tp"] = k + 1
    pst8["lc"][i] = tcc
    pst8["bc"][i] = mm
    pst8["i"] += 1

    prep_sp = [("dsp0", P.count.get("dsp0", 0)), ("dsp1", P.count.get("dsp1", 0))]
    prep_v = ("Sv", P.count["Sv"])

    prev = {}
    ffn_pa_free = [t_gbc, t_bbc]
    psw_free = [t_gbc, t_bbc]
    win_free = [None, None]
    wg_free = None; wu_free = None
    wd_free = [None, None]
    wo_free = [None, None]
    sc_prev = {}
    pending_m4 = None

    for ti in range(CFG_NT):
        t0 = ti * TC
        xw = list(prep_sp) + [prev.get("DtR_free")]
        t_x = None
        for s_ in range(SEQ):
            t_x = P.op(Q, "dma_start",
                       out=xt_v[:, 2 * s_:2 * s_ + 2, :],
                       in_=x_in.rearrange("s (a p) d -> s p a d", p=128)
                       [s_][:, 2 * ti:2 * ti + 2, :],
                       dma="dx", waits=xw if s_ == 0 else ())

        bn = []
        for j in range(4):
            bn.append(P.op(V, "bn_stats", out=bnb[:, j, 0:6],
                           in_=xt_v[:, j, 0:512],
                           waits=[t_x, prev.get("bn_free"),
                                  prep_v if ti == 0 else None]))
            bn.append(P.op(V, "bn_stats", out=bnb[:, j, 6:12],
                           in_=xt_v[:, j, 512:1024], waits=[]))
        t_mv = None
        for j in range(4):
            t_mv = P.op(V, "bn_aggr", out=mv[:, j, :], in_=bnb[:, j, :],
                        waits=[bn[-1]] if j == 0 else ())
        t_ln1 = P.op(S, "activation", out=lnt[:], in_=mv[:, :, 1], func=AF.Ln,
                     bias=col(cst, C_EPS),
                     waits=[t_mv, ld_all, prev.get("ln_free")])
        t_r = P.op(S, "activation", out=rr[:], in_=lnt[:], func=AF.Exp,
                   scale=col(cst, C_NEGH), waits=[t_ln1])
        t_ln2 = P.op(S, "activation", out=lnt[:], in_=mv[:, :, 1], func=AF.Ln,
                     scale=col(cst, C_LNS), bias=col(cst, C_LNB), waits=[t_r])
        t_rho = P.op(S, "activation", out=rho[:], in_=lnt[:], func=AF.Exp,
                     scale=col(cst, C_NEGH), waits=[t_ln2])
        t_rci = P.op(V, "reciprocal", out=lnt[:], in_=rr[:],
                     waits=[t_rho, t_mv])
        t_rat = P.op(V, "tensor_tensor", out=rat[:], in0=rho[:], in1=lnt[:],
                     op=ALU.mult, waits=[t_rci])

        t_xn = []
        t_xn2 = []
        for j in range(4):
            a = P.op(V, "tensor_scalar", out=xn_v[:, j, :], in0=xt_v[:, j, :],
                     scalar1=col(mv, j, 0), scalar2=col(rr, j),
                     op0=ALU.subtract, op1=ALU.mult,
                     waits=[t_r, prev.get("CbR_free")])
            t_xn.append(a)
            pass
            c_ = P.op(V, "tensor_scalar", out=p1s, in0=xn_v[:, j, :],
                      scalar1=col(rat, j), scalar2=None, op0=ALU.mult,
                      waits=[t_rat, a, prev.get("hqTR_free") if (ti and j == 0) else None])
            d_ = P.op(V, "tensor_tensor", out=p2s, in0=p1s, in1=g_bc[:],
                      op=ALU.mult, waits=[c_, t_gbc])
            e_ = P.op(V, "tensor_tensor", out=xn2b_v[:, j, :], in0=p2s,
                      in1=b_bc[:], op=ALU.add,
                      waits=[d_, t_bbc, prev.get("BbR_free") if j == 0 else None])
            t_xn2.append(e_)


        if pending_m4 is not None:
            t_m4p = {}
            for tcin in range(2):
                ps = (ps_w0, ps_w1)[tcin]
                ev = P.op(V, "tensor_tensor", out=acc[:, tcin, :],
                          in0=acc[:, tcin, :], in1=ps[:], op=ALU.add,
                          waits=[pending_m4["mm"][(0, tcin)]])
                t_m4p[(0, tcin)] = ev
            mm = None
            tp1mm = {}
            for dc in range(NDC):
                slot = dc % 2
                wl = P.op(G, "dma_start", out=wob[slot][:], in_=woutT_sc[dc],
                          dma=f"dwo{slot}",
                          waits=[wo_free[slot]] if wo_free[slot] else prep_sp)
                for tcin in range(2):
                    tc = 2 + tcin
                    ps = (ps_w0, ps_w1)[tcin]
                    if dc == 0:
                        wt = [wl, pending_m4["yg"], t_m4p[(0, tcin)]]
                    else:
                        wt = [wl]
                    for nh in range(2):
                        mm = P.op(T, "matmul", ps[:, nh * 512:(nh + 1) * 512],
                                  ygb_v[:, dc, tc * 128:(tc + 1) * 128],
                                  wob[slot][:, nh * 512:(nh + 1) * 512],
                                  start=(dc == 0), stop=(dc == NDC - 1),
                                  waits=wt if nh == 0 else ())
                    tp1mm[tcin] = mm
                wo_free[slot] = mm
            pending_m4["evp0"] = t_m4p
            pending_m4["tp1mm"] = tp1mm
            prev["SCANR_free"] = mm
            prev["ygb_free"] = mm

        t_xn2T = None
        for kc in range(NKD):
            t_xn2T = transpose_group(
                [(j, xn2b_v[:, j, kc * 128:(kc + 1) * 128]) for j in range(4)],
                [(xn2T_v[:, kc, :], 0, 512)],
                [t_xn2[-1], prev.get("hqTR_free") if kc == 0 else None])
        t_u = {}
        t_sz = {}
        ec_order = list(range(16, 32)) + list(range(16))
        for eci, ec in enumerate(ec_order):
            slot = eci % 2
            wl = P.op(Q, "dma_start",
                      out=winb[slot][:],
                      in_=winT_sc[ec].rearrange("p c e -> p (c e)"),
                      dma=f"dwin{slot}",
                      waits=[win_free[slot]] if win_free[slot] else prep_sp)
            bk = eci % 2
            ps = (ps_a0, ps_a1)[bk]
            wt = [wl, t_xn2T,
                  ffn_pa_free[bk] if (ti == 0 and eci < 2) else None,
                  st["paf"][bk]]
            mm = None
            for kc in range(NKD):
                mm = P.op(T, "matmul", ps[:], winb[slot][:, kc, :],
                          xn2T_v[:, kc, :], start=(kc == 0),
                          stop=(kc == NKD - 1), waits=wt if kc == 0 else ())
            if ec < 16:
                ev = P.op(S, "copy", out=u_v[:, ec, :], in_=ps[:],
                          waits=[mm, prev.get("SCANR_free")])
                t_u[ec] = ev
            else:
                ev = P.op(S, "activation", out=szb[:, ec - 16, :], in_=ps[:],
                          func=AF.Silu, waits=[mm, prev.get("szb_free")])
                t_sz[ec - 16] = ev
            st["paf"][bk] = ev
            win_free[slot] = mm

        t_xq = None
        for j in range(4):
            for hh in range(2):
                t_xq = quant(xn_v[:, j, hh * 512:(hh + 1) * 512],
                             xqb_v[:, j, hh * 512:(hh + 1) * 512], 32,
                             [t_xn[j],
                              t_xn2[-1] if (j == 0 and hh == 0) else None,
                              prev.get("hqTR_free") if (j == 0 and hh == 0) else None],
                             p1_scr, sq=True)

        if pending_m4 is not None:
            pi = pending_m4["ti"]
            t_m4p = pending_m4["evp0"]
            evs = [t_m4p[(0, 0)], t_m4p[(0, 1)]]
            for tcin in range(2):
                tc = 2 + tcin
                ps = (ps_w0, ps_w1)[tcin]
                ev = P.op(V, "tensor_tensor", out=acc[:, tc, :],
                          in0=acc[:, tc, :], in1=ps[:], op=ALU.add,
                          waits=[pending_m4["tp1mm"][tcin]])
                evs.append(ev)
                prev["psw_free%d" % tcin] = ev
                st["pw"][tcin] = ev
            for s_ in range(SEQ):
                P.op(G, "dma_start",
                     out=out_t.rearrange("s (a p) d -> s p a d", p=128)
                     [s_][:, 2 * pi:2 * pi + 2, :],
                     in_=acc[:, 2 * s_:2 * s_ + 2, :],
                     dma="dso", waits=evs if s_ == 0 else ())
            prev["acc_free"] = ("dso", P.count["dso"])
            pending_m4 = None

        t_acc0 = None
        for j in range(4):
            t_acc0 = P.op(V, "tensor_tensor", out=acc[:, j, :],
                          in0=xt_v[:, j, :], in1=xn_v[:, j, :], op=ALU.add,
                          waits=[t_xn[j], prev.get("acc_free") if j == 0 else None])

        t_xqT = None
        for kc in range(NKD):
            t_xqT = transpose_group(
                [(j, xqb_v[:, j, kc * 128:(kc + 1) * 128]) for j in range(4)],
                [(xqT_v[:, kc, :], 0, 512)],
                [t_xq, t_acc0 if kc == 0 else None,
                 t_xn[3] if kc == 0 else None])

        u4 = u_v.rearrange("p c (s t) -> p c s t", s=SEQ)
        th1 = P.op(V, "tensor_copy", out=uhead[:, :, :, 0:3], in_=ctail[:],
                   waits=[t_ct if ti == 0 else prev.get("ctail_w"),
                          prev.get("uhead_free")] + [t_u[ec] for ec in range(16)])
        th2 = P.op(V, "tensor_copy", out=uhead[:, :, :, 3:8],
                   in_=u4[:, :, :, 0:5], waits=[th1])
        t_tailw = P.op(V, "tensor_copy", out=ctail[:],
                       in_=u4[:, :, :, 253:256], waits=[th2])
        prev["ctail_w"] = t_tailw
        t_conv = {}
        for dc in range(NDC):
            ca = (ca_lo if dc < 8 else ca_hi)[:, dc % 8, :] \
                .rearrange("p (s t) -> p s t", s=SEQ)
            w0 = [t_tailw]
            if dc == 0:
                w0 += [prev.get("BbR_free2"), t_xn2[-1], t_xqT]
            if dc == 8:
                w0 += [prev.get("DtR_free2"), t_xn[-1]]
            base = None
            for k in range(4):
                srcm = u4[:, dc, :, max(0, 2 + k - 2): 253 + k]
            for k in range(4):
                srcm = u4[:, dc, :, 2 + k: 253 + k]
                if k == 0:
                    base = P.op(V, "tensor_scalar", out=ca[:, :, 5:256],
                                in0=srcm, scalar1=col(cvw_sc, dc, 0),
                                scalar2=None, op0=ALU.mult, waits=w0)
                else:
                    base = P.op(V, "scalar_tensor_tensor", out=ca[:, :, 5:256],
                                in0=srcm, scalar=col(cvw_sc, dc, k),
                                in1=ca[:, :, 5:256], op0=ALU.mult, op1=ALU.add,
                                waits=[base])
            for k in range(4):
                srch = uhead[:, dc, :, k:k + 5]
                if k == 0:
                    hh = P.op(V, "tensor_scalar", out=ca[:, :, 0:5], in0=srch,
                              scalar1=col(cvw_sc, dc, 0), scalar2=None,
                              op0=ALU.mult, waits=[base])
                else:
                    hh = P.op(V, "scalar_tensor_tensor", out=ca[:, :, 0:5],
                              in0=srch, scalar=col(cvw_sc, dc, k),
                              in1=ca[:, :, 0:5], op0=ALU.mult, op1=ALU.add,
                              waits=[hh])
            sl = P.op(S, "activation", out=ucb[:, dc, :],
                      in_=ca.rearrange("p s t -> p (s t)"), func=AF.Silu,
                      bias=col(cvb_sc, dc),
                      waits=[hh, prev.get("ucb_free") if dc == 0 else None])
            t_conv[dc] = sl
        t_convlast = t_conv[NDC - 1]
        prev["uhead_free"] = hh

        wt = [t_convlast, t_wxp, st["paf"][0]]
        mm = None
        for dc in range(NDC):
            mm = P.op(T, "matmul", ps_a0[0:64, :], wxpT[:, dc, 0:64],
                      ucb[:, dc, :], start=(dc == 0), stop=(dc == NDC - 1),
                      waits=wt if dc == 0 else ())
        t_dl = P.op(V, "tensor_copy", out=dl_b[:], in_=ps_a0[0:64, :],
                    waits=[mm])
        st["paf"][0] = t_dl
        t_bc2 = {}
        for half, dst in ((0, bcrB), (1, bcrC)):
            bk = (half + 1) % 2
            ps = (ps_a0, ps_a1)[bk]
            wt = [t_convlast, st["paf"][bk],
                  ffn_pa_free[bk] if ti == 0 else None]
            mm = None
            for dc in range(NDC):
                mm = P.op(T, "matmul", ps[0:16, :],
                          wxpT[:, dc, 64 + half * 16:80 + half * 16],
                          ucb[:, dc, :], start=(dc == 0), stop=(dc == NDC - 1),
                          waits=wt if dc == 0 else ())
            t_bc2[half] = P.op(V, "tensor_copy", out=dst[:], in_=ps[0:16, :],
                               waits=[mm])
            st["paf"][bk] = t_bc2[half]

        t_dt = {}
        for dc in range(NDC):
            bk = dc % 2
            ps = (ps_a0, ps_a1)[bk]
            wt = [t_dl, t_wdt, st["paf"][bk]]
            mm = P.op(T, "matmul", ps[:], wdtT[:, dc * 128:(dc + 1) * 128],
                      dl_b[:], start=True, stop=True, waits=wt)
            e1 = P.op(S, "activation", out=hsl, in_=ps[:], func=AF.Exp,
                      bias=col(bdt_sc, dc),
                      waits=[mm, t_convlast,
                             prev.get("hsl_free") if dc == 0 else t_dt[dc - 1]])
            e2 = P.op(S, "activation", out=dtb_v[:, dc, :], in_=hsl,
                      func=AF.Ln, bias=col(cst, C_ONE), waits=[e1])
            t_dt[dc] = e2
            st["paf"][bk] = e2

        t_hqT = {}
        last_f2 = None
        kff = 0
        for fsi in range(6):
            f0 = fsi * 512
            f1 = min(f0 + 512, DFF)
            fn = f1 - f0
            wlg = P.op(Q, "dma_start", out=wgb[:, :, 0:fn],
                       in_=wgT_sc.rearrange("k p f -> p k f")[:, :, f0:f1],
                       dma="dwg",
                       waits=[wg_free] if wg_free else list(prep_sp))
            wlu = P.op(Q, "dma_start", out=wub[:, :, 0:fn],
                       in_=wuT_sc.rearrange("k p f -> p k f")[:, :, f0:f1],
                       dma="dwu",
                       waits=[wu_free] if wu_free else list(prep_sp))
            for tc in range(4):
                par = kff % 2
                gps = (ps_a0[:, 0:fn], ps_w0[:, 0:fn])[par]
                ups = (ps_a1[:, 0:fn], ps_w0[:, 512:512 + fn])[par]
                gkey = ("paf", 0) if par == 0 else ("pw", 0)
                ukey = ("paf", 1) if par == 0 else ("pw", 1)
                g_mm = None
                wt0 = [wlg, t_xqT, st[gkey[0]][gkey[1]],
                       psw_free[0] if (ti == 0 and kff < 2) else None]
                for kc in range(NKD):
                    g_mm = P.op(T, "matmul", gps,
                                xqT_v[:, kc, tc * 128:(tc + 1) * 128],
                                wgb[:, kc, 0:fn], start=(kc == 0),
                                stop=(kc == NKD - 1),
                                waits=wt0 if kc == 0 else ())
                u_mm = None
                wt1 = [wlu, st[ukey[0]][ukey[1]]]
                for kc in range(NKD):
                    u_mm = P.op(T, "matmul", ups,
                                xqT_v[:, kc, tc * 128:(tc + 1) * 128],
                                wub[:, kc, 0:fn], start=(kc == 0),
                                stop=(kc == NKD - 1),
                                waits=wt1 if kc == 0 else ())
                hsl_k = hslp[par]
                hql_k = hqlp[par]
                sg = P.op(S, "activation", out=hsl_k[:, 0:fn], in_=gps,
                          func=AF.Silu, waits=[g_mm, t_dt[NDC - 1],
                                               st.get("hslv%d" % par)])
                hm = P.op(V, "tensor_tensor", out=hsl_k[:, 0:fn],
                          in0=hsl_k[:, 0:fn], in1=ups, op=ALU.mult,
                          waits=[sg, u_mm])
                st[gkey[0]][gkey[1]] = hm
                st[ukey[0]][ukey[1]] = hm
                tq = quant(hsl_k[:, 0:fn], hql_k[:, 0:fn], fn // 16,
                           [hm, st.get("hqlt%d" % par),
                            prev.get("SCANR_free") if (ti and kff == 0) else None,
                            t_convlast if kff == 0 else None],
                           main_scr)
                st["hslv%d" % par] = tq
                nkb = fn // 128
                ev = transpose_group(
                    [(j, hql_k[:, j * 128:(j + 1) * 128]) for j in range(nkb)],
                    [(hqT_v[:, (f0 // 128) + j, tc * 128:(tc + 1) * 128],
                      j * 128, (j + 1) * 128) for j in range(nkb)],
                    [tq, t_xn2T if kff == 0 else None,
                     prev.get("hqTR_free2") if kff == 0 else None])
                st["hqlt%d" % par] = st["tp_ev"][st["tp"] - 1]
                for j in range(nkb):
                    t_hqT[((f0 // 128) + j, tc)] = ev
                last_f2 = u_mm
                kff += 1
            wg_free = g_mm
            wu_free = u_mm
        t_hsl_last = st["qprev"]
        prev["hsl_free"] = t_hsl_last

        t_Bb = None
        t_Cb = None
        for n in range(2 * DST):
            bk = n % 2
            ps = (ps_a0, ps_a1)[bk]
            wt = [t_bc2[0], t_bc2[1], st["paf"][bk]]
            if n == 0:
                wt += [t_xn2[-1], t_xq, st["qprev"]]
            if n == DST:
                wt += [last_f2]
            src = bcrB if n < DST else bcrC
            mm = P.op(T, "matmul", ps[:], selb[:, (n % DST) * 128:(n % DST + 1) * 128],
                      src[:], start=True, stop=True, waits=wt)
            if n < DST:
                t_Bb = P.op(V, "tensor_copy", out=Bb_v[:, n, :], in_=ps[:],
                            waits=[mm])
                st["paf"][bk] = t_Bb
            else:
                t_Cb = P.op(V, "tensor_copy", out=Cb_v[:, n - DST, :],
                            in_=ps[:], waits=[mm])
                st["paf"][bk] = t_Cb

        f3_kc = {}
        f3_last_mm = {}
        t_f3_p0 = {}
        for tp in range(2):
            mm = None
            for kc in range(NKF):
                slot = kc % 2
                wl = P.op(Q if slot == 0 else G, "dma_start",
                          out=wdb[slot][:], in_=wdT_sc[kc],
                          dma=f"dwd{slot}",
                          waits=[wd_free[slot]] if wd_free[slot] else prep_sp)
                for tcin in range(2):
                    tc = tp * 2 + tcin
                    ps = (ps_w0, ps_w1)[tcin]
                    if kc == 0:
                        wt = [wl, t_hqT[(NKF - 1, tc)],
                              st["pw"][0], st["pw"][1], st.get("hslv1"),
                              psw_free[tcin] if (ti == 0 and tp == 0) else
                              (t_f3_p0[tcin] if tp == 1 else prev.get("psw_free%d" % tcin))]
                    else:
                        wt = [wl]
                    for nh in range(2):
                        mm = P.op(T, "matmul", ps[:, nh * 512:(nh + 1) * 512],
                                  hqT_v[:, kc, tc * 128:(tc + 1) * 128],
                                  wdb[slot][:, nh * 512:(nh + 1) * 512],
                                  start=(kc == 0), stop=(kc == NKF - 1),
                                  waits=[w for w in wt if w] if nh == 0 else ())
                    f3_last_mm[(tp, tcin)] = mm
                if tp == 1:
                    f3_kc[kc] = mm
                wd_free[slot] = mm
            if tp == 0:
                for tcin in range(2):
                    ps = (ps_w0, ps_w1)[tcin]
                    t_f3_p0[tcin] = P.op(V, "tensor_tensor", out=acc[:, tcin, :],
                                         in0=acc[:, tcin, :], in1=ps[:],
                                         op=ALU.add,
                                         waits=[f3_last_mm[(0, tcin)], t_acc0])

        abufs = [argb, dAb]
        iters = [(dc, s_) for dc in range(NDC) for s_ in range(SEQ)]
        t_y = {}
        t_du = {}

        def emit_args(k):
            dc, s_ = iters[k]
            buf = abufs[k % 2]
            w = [t_dt[dc], t_negAb if (ti == 0 and k == 0) else None,
                 sc_prev.get("tree%d" % (k % 2)),
                 sc_prev.get("exp%d" % (k % 2))]
            if ti == 0 and k == 0:
                w.append(t_xq)
            if ti == 0 and k <= 1:
                w.append(t_hsl_last)
            if k <= 1:
                w.append(prev.get("SCANR_free"))
            ta = P.op(V, "tensor_tensor", out=buf[:],
                      in0=dtb_v[:, dc, s_ * 256:(s_ + 1) * 256]
                      .unsqueeze(1).broadcast_to([128, DST, 256]),
                      in1=A_scb[:, dc, :].unsqueeze(2)
                      .broadcast_to([128, DST, 256]),
                      op=ALU.mult, waits=w)
            te = P.op(S, "activation", out=buf.rearrange("p n t -> p (n t)"),
                      in_=buf.rearrange("p n t -> p (n t)"), func=AF.Exp,
                      waits=[ta, sc_prev.get("ldt%d" % (k % 2))])
            sc_prev["exp%d" % (k % 2)] = te
            sc_prev["argt%d" % (k % 2)] = ta
            return te

        t_exp_k = {}
        t_exp_k[0] = emit_args(0)
        for k, (dc, s_) in enumerate(iters):
            buf = abufs[k % 2]
            if s_ == 0:
                t_du[dc] = P.op(V, "tensor_tensor", out=dtusl,
                                in0=dtb_v[:, dc, :], in1=ucb[:, dc, :],
                                op=ALU.mult,
                                waits=[t_dt[dc], t_conv[dc],
                                       sc_prev.get("dbu"),
                                       t_hsl_last if dc == 0 else None])
            tdbu = P.op(V, "tensor_tensor", out=dBub[:],
                        in0=dtusl[:, s_ * 256:(s_ + 1) * 256]
                        .unsqueeze(1).broadcast_to([128, DST, 256]),
                        in1=Bb_v[:, :, s_ * 256:(s_ + 1) * 256],
                        op=ALU.mult,
                        waits=[t_du[dc], t_Bb, sc_prev.get("hC"),
                               sc_prev.get("scan")])
            sc_prev["dbu"] = tdbu
            tsd = P.op(V, "tensor_tensor", out=sdt[:],
                       in0=buf[:, :, 0], in1=hcar[:, dc, :, s_],
                       op=ALU.mult,
                       waits=[t_exp_k[k], t_hc if ti == 0 else None,
                              sc_prev.get("carry")])
            tsd2 = P.op(V, "tensor_tensor", out=dBub[:, :, 0],
                        in0=dBub[:, :, 0], in1=sdt[:], op=ALU.add,
                        waits=[tsd, tdbu])
            tz = P.op(V, "memset", buf[:, :, 0:1], 0.0, waits=[tsd])
            sc_prev["ldt%d" % (k % 2)] = tz
            if k + 1 < len(iters):
                t_exp_k[k + 1] = emit_args(k + 1)
            tscan = P.op(V, "tensor_tensor_scan",
                         out=hb_v.rearrange("p n t -> p (n t)"),
                         data0=buf.rearrange("p n t -> p (n t)"),
                         data1=dBub.rearrange("p n t -> p (n t)"),
                         initial=0.0, op0=ALU.mult, op1=ALU.add,
                         waits=[tsd2, tz, sc_prev.get("hC"),
                                sc_prev.get("carry")])
            sc_prev["ldt%d" % (k % 2)] = tscan
            tcar = P.op(V, "tensor_copy", out=hcar[:, dc, :, s_],
                        in_=hb_v[:, :, 255], waits=[tscan])
            thc = P.op(V, "tensor_tensor", out=dBub[:], in0=hb_v[:],
                       in1=Cb_v[:, :, s_ * 256:(s_ + 1) * 256],
                       op=ALU.mult, waits=[tscan, t_Cb])
            b8 = buf[:, 0:8, :]
            b4 = buf[:, 8:12, :]
            b2 = buf[:, 12:14, :]
            tt8_ = P.op(V, "tensor_tensor", out=b8, in0=dBub[:, 0:8, :],
                        in1=dBub[:, 8:16, :], op=ALU.add,
                        waits=[thc, tscan])
            tt4_ = P.op(V, "tensor_tensor", out=b4, in0=b8[:, 0:4, :],
                        in1=b8[:, 4:8, :], op=ALU.add, waits=[tt8_])
            tt2_ = P.op(V, "tensor_tensor", out=b2, in0=b4[:, 0:2, :],
                        in1=b4[:, 2:4, :], op=ALU.add, waits=[tt4_])
            ty = P.op(V, "tensor_tensor",
                      out=yb_v[:, dc, s_ * 256:(s_ + 1) * 256],
                      in0=b2[:, 0, :], in1=b2[:, 1, :], op=ALU.add,
                      waits=[tt2_, f3_kc[dc]])
            t_y[dc] = ty
            sc_prev["tree%d" % (k % 2)] = ty
            sc_prev["scan"] = tscan
            sc_prev["carry"] = tcar
            sc_prev["hC"] = thc
            sc_prev["tree"] = ty
            sc_prev["exp"] = t_exp_k[k]
        t_f3 = {(0, 0): t_f3_p0[0], (0, 1): t_f3_p0[1]}
        for tcin in range(2):
            tc = 2 + tcin
            ps = (ps_w0, ps_w1)[tcin]
            ev = P.op(V, "tensor_tensor", out=acc[:, tc, :],
                      in0=acc[:, tc, :], in1=ps[:], op=ALU.add,
                      waits=[f3_last_mm[(1, tcin)], t_acc0])
            t_f3[(1, tcin)] = ev

        t_yg = {}
        for dc in range(NDC):
            g1 = P.op(V, "scalar_tensor_tensor", out=ygb_v[:, dc, :],
                      in0=ucb[:, dc, :], scalar=col(D_sc, dc),
                      in1=yb_v[:, dc, :], op0=ALU.mult, op1=ALU.add,
                      waits=[t_y[dc]] + ([sc_prev["exp"], sc_prev["tree"],
                                          prev.get("SCB_m4")] if dc == 0 else []))
            g2 = P.op(V, "tensor_tensor", out=ygb_v[:, dc, :],
                      in0=ygb_v[:, dc, :], in1=szb[:, dc, :], op=ALU.mult,
                      waits=[g1, t_sz[dc]])
            t_yg[dc] = g2

        m4_last_mm = {}
        mm = None
        for dc in range(NDC):
            slot = dc % 2
            wl = P.op(Q if slot == 0 else G, "dma_start",
                      out=wob[slot][:], in_=woutT_sc[dc],
                      dma=("dwoq0" if slot == 0 else "dwo1"),
                      waits=[wo_free[slot]] if wo_free[slot] else prep_sp)
            for tcin in range(2):
                tc = tcin
                ps = (ps_w0, ps_w1)[tcin]
                if dc == 0:
                    wt = [wl, t_yg[0], t_f3[(1, tcin)]]
                else:
                    wt = [wl, t_yg[dc]]
                for nh in range(2):
                    mm = P.op(T, "matmul", ps[:, nh * 512:(nh + 1) * 512],
                              ygb_v[:, dc, tc * 128:(tc + 1) * 128],
                              wob[slot][:, nh * 512:(nh + 1) * 512],
                              start=(dc == 0), stop=(dc == NDC - 1),
                              waits=wt if nh == 0 else ())
                m4_last_mm[(0, tcin)] = mm
            wo_free[slot] = mm
        pending_m4 = {"mm": dict(m4_last_mm), "ti": ti, "yg": t_yg[NDC - 1]}

        prev["DtR_free"] = sc_prev["argt%d" % ((len(iters) - 1) % 2)]
        prev["DtR_free2"] = t_xq
        prev["BbR_free"] = sc_prev["dbu"]
        prev["BbR_free2"] = t_xq
        prev["CbR_free"] = sc_prev["hC"]
        prev["SCANR_free"] = wo_free[0]
        prev["SCB_m4"] = None
        prev["hqTR_free"] = t_yg[NDC - 1]
        prev["hqTR_free2"] = None
        prev["szb_free"] = t_yg[NDC - 1]
        prev["ucb_free"] = t_yg[NDC - 1]
        prev["bn_free"] = t_mv
        prev["ln_free"] = t_rat
        prev["uhead_free"] = prev["uhead_free"]

    if pending_m4 is not None:
        pi = pending_m4["ti"]
        m4all = []
        for tcin in range(2):
            ps = (ps_w0, ps_w1)[tcin]
            ev = P.op(V, "tensor_tensor", out=acc[:, tcin, :],
                      in0=acc[:, tcin, :], in1=ps[:], op=ALU.add,
                      waits=[pending_m4["mm"][(0, tcin)]])
            m4all.append(ev)
        mm = None
        tp1mm = {}
        for dc in range(NDC):
            slot = dc % 2
            wl = P.op(Q if slot == 0 else G, "dma_start",
                      out=wob[slot][:], in_=woutT_sc[dc],
                      dma=("dwoq0" if slot == 0 else "dwo1"),
                      waits=[wo_free[slot]] if wo_free[slot] else prep_sp)
            for tcin in range(2):
                tc = 2 + tcin
                ps = (ps_w0, ps_w1)[tcin]
                if dc == 0:
                    wt = [wl, pending_m4["yg"], m4all[tcin]]
                else:
                    wt = [wl]
                for nh in range(2):
                    mm = P.op(T, "matmul", ps[:, nh * 512:(nh + 1) * 512],
                              ygb_v[:, dc, tc * 128:(tc + 1) * 128],
                              wob[slot][:, nh * 512:(nh + 1) * 512],
                              start=(dc == 0), stop=(dc == NDC - 1),
                              waits=wt if nh == 0 else ())
                tp1mm[tcin] = mm
            wo_free[slot] = mm
        for tcin in range(2):
            tc = 2 + tcin
            ps = (ps_w0, ps_w1)[tcin]
            ev = P.op(V, "tensor_tensor", out=acc[:, tc, :],
                      in0=acc[:, tc, :], in1=ps[:], op=ALU.add,
                      waits=[tp1mm[tcin]])
            m4all.append(ev)
        for s_ in range(SEQ):
            P.op(G, "dma_start",
                 out=out_t.rearrange("s (a p) d -> s p a d", p=128)
                 [s_][:, 2 * pi:2 * pi + 2, :],
                 in_=acc[:, 2 * s_:2 * s_ + 2, :],
                 dma="dso", waits=m4all if s_ == 0 else ())
        pending_m4 = None

    P.op(G, "engine_nop", waits=[("dso", P.count.get("dso", 0))])

    for name in sorted(P.semnames):
        P.sems[name] = stack.enter_context(nc.semaphore(name))
    block = stack.enter_context(nc.Block())
    P.emit(block)
    stack.close()
    return nc


def _consts_array():
    c = np.zeros((128, 8), np.float32)
    c[:, 0] = 1.0
    c[:, 1] = EPS
    c[:, 2] = 1.0 + EPS
    c[:, 3] = EPS * EPS
    c[:, 4] = -0.5
    return c


def _sel_array():
    s = np.zeros((16, 16 * 128), ml_dtypes.bfloat16)
    for n in range(16):
        s[n, n * 128:(n + 1) * 128] = 1
    return s


def _base_inputs(inputs):
    return {
        "ln_g": inputs["L0_attn_ln_g"], "ln_b": inputs["L0_attn_ln_b"],
        "w_in": inputs["L0_attn_W_in"], "conv_w": inputs["L0_attn_conv_w"],
        "conv_b": inputs["L0_attn_conv_b"], "w_xp": inputs["L0_attn_W_xp"],
        "w_dt": inputs["L0_attn_W_dt"], "b_dt": inputs["L0_attn_b_dt"],
        "a_log": inputs["L0_attn_A_log"], "d_vec": inputs["L0_attn_D"],
        "w_out": inputs["L0_attn_W_out"], "wg": inputs["L0_ffn_w_gate"],
        "wu": inputs["L0_ffn_w_up"], "wd": inputs["L0_ffn_w_down"],
        "eye_b": np.eye(128, dtype=ml_dtypes.bfloat16),
        "ones1f": np.ones((1, 128), np.float32),
        "sel": _sel_array(),
        "csts": _consts_array(),
    }


_NC_CACHE = {}


def kernel(**inputs):
    if "prog" not in _NC_CACHE:
        _NC_CACHE["prog"] = build_program()
    nc = _NC_CACHE["prog"]
    x = np.ascontiguousarray(inputs["L0_attn_x"])
    base = _base_inputs(inputs)
    in_maps = []
    for c in range(NCORES):
        m = dict(base)
        m["x"] = x[c * SEQ:(c + 1) * SEQ]
        in_maps.append(m)
    res = run_bass_kernel_spmd(nc, in_maps, core_ids=list(range(NCORES)))
    out = np.concatenate([r["out"] for r in res.results], axis=0)
    return out.astype(np.float32)


# version 10
# speedup vs baseline: 1.2234x; 1.1185x over previous
import sys
sys.path.insert(0, "/opt/trn_rl_repo")
import numpy as np
import ml_dtypes

import concourse.bass as bass
import concourse.mybir as mybir
from concourse.bass_utils import run_bass_kernel_spmd

F32 = mybir.dt.float32
BF16 = mybir.dt.bfloat16
AF = mybir.ActivationFunctionType
ALU = mybir.AluOpType
AX = mybir.AxisListType

B, L, D = 16, 2048, 1024
NCORES = 8
SEQ = 2
DI = 2048
DST = 16
DFF = 2816
TC = 256
TOK = SEQ * TC
NT = L // TC
NDC = DI // 128
NKD = D // 128
NKF = DFF // 128
EPS = 1e-5
MID = [0.25, 0.75, 1.25, 1.75, 2.5, 3.5, 5.0]
MIDSQ = [m * m for m in MID]
MIDW = [0.5, 0.5, 0.5, 0.5, 1.0, 1.0, 2.0]

CFG_NT = NT


class Plan:
    ENGS = ("v", "s", "t", "g", "q")

    def __init__(self):
        self.ops = {e: [] for e in self.ENGS}
        self.count = {}
        self.semnames = set("S" + e for e in self.ENGS)
        self.sems = {}

    def dma_sem(self, name):
        self.semnames.add(name)

    def op(self, eng, method, *args, waits=(), dma=None, **kw):
        import traceback
        fr = traceback.extract_stack(limit=4)
        tbs = ";".join(f"{f.lineno}" for f in fr[:-1])
        w = tuple(t for t in waits if t is not None)
        if dma is not None:
            self.count[dma] = self.count.get(dma, 0) + 16
            tok = (dma, self.count[dma])
        else:
            s = "S" + eng
            self.count[s] = self.count.get(s, 0) + 1
            tok = (s, self.count[s])
        self.ops[eng].append((method, args, kw, w, tok, tbs))
        return tok

    def emit(self, block):
        plan = self
        secs = {"v": block.vector, "s": block.scalar, "t": block.tensor,
                "g": block.gpsimd, "q": block.sync}

        def make_body(e):
            own = "S" + e

            def body(engine):
                waited = {}
                for (method, args, kw, waits, tok, src) in plan.ops[e]:
                    need = {}
                    for (sname, val) in waits:
                        if val > need.get(sname, -1):
                            need[sname] = val
                    for sname in sorted(need):
                        val = need[sname]
                        if waited.get(sname, -1) >= val:
                            continue
                        engine.wait_ge(plan.sems[sname], val)
                        waited[sname] = val
                    try:
                        instr = getattr(engine, method)(*args, **kw)
                    except Exception as ex:
                        raise RuntimeError(f"emit failed: {method} planned at {src}: {ex}") from ex
                    sname, _ = tok
                    instr.then_inc(plan.sems[sname], 1 if sname == own else 16)
            return body

        for e in self.ENGS:
            if self.ops[e]:
                secs[e](make_body(e))


def col(ap, *idx):
    sl = tuple([slice(None)] + [slice(i, i + 1) for i in idx])
    return ap[sl]


def cdiv(a, b):
    return (a + b - 1) // b


def build_program():
    nc = bass.Bass("TRN2", target_bir_lowering=False)
    P = Plan()
    V, S, T, G, Q = "v", "s", "t", "g", "q"

    def din(name, shape, dt=F32):
        return nc.dram_tensor(name, shape, dt, kind="ExternalInput").ap()

    x_in = din("x", [SEQ, L, D])
    ln_g = din("ln_g", [D]); ln_b = din("ln_b", [D])
    w_in = din("w_in", [2 * DI, D])
    conv_w = din("conv_w", [DI, 4]); conv_b = din("conv_b", [DI])
    w_xp = din("w_xp", [96, DI])
    w_dt = din("w_dt", [DI, 64]); b_dt = din("b_dt", [DI])
    a_log = din("a_log", [DI, DST]); d_vec = din("d_vec", [DI])
    w_out = din("w_out", [D, DI])
    wg_in = din("wg", [DFF, D]); wu_in = din("wu", [DFF, D])
    wd_in = din("wd", [D, DFF])
    eye_b = din("eye_b", [128, 128], BF16)
    ones1f = din("ones1f", [1, 128], F32)
    sel_in = din("sel", [16, 16 * 128], BF16)
    csts = din("csts", [128, 8], F32)
    C_ONE, C_EPS, C_LNS, C_LNB, C_NEGH = 0, 1, 2, 3, 4

    out_t = nc.dram_tensor("out", [SEQ, L, D], F32, kind="ExternalOutput").ap()

    def scr(name, shape, dt=BF16):
        return nc.dram_tensor(name, shape, dt, kind="Internal").ap()

    winT_sc = scr("winT_sc", [32, 128, NKD, 128])
    woutT_sc = scr("woutT_sc", [NDC, 128, D])
    wgT_sc = scr("wgT_sc", [NKD, 128, DFF])
    wuT_sc = scr("wuT_sc", [NKD, 128, DFF])
    wdT_sc = scr("wdT_sc", [NKF, 128, D])

    from contextlib import ExitStack
    stack = ExitStack()

    def sb(name, shape, dt=F32):
        return stack.enter_context(nc.sbuf_tensor(name, shape, dt))

    def psum(name, shape, dt=F32):
        return stack.enter_context(nc.psum_tensor(name, shape, dt))

    cst = sb("cst", [128, 8])
    eyeb = sb("eyeb", [128, 128], BF16)
    onesf = sb("onesf", [1, 128])
    selb = sb("selb", [16, 16 * 128], BF16)
    A_sc = sb("A_sc", [128, NDC, DST])
    A_scb = sb("A_scb", [128, NDC, DST], BF16)
    D_sc = sb("D_sc", [128, NDC])
    bdt_sc = sb("bdt_sc", [128, NDC])
    cvb_sc = sb("cvb_sc", [128, NDC])
    cvw_sc = sb("cvw_sc", [128, NDC, 4])
    g_bc = sb("g_bc", [128, D])
    b_bc = sb("b_bc", [128, D])
    wxpT = sb("wxpT", [128, NDC, 96], BF16)
    wdtT = sb("wdtT", [64, DI], BF16)
    hcar = sb("hcar", [128, NDC, DST, SEQ])
    ctail = sb("ctail", [128, NDC, SEQ, 3])
    bnb = sb("bnb", [128, 4, 12])
    mv = sb("mv", [128, 4, 2])
    rr = sb("rr", [128, 4])
    rho = sb("rho", [128, 4])
    rat = sb("rat", [128, 4])
    lnt = sb("lnt", [128, 4])
    qm = sb("qm", [128, 64])
    qs = sb("qs", [128, 64])
    qr = sb("qr", [128, 64])
    qsb = sb("qsb", [128, 64], BF16)
    sdt = sb("sdt", [128, DST])
    uhead = sb("uhead", [128, NDC, SEQ, 8])
    uheadb = None
    dl_b = sb("dl_b", [64, TOK], BF16)
    bcrB = sb("bcrB", [16, TOK], BF16)
    bcrC = sb("bcrC", [16, TOK], BF16)

    acc = sb("acc", [128, 4, D])
    szb = sb("szb", [128, NDC, TOK], BF16)
    ucb = sb("ucb", [128, NDC, TOK], BF16)

    SCANR = sb("SCANR", [128, 8192])
    BbR = sb("BbR", [128, 8192], BF16)
    CbR = sb("CbR", [128, 8192], BF16)
    DtR = sb("DtR", [128, 8192], BF16)
    hqTR = sb("hqTR", [128, 11264], BF16)

    winb = [sb(f"winb{i}", [128, NKD, 128], BF16) for i in range(2)]
    wgb = sb("wgb", [128, NKD, 512], BF16)
    wub = sb("wub", [128, NKD, 512], BF16)
    wdb = [sb(f"wdb{i}", [128, D], BF16) for i in range(2)]
    wob = [sb(f"wob{i}", [128, D], BF16) for i in range(2)]

    SCB = SCANR[:].bitcast(BF16)
    u_v = SCANR[:].rearrange("p (c t) -> p c t", c=NDC)
    pp32 = [SCANR[:, 0:4096], SCANR[:, 4096:8192]]
    qv_v = SCANR[:, 0:1024]
    qa2_v = SCANR[:, 1024:2048]
    qABCD = [SCB[:, 4096 + i * 1024: 4096 + (i + 1) * 1024] for i in range(4)]
    argb = SCB[:, 0:4096].rearrange("p (n t) -> p n t", n=DST)
    dAb = SCB[:, 4096:8192].rearrange("p (n t) -> p n t", n=DST)
    dBub = SCB[:, 8192:12288].rearrange("p (n t) -> p n t", n=DST)
    hb_v = SCB[:, 12288:16384].rearrange("p (n t) -> p n t", n=DST)
    tr8 = SCB[:, 0:2048].rearrange("p (n t) -> p n t", n=8)
    tr4 = SCB[:, 2048:3072].rearrange("p (n t) -> p n t", n=4)
    tr2 = SCB[:, 3072:3584].rearrange("p (n t) -> p n t", n=2)
    ygb_v = SCB[:, 0:8192].rearrange("p (c t) -> p c t", c=NDC)

    BbF = BbR[:].bitcast(F32)
    xqb_v = BbR[:, 0:4096].rearrange("p (j d) -> p j d", j=4)
    xn2b_v = BbR[:, 4096:8192].rearrange("p (j d) -> p j d", j=4)
    ca_lo = BbF[:].rearrange("p (c t) -> p c t", c=8)
    Bb_v = BbR[:].rearrange("p (n t) -> p n t", n=DST)
    ppb = [BbR[:, 0:4096], BbR[:, 4096:8192]]
    hslp = [BbF[:, 0:512], BbF[:, 512:1024]]
    hsl = hslp[0]
    hqlp = [BbR[:, 2048:2560], BbR[:, 2560:3072]]

    CbF = CbR[:].bitcast(F32)
    xn_v = CbF[:].rearrange("p (j d) -> p j d", j=4)
    xqT_v = CbR[:, 0:4096].rearrange("p (k t) -> p k t", k=NKD)
    Cb_v = CbR[:].rearrange("p (n t) -> p n t", n=DST)
    ppq = [CbR[:, 0:4096], CbR[:, 4096:8192]]

    DtF = DtR[:].bitcast(F32)
    xt_v = DtF[:].rearrange("p (j d) -> p j d", j=4)
    ca_hi = DtF[:].rearrange("p (c t) -> p c t", c=8)
    dtb_v = DtR[:].rearrange("p (c t) -> p c t", c=NDC)

    hqF = hqTR[:].bitcast(F32)
    xn2T_v = hqTR[:, 0:4096].rearrange("p (k t) -> p k t", k=NKD)
    hqT_v = hqTR[:].rearrange("p (k t) -> p k t", k=NKF)
    yb_v = hqTR[:, 0:8192].rearrange("p (c t) -> p c t", c=NDC)
    dtusl = uhead[:].rearrange("p a b c -> p (a b c)").bitcast(BF16)[:, 0:512]
    pqv = hqF[:, 0:1024]
    pqa2 = hqF[:, 1024:2048]
    pqABCD = [hqTR[:, 4096 + i * 1024: 4096 + (i + 1) * 1024] for i in range(4)]
    x1qv = hqF[:, 2048:2560]
    x1qa2 = hqF[:, 2560:3072]
    x1ABCD = [hqTR[:, 6144 + i * 512: 6144 + (i + 1) * 512] for i in range(4)]
    p1s = hqF[:, 2048:3072]
    p2s = hqF[:, 3072:4096]

    ps_a0 = psum("ps_a0", [128, 512]); ps_a1 = psum("ps_a1", [128, 512])
    ps_w0 = psum("ps_w0", [128, 1024]); ps_w1 = psum("ps_w1", [128, 1024])
    ps_t0 = psum("ps_t0", [128, 1024], BF16)
    ps_t1 = psum("ps_t1", [128, 1024], BF16)
    ps_ts = [ps_t0, ps_t1]

    for s in ["dld0", "dld1", "dsp0", "dsp1", "dx", "dso", "dwin0", "dwin1",
              "dwg", "dwu", "dwd0", "dwd1", "dwo0", "dwo1", "dwoq0"]:
        P.dma_sem(s)

    st = {"tp": 0, "tp_ev": {}, "paf": {0: None, 1: None}, "pw": {0: None, 1: None}, "pab": 0, "qprev": None}

    def transpose_group(blocks, evacs, wt):
        k = st["tp"]
        pst = ps_ts[k % 2]
        w = list(wt)
        if (k - 2) in st["tp_ev"]:
            w.append(st["tp_ev"][k - 2])
        mm = None
        for (j, src) in blocks:
            mm = P.op(T, "transpose", pst[:, j * 128:(j + 1) * 128], src,
                      eyeb[:], waits=(w + [ld_all]) if mm is None else ())
        ev = None
        for (dst, c0, c1) in evacs:
            ev = P.op(S, "copy", out=dst, in_=pst[:, c0:c1], waits=[mm])
        st["tp_ev"][k] = ev
        st["tp"] = k + 1
        return ev

    def quant(src_ap, dst_ap, nblk, wtoks, scratch, sq=False):
        qvv, qaa, (qa, qb_, qc, qd) = scratch
        n = nblk * 16
        mm_ = qm[:, 0:nblk]; ss = qs[:, 0:nblk]; rs = qr[:, 0:nblk]
        sbv = qsb[:, 0:nblk]
        vv = qvv[:, 0:n]; aa = qaa[:, 0:n]
        qa = qa[:, 0:n]; qb_ = qb_[:, 0:n]; qc = qc[:, 0:n]; qd = qd[:, 0:n]
        w0 = list(wtoks) + ([st["qprev"]] if st["qprev"] else [])
        src3 = src_ap.rearrange("p (b j) -> p b j", j=16)
        t_m = P.op(V, "tensor_reduce", out=mm_, in_=src3, axis=AX.X, op=ALU.max,
                   apply_absolute_value=True, waits=w0)
        t_s = P.op(V, "tensor_scalar", out=ss, in0=mm_, scalar1=1.0 / 6.0,
                   scalar2=1e-12, op0=ALU.mult, op1=ALU.max, waits=[t_m])
        t_r = P.op(V, "reciprocal", out=rs, in_=ss, waits=[t_s])
        t_sb = P.op(V, "tensor_copy", out=sbv, in_=ss, waits=[t_s])
        rb = rs.unsqueeze(2).broadcast_to([128, nblk, 16])
        t_v = P.op(V, "tensor_tensor", out=vv.rearrange("p (b j) -> p b j", j=16),
                   in0=src3, in1=rb, op=ALU.mult, waits=[t_r])
        if sq:
            t_a = P.op(V, "tensor_tensor", out=aa, in0=vv, in1=vv,
                       op=ALU.mult, waits=[t_v])
        else:
            t_a = P.op(S, "activation", out=aa, in_=vv, func=AF.Abs,
                       waits=[t_v])
        thr = MIDSQ if sq else MID
        def cmp(dst, i, w):
            return P.op(V, "tensor_scalar", out=dst, in0=aa, scalar1=thr[i],
                        scalar2=MIDW[i], op0=ALU.is_gt, op1=ALU.mult, waits=w)
        c0 = cmp(qa, 0, [t_a]); c1 = cmp(qb_, 1, [t_a])
        s01 = P.op(V, "tensor_tensor", out=qc, in0=qa, in1=qb_, op=ALU.add,
                   waits=[c0, c1])
        c2 = cmp(qa, 2, [s01]); c3 = cmp(qb_, 3, [s01])
        s23 = P.op(V, "tensor_tensor", out=qd, in0=qa, in1=qb_, op=ALU.add,
                   waits=[c2, c3])
        s03 = P.op(V, "tensor_tensor", out=qc, in0=qc, in1=qd, op=ALU.add,
                   waits=[s23])
        c4 = cmp(qa, 4, [s03]); c5 = cmp(qb_, 5, [s03])
        s45 = P.op(V, "tensor_tensor", out=qd, in0=qa, in1=qb_, op=ALU.add,
                   waits=[c4, c5])
        c6 = cmp(qa, 6, [s45])
        s46 = P.op(V, "tensor_tensor", out=qd, in0=qd, in1=qa, op=ALU.add,
                   waits=[c6])
        lvl = P.op(V, "tensor_tensor", out=qc, in0=qc, in1=qd, op=ALU.add,
                   waits=[s46])
        sg = P.op(V, "tensor_scalar", out=qb_, in0=vv, scalar1=0.0,
                  scalar2=None, op0=ALU.is_ge, waits=[lvl])
        sg2 = P.op(V, "tensor_scalar", out=qb_, in0=qb_, scalar1=2.0,
                   scalar2=-1.0, op0=ALU.mult, op1=ALU.add, waits=[sg])
        tw = P.op(V, "tensor_tensor", out=qa, in0=qc, in1=qb_, op=ALU.mult,
                  waits=[sg2])
        sbb = sbv.unsqueeze(2).broadcast_to([128, nblk, 16])
        tq = P.op(V, "tensor_tensor",
                  out=dst_ap.rearrange("p (b j) -> p b j", j=16),
                  in0=qa.rearrange("p (b j) -> p b j", j=16), in1=sbb,
                  op=ALU.mult, waits=[tw, t_sb])
        st["qprev"] = tq
        return tq

    MAIN_SCR = (qv_v.unsqueeze(0)[0:0] if False else None)
    main_scr = (SCANR[:, 0:1024], SCANR[:, 1024:2048], qABCD)
    prep_scr = (pqv, pqa2, pqABCD)
    p1_scr = (x1qv, x1qa2, x1ABCD)

    dl = lambda **kw: P.op(Q, "dma_start", **kw)
    dl(out=cst[:], in_=csts, dma="dld0")
    dl(out=eyeb[:], in_=eye_b, dma="dld0")
    dl(out=onesf[:], in_=ones1f, dma="dld0")
    dl(out=selb[:], in_=sel_in, dma="dld0")
    dl(out=A_sc[:], in_=a_log.rearrange("(c p) n -> p c n", p=128), dma="dld0")
    dl(out=D_sc[:], in_=d_vec.rearrange("(c p) -> p c", p=128), dma="dld0", allow_slow_non_contiguous=True)
    dl(out=bdt_sc[:], in_=b_dt.rearrange("(c p) -> p c", p=128), dma="dld0", allow_slow_non_contiguous=True)
    dl(out=cvb_sc[:], in_=conv_b.rearrange("(c p) -> p c", p=128), dma="dld0", allow_slow_non_contiguous=True)
    dl(out=cvw_sc[:], in_=conv_w.rearrange("(c p) k -> p c k", p=128),
       dma="dld0")
    dl(out=g_bc[0:1, :], in_=ln_g.unsqueeze(0), dma="dld0")
    ld_all = dl(out=b_bc[0:1, :], in_=ln_b.unsqueeze(0), dma="dld0")

    t_exp = P.op(S, "activation", out=A_sc[:], in_=A_sc[:], func=AF.Exp,
                 waits=[ld_all])
    t_negA = P.op(V, "tensor_scalar_mul", out=A_sc[:], in0=A_sc[:],
                  scalar1=-1.0, waits=[t_exp])
    t_negAb = P.op(V, "tensor_copy", out=A_scb[:], in_=A_sc[:], waits=[t_negA])

    brc = []
    for half in range(2):
        brc.append(P.op(T, "matmul", ps_w0[:, half * 512:(half + 1) * 512],
                        onesf[:], g_bc[0:1, half * 512:(half + 1) * 512],
                        start=True, stop=True, waits=[ld_all]))
        brc.append(P.op(T, "matmul", ps_w1[:, half * 512:(half + 1) * 512],
                        onesf[:], b_bc[0:1, half * 512:(half + 1) * 512],
                        start=True, stop=True, waits=[ld_all]))
    t_gbc = P.op(V, "tensor_copy", out=g_bc[:], in_=ps_w0[:], waits=[brc[-1]])
    t_bbc = P.op(V, "tensor_copy", out=b_bc[:], in_=ps_w1[:], waits=[brc[-1]])

    t_hc = P.op(V, "memset", hcar[:], 0.0)
    t_ct = P.op(V, "memset", ctail[:], 0.0)

    pst8 = {"lc": [None, None], "bc": [None, None], "qc": [None, None], "i": 0}

    def prep_slab(src_dram, n_kblk, quant_nblk, stores, part=128):
        i = pst8["i"] % 2
        n = n_kblk * 128
        tl = P.op(Q, "dma_start", out=pp32[i][0:part, 0:n], in_=src_dram,
                  dma=f"dld{i}", waits=[pst8["lc"][i], ld_all])
        if quant_nblk:
            tc_ = None
            for q0 in range(0, n, 1024):
                q1 = min(q0 + 1024, n)
                tc_ = quant(pp32[i][0:part, q0:q1], ppb[i][0:part, q0:q1],
                            (q1 - q0) // 16, [tl, pst8["bc"][i]], prep_scr)
        else:
            tc_ = P.op(V, "tensor_copy", out=ppb[i][0:part, 0:n],
                       in_=pp32[i][0:part, 0:n], waits=[tl, pst8["bc"][i]])
        pst8["lc"][i] = tc_
        evs = []
        lastmm = None
        for g0 in range(0, n_kblk, 4):
            gsz = min(4, n_kblk - g0)
            k = st["tp"]
            pst = ps_ts[k % 2]
            w = [tc_, pst8["qc"][i]]
            if (k - 2) in st["tp_ev"]:
                w.append(st["tp_ev"][k - 2])
            mm = None
            for j in range(gsz):
                mm = P.op(T, "transpose", pst[:, j * 128:(j + 1) * 128],
                          ppb[i][0:part, (g0 + j) * 128:(g0 + j + 1) * 128],
                          eyeb[:],
                          waits=([t for t in w if t] + [ld_all])
                          if mm is None else ())
            ev = P.op(V, "tensor_copy",
                      out=ppq[i][:, g0 * 128:(g0 + gsz) * 128],
                      in_=pst[:, 0:gsz * 128], waits=[mm])
            st["tp_ev"][k] = ev
            st["tp"] = k + 1
            evs.append(ev)
            lastmm = mm
        pst8["bc"][i] = lastmm
        stok = None
        for (dst, c0, c1) in stores:
            src = ppq[i][:, c0:c1]
            if dst.ndim == 3:
                src = src.rearrange("p (k e) -> p k e", e=128)
            stok = P.op(Q, "dma_start", out=dst, in_=src,
                        dma=f"dsp{i}", waits=evs)
        pst8["qc"][i] = stok
        pst8["i"] += 1
        return stok

    for ec in range(32):
        prep_slab(w_in[ec * 128:(ec + 1) * 128, :], NKD, 0,
                  [(winT_sc[ec].rearrange("p c e -> p (c e)"), 0, NKD * 128)])
    for oc in range(8):
        prep_slab(w_out[oc * 128:(oc + 1) * 128, :], NDC, 0,
                  [(woutT_sc.rearrange("c p f -> p c f")[:, :, oc * 128:(oc + 1) * 128], 0, NDC * 128)])
    for (w_dram, dstsc) in ((wg_in, wgT_sc), (wu_in, wuT_sc)):
        for fc in range(NKF):
            prep_slab(w_dram[fc * 128:(fc + 1) * 128, :], NKD, 64,
                      [(dstsc.rearrange("k p f -> p k f")[:, :, fc * 128:(fc + 1) * 128], 0, NKD * 128)])
    for oc in range(8):
        prep_slab(wd_in[oc * 128:(oc + 1) * 128, :], NKF, NKF * 8,
                  [(wdT_sc.rearrange("k p f -> p k f")[:, :, oc * 128:(oc + 1) * 128], 0, NKF * 128)])

    i = pst8["i"] % 2
    tl = P.op(Q, "dma_start", out=pp32[i][0:96, 0:DI], in_=w_xp,
              dma=f"dld{i}", waits=[pst8["lc"][i]])
    tcc = P.op(V, "tensor_copy", out=ppb[i][0:96, 0:DI],
               in_=pp32[i][0:96, 0:DI], waits=[tl, pst8["bc"][i]])
    t_wxp = None
    for g0 in range(0, NDC, 4):
        k = st["tp"]; pst = ps_ts[k % 2]
        w = [tcc] + ([st["tp_ev"][k - 2]] if (k - 2) in st["tp_ev"] else [])
        mm = None
        for j in range(4):
            dc = g0 + j
            mm = P.op(T, "transpose", pst[:, j * 128:j * 128 + 96],
                      ppb[i][0:96, dc * 128:(dc + 1) * 128], eyeb[0:96, 0:96],
                      waits=(w + [ld_all]) if mm is None else ())
        for j in range(4):
            dc = g0 + j
            t_wxp = P.op(V, "tensor_copy", out=wxpT[:, dc, :],
                         in_=pst[:, j * 128:j * 128 + 96], waits=[mm])
        st["tp_ev"][k] = t_wxp
        st["tp"] = k + 1
    pst8["lc"][i] = tcc
    pst8["bc"][i] = mm
    pst8["i"] += 1

    i = pst8["i"] % 2
    tl = P.op(Q, "dma_start",
              out=pp32[i][:, 0:NDC * 64].rearrange("p (c r) -> p c r", c=NDC),
              in_=w_dt.rearrange("(c p) r -> p c r", p=128),
              dma=f"dld{i}", waits=[pst8["lc"][i]])
    tcc = P.op(V, "tensor_copy", out=ppb[i][:, 0:NDC * 64],
               in_=pp32[i][:, 0:NDC * 64], waits=[tl, pst8["bc"][i]])
    t_wdt = None
    for g0 in range(0, NDC, 4):
        k = st["tp"]; pst = ps_ts[k % 2]
        w = [tcc] + ([st["tp_ev"][k - 2]] if (k - 2) in st["tp_ev"] else [])
        mm = None
        for j in range(4):
            dc = g0 + j
            mm = P.op(T, "transpose", pst[0:64, j * 128:(j + 1) * 128],
                      ppb[i][:, dc * 64:(dc + 1) * 64], eyeb[:],
                      waits=(w + [ld_all]) if mm is None else ())
        t_wdt = P.op(V, "tensor_copy",
                     out=wdtT[:, g0 * 128:(g0 + 4) * 128],
                     in_=pst[0:64, 0:512], waits=[mm])
        st["tp_ev"][k] = t_wdt
        st["tp"] = k + 1
    pst8["lc"][i] = tcc
    pst8["bc"][i] = mm
    pst8["i"] += 1

    prep_sp = [("dsp0", P.count.get("dsp0", 0)), ("dsp1", P.count.get("dsp1", 0))]
    prep_v = ("Sv", P.count["Sv"])

    prev = {}
    ffn_pa_free = [t_gbc, t_bbc]
    psw_free = [t_gbc, t_bbc]
    win_free = [None, None]
    wg_free = None; wu_free = None
    wd_free = [None, None]
    wo_free = [None, None]
    sc_prev = {}
    pending_m4 = None

    for ti in range(CFG_NT):
        t0 = ti * TC
        xw = list(prep_sp) + [prev.get("DtR_free")]
        t_x = None
        for s_ in range(SEQ):
            t_x = P.op(Q, "dma_start",
                       out=xt_v[:, 2 * s_:2 * s_ + 2, :],
                       in_=x_in.rearrange("s (a p) d -> s p a d", p=128)
                       [s_][:, 2 * ti:2 * ti + 2, :],
                       dma="dx", waits=xw if s_ == 0 else ())

        bn = []
        for j in range(4):
            bn.append(P.op(V, "bn_stats", out=bnb[:, j, 0:6],
                           in_=xt_v[:, j, 0:512],
                           waits=[t_x, prev.get("bn_free"),
                                  prep_v if ti == 0 else None]))
            bn.append(P.op(V, "bn_stats", out=bnb[:, j, 6:12],
                           in_=xt_v[:, j, 512:1024], waits=[]))
        t_mv = None
        for j in range(4):
            t_mv = P.op(V, "bn_aggr", out=mv[:, j, :], in_=bnb[:, j, :],
                        waits=[bn[-1]] if j == 0 else ())
        t_ln1 = P.op(S, "activation", out=lnt[:], in_=mv[:, :, 1], func=AF.Ln,
                     bias=col(cst, C_EPS),
                     waits=[t_mv, ld_all, prev.get("ln_free")])
        t_r = P.op(S, "activation", out=rr[:], in_=lnt[:], func=AF.Exp,
                   scale=col(cst, C_NEGH), waits=[t_ln1])
        t_ln2 = P.op(S, "activation", out=lnt[:], in_=mv[:, :, 1], func=AF.Ln,
                     scale=col(cst, C_LNS), bias=col(cst, C_LNB), waits=[t_r])
        t_rho = P.op(S, "activation", out=rho[:], in_=lnt[:], func=AF.Exp,
                     scale=col(cst, C_NEGH), waits=[t_ln2])
        t_rci = P.op(V, "reciprocal", out=lnt[:], in_=rr[:],
                     waits=[t_rho, t_mv])
        t_rat = P.op(V, "tensor_tensor", out=rat[:], in0=rho[:], in1=lnt[:],
                     op=ALU.mult, waits=[t_rci])

        t_xn = []
        t_xn2 = []
        for j in range(4):
            a = P.op(V, "tensor_scalar", out=xn_v[:, j, :], in0=xt_v[:, j, :],
                     scalar1=col(mv, j, 0), scalar2=col(rr, j),
                     op0=ALU.subtract, op1=ALU.mult,
                     waits=[t_r, prev.get("CbR_free")])
            t_xn.append(a)
            pass
            c_ = P.op(V, "tensor_scalar", out=p1s, in0=xn_v[:, j, :],
                      scalar1=col(rat, j), scalar2=None, op0=ALU.mult,
                      waits=[t_rat, a, prev.get("hqTR_free") if (ti and j == 0) else None])
            d_ = P.op(V, "tensor_tensor", out=p2s, in0=p1s, in1=g_bc[:],
                      op=ALU.mult, waits=[c_, t_gbc])
            e_ = P.op(V, "tensor_tensor", out=xn2b_v[:, j, :], in0=p2s,
                      in1=b_bc[:], op=ALU.add,
                      waits=[d_, t_bbc, prev.get("BbR_free") if j == 0 else None])
            t_xn2.append(e_)


        if pending_m4 is not None:
            t_m4p = {}
            for tcin in range(2):
                ps = (ps_w0, ps_w1)[tcin]
                ev = P.op(V, "tensor_tensor", out=acc[:, tcin, :],
                          in0=acc[:, tcin, :], in1=ps[:], op=ALU.add,
                          waits=[pending_m4["mm"][(0, tcin)]])
                t_m4p[(0, tcin)] = ev
            mm = None
            tp1mm = {}
            for dc in range(NDC):
                slot = dc % 2
                wl = P.op(G, "dma_start", out=wob[slot][:], in_=woutT_sc[dc],
                          dma=f"dwo{slot}",
                          waits=[wo_free[slot]] if wo_free[slot] else prep_sp)
                for tcin in range(2):
                    tc = 2 + tcin
                    ps = (ps_w0, ps_w1)[tcin]
                    if dc == 0:
                        wt = [wl, pending_m4["yg"], t_m4p[(0, tcin)]]
                    else:
                        wt = [wl]
                    for nh in range(2):
                        mm = P.op(T, "matmul", ps[:, nh * 512:(nh + 1) * 512],
                                  ygb_v[:, dc, tc * 128:(tc + 1) * 128],
                                  wob[slot][:, nh * 512:(nh + 1) * 512],
                                  start=(dc == 0), stop=(dc == NDC - 1),
                                  waits=wt if nh == 0 else ())
                    tp1mm[tcin] = mm
                wo_free[slot] = mm
            pending_m4["evp0"] = t_m4p
            pending_m4["tp1mm"] = tp1mm
            prev["SCANR_free"] = mm
            prev["ygb_free"] = mm

        t_xn2T = None
        for kc in range(NKD):
            t_xn2T = transpose_group(
                [(j, xn2b_v[:, j, kc * 128:(kc + 1) * 128]) for j in range(4)],
                [(xn2T_v[:, kc, :], 0, 512)],
                [t_xn2[-1], prev.get("hqTR_free") if kc == 0 else None])
        t_u = {}
        t_sz = {}
        ec_order = list(range(16, 32)) + list(range(16))
        for eci, ec in enumerate(ec_order):
            slot = eci % 2
            wl = P.op(Q, "dma_start",
                      out=winb[slot][:],
                      in_=winT_sc[ec].rearrange("p c e -> p (c e)"),
                      dma=f"dwin{slot}",
                      waits=[win_free[slot]] if win_free[slot] else prep_sp)
            bk = eci % 2
            ps = (ps_a0, ps_a1)[bk]
            wt = [wl, t_xn2T,
                  ffn_pa_free[bk] if (ti == 0 and eci < 2) else None,
                  st["paf"][bk]]
            mm = None
            for kc in range(NKD):
                mm = P.op(T, "matmul", ps[:], winb[slot][:, kc, :],
                          xn2T_v[:, kc, :], start=(kc == 0),
                          stop=(kc == NKD - 1), waits=wt if kc == 0 else ())
            if ec < 16:
                ev = P.op(S, "copy", out=u_v[:, ec, :], in_=ps[:],
                          waits=[mm, prev.get("SCANR_free")])
                t_u[ec] = ev
            else:
                ev = P.op(S, "activation", out=szb[:, ec - 16, :], in_=ps[:],
                          func=AF.Silu, waits=[mm, prev.get("szb_free")])
                t_sz[ec - 16] = ev
            st["paf"][bk] = ev
            win_free[slot] = mm

        t_xq = None
        for j in range(4):
            for hh in range(2):
                t_xq = quant(xn_v[:, j, hh * 512:(hh + 1) * 512],
                             xqb_v[:, j, hh * 512:(hh + 1) * 512], 32,
                             [t_xn[j],
                              t_xn2[-1] if (j == 0 and hh == 0) else None,
                              prev.get("hqTR_free") if (j == 0 and hh == 0) else None],
                             p1_scr, sq=True)

        if pending_m4 is not None:
            pi = pending_m4["ti"]
            t_m4p = pending_m4["evp0"]
            evs = [t_m4p[(0, 0)], t_m4p[(0, 1)]]
            for tcin in range(2):
                tc = 2 + tcin
                ps = (ps_w0, ps_w1)[tcin]
                ev = P.op(V, "tensor_tensor", out=acc[:, tc, :],
                          in0=acc[:, tc, :], in1=ps[:], op=ALU.add,
                          waits=[pending_m4["tp1mm"][tcin]])
                evs.append(ev)
                prev["psw_free%d" % tcin] = ev
                st["pw"][tcin] = ev
            for s_ in range(SEQ):
                P.op(G, "dma_start",
                     out=out_t.rearrange("s (a p) d -> s p a d", p=128)
                     [s_][:, 2 * pi:2 * pi + 2, :],
                     in_=acc[:, 2 * s_:2 * s_ + 2, :],
                     dma="dso", waits=evs if s_ == 0 else ())
            prev["acc_free"] = ("dso", P.count["dso"])
            pending_m4 = None

        t_acc0 = None
        for j in range(4):
            t_acc0 = P.op(V, "tensor_tensor", out=acc[:, j, :],
                          in0=xt_v[:, j, :], in1=xn_v[:, j, :], op=ALU.add,
                          waits=[t_xn[j], prev.get("acc_free") if j == 0 else None])

        t_xqT = None
        for kc in range(NKD):
            t_xqT = transpose_group(
                [(j, xqb_v[:, j, kc * 128:(kc + 1) * 128]) for j in range(4)],
                [(xqT_v[:, kc, :], 0, 512)],
                [t_xq, t_acc0 if kc == 0 else None,
                 t_xn[3] if kc == 0 else None])

        u4 = u_v.rearrange("p c (s t) -> p c s t", s=SEQ)
        th1 = P.op(V, "tensor_copy", out=uhead[:, :, :, 0:3], in_=ctail[:],
                   waits=[t_ct if ti == 0 else prev.get("ctail_w"),
                          prev.get("uhead_free")] + [t_u[ec] for ec in range(16)])
        th2 = P.op(V, "tensor_copy", out=uhead[:, :, :, 3:8],
                   in_=u4[:, :, :, 0:5], waits=[th1])
        t_tailw = P.op(V, "tensor_copy", out=ctail[:],
                       in_=u4[:, :, :, 253:256], waits=[th2])
        prev["ctail_w"] = t_tailw
        t_conv = {}
        for dc in range(NDC):
            ca = (ca_lo if dc < 8 else ca_hi)[:, dc % 8, :] \
                .rearrange("p (s t) -> p s t", s=SEQ)
            w0 = [t_tailw]
            if dc == 0:
                w0 += [prev.get("BbR_free2"), t_xn2[-1], t_xqT]
            if dc == 8:
                w0 += [prev.get("DtR_free2"), t_xn[-1]]
            base = None
            for k in range(4):
                srcm = u4[:, dc, :, max(0, 2 + k - 2): 253 + k]
            for k in range(4):
                srcm = u4[:, dc, :, 2 + k: 253 + k]
                if k == 0:
                    base = P.op(V, "tensor_scalar", out=ca[:, :, 5:256],
                                in0=srcm, scalar1=col(cvw_sc, dc, 0),
                                scalar2=None, op0=ALU.mult, waits=w0)
                else:
                    base = P.op(V, "scalar_tensor_tensor", out=ca[:, :, 5:256],
                                in0=srcm, scalar=col(cvw_sc, dc, k),
                                in1=ca[:, :, 5:256], op0=ALU.mult, op1=ALU.add,
                                waits=[base])
            for k in range(4):
                srch = uhead[:, dc, :, k:k + 5]
                if k == 0:
                    hh = P.op(V, "tensor_scalar", out=ca[:, :, 0:5], in0=srch,
                              scalar1=col(cvw_sc, dc, 0), scalar2=None,
                              op0=ALU.mult, waits=[base])
                else:
                    hh = P.op(V, "scalar_tensor_tensor", out=ca[:, :, 0:5],
                              in0=srch, scalar=col(cvw_sc, dc, k),
                              in1=ca[:, :, 0:5], op0=ALU.mult, op1=ALU.add,
                              waits=[hh])
            sl = P.op(S, "activation", out=ucb[:, dc, :],
                      in_=ca.rearrange("p s t -> p (s t)"), func=AF.Silu,
                      bias=col(cvb_sc, dc),
                      waits=[hh, prev.get("ucb_free") if dc == 0 else None])
            t_conv[dc] = sl
        t_convlast = t_conv[NDC - 1]
        prev["uhead_free"] = hh

        wt = [t_convlast, t_wxp, st["paf"][0]]
        mm = None
        for dc in range(NDC):
            mm = P.op(T, "matmul", ps_a0[0:64, :], wxpT[:, dc, 0:64],
                      ucb[:, dc, :], start=(dc == 0), stop=(dc == NDC - 1),
                      waits=wt if dc == 0 else ())
        t_dl = P.op(V, "tensor_copy", out=dl_b[:], in_=ps_a0[0:64, :],
                    waits=[mm])
        st["paf"][0] = t_dl
        t_bc2 = {}
        for half, dst in ((0, bcrB), (1, bcrC)):
            bk = (half + 1) % 2
            ps = (ps_a0, ps_a1)[bk]
            wt = [t_convlast, st["paf"][bk],
                  ffn_pa_free[bk] if ti == 0 else None]
            mm = None
            for dc in range(NDC):
                mm = P.op(T, "matmul", ps[0:16, :],
                          wxpT[:, dc, 64 + half * 16:80 + half * 16],
                          ucb[:, dc, :], start=(dc == 0), stop=(dc == NDC - 1),
                          waits=wt if dc == 0 else ())
            t_bc2[half] = P.op(V, "tensor_copy", out=dst[:], in_=ps[0:16, :],
                               waits=[mm])
            st["paf"][bk] = t_bc2[half]

        m3t = SCANR[:, 7680:8192]
        t_dt = {}
        m3e1 = {}

        def emit_m3(dc):
            half = dc % 2
            ps = ps_w1[:, half * 512:(half + 1) * 512]
            wt = [t_dl, t_wdt,
                  m3e1[dc - 2] if dc >= 2 else st["pw"][1]]
            mm = P.op(T, "matmul", ps, wdtT[:, dc * 128:(dc + 1) * 128],
                      dl_b[:], start=True, stop=True, waits=wt)
            e1 = P.op(S, "activation", out=m3t, in_=ps, func=AF.Exp,
                      bias=col(bdt_sc, dc),
                      waits=[mm, t_convlast if dc == 0 else t_dt[dc - 1]])
            e2 = P.op(S, "activation", out=dtb_v[:, dc, :], in_=m3t,
                      func=AF.Ln, bias=col(cst, C_ONE), waits=[e1])
            m3e1[dc] = e1
            t_dt[dc] = e2

        t_hqT = {}
        last_f2 = None
        kff = 0
        pend_tp = None

        def emit_pend_tp():
            (p_hql, p_par, p_f0, p_tc, p_nkb, p_tq, p_kff) = pend_tp
            ev = transpose_group(
                [(j, p_hql[:, j * 128:(j + 1) * 128]) for j in range(p_nkb)],
                [(hqT_v[:, (p_f0 // 128) + j, p_tc * 128:(p_tc + 1) * 128],
                  j * 128, (j + 1) * 128) for j in range(p_nkb)],
                [p_tq, t_xn2T if p_kff == 0 else None,
                 prev.get("hqTR_free2") if p_kff == 0 else None])
            st["hqlt%d" % p_par] = st["tp_ev"][st["tp"] - 1]
            for j in range(p_nkb):
                t_hqT[((p_f0 // 128) + j, p_tc)] = ev

        for fsi in range(6):
            f0 = fsi * 512
            f1 = min(f0 + 512, DFF)
            fn = f1 - f0
            wlg = P.op(Q, "dma_start", out=wgb[:, :, 0:fn],
                       in_=wgT_sc.rearrange("k p f -> p k f")[:, :, f0:f1],
                       dma="dwg",
                       waits=[wg_free] if wg_free else list(prep_sp))
            wlu = P.op(Q, "dma_start", out=wub[:, :, 0:fn],
                       in_=wuT_sc.rearrange("k p f -> p k f")[:, :, f0:f1],
                       dma="dwu",
                       waits=[wu_free] if wu_free else list(prep_sp))
            for tc in range(4):
                if kff < NDC:
                    emit_m3(kff)
                par = kff % 2
                gps = (ps_a0[:, 0:fn], ps_w0[:, 0:fn])[par]
                ups = (ps_a1[:, 0:fn], ps_w0[:, 512:512 + fn])[par]
                gkey = ("paf", 0) if par == 0 else ("pw", 0)
                ukey = ("paf", 1) if par == 0 else ("pw", 1)
                g_mm = None
                wt0 = [wlg, t_xqT, st[gkey[0]][gkey[1]],
                       psw_free[0] if (ti == 0 and kff < 2) else None]
                for kc in range(NKD):
                    g_mm = P.op(T, "matmul", gps,
                                xqT_v[:, kc, tc * 128:(tc + 1) * 128],
                                wgb[:, kc, 0:fn], start=(kc == 0),
                                stop=(kc == NKD - 1),
                                waits=wt0 if kc == 0 else ())
                u_mm = None
                wt1 = [wlu, st[ukey[0]][ukey[1]]]
                for kc in range(NKD):
                    u_mm = P.op(T, "matmul", ups,
                                xqT_v[:, kc, tc * 128:(tc + 1) * 128],
                                wub[:, kc, 0:fn], start=(kc == 0),
                                stop=(kc == NKD - 1),
                                waits=wt1 if kc == 0 else ())
                hsl_k = hslp[par]
                hql_k = hqlp[par]
                sg = P.op(S, "activation", out=hsl_k[:, 0:fn], in_=gps,
                          func=AF.Silu,
                          waits=[g_mm, st.get("hslv%d" % par),
                                 t_conv[kff] if kff < 2 else None])
                hm = P.op(V, "tensor_tensor", out=hsl_k[:, 0:fn],
                          in0=hsl_k[:, 0:fn], in1=ups, op=ALU.mult,
                          waits=[sg, u_mm])
                st[gkey[0]][gkey[1]] = hm
                st[ukey[0]][ukey[1]] = hm
                tq = quant(hsl_k[:, 0:fn], hql_k[:, 0:fn], fn // 16,
                           [hm, st.get("hqlt%d" % par),
                            prev.get("SCANR_free") if (ti and kff == 0) else None,
                            t_convlast if kff == 0 else None],
                           main_scr)
                st["hslv%d" % par] = tq
                nkb = fn // 128
                if pend_tp is not None:
                    emit_pend_tp()
                pend_tp = (hql_k, par, f0, tc, nkb, tq, kff)
                last_f2 = u_mm
                kff += 1
            wg_free = g_mm
            wu_free = u_mm
        for dc in range(kff, NDC):
            emit_m3(dc)
        emit_pend_tp()
        pend_tp = None
        st["pw"][1] = m3e1[NDC - 1]
        t_hsl_last = st["qprev"]
        prev["hsl_free"] = t_hsl_last

        t_Bb = None
        t_Cb = None
        for n in range(2 * DST):
            bk = n % 2
            ps = (ps_a0, ps_a1)[bk]
            wt = [t_bc2[0], t_bc2[1], st["paf"][bk]]
            if n == 0:
                wt += [t_xn2[-1], t_xq, st["qprev"]]
            if n == DST:
                wt += [last_f2]
            src = bcrB if n < DST else bcrC
            mm = P.op(T, "matmul", ps[:], selb[:, (n % DST) * 128:(n % DST + 1) * 128],
                      src[:], start=True, stop=True, waits=wt)
            if n < DST:
                t_Bb = P.op(V, "tensor_copy", out=Bb_v[:, n, :], in_=ps[:],
                            waits=[mm])
                st["paf"][bk] = t_Bb
            else:
                t_Cb = P.op(V, "tensor_copy", out=Cb_v[:, n - DST, :],
                            in_=ps[:], waits=[mm])
                st["paf"][bk] = t_Cb

        f3_kc = {}
        f3_last_mm = {}
        t_f3_p0 = {}

        def emit_f3_tp(tp):
            mm = None
            for kc in range(NKF):
                slot = kc % 2
                wl = P.op(Q if slot == 0 else G, "dma_start",
                          out=wdb[slot][:], in_=wdT_sc[kc],
                          dma=f"dwd{slot}",
                          waits=[wd_free[slot]] if wd_free[slot] else prep_sp)
                for tcin in range(2):
                    tc = tp * 2 + tcin
                    ps = (ps_w0, ps_w1)[tcin]
                    if kc == 0:
                        wt = [wl, t_hqT[(NKF - 1, tc)],
                              st["pw"][0], st["pw"][1], st.get("hslv1"),
                              psw_free[tcin] if (ti == 0 and tp == 0) else
                              (t_f3_p0[tcin] if tp == 1 else prev.get("psw_free%d" % tcin))]
                    else:
                        wt = [wl]
                    for nh in range(2):
                        mm = P.op(T, "matmul", ps[:, nh * 512:(nh + 1) * 512],
                                  hqT_v[:, kc, tc * 128:(tc + 1) * 128],
                                  wdb[slot][:, nh * 512:(nh + 1) * 512],
                                  start=(kc == 0), stop=(kc == NKF - 1),
                                  waits=[w for w in wt if w] if nh == 0 else ())
                    f3_last_mm[(tp, tcin)] = mm
                if tp == 1:
                    f3_kc[kc] = mm
                wd_free[slot] = mm

        emit_f3_tp(0)

        abufs = [argb, dAb]
        iters = [(dc, s_) for dc in range(NDC) for s_ in range(SEQ)]
        t_y = {}
        t_du = {}

        def emit_args(k):
            dc, s_ = iters[k]
            buf = abufs[k % 2]
            w = [t_dt[dc], t_negAb if (ti == 0 and k == 0) else None,
                 sc_prev.get("tree%d" % (k % 2)),
                 sc_prev.get("exp%d" % (k % 2))]
            if ti == 0 and k == 0:
                w.append(t_xq)
            if ti == 0 and k <= 1:
                w.append(t_hsl_last)
            if k <= 1:
                w.append(prev.get("SCANR_free"))
            ta = P.op(V, "tensor_tensor", out=buf[:],
                      in0=dtb_v[:, dc, s_ * 256:(s_ + 1) * 256]
                      .unsqueeze(1).broadcast_to([128, DST, 256]),
                      in1=A_scb[:, dc, :].unsqueeze(2)
                      .broadcast_to([128, DST, 256]),
                      op=ALU.mult, waits=w)
            te = P.op(S, "activation", out=buf.rearrange("p n t -> p (n t)"),
                      in_=buf.rearrange("p n t -> p (n t)"), func=AF.Exp,
                      waits=[ta, sc_prev.get("ldt%d" % (k % 2))])
            sc_prev["exp%d" % (k % 2)] = te
            sc_prev["argt%d" % (k % 2)] = ta
            return te

        t_exp_k = {}
        t_exp_k[0] = emit_args(0)
        for k, (dc, s_) in enumerate(iters):
            buf = abufs[k % 2]
            if s_ == 0:
                t_du[dc] = P.op(V, "tensor_tensor", out=dtusl,
                                in0=dtb_v[:, dc, :], in1=ucb[:, dc, :],
                                op=ALU.mult,
                                waits=[t_dt[dc], t_conv[dc],
                                       sc_prev.get("dbu"),
                                       t_hsl_last if dc == 0 else None])
            tdbu = P.op(V, "tensor_tensor", out=dBub[:],
                        in0=dtusl[:, s_ * 256:(s_ + 1) * 256]
                        .unsqueeze(1).broadcast_to([128, DST, 256]),
                        in1=Bb_v[:, :, s_ * 256:(s_ + 1) * 256],
                        op=ALU.mult,
                        waits=[t_du[dc], t_Bb, sc_prev.get("hC"),
                               sc_prev.get("scan")])
            sc_prev["dbu"] = tdbu
            tsd = P.op(V, "tensor_tensor", out=sdt[:],
                       in0=buf[:, :, 0], in1=hcar[:, dc, :, s_],
                       op=ALU.mult,
                       waits=[t_exp_k[k], t_hc if ti == 0 else None,
                              sc_prev.get("carry")])
            tsd2 = P.op(V, "tensor_tensor", out=dBub[:, :, 0],
                        in0=dBub[:, :, 0], in1=sdt[:], op=ALU.add,
                        waits=[tsd, tdbu])
            tz = P.op(V, "memset", buf[:, :, 0:1], 0.0, waits=[tsd])
            sc_prev["ldt%d" % (k % 2)] = tz
            if k + 1 < len(iters):
                t_exp_k[k + 1] = emit_args(k + 1)
            tscan = P.op(V, "tensor_tensor_scan",
                         out=hb_v.rearrange("p n t -> p (n t)"),
                         data0=buf.rearrange("p n t -> p (n t)"),
                         data1=dBub.rearrange("p n t -> p (n t)"),
                         initial=0.0, op0=ALU.mult, op1=ALU.add,
                         waits=[tsd2, tz, sc_prev.get("hC"),
                                sc_prev.get("carry")])
            sc_prev["ldt%d" % (k % 2)] = tscan
            tcar = P.op(V, "tensor_copy", out=hcar[:, dc, :, s_],
                        in_=hb_v[:, :, 255], waits=[tscan])
            thc = P.op(V, "tensor_tensor", out=dBub[:], in0=hb_v[:],
                       in1=Cb_v[:, :, s_ * 256:(s_ + 1) * 256],
                       op=ALU.mult, waits=[tscan, t_Cb])
            if k == 0:
                for tcin in range(2):
                    ps_ = (ps_w0, ps_w1)[tcin]
                    t_f3_p0[tcin] = P.op(V, "tensor_tensor",
                                         out=acc[:, tcin, :],
                                         in0=acc[:, tcin, :], in1=ps_[:],
                                         op=ALU.add,
                                         waits=[f3_last_mm[(0, tcin)], t_acc0])
                emit_f3_tp(1)
            b8 = buf[:, 0:8, :]
            b4 = buf[:, 8:12, :]
            b2 = buf[:, 12:14, :]
            tt8_ = P.op(V, "tensor_tensor", out=b8, in0=dBub[:, 0:8, :],
                        in1=dBub[:, 8:16, :], op=ALU.add,
                        waits=[thc, tscan])
            tt4_ = P.op(V, "tensor_tensor", out=b4, in0=b8[:, 0:4, :],
                        in1=b8[:, 4:8, :], op=ALU.add, waits=[tt8_])
            tt2_ = P.op(V, "tensor_tensor", out=b2, in0=b4[:, 0:2, :],
                        in1=b4[:, 2:4, :], op=ALU.add, waits=[tt4_])
            ty = P.op(V, "tensor_tensor",
                      out=yb_v[:, dc, s_ * 256:(s_ + 1) * 256],
                      in0=b2[:, 0, :], in1=b2[:, 1, :], op=ALU.add,
                      waits=[tt2_, f3_kc[dc]])
            t_y[dc] = ty
            sc_prev["tree%d" % (k % 2)] = ty
            sc_prev["scan"] = tscan
            sc_prev["carry"] = tcar
            sc_prev["hC"] = thc
            sc_prev["tree"] = ty
            sc_prev["exp"] = t_exp_k[k]
        t_f3 = {(0, 0): t_f3_p0[0], (0, 1): t_f3_p0[1]}
        for tcin in range(2):
            tc = 2 + tcin
            ps = (ps_w0, ps_w1)[tcin]
            ev = P.op(V, "tensor_tensor", out=acc[:, tc, :],
                      in0=acc[:, tc, :], in1=ps[:], op=ALU.add,
                      waits=[f3_last_mm[(1, tcin)], t_acc0])
            t_f3[(1, tcin)] = ev

        t_yg = {}
        for dc in range(NDC):
            g1 = P.op(V, "scalar_tensor_tensor", out=ygb_v[:, dc, :],
                      in0=ucb[:, dc, :], scalar=col(D_sc, dc),
                      in1=yb_v[:, dc, :], op0=ALU.mult, op1=ALU.add,
                      waits=[t_y[dc]] + ([sc_prev["exp"], sc_prev["tree"],
                                          prev.get("SCB_m4")] if dc == 0 else []))
            g2 = P.op(V, "tensor_tensor", out=ygb_v[:, dc, :],
                      in0=ygb_v[:, dc, :], in1=szb[:, dc, :], op=ALU.mult,
                      waits=[g1, t_sz[dc]])
            t_yg[dc] = g2

        m4_last_mm = {}
        mm = None
        for dc in range(NDC):
            slot = dc % 2
            wl = P.op(Q if slot == 0 else G, "dma_start",
                      out=wob[slot][:], in_=woutT_sc[dc],
                      dma=("dwoq0" if slot == 0 else "dwo1"),
                      waits=[wo_free[slot]] if wo_free[slot] else prep_sp)
            for tcin in range(2):
                tc = tcin
                ps = (ps_w0, ps_w1)[tcin]
                if dc == 0:
                    wt = [wl, t_yg[0], t_f3[(1, tcin)]]
                else:
                    wt = [wl, t_yg[dc]]
                for nh in range(2):
                    mm = P.op(T, "matmul", ps[:, nh * 512:(nh + 1) * 512],
                              ygb_v[:, dc, tc * 128:(tc + 1) * 128],
                              wob[slot][:, nh * 512:(nh + 1) * 512],
                              start=(dc == 0), stop=(dc == NDC - 1),
                              waits=wt if nh == 0 else ())
                m4_last_mm[(0, tcin)] = mm
            wo_free[slot] = mm
        pending_m4 = {"mm": dict(m4_last_mm), "ti": ti, "yg": t_yg[NDC - 1]}

        prev["DtR_free"] = sc_prev["argt%d" % ((len(iters) - 1) % 2)]
        prev["DtR_free2"] = t_xq
        prev["BbR_free"] = sc_prev["dbu"]
        prev["BbR_free2"] = t_xq
        prev["CbR_free"] = sc_prev["hC"]
        prev["SCANR_free"] = wo_free[0]
        prev["SCB_m4"] = None
        prev["hqTR_free"] = t_yg[NDC - 1]
        prev["hqTR_free2"] = None
        prev["szb_free"] = t_yg[NDC - 1]
        prev["ucb_free"] = t_yg[NDC - 1]
        prev["bn_free"] = t_mv
        prev["ln_free"] = t_rat
        prev["uhead_free"] = prev["uhead_free"]

    if pending_m4 is not None:
        pi = pending_m4["ti"]
        m4all = []
        for tcin in range(2):
            ps = (ps_w0, ps_w1)[tcin]
            ev = P.op(V, "tensor_tensor", out=acc[:, tcin, :],
                      in0=acc[:, tcin, :], in1=ps[:], op=ALU.add,
                      waits=[pending_m4["mm"][(0, tcin)]])
            m4all.append(ev)
        mm = None
        tp1mm = {}
        for dc in range(NDC):
            slot = dc % 2
            wl = P.op(Q if slot == 0 else G, "dma_start",
                      out=wob[slot][:], in_=woutT_sc[dc],
                      dma=("dwoq0" if slot == 0 else "dwo1"),
                      waits=[wo_free[slot]] if wo_free[slot] else prep_sp)
            for tcin in range(2):
                tc = 2 + tcin
                ps = (ps_w0, ps_w1)[tcin]
                if dc == 0:
                    wt = [wl, pending_m4["yg"], m4all[tcin]]
                else:
                    wt = [wl]
                for nh in range(2):
                    mm = P.op(T, "matmul", ps[:, nh * 512:(nh + 1) * 512],
                              ygb_v[:, dc, tc * 128:(tc + 1) * 128],
                              wob[slot][:, nh * 512:(nh + 1) * 512],
                              start=(dc == 0), stop=(dc == NDC - 1),
                              waits=wt if nh == 0 else ())
                tp1mm[tcin] = mm
            wo_free[slot] = mm
        for tcin in range(2):
            tc = 2 + tcin
            ps = (ps_w0, ps_w1)[tcin]
            ev = P.op(V, "tensor_tensor", out=acc[:, tc, :],
                      in0=acc[:, tc, :], in1=ps[:], op=ALU.add,
                      waits=[tp1mm[tcin]])
            m4all.append(ev)
        for s_ in range(SEQ):
            P.op(G, "dma_start",
                 out=out_t.rearrange("s (a p) d -> s p a d", p=128)
                 [s_][:, 2 * pi:2 * pi + 2, :],
                 in_=acc[:, 2 * s_:2 * s_ + 2, :],
                 dma="dso", waits=m4all if s_ == 0 else ())
        pending_m4 = None

    P.op(G, "engine_nop", waits=[("dso", P.count.get("dso", 0))])

    for name in sorted(P.semnames):
        P.sems[name] = stack.enter_context(nc.semaphore(name))
    block = stack.enter_context(nc.Block())
    P.emit(block)
    stack.close()
    return nc


def _consts_array():
    c = np.zeros((128, 8), np.float32)
    c[:, 0] = 1.0
    c[:, 1] = EPS
    c[:, 2] = 1.0 + EPS
    c[:, 3] = EPS * EPS
    c[:, 4] = -0.5
    return c


def _sel_array():
    s = np.zeros((16, 16 * 128), ml_dtypes.bfloat16)
    for n in range(16):
        s[n, n * 128:(n + 1) * 128] = 1
    return s


def _base_inputs(inputs):
    return {
        "ln_g": inputs["L0_attn_ln_g"], "ln_b": inputs["L0_attn_ln_b"],
        "w_in": inputs["L0_attn_W_in"], "conv_w": inputs["L0_attn_conv_w"],
        "conv_b": inputs["L0_attn_conv_b"], "w_xp": inputs["L0_attn_W_xp"],
        "w_dt": inputs["L0_attn_W_dt"], "b_dt": inputs["L0_attn_b_dt"],
        "a_log": inputs["L0_attn_A_log"], "d_vec": inputs["L0_attn_D"],
        "w_out": inputs["L0_attn_W_out"], "wg": inputs["L0_ffn_w_gate"],
        "wu": inputs["L0_ffn_w_up"], "wd": inputs["L0_ffn_w_down"],
        "eye_b": np.eye(128, dtype=ml_dtypes.bfloat16),
        "ones1f": np.ones((1, 128), np.float32),
        "sel": _sel_array(),
        "csts": _consts_array(),
    }


_NC_CACHE = {}


def kernel(**inputs):
    if "prog" not in _NC_CACHE:
        _NC_CACHE["prog"] = build_program()
    nc = _NC_CACHE["prog"]
    x = np.ascontiguousarray(inputs["L0_attn_x"])
    base = _base_inputs(inputs)
    in_maps = []
    for c in range(NCORES):
        m = dict(base)
        m["x"] = x[c * SEQ:(c + 1) * SEQ]
        in_maps.append(m)
    res = run_bass_kernel_spmd(nc, in_maps, core_ids=list(range(NCORES)))
    out = np.concatenate([r["out"] for r in res.results], axis=0)
    return out.astype(np.float32)


# version 11
# speedup vs baseline: 1.3240x; 1.0822x over previous
import sys
sys.path.insert(0, "/opt/trn_rl_repo")
import numpy as np
import ml_dtypes

import concourse.bass as bass
import concourse.mybir as mybir
from concourse.bass_utils import run_bass_kernel_spmd

F32 = mybir.dt.float32
BF16 = mybir.dt.bfloat16
AF = mybir.ActivationFunctionType
ALU = mybir.AluOpType
AX = mybir.AxisListType

B, L, D = 16, 2048, 1024
NCORES = 8
SEQ = 2
DI = 2048
DST = 16
DFF = 2816
TC = 256
TOK = SEQ * TC
NT = L // TC
NDC = DI // 128
NKD = D // 128
NKF = DFF // 128
EPS = 1e-5
MID = [0.25, 0.75, 1.25, 1.75, 2.5, 3.5, 5.0]
MIDSQ = [m * m for m in MID]
MIDW = [0.5, 0.5, 0.5, 0.5, 1.0, 1.0, 2.0]

CFG_NT = NT


class Plan:
    ENGS = ("v", "s", "t", "g", "q")

    def __init__(self):
        self.ops = {e: [] for e in self.ENGS}
        self.count = {}
        self.semnames = set("S" + e for e in self.ENGS)
        self.sems = {}

    def dma_sem(self, name):
        self.semnames.add(name)

    def op(self, eng, method, *args, waits=(), dma=None, **kw):
        import traceback
        fr = traceback.extract_stack(limit=4)
        tbs = ";".join(f"{f.lineno}" for f in fr[:-1])
        w = tuple(t for t in waits if t is not None)
        if dma is not None:
            self.count[dma] = self.count.get(dma, 0) + 16
            tok = (dma, self.count[dma])
        else:
            s = "S" + eng
            self.count[s] = self.count.get(s, 0) + 1
            tok = (s, self.count[s])
        self.ops[eng].append((method, args, kw, w, tok, tbs))
        return tok

    def emit(self, block):
        plan = self
        secs = {"v": block.vector, "s": block.scalar, "t": block.tensor,
                "g": block.gpsimd, "q": block.sync}

        def make_body(e):
            own = "S" + e

            def body(engine):
                waited = {}
                for (method, args, kw, waits, tok, src) in plan.ops[e]:
                    need = {}
                    for (sname, val) in waits:
                        if val > need.get(sname, -1):
                            need[sname] = val
                    for sname in sorted(need):
                        val = need[sname]
                        if waited.get(sname, -1) >= val:
                            continue
                        engine.wait_ge(plan.sems[sname], val)
                        waited[sname] = val
                    try:
                        instr = getattr(engine, method)(*args, **kw)
                    except Exception as ex:
                        raise RuntimeError(f"emit failed: {method} planned at {src}: {ex}") from ex
                    sname, _ = tok
                    instr.then_inc(plan.sems[sname], 1 if sname == own else 16)
            return body

        for e in self.ENGS:
            if self.ops[e]:
                secs[e](make_body(e))


def col(ap, *idx):
    sl = tuple([slice(None)] + [slice(i, i + 1) for i in idx])
    return ap[sl]


def cdiv(a, b):
    return (a + b - 1) // b


def build_program():
    nc = bass.Bass("TRN2", target_bir_lowering=False)
    P = Plan()
    V, S, T, G, Q = "v", "s", "t", "g", "q"

    def din(name, shape, dt=F32):
        return nc.dram_tensor(name, shape, dt, kind="ExternalInput").ap()

    x_in = din("x", [SEQ, L, D])
    ln_g = din("ln_g", [D]); ln_b = din("ln_b", [D])
    w_in = din("w_in", [2 * DI, D])
    conv_w = din("conv_w", [DI, 4]); conv_b = din("conv_b", [DI])
    w_xp = din("w_xp", [96, DI])
    w_dt = din("w_dt", [DI, 64]); b_dt = din("b_dt", [DI])
    a_log = din("a_log", [DI, DST]); d_vec = din("d_vec", [DI])
    w_out = din("w_out", [D, DI])
    wg_in = din("wg", [DFF, D]); wu_in = din("wu", [DFF, D])
    wd_in = din("wd", [D, DFF])
    eye_b = din("eye_b", [128, 128], BF16)
    ones1f = din("ones1f", [1, 128], F32)
    sel_in = din("sel", [16, 16 * 128], BF16)
    csts = din("csts", [128, 8], F32)
    C_ONE, C_EPS, C_LNS, C_LNB, C_NEGH = 0, 1, 2, 3, 4

    out_t = nc.dram_tensor("out", [SEQ, L, D], F32, kind="ExternalOutput").ap()

    def scr(name, shape, dt=BF16):
        return nc.dram_tensor(name, shape, dt, kind="Internal").ap()

    winT_sc = scr("winT_sc", [32, 128, NKD, 128])
    woutT_sc = scr("woutT_sc", [NDC, 128, D])
    wgT_sc = scr("wgT_sc", [NKD, 128, DFF])
    wuT_sc = scr("wuT_sc", [NKD, 128, DFF])
    wdT_sc = scr("wdT_sc", [NKF, 128, D])

    from contextlib import ExitStack
    stack = ExitStack()

    def sb(name, shape, dt=F32):
        return stack.enter_context(nc.sbuf_tensor(name, shape, dt))

    def psum(name, shape, dt=F32):
        return stack.enter_context(nc.psum_tensor(name, shape, dt))

    cst = sb("cst", [128, 8])
    eyeb = sb("eyeb", [128, 128], BF16)
    onesf = sb("onesf", [1, 128])
    selb = sb("selb", [16, 16 * 128], BF16)
    A_sc = sb("A_sc", [128, NDC, DST])
    A_scb = sb("A_scb", [128, NDC, DST], BF16)
    D_sc = sb("D_sc", [128, NDC])
    bdt_sc = sb("bdt_sc", [128, NDC])
    cvb_sc = sb("cvb_sc", [128, NDC])
    cvw_sc = sb("cvw_sc", [128, NDC, 4])
    g_bc = sb("g_bc", [128, D])
    b_bc = sb("b_bc", [128, D])
    wxpT = sb("wxpT", [128, NDC, 96], BF16)
    wdtT = sb("wdtT", [64, DI], BF16)
    hcar = sb("hcar", [128, NDC, DST, SEQ])
    ctail = sb("ctail", [128, NDC, SEQ, 3])
    bnb = sb("bnb", [128, 4, 12])
    mv = sb("mv", [128, 4, 2])
    rr = sb("rr", [128, 4])
    rho = sb("rho", [128, 4])
    rat = sb("rat", [128, 4])
    lnt = sb("lnt", [128, 4])
    qm = sb("qm", [128, 64])
    qs = sb("qs", [128, 64])
    qr = sb("qr", [128, 64])
    qsb = sb("qsb", [128, 64], BF16)
    sdt = sb("sdt", [128, DST])
    uhead = sb("uhead", [128, NDC, SEQ, 8])
    uheadb = None
    dl_b = sb("dl_b", [64, TOK], BF16)
    bcrB = sb("bcrB", [16, TOK], BF16)
    bcrC = sb("bcrC", [16, TOK], BF16)

    acc = sb("acc", [128, 4, D])
    szb = sb("szb", [128, NDC, TOK], BF16)
    ucb = sb("ucb", [128, NDC, TOK], BF16)

    SCANR = sb("SCANR", [128, 8192])
    BbR = sb("BbR", [128, 8192], BF16)
    CbR = sb("CbR", [128, 8192], BF16)
    DtR = sb("DtR", [128, 8192], BF16)
    hqTR = sb("hqTR", [128, 11264], BF16)

    winb = [sb(f"winb{i}", [128, NKD, 128], BF16) for i in range(2)]
    wgb = sb("wgb", [128, NKD, 512], BF16)
    wub = sb("wub", [128, NKD, 512], BF16)
    wdb = [sb(f"wdb{i}", [128, D], BF16) for i in range(2)]
    wob = [sb(f"wob{i}", [128, D], BF16) for i in range(2)]

    SCB = SCANR[:].bitcast(BF16)
    u_v = SCANR[:].rearrange("p (c t) -> p c t", c=NDC)
    pp32 = [SCANR[:, 0:4096], SCANR[:, 4096:8192]]
    qv_v = SCANR[:, 0:1024]
    qa2_v = SCANR[:, 1024:2048]
    qABCD = [SCB[:, 4096 + i * 1024: 4096 + (i + 1) * 1024] for i in range(4)]
    argb = SCB[:, 0:4096].rearrange("p (n t) -> p n t", n=DST)
    dAb = SCB[:, 4096:8192].rearrange("p (n t) -> p n t", n=DST)
    dBub = SCB[:, 8192:12288].rearrange("p (n t) -> p n t", n=DST)
    hb_v = SCB[:, 12288:16384].rearrange("p (n t) -> p n t", n=DST)
    tr8 = SCB[:, 0:2048].rearrange("p (n t) -> p n t", n=8)
    tr4 = SCB[:, 2048:3072].rearrange("p (n t) -> p n t", n=4)
    tr2 = SCB[:, 3072:3584].rearrange("p (n t) -> p n t", n=2)
    ygb_v = SCB[:, 0:8192].rearrange("p (c t) -> p c t", c=NDC)

    BbF = BbR[:].bitcast(F32)
    xqb_v = BbR[:, 0:4096].rearrange("p (j d) -> p j d", j=4)
    xn2b_v = BbR[:, 4096:8192].rearrange("p (j d) -> p j d", j=4)
    ca_lo = BbF[:].rearrange("p (c t) -> p c t", c=8)
    Bb_v = BbR[:].rearrange("p (n t) -> p n t", n=DST)
    ppb = [BbR[:, 0:4096], BbR[:, 4096:8192]]
    hslp = [BbF[:, 0:512], BbF[:, 512:1024]]
    hsl = hslp[0]
    hqlp = [BbR[:, 2048:2560], BbR[:, 2560:3072]]

    CbF = CbR[:].bitcast(F32)
    xn_v = CbF[:].rearrange("p (j d) -> p j d", j=4)
    xqT_v = CbR[:, 0:4096].rearrange("p (k t) -> p k t", k=NKD)
    Cb_v = CbR[:].rearrange("p (n t) -> p n t", n=DST)
    ppq = [CbR[:, 0:4096], CbR[:, 4096:8192]]

    DtF = DtR[:].bitcast(F32)
    xt_v = DtF[:].rearrange("p (j d) -> p j d", j=4)
    ca_hi = DtF[:].rearrange("p (c t) -> p c t", c=8)
    dtb_v = DtR[:].rearrange("p (c t) -> p c t", c=NDC)

    hqF = hqTR[:].bitcast(F32)
    xn2T_v = hqTR[:, 0:4096].rearrange("p (k t) -> p k t", k=NKD)
    hqT_v = hqTR[:].rearrange("p (k t) -> p k t", k=NKF)
    yb_v = hqTR[:, 0:8192].rearrange("p (c t) -> p c t", c=NDC)
    dtusl = uhead[:].rearrange("p a b c -> p (a b c)").bitcast(BF16)[:, 0:512]
    pqv = hqF[:, 0:1024]
    pqa2 = hqF[:, 1024:2048]
    pqABCD = [hqTR[:, 4096 + i * 1024: 4096 + (i + 1) * 1024] for i in range(4)]
    x1qv = hqF[:, 2048:2560]
    x1qa2 = hqF[:, 2560:3072]
    x1ABCD = [hqTR[:, 6144 + i * 512: 6144 + (i + 1) * 512] for i in range(4)]
    p1s = hqF[:, 2048:3072]
    p2s = hqF[:, 3072:4096]

    ps_a0 = psum("ps_a0", [128, 512]); ps_a1 = psum("ps_a1", [128, 512])
    ps_w0 = psum("ps_w0", [128, 1024]); ps_w1 = psum("ps_w1", [128, 1024])
    ps_t0 = psum("ps_t0", [128, 1024], BF16)
    ps_t1 = psum("ps_t1", [128, 1024], BF16)
    ps_ts = [ps_t0, ps_t1]

    for s in ["dld0", "dld1", "dsp0", "dsp1", "dx", "dso", "dwin0", "dwin1",
              "dwg", "dwu", "dwd0", "dwd1", "dwo0", "dwo1", "dwoq0"]:
        P.dma_sem(s)

    st = {"tp": 0, "tp_ev": {}, "paf": {0: None, 1: None}, "pw": {0: None, 1: None}, "pab": 0, "qprev": None}

    def transpose_group(blocks, evacs, wt):
        k = st["tp"]
        pst = ps_ts[k % 2]
        w = list(wt)
        if (k - 2) in st["tp_ev"]:
            w.append(st["tp_ev"][k - 2])
        mm = None
        for (j, src) in blocks:
            mm = P.op(T, "transpose", pst[:, j * 128:(j + 1) * 128], src,
                      eyeb[:], waits=(w + [ld_all]) if mm is None else ())
        ev = None
        for (dst, c0, c1) in evacs:
            ev = P.op(S, "copy", out=dst, in_=pst[:, c0:c1], waits=[mm])
        st["tp_ev"][k] = ev
        st["tp"] = k + 1
        return ev

    def quant(src_ap, dst_ap, nblk, wtoks, scratch, sq=False):
        qvv, qaa, (qa, qb_, qc, qd) = scratch
        n = nblk * 16
        mm_ = qm[:, 0:nblk]; ss = qs[:, 0:nblk]; rs = qr[:, 0:nblk]
        sbv = qsb[:, 0:nblk]
        vv = qvv[:, 0:n]; aa = qaa[:, 0:n]
        qa = qa[:, 0:n]; qb_ = qb_[:, 0:n]; qc = qc[:, 0:n]; qd = qd[:, 0:n]
        w0 = list(wtoks) + ([st["qprev"]] if st["qprev"] else [])
        src3 = src_ap.rearrange("p (b j) -> p b j", j=16)
        t_m = P.op(V, "tensor_reduce", out=mm_, in_=src3, axis=AX.X, op=ALU.max,
                   apply_absolute_value=True, waits=w0)
        t_s = P.op(V, "tensor_scalar", out=ss, in0=mm_, scalar1=1.0 / 6.0,
                   scalar2=1e-12, op0=ALU.mult, op1=ALU.max, waits=[t_m])
        t_r = P.op(V, "reciprocal", out=rs, in_=ss, waits=[t_s])
        t_sb = P.op(V, "tensor_copy", out=sbv, in_=ss, waits=[t_s])
        rb = rs.unsqueeze(2).broadcast_to([128, nblk, 16])
        t_v = P.op(V, "tensor_tensor", out=vv.rearrange("p (b j) -> p b j", j=16),
                   in0=src3, in1=rb, op=ALU.mult, waits=[t_r])
        if sq:
            t_a = P.op(V, "tensor_tensor", out=aa, in0=vv, in1=vv,
                       op=ALU.mult, waits=[t_v])
        else:
            t_a = P.op(S, "activation", out=aa, in_=vv, func=AF.Abs,
                       waits=[t_v])
        thr = MIDSQ if sq else MID
        def cmp(dst, i, w):
            return P.op(V, "tensor_scalar", out=dst, in0=aa, scalar1=thr[i],
                        scalar2=MIDW[i], op0=ALU.is_gt, op1=ALU.mult, waits=w)
        c0 = cmp(qa, 0, [t_a]); c1 = cmp(qb_, 1, [t_a])
        s01 = P.op(V, "tensor_tensor", out=qc, in0=qa, in1=qb_, op=ALU.add,
                   waits=[c0, c1])
        c2 = cmp(qa, 2, [s01]); c3 = cmp(qb_, 3, [s01])
        s23 = P.op(V, "tensor_tensor", out=qd, in0=qa, in1=qb_, op=ALU.add,
                   waits=[c2, c3])
        s03 = P.op(V, "tensor_tensor", out=qc, in0=qc, in1=qd, op=ALU.add,
                   waits=[s23])
        c4 = cmp(qa, 4, [s03]); c5 = cmp(qb_, 5, [s03])
        s45 = P.op(V, "tensor_tensor", out=qd, in0=qa, in1=qb_, op=ALU.add,
                   waits=[c4, c5])
        c6 = cmp(qa, 6, [s45])
        s46 = P.op(V, "tensor_tensor", out=qd, in0=qd, in1=qa, op=ALU.add,
                   waits=[c6])
        lvl = P.op(V, "tensor_tensor", out=qc, in0=qc, in1=qd, op=ALU.add,
                   waits=[s46])
        sg = P.op(V, "tensor_scalar", out=qb_, in0=vv, scalar1=0.0,
                  scalar2=None, op0=ALU.is_ge, waits=[lvl])
        sg2 = P.op(V, "tensor_scalar", out=qb_, in0=qb_, scalar1=2.0,
                   scalar2=-1.0, op0=ALU.mult, op1=ALU.add, waits=[sg])
        tw = P.op(V, "tensor_tensor", out=qa, in0=qc, in1=qb_, op=ALU.mult,
                  waits=[sg2])
        sbb = sbv.unsqueeze(2).broadcast_to([128, nblk, 16])
        tq = P.op(V, "tensor_tensor",
                  out=dst_ap.rearrange("p (b j) -> p b j", j=16),
                  in0=qa.rearrange("p (b j) -> p b j", j=16), in1=sbb,
                  op=ALU.mult, waits=[tw, t_sb])
        st["qprev"] = tq
        return tq

    MAIN_SCR = (qv_v.unsqueeze(0)[0:0] if False else None)
    main_scr = (SCANR[:, 0:1024], SCANR[:, 1024:2048], qABCD)
    prep_scr = (pqv, pqa2, pqABCD)
    p1_scr = (x1qv, x1qa2, x1ABCD)

    dl = lambda **kw: P.op(Q, "dma_start", **kw)
    dl(out=cst[:], in_=csts, dma="dld0")
    dl(out=eyeb[:], in_=eye_b, dma="dld0")
    dl(out=onesf[:], in_=ones1f, dma="dld0")
    dl(out=selb[:], in_=sel_in, dma="dld0")
    dl(out=A_sc[:], in_=a_log.rearrange("(c p) n -> p c n", p=128), dma="dld0")
    dl(out=D_sc[:], in_=d_vec.rearrange("(c p) -> p c", p=128), dma="dld0", allow_slow_non_contiguous=True)
    dl(out=bdt_sc[:], in_=b_dt.rearrange("(c p) -> p c", p=128), dma="dld0", allow_slow_non_contiguous=True)
    dl(out=cvb_sc[:], in_=conv_b.rearrange("(c p) -> p c", p=128), dma="dld0", allow_slow_non_contiguous=True)
    dl(out=cvw_sc[:], in_=conv_w.rearrange("(c p) k -> p c k", p=128),
       dma="dld0")
    dl(out=g_bc[0:1, :], in_=ln_g.unsqueeze(0), dma="dld0")
    ld_all = dl(out=b_bc[0:1, :], in_=ln_b.unsqueeze(0), dma="dld0")

    t_exp = P.op(S, "activation", out=A_sc[:], in_=A_sc[:], func=AF.Exp,
                 waits=[ld_all])
    t_negA = P.op(V, "tensor_scalar_mul", out=A_sc[:], in0=A_sc[:],
                  scalar1=-1.0, waits=[t_exp])
    t_negAb = P.op(V, "tensor_copy", out=A_scb[:], in_=A_sc[:], waits=[t_negA])

    brc = []
    for half in range(2):
        brc.append(P.op(T, "matmul", ps_w0[:, half * 512:(half + 1) * 512],
                        onesf[:], g_bc[0:1, half * 512:(half + 1) * 512],
                        start=True, stop=True, waits=[ld_all]))
        brc.append(P.op(T, "matmul", ps_w1[:, half * 512:(half + 1) * 512],
                        onesf[:], b_bc[0:1, half * 512:(half + 1) * 512],
                        start=True, stop=True, waits=[ld_all]))
    t_gbc = P.op(V, "tensor_copy", out=g_bc[:], in_=ps_w0[:], waits=[brc[-1]])
    t_bbc = P.op(V, "tensor_copy", out=b_bc[:], in_=ps_w1[:], waits=[brc[-1]])

    t_hc = P.op(V, "memset", hcar[:], 0.0)
    t_ct = P.op(V, "memset", ctail[:], 0.0)

    pst8 = {"lc": [None, None], "bc": [None, None], "qc": [None, None], "i": 0}

    def prep_slab(src_dram, n_kblk, quant_nblk, stores, part=128):
        i = pst8["i"] % 2
        n = n_kblk * 128
        tl = P.op(Q, "dma_start", out=pp32[i][0:part, 0:n], in_=src_dram,
                  dma=f"dld{i}", waits=[pst8["lc"][i], ld_all])
        if quant_nblk:
            tc_ = None
            for q0 in range(0, n, 1024):
                q1 = min(q0 + 1024, n)
                tc_ = quant(pp32[i][0:part, q0:q1], ppb[i][0:part, q0:q1],
                            (q1 - q0) // 16, [tl, pst8["bc"][i]], prep_scr)
        else:
            tc_ = P.op(V, "tensor_copy", out=ppb[i][0:part, 0:n],
                       in_=pp32[i][0:part, 0:n], waits=[tl, pst8["bc"][i]])
        pst8["lc"][i] = tc_
        evs = []
        lastmm = None
        for g0 in range(0, n_kblk, 4):
            gsz = min(4, n_kblk - g0)
            k = st["tp"]
            pst = ps_ts[k % 2]
            w = [tc_, pst8["qc"][i]]
            if (k - 2) in st["tp_ev"]:
                w.append(st["tp_ev"][k - 2])
            mm = None
            for j in range(gsz):
                mm = P.op(T, "transpose", pst[:, j * 128:(j + 1) * 128],
                          ppb[i][0:part, (g0 + j) * 128:(g0 + j + 1) * 128],
                          eyeb[:],
                          waits=([t for t in w if t] + [ld_all])
                          if mm is None else ())
            ev = P.op(V, "tensor_copy",
                      out=ppq[i][:, g0 * 128:(g0 + gsz) * 128],
                      in_=pst[:, 0:gsz * 128], waits=[mm])
            st["tp_ev"][k] = ev
            st["tp"] = k + 1
            evs.append(ev)
            lastmm = mm
        pst8["bc"][i] = lastmm
        stok = None
        for (dst, c0, c1) in stores:
            src = ppq[i][:, c0:c1]
            if dst.ndim == 3:
                src = src.rearrange("p (k e) -> p k e", e=128)
            stok = P.op(Q, "dma_start", out=dst, in_=src,
                        dma=f"dsp{i}", waits=evs)
        pst8["qc"][i] = stok
        pst8["i"] += 1
        return stok

    for ec in range(32):
        prep_slab(w_in[ec * 128:(ec + 1) * 128, :], NKD, 0,
                  [(winT_sc[ec].rearrange("p c e -> p (c e)"), 0, NKD * 128)])
    for oc in range(8):
        prep_slab(w_out[oc * 128:(oc + 1) * 128, :], NDC, 0,
                  [(woutT_sc.rearrange("c p f -> p c f")[:, :, oc * 128:(oc + 1) * 128], 0, NDC * 128)])
    for (w_dram, dstsc) in ((wg_in, wgT_sc), (wu_in, wuT_sc)):
        for fc in range(NKF):
            prep_slab(w_dram[fc * 128:(fc + 1) * 128, :], NKD, 64,
                      [(dstsc.rearrange("k p f -> p k f")[:, :, fc * 128:(fc + 1) * 128], 0, NKD * 128)])
    for oc in range(8):
        prep_slab(wd_in[oc * 128:(oc + 1) * 128, :], NKF, NKF * 8,
                  [(wdT_sc.rearrange("k p f -> p k f")[:, :, oc * 128:(oc + 1) * 128], 0, NKF * 128)])

    i = pst8["i"] % 2
    tl = P.op(Q, "dma_start", out=pp32[i][0:96, 0:DI], in_=w_xp,
              dma=f"dld{i}", waits=[pst8["lc"][i]])
    tcc = P.op(V, "tensor_copy", out=ppb[i][0:96, 0:DI],
               in_=pp32[i][0:96, 0:DI], waits=[tl, pst8["bc"][i]])
    t_wxp = None
    for g0 in range(0, NDC, 4):
        k = st["tp"]; pst = ps_ts[k % 2]
        w = [tcc] + ([st["tp_ev"][k - 2]] if (k - 2) in st["tp_ev"] else [])
        mm = None
        for j in range(4):
            dc = g0 + j
            mm = P.op(T, "transpose", pst[:, j * 128:j * 128 + 96],
                      ppb[i][0:96, dc * 128:(dc + 1) * 128], eyeb[0:96, 0:96],
                      waits=(w + [ld_all]) if mm is None else ())
        for j in range(4):
            dc = g0 + j
            t_wxp = P.op(V, "tensor_copy", out=wxpT[:, dc, :],
                         in_=pst[:, j * 128:j * 128 + 96], waits=[mm])
        st["tp_ev"][k] = t_wxp
        st["tp"] = k + 1
    pst8["lc"][i] = tcc
    pst8["bc"][i] = mm
    pst8["i"] += 1

    i = pst8["i"] % 2
    tl = P.op(Q, "dma_start",
              out=pp32[i][:, 0:NDC * 64].rearrange("p (c r) -> p c r", c=NDC),
              in_=w_dt.rearrange("(c p) r -> p c r", p=128),
              dma=f"dld{i}", waits=[pst8["lc"][i]])
    tcc = P.op(V, "tensor_copy", out=ppb[i][:, 0:NDC * 64],
               in_=pp32[i][:, 0:NDC * 64], waits=[tl, pst8["bc"][i]])
    t_wdt = None
    for g0 in range(0, NDC, 4):
        k = st["tp"]; pst = ps_ts[k % 2]
        w = [tcc] + ([st["tp_ev"][k - 2]] if (k - 2) in st["tp_ev"] else [])
        mm = None
        for j in range(4):
            dc = g0 + j
            mm = P.op(T, "transpose", pst[0:64, j * 128:(j + 1) * 128],
                      ppb[i][:, dc * 64:(dc + 1) * 64], eyeb[:],
                      waits=(w + [ld_all]) if mm is None else ())
        t_wdt = P.op(V, "tensor_copy",
                     out=wdtT[:, g0 * 128:(g0 + 4) * 128],
                     in_=pst[0:64, 0:512], waits=[mm])
        st["tp_ev"][k] = t_wdt
        st["tp"] = k + 1
    pst8["lc"][i] = tcc
    pst8["bc"][i] = mm
    pst8["i"] += 1

    prep_sp = [("dsp0", P.count.get("dsp0", 0)), ("dsp1", P.count.get("dsp1", 0))]
    prep_v = ("Sv", P.count["Sv"])

    prev = {}
    ffn_pa_free = [t_gbc, t_bbc]
    psw_free = [t_gbc, t_bbc]
    win_free = [None, None]
    wg_free = None; wu_free = None
    wd_free = [None, None]
    wo_free = [None, None]
    sc_prev = {}
    pending_m4 = None

    for ti in range(CFG_NT):
        t0 = ti * TC
        xw = list(prep_sp) + [prev.get("DtR_free")]
        t_x = None
        for s_ in range(SEQ):
            t_x = P.op(Q, "dma_start",
                       out=xt_v[:, 2 * s_:2 * s_ + 2, :],
                       in_=x_in.rearrange("s (a p) d -> s p a d", p=128)
                       [s_][:, 2 * ti:2 * ti + 2, :],
                       dma="dx", waits=xw if s_ == 0 else ())

        bn = []
        for j in range(4):
            bn.append(P.op(V, "bn_stats", out=bnb[:, j, 0:6],
                           in_=xt_v[:, j, 0:512],
                           waits=[t_x, prev.get("bn_free"),
                                  prep_v if ti == 0 else None]))
            bn.append(P.op(V, "bn_stats", out=bnb[:, j, 6:12],
                           in_=xt_v[:, j, 512:1024], waits=[]))
        t_mv = None
        for j in range(4):
            t_mv = P.op(V, "bn_aggr", out=mv[:, j, :], in_=bnb[:, j, :],
                        waits=[bn[-1]] if j == 0 else ())
        t_ln1 = P.op(S, "activation", out=lnt[:], in_=mv[:, :, 1], func=AF.Ln,
                     bias=col(cst, C_EPS),
                     waits=[t_mv, ld_all, prev.get("ln_free")])
        t_r = P.op(S, "activation", out=rr[:], in_=lnt[:], func=AF.Exp,
                   scale=col(cst, C_NEGH), waits=[t_ln1])
        t_ln2 = P.op(S, "activation", out=lnt[:], in_=mv[:, :, 1], func=AF.Ln,
                     scale=col(cst, C_LNS), bias=col(cst, C_LNB), waits=[t_r])
        t_rho = P.op(S, "activation", out=rho[:], in_=lnt[:], func=AF.Exp,
                     scale=col(cst, C_NEGH), waits=[t_ln2])
        t_rci = P.op(V, "reciprocal", out=lnt[:], in_=rr[:],
                     waits=[t_rho, t_mv])
        t_rat = P.op(V, "tensor_tensor", out=rat[:], in0=rho[:], in1=lnt[:],
                     op=ALU.mult, waits=[t_rci])

        t_xn = []
        t_xn2 = []
        for j in range(4):
            a = P.op(V, "tensor_scalar", out=xn_v[:, j, :], in0=xt_v[:, j, :],
                     scalar1=col(mv, j, 0), scalar2=col(rr, j),
                     op0=ALU.subtract, op1=ALU.mult,
                     waits=[t_r, prev.get("CbR_free")])
            t_xn.append(a)
            pass
            c_ = P.op(V, "tensor_scalar", out=p1s, in0=xn_v[:, j, :],
                      scalar1=col(rat, j), scalar2=None, op0=ALU.mult,
                      waits=[t_rat, a, prev.get("hqTR_free") if (ti and j == 0) else None])
            d_ = P.op(V, "tensor_tensor", out=p2s, in0=p1s, in1=g_bc[:],
                      op=ALU.mult, waits=[c_, t_gbc])
            e_ = P.op(V, "tensor_tensor", out=xn2b_v[:, j, :], in0=p2s,
                      in1=b_bc[:], op=ALU.add,
                      waits=[d_, t_bbc, prev.get("BbR_free") if j == 0 else None])
            t_xn2.append(e_)


        if pending_m4 is not None:
            t_m4p = {}
            for tcin in range(2):
                ps = (ps_w0, ps_w1)[tcin]
                ev = P.op(V, "tensor_tensor", out=acc[:, tcin, :],
                          in0=acc[:, tcin, :], in1=ps[:], op=ALU.add,
                          waits=[pending_m4["mm"][(0, tcin)]])
                t_m4p[(0, tcin)] = ev
            mm = None
            tp1mm = {}
            for dc in range(NDC):
                slot = dc % 2
                wl = P.op(G, "dma_start", out=wob[slot][:], in_=woutT_sc[dc],
                          dma=f"dwo{slot}",
                          waits=[wo_free[slot]] if wo_free[slot] else prep_sp)
                for tcin in range(2):
                    tc = 2 + tcin
                    ps = (ps_w0, ps_w1)[tcin]
                    if dc == 0:
                        wt = [wl, pending_m4["yg"], t_m4p[(0, tcin)]]
                    else:
                        wt = [wl]
                    for nh in range(2):
                        mm = P.op(T, "matmul", ps[:, nh * 512:(nh + 1) * 512],
                                  ygb_v[:, dc, tc * 128:(tc + 1) * 128],
                                  wob[slot][:, nh * 512:(nh + 1) * 512],
                                  start=(dc == 0), stop=(dc == NDC - 1),
                                  waits=wt if nh == 0 else ())
                    tp1mm[tcin] = mm
                wo_free[slot] = mm
            pending_m4["evp0"] = t_m4p
            pending_m4["tp1mm"] = tp1mm
            prev["SCANR_free"] = mm
            prev["ygb_free"] = mm

        t_xn2T = None
        for kc in range(NKD):
            t_xn2T = transpose_group(
                [(j, xn2b_v[:, j, kc * 128:(kc + 1) * 128]) for j in range(4)],
                [(xn2T_v[:, kc, :], 0, 512)],
                [t_xn2[-1], prev.get("hqTR_free") if kc == 0 else None])
        t_u = {}
        t_sz = {}
        ec_order = list(range(16, 32)) + list(range(16))
        for eci, ec in enumerate(ec_order):
            slot = eci % 2
            wl = P.op(Q, "dma_start",
                      out=winb[slot][:],
                      in_=winT_sc[ec].rearrange("p c e -> p (c e)"),
                      dma=f"dwin{slot}",
                      waits=[win_free[slot]] if win_free[slot] else prep_sp)
            bk = eci % 2
            ps = (ps_a0, ps_a1)[bk]
            wt = [wl, t_xn2T,
                  ffn_pa_free[bk] if (ti == 0 and eci < 2) else None,
                  st["paf"][bk]]
            mm = None
            for kc in range(NKD):
                mm = P.op(T, "matmul", ps[:], winb[slot][:, kc, :],
                          xn2T_v[:, kc, :], start=(kc == 0),
                          stop=(kc == NKD - 1), waits=wt if kc == 0 else ())
            if ec < 16:
                ev = P.op(S, "copy", out=u_v[:, ec, :], in_=ps[:],
                          waits=[mm, prev.get("SCANR_free")])
                t_u[ec] = ev
            else:
                ev = P.op(S, "activation", out=szb[:, ec - 16, :], in_=ps[:],
                          func=AF.Silu, waits=[mm, prev.get("szb_free")])
                t_sz[ec - 16] = ev
            st["paf"][bk] = ev
            win_free[slot] = mm

        t_xq = None
        for j in range(4):
            for hh in range(2):
                t_xq = quant(xn_v[:, j, hh * 512:(hh + 1) * 512],
                             xqb_v[:, j, hh * 512:(hh + 1) * 512], 32,
                             [t_xn[j],
                              t_xn2[-1] if (j == 0 and hh == 0) else None,
                              prev.get("hqTR_free") if (j == 0 and hh == 0) else None],
                             p1_scr, sq=True)

        if pending_m4 is not None:
            pi = pending_m4["ti"]
            t_m4p = pending_m4["evp0"]
            evs = [t_m4p[(0, 0)], t_m4p[(0, 1)]]
            for tcin in range(2):
                tc = 2 + tcin
                ps = (ps_w0, ps_w1)[tcin]
                ev = P.op(V, "tensor_tensor", out=acc[:, tc, :],
                          in0=acc[:, tc, :], in1=ps[:], op=ALU.add,
                          waits=[pending_m4["tp1mm"][tcin]])
                evs.append(ev)
                prev["psw_free%d" % tcin] = ev
                st["pw"][tcin] = ev
            for s_ in range(SEQ):
                P.op(G, "dma_start",
                     out=out_t.rearrange("s (a p) d -> s p a d", p=128)
                     [s_][:, 2 * pi:2 * pi + 2, :],
                     in_=acc[:, 2 * s_:2 * s_ + 2, :],
                     dma="dso", waits=evs if s_ == 0 else ())
            prev["acc_free"] = ("dso", P.count["dso"])
            pending_m4 = None

        t_acc0 = None
        for j in range(4):
            t_acc0 = P.op(V, "tensor_tensor", out=acc[:, j, :],
                          in0=xt_v[:, j, :], in1=xn_v[:, j, :], op=ALU.add,
                          waits=[t_xn[j], prev.get("acc_free") if j == 0 else None])

        t_xqT = None
        for kc in range(NKD):
            t_xqT = transpose_group(
                [(j, xqb_v[:, j, kc * 128:(kc + 1) * 128]) for j in range(4)],
                [(xqT_v[:, kc, :], 0, 512)],
                [t_xq, t_acc0 if kc == 0 else None,
                 t_xn[3] if kc == 0 else None])

        u4 = u_v.rearrange("p c (s t) -> p c s t", s=SEQ)
        th1 = P.op(V, "tensor_copy", out=uhead[:, :, :, 0:3], in_=ctail[:],
                   waits=[t_ct if ti == 0 else prev.get("ctail_w"),
                          prev.get("uhead_free")] + [t_u[ec] for ec in range(16)])
        th2 = P.op(V, "tensor_copy", out=uhead[:, :, :, 3:8],
                   in_=u4[:, :, :, 0:5], waits=[th1])
        t_tailw = P.op(V, "tensor_copy", out=ctail[:],
                       in_=u4[:, :, :, 253:256], waits=[th2])
        prev["ctail_w"] = t_tailw
        t_conv = {}
        for dc in range(NDC):
            ca = (ca_lo if dc < 8 else ca_hi)[:, dc % 8, :] \
                .rearrange("p (s t) -> p s t", s=SEQ)
            w0 = [t_tailw]
            if dc == 0:
                w0 += [prev.get("BbR_free2"), t_xn2[-1], t_xqT]
            if dc == 8:
                w0 += [prev.get("DtR_free2"), t_xn[-1]]
            base = None
            for k in range(4):
                srcm = u4[:, dc, :, max(0, 2 + k - 2): 253 + k]
            for k in range(4):
                srcm = u4[:, dc, :, 2 + k: 253 + k]
                if k == 0:
                    base = P.op(V, "tensor_scalar", out=ca[:, :, 5:256],
                                in0=srcm, scalar1=col(cvw_sc, dc, 0),
                                scalar2=None, op0=ALU.mult, waits=w0)
                else:
                    base = P.op(V, "scalar_tensor_tensor", out=ca[:, :, 5:256],
                                in0=srcm, scalar=col(cvw_sc, dc, k),
                                in1=ca[:, :, 5:256], op0=ALU.mult, op1=ALU.add,
                                waits=[base])
            for k in range(4):
                srch = uhead[:, dc, :, k:k + 5]
                if k == 0:
                    hh = P.op(V, "tensor_scalar", out=ca[:, :, 0:5], in0=srch,
                              scalar1=col(cvw_sc, dc, 0), scalar2=None,
                              op0=ALU.mult, waits=[base])
                else:
                    hh = P.op(V, "scalar_tensor_tensor", out=ca[:, :, 0:5],
                              in0=srch, scalar=col(cvw_sc, dc, k),
                              in1=ca[:, :, 0:5], op0=ALU.mult, op1=ALU.add,
                              waits=[hh])
            sl = P.op(S, "activation", out=ucb[:, dc, :],
                      in_=ca.rearrange("p s t -> p (s t)"), func=AF.Silu,
                      bias=col(cvb_sc, dc),
                      waits=[hh, prev.get("ucb_free") if dc == 0 else None])
            t_conv[dc] = sl
        t_convlast = t_conv[NDC - 1]
        prev["uhead_free"] = hh

        wt = [t_convlast, t_wxp, st["paf"][0]]
        mm = None
        for dc in range(NDC):
            mm = P.op(T, "matmul", ps_a0[0:64, :], wxpT[:, dc, 0:64],
                      ucb[:, dc, :], start=(dc == 0), stop=(dc == NDC - 1),
                      waits=wt if dc == 0 else ())
        t_dl = P.op(V, "tensor_copy", out=dl_b[:], in_=ps_a0[0:64, :],
                    waits=[mm])
        st["paf"][0] = t_dl
        t_bc2 = {}
        for half, dst in ((0, bcrB), (1, bcrC)):
            bk = (half + 1) % 2
            ps = (ps_a0, ps_a1)[bk]
            wt = [t_convlast, st["paf"][bk],
                  ffn_pa_free[bk] if ti == 0 else None]
            mm = None
            for dc in range(NDC):
                mm = P.op(T, "matmul", ps[0:16, :],
                          wxpT[:, dc, 64 + half * 16:80 + half * 16],
                          ucb[:, dc, :], start=(dc == 0), stop=(dc == NDC - 1),
                          waits=wt if dc == 0 else ())
            t_bc2[half] = P.op(V, "tensor_copy", out=dst[:], in_=ps[0:16, :],
                               waits=[mm])
            st["paf"][bk] = t_bc2[half]

        m3t = SCANR[:, 7680:8192]
        t_dt = {}
        m3e1 = {}

        def emit_m3(dc):
            half = dc % 2
            ps = ps_w1[:, half * 512:(half + 1) * 512]
            wt = [t_dl, t_wdt,
                  m3e1[dc - 2] if dc >= 2 else st["pw"][1]]
            mm = P.op(T, "matmul", ps, wdtT[:, dc * 128:(dc + 1) * 128],
                      dl_b[:], start=True, stop=True, waits=wt)
            e1 = P.op(S, "activation", out=m3t, in_=ps, func=AF.Exp,
                      bias=col(bdt_sc, dc),
                      waits=[mm, t_convlast if dc == 0 else t_dt[dc - 1]])
            e2 = P.op(S, "activation", out=dtb_v[:, dc, :], in_=m3t,
                      func=AF.Ln, bias=col(cst, C_ONE), waits=[e1])
            m3e1[dc] = e1
            t_dt[dc] = e2

        t_hqT = {}
        last_f2 = None
        kff = 0
        pend_tp = None

        def emit_pend_tp():
            (p_hql, p_par, p_f0, p_tc, p_nkb, p_tq, p_kff) = pend_tp
            ev = transpose_group(
                [(j, p_hql[:, j * 128:(j + 1) * 128]) for j in range(p_nkb)],
                [(hqT_v[:, (p_f0 // 128) + j, p_tc * 128:(p_tc + 1) * 128],
                  j * 128, (j + 1) * 128) for j in range(p_nkb)],
                [p_tq, t_xn2T if p_kff == 0 else None,
                 prev.get("hqTR_free2") if p_kff == 0 else None])
            st["hqlt%d" % p_par] = st["tp_ev"][st["tp"] - 1]
            for j in range(p_nkb):
                t_hqT[((p_f0 // 128) + j, p_tc)] = ev

        for fsi in range(6):
            f0 = fsi * 512
            f1 = min(f0 + 512, DFF)
            fn = f1 - f0
            wlg = P.op(Q, "dma_start", out=wgb[:, :, 0:fn],
                       in_=wgT_sc.rearrange("k p f -> p k f")[:, :, f0:f1],
                       dma="dwg",
                       waits=[wg_free] if wg_free else list(prep_sp))
            wlu = P.op(Q, "dma_start", out=wub[:, :, 0:fn],
                       in_=wuT_sc.rearrange("k p f -> p k f")[:, :, f0:f1],
                       dma="dwu",
                       waits=[wu_free] if wu_free else list(prep_sp))
            for tc in range(4):
                if kff < NDC:
                    emit_m3(kff)
                par = kff % 2
                gps = (ps_a0[:, 0:fn], ps_w0[:, 0:fn])[par]
                ups = (ps_a1[:, 0:fn], ps_w0[:, 512:512 + fn])[par]
                gkey = ("paf", 0) if par == 0 else ("pw", 0)
                ukey = ("paf", 1) if par == 0 else ("pw", 1)
                g_mm = None
                wt0 = [wlg, t_xqT, st[gkey[0]][gkey[1]],
                       psw_free[0] if (ti == 0 and kff < 2) else None]
                for kc in range(NKD):
                    g_mm = P.op(T, "matmul", gps,
                                xqT_v[:, kc, tc * 128:(tc + 1) * 128],
                                wgb[:, kc, 0:fn], start=(kc == 0),
                                stop=(kc == NKD - 1),
                                waits=wt0 if kc == 0 else ())
                u_mm = None
                wt1 = [wlu, st[ukey[0]][ukey[1]]]
                for kc in range(NKD):
                    u_mm = P.op(T, "matmul", ups,
                                xqT_v[:, kc, tc * 128:(tc + 1) * 128],
                                wub[:, kc, 0:fn], start=(kc == 0),
                                stop=(kc == NKD - 1),
                                waits=wt1 if kc == 0 else ())
                hsl_k = hslp[par]
                hql_k = hqlp[par]
                sg = P.op(S, "activation", out=hsl_k[:, 0:fn], in_=gps,
                          func=AF.Silu,
                          waits=[g_mm, st.get("hslv%d" % par),
                                 t_conv[kff] if kff < 2 else None])
                hm = P.op(V, "tensor_tensor", out=hsl_k[:, 0:fn],
                          in0=hsl_k[:, 0:fn], in1=ups, op=ALU.mult,
                          waits=[sg, u_mm])
                st[gkey[0]][gkey[1]] = hm
                st[ukey[0]][ukey[1]] = hm
                tq = quant(hsl_k[:, 0:fn], hql_k[:, 0:fn], fn // 16,
                           [hm, st.get("hqlt%d" % par),
                            prev.get("SCANR_free") if (ti and kff == 0) else None,
                            t_convlast if kff == 0 else None],
                           main_scr)
                st["hslv%d" % par] = tq
                nkb = fn // 128
                if pend_tp is not None:
                    emit_pend_tp()
                pend_tp = (hql_k, par, f0, tc, nkb, tq, kff)
                last_f2 = u_mm
                kff += 1
            wg_free = g_mm
            wu_free = u_mm
        for dc in range(kff, NDC):
            emit_m3(dc)
        emit_pend_tp()
        pend_tp = None
        st["pw"][1] = m3e1[NDC - 1]
        t_hsl_last = st["qprev"]
        prev["hsl_free"] = t_hsl_last

        t_Bb = None
        t_Cb = None
        for n in range(2 * DST):
            bk = n % 2
            ps = (ps_a0, ps_a1)[bk]
            wt = [t_bc2[0], t_bc2[1], st["paf"][bk]]
            if n == 0:
                wt += [t_xn2[-1], t_xq, st["qprev"]]
            if n == DST:
                wt += [last_f2]
            src = bcrB if n < DST else bcrC
            mm = P.op(T, "matmul", ps[:], selb[:, (n % DST) * 128:(n % DST + 1) * 128],
                      src[:], start=True, stop=True, waits=wt)
            if n < DST:
                t_Bb = P.op(V, "tensor_copy", out=Bb_v[:, n, :], in_=ps[:],
                            waits=[mm])
                st["paf"][bk] = t_Bb
            else:
                t_Cb = P.op(V, "tensor_copy", out=Cb_v[:, n - DST, :],
                            in_=ps[:], waits=[mm])
                st["paf"][bk] = t_Cb

        f3_kc = {}
        f3_last_mm = {}
        t_f3_p0 = {}

        def emit_f3_tp(tp):
            mm = None
            for kc in range(NKF):
                slot = kc % 2
                wl = P.op(Q if slot == 0 else G, "dma_start",
                          out=wdb[slot][:], in_=wdT_sc[kc],
                          dma=f"dwd{slot}",
                          waits=[wd_free[slot]] if wd_free[slot] else prep_sp)
                for tcin in range(2):
                    tc = tp * 2 + tcin
                    ps = (ps_w0, ps_w1)[tcin]
                    if kc == 0:
                        wt = [wl, t_hqT[(NKF - 1, tc)],
                              st["pw"][0], st["pw"][1], st.get("hslv1"),
                              psw_free[tcin] if (ti == 0 and tp == 0) else
                              (t_f3_p0[tcin] if tp == 1 else prev.get("psw_free%d" % tcin))]
                    else:
                        wt = [wl]
                    for nh in range(2):
                        mm = P.op(T, "matmul", ps[:, nh * 512:(nh + 1) * 512],
                                  hqT_v[:, kc, tc * 128:(tc + 1) * 128],
                                  wdb[slot][:, nh * 512:(nh + 1) * 512],
                                  start=(kc == 0), stop=(kc == NKF - 1),
                                  waits=[w for w in wt if w] if nh == 0 else ())
                    f3_last_mm[(tp, tcin)] = mm
                if tp == 1:
                    f3_kc[kc] = mm
                wd_free[slot] = mm

        emit_f3_tp(0)

        abufs = [argb, dAb]
        iters = [(dc, s_) for dc in range(NDC) for s_ in range(SEQ)]
        t_y = {}
        t_du = {}

        def emit_args(k):
            dc, s_ = iters[k]
            buf = abufs[k % 2]
            w = [t_dt[dc], t_negA if (ti == 0 and k == 0) else None,
                 sc_prev.get("tree%d" % (k % 2)),
                 sc_prev.get("exp%d" % (k % 2)),
                 sc_prev.get("ldt%d" % (k % 2))]
            if ti == 0 and k == 0:
                w.append(t_xq)
            if ti == 0 and k <= 1:
                w.append(t_hsl_last)
            if k <= 1:
                w.append(prev.get("SCANR_free"))
            te = None
            for n in range(DST):
                te = P.op(S, "activation", out=buf[:, n, :],
                          in_=dtb_v[:, dc, s_ * 256:(s_ + 1) * 256],
                          func=AF.Exp, scale=col(A_sc, dc, n),
                          waits=w if n == 0 else ())
            sc_prev["exp%d" % (k % 2)] = te
            sc_prev["argt%d" % (k % 2)] = te
            return te

        t_exp_k = {}
        t_exp_k[0] = emit_args(0)
        for k, (dc, s_) in enumerate(iters):
            buf = abufs[k % 2]
            if s_ == 0:
                t_du[dc] = P.op(V, "tensor_tensor", out=dtusl,
                                in0=dtb_v[:, dc, :], in1=ucb[:, dc, :],
                                op=ALU.mult,
                                waits=[t_dt[dc], t_conv[dc],
                                       sc_prev.get("dbu"),
                                       t_hsl_last if dc == 0 else None])
            tdbu = P.op(V, "tensor_tensor", out=dBub[:],
                        in0=dtusl[:, s_ * 256:(s_ + 1) * 256]
                        .unsqueeze(1).broadcast_to([128, DST, 256]),
                        in1=Bb_v[:, :, s_ * 256:(s_ + 1) * 256],
                        op=ALU.mult,
                        waits=[t_du[dc], t_Bb, sc_prev.get("hC"),
                               sc_prev.get("scan")])
            sc_prev["dbu"] = tdbu
            tsd = P.op(V, "tensor_tensor", out=sdt[:],
                       in0=buf[:, :, 0], in1=hcar[:, dc, :, s_],
                       op=ALU.mult,
                       waits=[t_exp_k[k], t_hc if ti == 0 else None,
                              sc_prev.get("carry")])
            tsd2 = P.op(V, "tensor_tensor", out=dBub[:, :, 0],
                        in0=dBub[:, :, 0], in1=sdt[:], op=ALU.add,
                        waits=[tsd, tdbu])
            tz = P.op(V, "memset", buf[:, :, 0:1], 0.0, waits=[tsd])
            sc_prev["ldt%d" % (k % 2)] = tz
            if k + 1 < len(iters):
                t_exp_k[k + 1] = emit_args(k + 1)
            tscan = P.op(V, "tensor_tensor_scan",
                         out=hb_v.rearrange("p n t -> p (n t)"),
                         data0=buf.rearrange("p n t -> p (n t)"),
                         data1=dBub.rearrange("p n t -> p (n t)"),
                         initial=0.0, op0=ALU.mult, op1=ALU.add,
                         waits=[tsd2, tz, sc_prev.get("hC"),
                                sc_prev.get("carry")])
            sc_prev["ldt%d" % (k % 2)] = tscan
            tcar = P.op(V, "tensor_copy", out=hcar[:, dc, :, s_],
                        in_=hb_v[:, :, 255], waits=[tscan])
            thc = P.op(V, "tensor_tensor", out=dBub[:], in0=hb_v[:],
                       in1=Cb_v[:, :, s_ * 256:(s_ + 1) * 256],
                       op=ALU.mult, waits=[tscan, t_Cb])
            if k == 0:
                for tcin in range(2):
                    ps_ = (ps_w0, ps_w1)[tcin]
                    t_f3_p0[tcin] = P.op(V, "tensor_tensor",
                                         out=acc[:, tcin, :],
                                         in0=acc[:, tcin, :], in1=ps_[:],
                                         op=ALU.add,
                                         waits=[f3_last_mm[(0, tcin)], t_acc0])
                emit_f3_tp(1)
            b8 = buf[:, 0:8, :]
            b4 = buf[:, 8:12, :]
            b2 = buf[:, 12:14, :]
            tt8_ = P.op(V, "tensor_tensor", out=b8, in0=dBub[:, 0:8, :],
                        in1=dBub[:, 8:16, :], op=ALU.add,
                        waits=[thc, tscan])
            tt4_ = P.op(V, "tensor_tensor", out=b4, in0=b8[:, 0:4, :],
                        in1=b8[:, 4:8, :], op=ALU.add, waits=[tt8_])
            tt2_ = P.op(V, "tensor_tensor", out=b2, in0=b4[:, 0:2, :],
                        in1=b4[:, 2:4, :], op=ALU.add, waits=[tt4_])
            ty = P.op(V, "tensor_tensor",
                      out=yb_v[:, dc, s_ * 256:(s_ + 1) * 256],
                      in0=b2[:, 0, :], in1=b2[:, 1, :], op=ALU.add,
                      waits=[tt2_, f3_kc[dc]])
            t_y[dc] = ty
            sc_prev["tree%d" % (k % 2)] = ty
            sc_prev["scan"] = tscan
            sc_prev["carry"] = tcar
            sc_prev["hC"] = thc
            sc_prev["tree"] = ty
            sc_prev["exp"] = t_exp_k[k]
        t_f3 = {(0, 0): t_f3_p0[0], (0, 1): t_f3_p0[1]}
        for tcin in range(2):
            tc = 2 + tcin
            ps = (ps_w0, ps_w1)[tcin]
            ev = P.op(V, "tensor_tensor", out=acc[:, tc, :],
                      in0=acc[:, tc, :], in1=ps[:], op=ALU.add,
                      waits=[f3_last_mm[(1, tcin)], t_acc0])
            t_f3[(1, tcin)] = ev

        t_yg = {}
        for dc in range(NDC):
            g1 = P.op(V, "scalar_tensor_tensor", out=ygb_v[:, dc, :],
                      in0=ucb[:, dc, :], scalar=col(D_sc, dc),
                      in1=yb_v[:, dc, :], op0=ALU.mult, op1=ALU.add,
                      waits=[t_y[dc]] + ([sc_prev["exp"], sc_prev["tree"],
                                          prev.get("SCB_m4")] if dc == 0 else []))
            g2 = P.op(V, "tensor_tensor", out=ygb_v[:, dc, :],
                      in0=ygb_v[:, dc, :], in1=szb[:, dc, :], op=ALU.mult,
                      waits=[g1, t_sz[dc]])
            t_yg[dc] = g2

        m4_last_mm = {}
        mm = None
        for dc in range(NDC):
            slot = dc % 2
            wl = P.op(Q if slot == 0 else G, "dma_start",
                      out=wob[slot][:], in_=woutT_sc[dc],
                      dma=("dwoq0" if slot == 0 else "dwo1"),
                      waits=[wo_free[slot]] if wo_free[slot] else prep_sp)
            for tcin in range(2):
                tc = tcin
                ps = (ps_w0, ps_w1)[tcin]
                if dc == 0:
                    wt = [wl, t_yg[0], t_f3[(1, tcin)]]
                else:
                    wt = [wl, t_yg[dc]]
                for nh in range(2):
                    mm = P.op(T, "matmul", ps[:, nh * 512:(nh + 1) * 512],
                              ygb_v[:, dc, tc * 128:(tc + 1) * 128],
                              wob[slot][:, nh * 512:(nh + 1) * 512],
                              start=(dc == 0), stop=(dc == NDC - 1),
                              waits=wt if nh == 0 else ())
                m4_last_mm[(0, tcin)] = mm
            wo_free[slot] = mm
        pending_m4 = {"mm": dict(m4_last_mm), "ti": ti, "yg": t_yg[NDC - 1]}

        prev["DtR_free"] = sc_prev["argt%d" % ((len(iters) - 1) % 2)]
        prev["DtR_free2"] = t_xq
        prev["BbR_free"] = sc_prev["dbu"]
        prev["BbR_free2"] = t_xq
        prev["CbR_free"] = sc_prev["hC"]
        prev["SCANR_free"] = wo_free[0]
        prev["SCB_m4"] = None
        prev["hqTR_free"] = t_yg[NDC - 1]
        prev["hqTR_free2"] = None
        prev["szb_free"] = t_yg[NDC - 1]
        prev["ucb_free"] = t_yg[NDC - 1]
        prev["bn_free"] = t_mv
        prev["ln_free"] = t_rat
        prev["uhead_free"] = prev["uhead_free"]

    if pending_m4 is not None:
        pi = pending_m4["ti"]
        m4all = []
        for tcin in range(2):
            ps = (ps_w0, ps_w1)[tcin]
            ev = P.op(V, "tensor_tensor", out=acc[:, tcin, :],
                      in0=acc[:, tcin, :], in1=ps[:], op=ALU.add,
                      waits=[pending_m4["mm"][(0, tcin)]])
            m4all.append(ev)
        mm = None
        tp1mm = {}
        for dc in range(NDC):
            slot = dc % 2
            wl = P.op(Q if slot == 0 else G, "dma_start",
                      out=wob[slot][:], in_=woutT_sc[dc],
                      dma=("dwoq0" if slot == 0 else "dwo1"),
                      waits=[wo_free[slot]] if wo_free[slot] else prep_sp)
            for tcin in range(2):
                tc = 2 + tcin
                ps = (ps_w0, ps_w1)[tcin]
                if dc == 0:
                    wt = [wl, pending_m4["yg"], m4all[tcin]]
                else:
                    wt = [wl]
                for nh in range(2):
                    mm = P.op(T, "matmul", ps[:, nh * 512:(nh + 1) * 512],
                              ygb_v[:, dc, tc * 128:(tc + 1) * 128],
                              wob[slot][:, nh * 512:(nh + 1) * 512],
                              start=(dc == 0), stop=(dc == NDC - 1),
                              waits=wt if nh == 0 else ())
                tp1mm[tcin] = mm
            wo_free[slot] = mm
        for tcin in range(2):
            tc = 2 + tcin
            ps = (ps_w0, ps_w1)[tcin]
            ev = P.op(V, "tensor_tensor", out=acc[:, tc, :],
                      in0=acc[:, tc, :], in1=ps[:], op=ALU.add,
                      waits=[tp1mm[tcin]])
            m4all.append(ev)
        for s_ in range(SEQ):
            P.op(G, "dma_start",
                 out=out_t.rearrange("s (a p) d -> s p a d", p=128)
                 [s_][:, 2 * pi:2 * pi + 2, :],
                 in_=acc[:, 2 * s_:2 * s_ + 2, :],
                 dma="dso", waits=m4all if s_ == 0 else ())
        pending_m4 = None

    P.op(G, "engine_nop", waits=[("dso", P.count.get("dso", 0))])

    for name in sorted(P.semnames):
        P.sems[name] = stack.enter_context(nc.semaphore(name))
    block = stack.enter_context(nc.Block())
    P.emit(block)
    stack.close()
    return nc


def _consts_array():
    c = np.zeros((128, 8), np.float32)
    c[:, 0] = 1.0
    c[:, 1] = EPS
    c[:, 2] = 1.0 + EPS
    c[:, 3] = EPS * EPS
    c[:, 4] = -0.5
    return c


def _sel_array():
    s = np.zeros((16, 16 * 128), ml_dtypes.bfloat16)
    for n in range(16):
        s[n, n * 128:(n + 1) * 128] = 1
    return s


def _base_inputs(inputs):
    return {
        "ln_g": inputs["L0_attn_ln_g"], "ln_b": inputs["L0_attn_ln_b"],
        "w_in": inputs["L0_attn_W_in"], "conv_w": inputs["L0_attn_conv_w"],
        "conv_b": inputs["L0_attn_conv_b"], "w_xp": inputs["L0_attn_W_xp"],
        "w_dt": inputs["L0_attn_W_dt"], "b_dt": inputs["L0_attn_b_dt"],
        "a_log": inputs["L0_attn_A_log"], "d_vec": inputs["L0_attn_D"],
        "w_out": inputs["L0_attn_W_out"], "wg": inputs["L0_ffn_w_gate"],
        "wu": inputs["L0_ffn_w_up"], "wd": inputs["L0_ffn_w_down"],
        "eye_b": np.eye(128, dtype=ml_dtypes.bfloat16),
        "ones1f": np.ones((1, 128), np.float32),
        "sel": _sel_array(),
        "csts": _consts_array(),
    }


_NC_CACHE = {}


def kernel(**inputs):
    if "prog" not in _NC_CACHE:
        _NC_CACHE["prog"] = build_program()
    nc = _NC_CACHE["prog"]
    x = np.ascontiguousarray(inputs["L0_attn_x"])
    base = _base_inputs(inputs)
    in_maps = []
    for c in range(NCORES):
        m = dict(base)
        m["x"] = x[c * SEQ:(c + 1) * SEQ]
        in_maps.append(m)
    res = run_bass_kernel_spmd(nc, in_maps, core_ids=list(range(NCORES)))
    out = np.concatenate([r["out"] for r in res.results], axis=0)
    return out.astype(np.float32)


# version 13
# speedup vs baseline: 1.3393x; 1.0116x over previous
import sys
sys.path.insert(0, "/opt/trn_rl_repo")
import numpy as np
import ml_dtypes

import concourse.bass as bass
import concourse.mybir as mybir
from concourse.bass_utils import run_bass_kernel_spmd

F32 = mybir.dt.float32
BF16 = mybir.dt.bfloat16
AF = mybir.ActivationFunctionType
ALU = mybir.AluOpType
AX = mybir.AxisListType

B, L, D = 16, 2048, 1024
NCORES = 8
SEQ = 2
DI = 2048
DST = 16
DFF = 2816
TC = 256
TOK = SEQ * TC
NT = L // TC
NDC = DI // 128
NKD = D // 128
NKF = DFF // 128
EPS = 1e-5
MID = [0.25, 0.75, 1.25, 1.75, 2.5, 3.5, 5.0]
MIDSQ = [m * m for m in MID]
MIDW = [0.5, 0.5, 0.5, 0.5, 1.0, 1.0, 2.0]

CFG_NT = NT


class Plan:
    ENGS = ("v", "s", "t", "g", "q")

    def __init__(self):
        self.ops = {e: [] for e in self.ENGS}
        self.count = {}
        self.semnames = set("S" + e for e in self.ENGS)
        self.sems = {}

    def dma_sem(self, name):
        self.semnames.add(name)

    def op(self, eng, method, *args, waits=(), dma=None, **kw):
        import traceback
        fr = traceback.extract_stack(limit=4)
        tbs = ";".join(f"{f.lineno}" for f in fr[:-1])
        w = tuple(t for t in waits if t is not None)
        if dma is not None:
            self.count[dma] = self.count.get(dma, 0) + 16
            tok = (dma, self.count[dma])
        else:
            s = "S" + eng
            self.count[s] = self.count.get(s, 0) + 1
            tok = (s, self.count[s])
        self.ops[eng].append((method, args, kw, w, tok, tbs))
        return tok

    def emit(self, block):
        plan = self
        secs = {"v": block.vector, "s": block.scalar, "t": block.tensor,
                "g": block.gpsimd, "q": block.sync}

        def make_body(e):
            own = "S" + e

            def body(engine):
                waited = {}
                for (method, args, kw, waits, tok, src) in plan.ops[e]:
                    need = {}
                    for (sname, val) in waits:
                        if val > need.get(sname, -1):
                            need[sname] = val
                    for sname in sorted(need):
                        val = need[sname]
                        if waited.get(sname, -1) >= val:
                            continue
                        engine.wait_ge(plan.sems[sname], val)
                        waited[sname] = val
                    try:
                        instr = getattr(engine, method)(*args, **kw)
                    except Exception as ex:
                        raise RuntimeError(f"emit failed: {method} planned at {src}: {ex}") from ex
                    sname, _ = tok
                    instr.then_inc(plan.sems[sname], 1 if sname == own else 16)
            return body

        for e in self.ENGS:
            if self.ops[e]:
                secs[e](make_body(e))


def col(ap, *idx):
    sl = tuple([slice(None)] + [slice(i, i + 1) for i in idx])
    return ap[sl]


def cdiv(a, b):
    return (a + b - 1) // b


def build_program():
    nc = bass.Bass("TRN2", target_bir_lowering=False)
    P = Plan()
    V, S, T, G, Q = "v", "s", "t", "g", "q"

    def din(name, shape, dt=F32):
        return nc.dram_tensor(name, shape, dt, kind="ExternalInput").ap()

    x_in = din("x", [SEQ, L, D])
    ln_g = din("ln_g", [D]); ln_b = din("ln_b", [D])
    w_in = din("w_in", [2 * DI, D])
    conv_w = din("conv_w", [DI, 4]); conv_b = din("conv_b", [DI])
    w_xp = din("w_xp", [96, DI])
    w_dt = din("w_dt", [DI, 64]); b_dt = din("b_dt", [DI])
    a_log = din("a_log", [DI, DST]); d_vec = din("d_vec", [DI])
    w_out = din("w_out", [D, DI])
    wg_in = din("wg", [DFF, D]); wu_in = din("wu", [DFF, D])
    wd_in = din("wd", [D, DFF])
    eye_b = din("eye_b", [128, 128], BF16)
    ones1f = din("ones1f", [1, 128], F32)
    sel_in = din("sel", [16, 16 * 128], BF16)
    csts = din("csts", [128, 8], F32)
    C_ONE, C_EPS, C_LNS, C_LNB, C_NEGH = 0, 1, 2, 3, 4

    out_t = nc.dram_tensor("out", [SEQ, L, D], F32, kind="ExternalOutput").ap()

    def scr(name, shape, dt=BF16):
        return nc.dram_tensor(name, shape, dt, kind="Internal").ap()

    winT_sc = scr("winT_sc", [32, 128, NKD, 128])
    woutT_sc = scr("woutT_sc", [NDC, 128, D])
    wgT_sc = scr("wgT_sc", [NKD, 128, DFF])
    wuT_sc = scr("wuT_sc", [NKD, 128, DFF])
    wdT_sc = scr("wdT_sc", [NKF, 128, D])

    from contextlib import ExitStack
    stack = ExitStack()

    def sb(name, shape, dt=F32):
        return stack.enter_context(nc.sbuf_tensor(name, shape, dt))

    def psum(name, shape, dt=F32):
        return stack.enter_context(nc.psum_tensor(name, shape, dt))

    cst = sb("cst", [128, 8])
    eyeb = sb("eyeb", [128, 128], BF16)
    onesf = sb("onesf", [1, 128])
    selb = sb("selb", [16, 16 * 128], BF16)
    A_sc = sb("A_sc", [128, NDC, DST])
    A_scb = sb("A_scb", [128, NDC, DST], BF16)
    D_sc = sb("D_sc", [128, NDC])
    bdt_sc = sb("bdt_sc", [128, NDC])
    cvb_sc = sb("cvb_sc", [128, NDC])
    cvw_sc = sb("cvw_sc", [128, NDC, 4])
    g_bc = sb("g_bc", [128, D])
    b_bc = sb("b_bc", [128, D])
    wxpT = sb("wxpT", [128, NDC, 96], BF16)
    wdtT = sb("wdtT", [64, DI], BF16)
    hcar = sb("hcar", [128, NDC, DST, SEQ])
    ctail = sb("ctail", [128, NDC, SEQ, 3])
    bnb = sb("bnb", [128, 4, 12])
    mv = sb("mv", [128, 4, 2])
    rr = sb("rr", [128, 4])
    rho = sb("rho", [128, 4])
    rat = sb("rat", [128, 4])
    lnt = sb("lnt", [128, 4])
    qm = sb("qm", [128, 64])
    qs = sb("qs", [128, 64])
    qr = sb("qr", [128, 64])
    qsb = sb("qsb", [128, 64], BF16)
    sdt = sb("sdt", [128, DST])
    uhead = sb("uhead", [128, NDC, SEQ, 8])
    uheadb = None
    dl_b = sb("dl_b", [64, TOK], BF16)
    bcrB = sb("bcrB", [16, TOK], BF16)
    bcrC = sb("bcrC", [16, TOK], BF16)

    acc = sb("acc", [128, 4, D])
    szb = sb("szb", [128, NDC, TOK], BF16)
    ucb = sb("ucb", [128, NDC, TOK], BF16)

    SCANR = sb("SCANR", [128, 8192])
    BbR = sb("BbR", [128, 8192], BF16)
    CbR = sb("CbR", [128, 8192], BF16)
    DtR = sb("DtR", [128, 8192], BF16)
    hqTR = sb("hqTR", [128, 11264], BF16)

    winb = [sb(f"winb{i}", [128, NKD, 128], BF16) for i in range(2)]
    wgb = sb("wgb", [128, NKD, 512], BF16)
    wub = sb("wub", [128, NKD, 512], BF16)
    wdb = [sb(f"wdb{i}", [128, D], BF16) for i in range(2)]
    wob = [sb(f"wob{i}", [128, D], BF16) for i in range(2)]

    SCB = SCANR[:].bitcast(BF16)
    u_v = SCANR[:].rearrange("p (c t) -> p c t", c=NDC)
    pp32 = [SCANR[:, 0:4096], SCANR[:, 4096:8192]]
    qv_v = SCANR[:, 0:1024]
    qa2_v = SCANR[:, 1024:2048]
    qABCD = [SCB[:, 4096 + i * 1024: 4096 + (i + 1) * 1024] for i in range(4)]
    argb = SCB[:, 0:4096].rearrange("p (n t) -> p n t", n=DST)
    dAb = SCB[:, 4096:8192].rearrange("p (n t) -> p n t", n=DST)
    dBub = SCB[:, 8192:12288].rearrange("p (n t) -> p n t", n=DST)
    hb_v = SCB[:, 12288:16384].rearrange("p (n t) -> p n t", n=DST)
    tr8 = SCB[:, 0:2048].rearrange("p (n t) -> p n t", n=8)
    tr4 = SCB[:, 2048:3072].rearrange("p (n t) -> p n t", n=4)
    tr2 = SCB[:, 3072:3584].rearrange("p (n t) -> p n t", n=2)
    ygb_v = SCB[:, 0:8192].rearrange("p (c t) -> p c t", c=NDC)

    BbF = BbR[:].bitcast(F32)
    xqb_v = BbR[:, 0:4096].rearrange("p (j d) -> p j d", j=4)
    xn2b_v = BbR[:, 4096:8192].rearrange("p (j d) -> p j d", j=4)
    ca_lo = BbF[:].rearrange("p (c t) -> p c t", c=8)
    Bb_v = BbR[:].rearrange("p (n t) -> p n t", n=DST)
    ppb = [BbR[:, 0:4096], BbR[:, 4096:8192]]
    hslp = [BbF[:, 0:512], BbF[:, 512:1024]]
    hsl = hslp[0]
    hqlp = [BbR[:, 2048:2560], BbR[:, 2560:3072]]

    CbF = CbR[:].bitcast(F32)
    xn_v = CbF[:].rearrange("p (j d) -> p j d", j=4)
    xqT_v = CbR[:, 0:4096].rearrange("p (k t) -> p k t", k=NKD)
    Cb_v = CbR[:].rearrange("p (n t) -> p n t", n=DST)
    ppq = [CbR[:, 0:4096], CbR[:, 4096:8192]]

    DtF = DtR[:].bitcast(F32)
    xt_v = DtF[:].rearrange("p (j d) -> p j d", j=4)
    ca_hi = DtF[:].rearrange("p (c t) -> p c t", c=8)
    dtb_v = DtR[:].rearrange("p (c t) -> p c t", c=NDC)

    hqF = hqTR[:].bitcast(F32)
    xn2T_v = hqTR[:, 0:4096].rearrange("p (k t) -> p k t", k=NKD)
    hqT_v = hqTR[:].rearrange("p (k t) -> p k t", k=NKF)
    yb_v = hqTR[:, 0:8192].rearrange("p (c t) -> p c t", c=NDC)
    dtusl = uhead[:].rearrange("p a b c -> p (a b c)").bitcast(BF16)[:, 0:512]
    pqv = hqF[:, 0:1024]
    pqa2 = hqF[:, 1024:2048]
    pqABCD = [hqTR[:, 4096 + i * 1024: 4096 + (i + 1) * 1024] for i in range(4)]
    x1qv = hqF[:, 2048:2560]
    x1qa2 = hqF[:, 2560:3072]
    x1ABCD = [hqTR[:, 6144 + i * 512: 6144 + (i + 1) * 512] for i in range(4)]
    p1s = hqF[:, 2048:3072]
    p2s = hqF[:, 3072:4096]

    ps_a0 = psum("ps_a0", [128, 512]); ps_a1 = psum("ps_a1", [128, 512])
    ps_w0 = psum("ps_w0", [128, 1024]); ps_w1 = psum("ps_w1", [128, 1024])
    ps_t0 = psum("ps_t0", [128, 1024], BF16)
    ps_t1 = psum("ps_t1", [128, 1024], BF16)
    ps_ts = [ps_t0, ps_t1]

    for s in ["dld0", "dld1", "dsp0", "dsp1", "dx", "dso", "dwin0", "dwin1",
              "dwg", "dwu", "dwd0", "dwd1", "dwo0", "dwo1", "dwoq0"]:
        P.dma_sem(s)

    st = {"tp": 0, "tp_ev": {}, "paf": {0: None, 1: None}, "pw": {0: None, 1: None}, "pab": 0, "qprev": None}

    def transpose_group(blocks, evacs, wt):
        k = st["tp"]
        pst = ps_ts[k % 2]
        w = list(wt)
        if (k - 2) in st["tp_ev"]:
            w.append(st["tp_ev"][k - 2])
        mm = None
        for (j, src) in blocks:
            mm = P.op(T, "transpose", pst[:, j * 128:(j + 1) * 128], src,
                      eyeb[:], waits=(w + [ld_all]) if mm is None else ())
        ev = None
        for (dst, c0, c1) in evacs:
            ev = P.op(S, "copy", out=dst, in_=pst[:, c0:c1], waits=[mm])
        st["tp_ev"][k] = ev
        st["tp"] = k + 1
        return ev

    def quant(src_ap, dst_ap, nblk, wtoks, scratch, sq=False):
        qvv, qaa, (qa, qb_, qc, qd) = scratch
        n = nblk * 16
        mm_ = qm[:, 0:nblk]; ss = qs[:, 0:nblk]; rs = qr[:, 0:nblk]
        sbv = qsb[:, 0:nblk]
        vv = qvv[:, 0:n]; aa = qaa[:, 0:n]
        qa = qa[:, 0:n]; qb_ = qb_[:, 0:n]; qc = qc[:, 0:n]; qd = qd[:, 0:n]
        w0 = list(wtoks) + ([st["qprev"]] if st["qprev"] else [])
        src3 = src_ap.rearrange("p (b j) -> p b j", j=16)
        t_m = P.op(V, "tensor_reduce", out=mm_, in_=src3, axis=AX.X, op=ALU.max,
                   apply_absolute_value=True, waits=w0)
        t_s = P.op(V, "tensor_scalar", out=ss, in0=mm_, scalar1=1.0 / 6.0,
                   scalar2=1e-12, op0=ALU.mult, op1=ALU.max, waits=[t_m])
        t_r = P.op(V, "reciprocal", out=rs, in_=ss, waits=[t_s])
        t_sb = P.op(V, "tensor_copy", out=sbv, in_=ss, waits=[t_s])
        rb = rs.unsqueeze(2).broadcast_to([128, nblk, 16])
        t_v = P.op(V, "tensor_tensor", out=vv.rearrange("p (b j) -> p b j", j=16),
                   in0=src3, in1=rb, op=ALU.mult, waits=[t_r])
        if sq:
            t_a = P.op(V, "tensor_tensor", out=aa, in0=vv, in1=vv,
                       op=ALU.mult, waits=[t_v])
        else:
            t_a = P.op(S, "activation", out=aa, in_=vv, func=AF.Abs,
                       waits=[t_v])
        thr = MIDSQ if sq else MID
        def cmp(dst, i, w):
            return P.op(V, "tensor_scalar", out=dst, in0=aa, scalar1=thr[i],
                        scalar2=MIDW[i], op0=ALU.is_gt, op1=ALU.mult, waits=w)
        c0 = cmp(qa, 0, [t_a]); c1 = cmp(qb_, 1, [t_a])
        s01 = P.op(V, "tensor_tensor", out=qc, in0=qa, in1=qb_, op=ALU.add,
                   waits=[c0, c1])
        c2 = cmp(qa, 2, [s01]); c3 = cmp(qb_, 3, [s01])
        s23 = P.op(V, "tensor_tensor", out=qd, in0=qa, in1=qb_, op=ALU.add,
                   waits=[c2, c3])
        s03 = P.op(V, "tensor_tensor", out=qc, in0=qc, in1=qd, op=ALU.add,
                   waits=[s23])
        c4 = cmp(qa, 4, [s03]); c5 = cmp(qb_, 5, [s03])
        s45 = P.op(V, "tensor_tensor", out=qd, in0=qa, in1=qb_, op=ALU.add,
                   waits=[c4, c5])
        c6 = cmp(qa, 6, [s45])
        s46 = P.op(V, "tensor_tensor", out=qd, in0=qd, in1=qa, op=ALU.add,
                   waits=[c6])
        lvl = P.op(V, "tensor_tensor", out=qc, in0=qc, in1=qd, op=ALU.add,
                   waits=[s46])
        sg = P.op(V, "tensor_scalar", out=qb_, in0=vv, scalar1=0.0,
                  scalar2=None, op0=ALU.is_ge, waits=[lvl])
        sg2 = P.op(V, "tensor_scalar", out=qb_, in0=qb_, scalar1=2.0,
                   scalar2=-1.0, op0=ALU.mult, op1=ALU.add, waits=[sg])
        tw = P.op(V, "tensor_tensor", out=qa, in0=qc, in1=qb_, op=ALU.mult,
                  waits=[sg2])
        sbb = sbv.unsqueeze(2).broadcast_to([128, nblk, 16])
        tq = P.op(V, "tensor_tensor",
                  out=dst_ap.rearrange("p (b j) -> p b j", j=16),
                  in0=qa.rearrange("p (b j) -> p b j", j=16), in1=sbb,
                  op=ALU.mult, waits=[tw, t_sb])
        st["qprev"] = tq
        return tq

    MAIN_SCR = (qv_v.unsqueeze(0)[0:0] if False else None)
    main_scr = (SCANR[:, 0:1024], SCANR[:, 1024:2048], qABCD)
    prep_scr = (pqv, pqa2, pqABCD)
    p1_scr = (x1qv, x1qa2, x1ABCD)

    dl = lambda **kw: P.op(Q, "dma_start", **kw)
    dl(out=cst[:], in_=csts, dma="dld0")
    dl(out=eyeb[:], in_=eye_b, dma="dld0")
    dl(out=onesf[:], in_=ones1f, dma="dld0")
    dl(out=selb[:], in_=sel_in, dma="dld0")
    dl(out=A_sc[:], in_=a_log.rearrange("(c p) n -> p c n", p=128), dma="dld0")
    dl(out=D_sc[:], in_=d_vec.rearrange("(c p) -> p c", p=128), dma="dld0", allow_slow_non_contiguous=True)
    dl(out=bdt_sc[:], in_=b_dt.rearrange("(c p) -> p c", p=128), dma="dld0", allow_slow_non_contiguous=True)
    dl(out=cvb_sc[:], in_=conv_b.rearrange("(c p) -> p c", p=128), dma="dld0", allow_slow_non_contiguous=True)
    dl(out=cvw_sc[:], in_=conv_w.rearrange("(c p) k -> p c k", p=128),
       dma="dld0")
    dl(out=g_bc[0:1, :], in_=ln_g.unsqueeze(0), dma="dld0")
    ld_all = dl(out=b_bc[0:1, :], in_=ln_b.unsqueeze(0), dma="dld0")

    t_exp = P.op(S, "activation", out=A_sc[:], in_=A_sc[:], func=AF.Exp,
                 waits=[ld_all])
    t_negA = P.op(V, "tensor_scalar_mul", out=A_sc[:], in0=A_sc[:],
                  scalar1=-1.0, waits=[t_exp])
    t_negAb = P.op(V, "tensor_copy", out=A_scb[:], in_=A_sc[:], waits=[t_negA])

    brc = []
    for half in range(2):
        brc.append(P.op(T, "matmul", ps_w0[:, half * 512:(half + 1) * 512],
                        onesf[:], g_bc[0:1, half * 512:(half + 1) * 512],
                        start=True, stop=True, waits=[ld_all]))
        brc.append(P.op(T, "matmul", ps_w1[:, half * 512:(half + 1) * 512],
                        onesf[:], b_bc[0:1, half * 512:(half + 1) * 512],
                        start=True, stop=True, waits=[ld_all]))
    t_gbc = P.op(V, "tensor_copy", out=g_bc[:], in_=ps_w0[:], waits=[brc[-1]])
    t_bbc = P.op(V, "tensor_copy", out=b_bc[:], in_=ps_w1[:], waits=[brc[-1]])

    t_hc = P.op(V, "memset", hcar[:], 0.0)
    t_ct = P.op(V, "memset", ctail[:], 0.0)

    pst8 = {"lc": [None, None], "bc": [None, None], "qc": [None, None], "i": 0}

    def prep_slab(src_dram, n_kblk, quant_nblk, stores, part=128):
        i = pst8["i"] % 2
        n = n_kblk * 128
        tl = P.op(Q if i == 0 else G, "dma_start",
                  out=pp32[i][0:part, 0:n], in_=src_dram,
                  dma=f"dld{i}", waits=[pst8["lc"][i], ld_all])
        if quant_nblk:
            tc_ = None
            for q0 in range(0, n, 1024):
                q1 = min(q0 + 1024, n)
                tc_ = quant(pp32[i][0:part, q0:q1], ppb[i][0:part, q0:q1],
                            (q1 - q0) // 16, [tl, pst8["bc"][i]], prep_scr)
        else:
            tc_ = P.op(V, "tensor_copy", out=ppb[i][0:part, 0:n],
                       in_=pp32[i][0:part, 0:n], waits=[tl, pst8["bc"][i]])
        pst8["lc"][i] = tc_
        evs = []
        lastmm = None
        for g0 in range(0, n_kblk, 4):
            gsz = min(4, n_kblk - g0)
            k = st["tp"]
            pst = ps_ts[k % 2]
            w = [tc_, pst8["qc"][i]]
            if (k - 2) in st["tp_ev"]:
                w.append(st["tp_ev"][k - 2])
            mm = None
            for j in range(gsz):
                mm = P.op(T, "transpose", pst[:, j * 128:(j + 1) * 128],
                          ppb[i][0:part, (g0 + j) * 128:(g0 + j + 1) * 128],
                          eyeb[:],
                          waits=([t for t in w if t] + [ld_all])
                          if mm is None else ())
            ev = P.op(S, "copy",
                      out=ppq[i][:, g0 * 128:(g0 + gsz) * 128],
                      in_=pst[:, 0:gsz * 128], waits=[mm])
            st["tp_ev"][k] = ev
            st["tp"] = k + 1
            evs.append(ev)
            lastmm = mm
        pst8["bc"][i] = lastmm
        stok = None
        for (dst, c0, c1) in stores:
            src = ppq[i][:, c0:c1]
            if dst.ndim == 3:
                src = src.rearrange("p (k e) -> p k e", e=128)
            stok = P.op(Q, "dma_start", out=dst, in_=src,
                        dma=f"dsp{i}", waits=evs)
        pst8["qc"][i] = stok
        pst8["i"] += 1
        return stok

    for ec in range(32):
        prep_slab(w_in[ec * 128:(ec + 1) * 128, :], NKD, 0,
                  [(winT_sc[ec].rearrange("p c e -> p (c e)"), 0, NKD * 128)])
    for oc in range(8):
        prep_slab(w_out[oc * 128:(oc + 1) * 128, :], NDC, 0,
                  [(woutT_sc.rearrange("c p f -> p c f")[:, :, oc * 128:(oc + 1) * 128], 0, NDC * 128)])
    for (w_dram, dstsc) in ((wg_in, wgT_sc), (wu_in, wuT_sc)):
        for fc in range(NKF):
            prep_slab(w_dram[fc * 128:(fc + 1) * 128, :], NKD, 64,
                      [(dstsc.rearrange("k p f -> p k f")[:, :, fc * 128:(fc + 1) * 128], 0, NKD * 128)])
    for oc in range(8):
        prep_slab(wd_in[oc * 128:(oc + 1) * 128, :], NKF, NKF * 8,
                  [(wdT_sc.rearrange("k p f -> p k f")[:, :, oc * 128:(oc + 1) * 128], 0, NKF * 128)])

    i = pst8["i"] % 2
    tl = P.op(Q if i == 0 else G, "dma_start",
              out=pp32[i][0:96, 0:DI], in_=w_xp,
              dma=f"dld{i}", waits=[pst8["lc"][i]])
    tcc = P.op(V, "tensor_copy", out=ppb[i][0:96, 0:DI],
               in_=pp32[i][0:96, 0:DI], waits=[tl, pst8["bc"][i]])
    t_wxp = None
    for g0 in range(0, NDC, 4):
        k = st["tp"]; pst = ps_ts[k % 2]
        w = [tcc] + ([st["tp_ev"][k - 2]] if (k - 2) in st["tp_ev"] else [])
        mm = None
        for j in range(4):
            dc = g0 + j
            mm = P.op(T, "transpose", pst[:, j * 128:j * 128 + 96],
                      ppb[i][0:96, dc * 128:(dc + 1) * 128], eyeb[0:96, 0:96],
                      waits=(w + [ld_all]) if mm is None else ())
        for j in range(4):
            dc = g0 + j
            t_wxp = P.op(V, "tensor_copy", out=wxpT[:, dc, :],
                         in_=pst[:, j * 128:j * 128 + 96], waits=[mm])
        st["tp_ev"][k] = t_wxp
        st["tp"] = k + 1
    pst8["lc"][i] = tcc
    pst8["bc"][i] = mm
    pst8["i"] += 1

    i = pst8["i"] % 2
    tl = P.op(Q if i == 0 else G, "dma_start",
              out=pp32[i][:, 0:NDC * 64].rearrange("p (c r) -> p c r", c=NDC),
              in_=w_dt.rearrange("(c p) r -> p c r", p=128),
              dma=f"dld{i}", waits=[pst8["lc"][i]])
    tcc = P.op(V, "tensor_copy", out=ppb[i][:, 0:NDC * 64],
               in_=pp32[i][:, 0:NDC * 64], waits=[tl, pst8["bc"][i]])
    t_wdt = None
    for g0 in range(0, NDC, 4):
        k = st["tp"]; pst = ps_ts[k % 2]
        w = [tcc] + ([st["tp_ev"][k - 2]] if (k - 2) in st["tp_ev"] else [])
        mm = None
        for j in range(4):
            dc = g0 + j
            mm = P.op(T, "transpose", pst[0:64, j * 128:(j + 1) * 128],
                      ppb[i][:, dc * 64:(dc + 1) * 64], eyeb[:],
                      waits=(w + [ld_all]) if mm is None else ())
        t_wdt = P.op(V, "tensor_copy",
                     out=wdtT[:, g0 * 128:(g0 + 4) * 128],
                     in_=pst[0:64, 0:512], waits=[mm])
        st["tp_ev"][k] = t_wdt
        st["tp"] = k + 1
    pst8["lc"][i] = tcc
    pst8["bc"][i] = mm
    pst8["i"] += 1

    prep_sp = [("dsp0", P.count.get("dsp0", 0)), ("dsp1", P.count.get("dsp1", 0))]
    prep_v = ("Sv", P.count["Sv"])
    prep_s = ("Ss", P.count.get("Ss", 0))

    prev = {}
    ffn_pa_free = [t_gbc, t_bbc]
    psw_free = [t_gbc, t_bbc]
    win_free = [None, None]
    wg_free = None; wu_free = None
    wd_free = [None, None]
    wo_free = [None, None]
    sc_prev = {}
    pending_m4 = None

    for ti in range(CFG_NT):
        t0 = ti * TC
        xw = list(prep_sp) + [prev.get("DtR_free")]
        t_x = None
        for s_ in range(SEQ):
            t_x = P.op(Q, "dma_start",
                       out=xt_v[:, 2 * s_:2 * s_ + 2, :],
                       in_=x_in.rearrange("s (a p) d -> s p a d", p=128)
                       [s_][:, 2 * ti:2 * ti + 2, :],
                       dma="dx", waits=xw if s_ == 0 else ())

        bn = []
        for j in range(4):
            bn.append(P.op(V, "bn_stats", out=bnb[:, j, 0:6],
                           in_=xt_v[:, j, 0:512],
                           waits=[t_x, prev.get("bn_free"),
                                  prep_v if ti == 0 else None,
                                  prep_s if ti == 0 else None]))
            bn.append(P.op(V, "bn_stats", out=bnb[:, j, 6:12],
                           in_=xt_v[:, j, 512:1024], waits=[]))
        t_mv = None
        for j in range(4):
            t_mv = P.op(V, "bn_aggr", out=mv[:, j, :], in_=bnb[:, j, :],
                        waits=[bn[-1]] if j == 0 else ())
        t_ln1 = P.op(S, "activation", out=lnt[:], in_=mv[:, :, 1], func=AF.Ln,
                     bias=col(cst, C_EPS),
                     waits=[t_mv, ld_all, prev.get("ln_free")])
        t_r = P.op(S, "activation", out=rr[:], in_=lnt[:], func=AF.Exp,
                   scale=col(cst, C_NEGH), waits=[t_ln1])
        t_ln2 = P.op(S, "activation", out=lnt[:], in_=mv[:, :, 1], func=AF.Ln,
                     scale=col(cst, C_LNS), bias=col(cst, C_LNB), waits=[t_r])
        t_rho = P.op(S, "activation", out=rho[:], in_=lnt[:], func=AF.Exp,
                     scale=col(cst, C_NEGH), waits=[t_ln2])
        t_rci = P.op(V, "reciprocal", out=lnt[:], in_=rr[:],
                     waits=[t_rho, t_mv])
        t_rat = P.op(V, "tensor_tensor", out=rat[:], in0=rho[:], in1=lnt[:],
                     op=ALU.mult, waits=[t_rci])

        t_xn = []
        t_xn2 = []
        for j in range(4):
            a = P.op(V, "tensor_scalar", out=xn_v[:, j, :], in0=xt_v[:, j, :],
                     scalar1=col(mv, j, 0), scalar2=col(rr, j),
                     op0=ALU.subtract, op1=ALU.mult,
                     waits=[t_r, prev.get("CbR_free"),
                            prep_s if (ti == 0 and j == 0) else None])
            t_xn.append(a)
            pass
            c_ = P.op(V, "tensor_scalar", out=p1s, in0=xn_v[:, j, :],
                      scalar1=col(rat, j), scalar2=None, op0=ALU.mult,
                      waits=[t_rat, a, prev.get("hqTR_free") if (ti and j == 0) else None])
            d_ = P.op(V, "tensor_tensor", out=p2s, in0=p1s, in1=g_bc[:],
                      op=ALU.mult, waits=[c_, t_gbc])
            e_ = P.op(V, "tensor_tensor", out=xn2b_v[:, j, :], in0=p2s,
                      in1=b_bc[:], op=ALU.add,
                      waits=[d_, t_bbc, prev.get("BbR_free") if j == 0 else None])
            t_xn2.append(e_)


        if pending_m4 is not None:
            t_m4p = {}
            for tcin in range(2):
                ps = (ps_w0, ps_w1)[tcin]
                ev = P.op(V, "tensor_tensor", out=acc[:, tcin, :],
                          in0=acc[:, tcin, :], in1=ps[:], op=ALU.add,
                          waits=[pending_m4["mm"][(0, tcin)]])
                t_m4p[(0, tcin)] = ev
            mm = None
            tp1mm = {}
            for dc in range(NDC):
                slot = dc % 2
                wl = P.op(G, "dma_start", out=wob[slot][:], in_=woutT_sc[dc],
                          dma=f"dwo{slot}",
                          waits=[wo_free[slot]] if wo_free[slot] else prep_sp)
                for tcin in range(2):
                    tc = 2 + tcin
                    ps = (ps_w0, ps_w1)[tcin]
                    if dc == 0:
                        wt = [wl, pending_m4["yg"], t_m4p[(0, tcin)]]
                    else:
                        wt = [wl]
                    for nh in range(2):
                        mm = P.op(T, "matmul", ps[:, nh * 512:(nh + 1) * 512],
                                  ygb_v[:, dc, tc * 128:(tc + 1) * 128],
                                  wob[slot][:, nh * 512:(nh + 1) * 512],
                                  start=(dc == 0), stop=(dc == NDC - 1),
                                  waits=wt if nh == 0 else ())
                    tp1mm[tcin] = mm
                wo_free[slot] = mm
            pending_m4["evp0"] = t_m4p
            pending_m4["tp1mm"] = tp1mm
            prev["SCANR_free"] = mm
            prev["ygb_free"] = mm

        t_xn2T = None
        for kc in range(NKD):
            t_xn2T = transpose_group(
                [(j, xn2b_v[:, j, kc * 128:(kc + 1) * 128]) for j in range(4)],
                [(xn2T_v[:, kc, :], 0, 512)],
                [t_xn2[-1], prev.get("hqTR_free") if kc == 0 else None])
        t_u = {}
        t_sz = {}
        ec_order = list(range(16, 32)) + list(range(16))
        for eci, ec in enumerate(ec_order):
            slot = eci % 2
            wl = P.op(Q, "dma_start",
                      out=winb[slot][:],
                      in_=winT_sc[ec].rearrange("p c e -> p (c e)"),
                      dma=f"dwin{slot}",
                      waits=[win_free[slot]] if win_free[slot] else prep_sp)
            bk = eci % 2
            ps = (ps_a0, ps_a1)[bk]
            wt = [wl, t_xn2T,
                  ffn_pa_free[bk] if (ti == 0 and eci < 2) else None,
                  st["paf"][bk]]
            mm = None
            for kc in range(NKD):
                mm = P.op(T, "matmul", ps[:], winb[slot][:, kc, :],
                          xn2T_v[:, kc, :], start=(kc == 0),
                          stop=(kc == NKD - 1), waits=wt if kc == 0 else ())
            if ec < 16:
                ev = P.op(S, "copy", out=u_v[:, ec, :], in_=ps[:],
                          waits=[mm, prev.get("SCANR_free")])
                t_u[ec] = ev
            else:
                ev = P.op(S, "activation", out=szb[:, ec - 16, :], in_=ps[:],
                          func=AF.Silu, waits=[mm, prev.get("szb_free")])
                t_sz[ec - 16] = ev
            st["paf"][bk] = ev
            win_free[slot] = mm

        t_xq = None
        for j in range(4):
            for hh in range(2):
                t_xq = quant(xn_v[:, j, hh * 512:(hh + 1) * 512],
                             xqb_v[:, j, hh * 512:(hh + 1) * 512], 32,
                             [t_xn[j],
                              t_xn2[-1] if (j == 0 and hh == 0) else None,
                              prev.get("hqTR_free") if (j == 0 and hh == 0) else None],
                             p1_scr, sq=True)

        if pending_m4 is not None:
            pi = pending_m4["ti"]
            t_m4p = pending_m4["evp0"]
            evs = [t_m4p[(0, 0)], t_m4p[(0, 1)]]
            for tcin in range(2):
                tc = 2 + tcin
                ps = (ps_w0, ps_w1)[tcin]
                ev = P.op(V, "tensor_tensor", out=acc[:, tc, :],
                          in0=acc[:, tc, :], in1=ps[:], op=ALU.add,
                          waits=[pending_m4["tp1mm"][tcin]])
                evs.append(ev)
                prev["psw_free%d" % tcin] = ev
                st["pw"][tcin] = ev
            for s_ in range(SEQ):
                P.op(G, "dma_start",
                     out=out_t.rearrange("s (a p) d -> s p a d", p=128)
                     [s_][:, 2 * pi:2 * pi + 2, :],
                     in_=acc[:, 2 * s_:2 * s_ + 2, :],
                     dma="dso", waits=evs if s_ == 0 else ())
            prev["acc_free"] = ("dso", P.count["dso"])
            pending_m4 = None

        t_acc0 = None
        for j in range(4):
            t_acc0 = P.op(V, "tensor_tensor", out=acc[:, j, :],
                          in0=xt_v[:, j, :], in1=xn_v[:, j, :], op=ALU.add,
                          waits=[t_xn[j], prev.get("acc_free") if j == 0 else None])

        t_xqT = None
        for kc in range(NKD):
            t_xqT = transpose_group(
                [(j, xqb_v[:, j, kc * 128:(kc + 1) * 128]) for j in range(4)],
                [(xqT_v[:, kc, :], 0, 512)],
                [t_xq, t_acc0 if kc == 0 else None,
                 t_xn[3] if kc == 0 else None])

        u4 = u_v.rearrange("p c (s t) -> p c s t", s=SEQ)
        th1 = P.op(V, "tensor_copy", out=uhead[:, :, :, 0:3], in_=ctail[:],
                   waits=[t_ct if ti == 0 else prev.get("ctail_w"),
                          prev.get("uhead_free")] + [t_u[ec] for ec in range(16)])
        th2 = P.op(V, "tensor_copy", out=uhead[:, :, :, 3:8],
                   in_=u4[:, :, :, 0:5], waits=[th1])
        t_tailw = P.op(V, "tensor_copy", out=ctail[:],
                       in_=u4[:, :, :, 253:256], waits=[th2])
        prev["ctail_w"] = t_tailw
        t_conv = {}
        for dc in range(NDC):
            ca = (ca_lo if dc < 8 else ca_hi)[:, dc % 8, :] \
                .rearrange("p (s t) -> p s t", s=SEQ)
            w0 = [t_tailw]
            if dc == 0:
                w0 += [prev.get("BbR_free2"), t_xn2[-1], t_xqT]
            if dc == 8:
                w0 += [prev.get("DtR_free2"), t_xn[-1]]
            base = None
            for k in range(4):
                srcm = u4[:, dc, :, max(0, 2 + k - 2): 253 + k]
            for k in range(4):
                srcm = u4[:, dc, :, 2 + k: 253 + k]
                if k == 0:
                    base = P.op(V, "tensor_scalar", out=ca[:, :, 5:256],
                                in0=srcm, scalar1=col(cvw_sc, dc, 0),
                                scalar2=None, op0=ALU.mult, waits=w0)
                else:
                    base = P.op(V, "scalar_tensor_tensor", out=ca[:, :, 5:256],
                                in0=srcm, scalar=col(cvw_sc, dc, k),
                                in1=ca[:, :, 5:256], op0=ALU.mult, op1=ALU.add,
                                waits=[base])
            for k in range(4):
                srch = uhead[:, dc, :, k:k + 5]
                if k == 0:
                    hh = P.op(V, "tensor_scalar", out=ca[:, :, 0:5], in0=srch,
                              scalar1=col(cvw_sc, dc, 0), scalar2=None,
                              op0=ALU.mult, waits=[base])
                else:
                    hh = P.op(V, "scalar_tensor_tensor", out=ca[:, :, 0:5],
                              in0=srch, scalar=col(cvw_sc, dc, k),
                              in1=ca[:, :, 0:5], op0=ALU.mult, op1=ALU.add,
                              waits=[hh])
            sl = P.op(S, "activation", out=ucb[:, dc, :],
                      in_=ca.rearrange("p s t -> p (s t)"), func=AF.Silu,
                      bias=col(cvb_sc, dc),
                      waits=[hh, prev.get("ucb_free") if dc == 0 else None])
            t_conv[dc] = sl
        t_convlast = t_conv[NDC - 1]
        prev["uhead_free"] = hh

        wt = [t_convlast, t_wxp, st["paf"][0]]
        mm = None
        for dc in range(NDC):
            mm = P.op(T, "matmul", ps_a0[0:64, :], wxpT[:, dc, 0:64],
                      ucb[:, dc, :], start=(dc == 0), stop=(dc == NDC - 1),
                      waits=wt if dc == 0 else ())
        t_dl = P.op(V, "tensor_copy", out=dl_b[:], in_=ps_a0[0:64, :],
                    waits=[mm])
        st["paf"][0] = t_dl
        t_bc2 = {}
        for half, dst in ((0, bcrB), (1, bcrC)):
            bk = (half + 1) % 2
            ps = (ps_a0, ps_a1)[bk]
            wt = [t_convlast, st["paf"][bk],
                  ffn_pa_free[bk] if ti == 0 else None]
            mm = None
            for dc in range(NDC):
                mm = P.op(T, "matmul", ps[0:16, :],
                          wxpT[:, dc, 64 + half * 16:80 + half * 16],
                          ucb[:, dc, :], start=(dc == 0), stop=(dc == NDC - 1),
                          waits=wt if dc == 0 else ())
            t_bc2[half] = P.op(V, "tensor_copy", out=dst[:], in_=ps[0:16, :],
                               waits=[mm])
            st["paf"][bk] = t_bc2[half]

        m3t = SCANR[:, 7680:8192]
        t_dt = {}
        m3e1 = {}

        def emit_m3(dc):
            half = dc % 2
            ps = ps_w1[:, half * 512:(half + 1) * 512]
            wt = [t_dl, t_wdt,
                  m3e1[dc - 2] if dc >= 2 else st["pw"][1]]
            mm = P.op(T, "matmul", ps, wdtT[:, dc * 128:(dc + 1) * 128],
                      dl_b[:], start=True, stop=True, waits=wt)
            e1 = P.op(S, "activation", out=m3t, in_=ps, func=AF.Exp,
                      bias=col(bdt_sc, dc),
                      waits=[mm, t_convlast if dc == 0 else t_dt[dc - 1]])
            e2 = P.op(S, "activation", out=dtb_v[:, dc, :], in_=m3t,
                      func=AF.Ln, bias=col(cst, C_ONE), waits=[e1])
            m3e1[dc] = e1
            t_dt[dc] = e2

        t_hqT = {}
        last_f2 = None
        kff = 0
        pend_tp = None

        def emit_pend_tp():
            (p_hql, p_par, p_f0, p_tc, p_nkb, p_tq, p_kff) = pend_tp
            ev = transpose_group(
                [(j, p_hql[:, j * 128:(j + 1) * 128]) for j in range(p_nkb)],
                [(hqT_v[:, (p_f0 // 128) + j, p_tc * 128:(p_tc + 1) * 128],
                  j * 128, (j + 1) * 128) for j in range(p_nkb)],
                [p_tq, t_xn2T if p_kff == 0 else None,
                 prev.get("hqTR_free2") if p_kff == 0 else None])
            st["hqlt%d" % p_par] = st["tp_ev"][st["tp"] - 1]
            for j in range(p_nkb):
                t_hqT[((p_f0 // 128) + j, p_tc)] = ev

        for fsi in range(6):
            f0 = fsi * 512
            f1 = min(f0 + 512, DFF)
            fn = f1 - f0
            wlg = P.op(Q, "dma_start", out=wgb[:, :, 0:fn],
                       in_=wgT_sc.rearrange("k p f -> p k f")[:, :, f0:f1],
                       dma="dwg",
                       waits=[wg_free] if wg_free else list(prep_sp))
            wlu = P.op(Q, "dma_start", out=wub[:, :, 0:fn],
                       in_=wuT_sc.rearrange("k p f -> p k f")[:, :, f0:f1],
                       dma="dwu",
                       waits=[wu_free] if wu_free else list(prep_sp))
            for tc in range(4):
                if kff < NDC:
                    emit_m3(kff)
                par = kff % 2
                gps = (ps_a0[:, 0:fn], ps_w0[:, 0:fn])[par]
                ups = (ps_a1[:, 0:fn], ps_w0[:, 512:512 + fn])[par]
                gkey = ("paf", 0) if par == 0 else ("pw", 0)
                ukey = ("paf", 1) if par == 0 else ("pw", 1)
                g_mm = None
                wt0 = [wlg, t_xqT, st[gkey[0]][gkey[1]],
                       psw_free[0] if (ti == 0 and kff < 2) else None]
                for kc in range(NKD):
                    g_mm = P.op(T, "matmul", gps,
                                xqT_v[:, kc, tc * 128:(tc + 1) * 128],
                                wgb[:, kc, 0:fn], start=(kc == 0),
                                stop=(kc == NKD - 1),
                                waits=wt0 if kc == 0 else ())
                u_mm = None
                wt1 = [wlu, st[ukey[0]][ukey[1]]]
                for kc in range(NKD):
                    u_mm = P.op(T, "matmul", ups,
                                xqT_v[:, kc, tc * 128:(tc + 1) * 128],
                                wub[:, kc, 0:fn], start=(kc == 0),
                                stop=(kc == NKD - 1),
                                waits=wt1 if kc == 0 else ())
                hsl_k = hslp[par]
                hql_k = hqlp[par]
                sg = P.op(S, "activation", out=hsl_k[:, 0:fn], in_=gps,
                          func=AF.Silu,
                          waits=[g_mm, st.get("hslv%d" % par),
                                 t_conv[kff] if kff < 2 else None])
                hm = P.op(V, "tensor_tensor", out=hsl_k[:, 0:fn],
                          in0=hsl_k[:, 0:fn], in1=ups, op=ALU.mult,
                          waits=[sg, u_mm])
                st[gkey[0]][gkey[1]] = hm
                st[ukey[0]][ukey[1]] = hm
                tq = quant(hsl_k[:, 0:fn], hql_k[:, 0:fn], fn // 16,
                           [hm, st.get("hqlt%d" % par),
                            prev.get("SCANR_free") if (ti and kff == 0) else None,
                            t_convlast if kff == 0 else None],
                           main_scr)
                st["hslv%d" % par] = tq
                nkb = fn // 128
                if pend_tp is not None:
                    emit_pend_tp()
                pend_tp = (hql_k, par, f0, tc, nkb, tq, kff)
                last_f2 = u_mm
                kff += 1
            wg_free = g_mm
            wu_free = u_mm
        for dc in range(kff, NDC):
            emit_m3(dc)
        emit_pend_tp()
        pend_tp = None
        st["pw"][1] = m3e1[NDC - 1]
        t_hsl_last = st["qprev"]
        prev["hsl_free"] = t_hsl_last

        t_Bb = None
        t_Cb = None
        for n in range(2 * DST):
            bk = n % 2
            ps = (ps_a0, ps_a1)[bk]
            wt = [t_bc2[0], t_bc2[1], st["paf"][bk]]
            if n == 0:
                wt += [t_xn2[-1], t_xq, st["qprev"]]
            if n == DST:
                wt += [last_f2]
            src = bcrB if n < DST else bcrC
            mm = P.op(T, "matmul", ps[:], selb[:, (n % DST) * 128:(n % DST + 1) * 128],
                      src[:], start=True, stop=True, waits=wt)
            if n < DST:
                t_Bb = P.op(V, "tensor_copy", out=Bb_v[:, n, :], in_=ps[:],
                            waits=[mm])
                st["paf"][bk] = t_Bb
            else:
                t_Cb = P.op(V, "tensor_copy", out=Cb_v[:, n - DST, :],
                            in_=ps[:], waits=[mm])
                st["paf"][bk] = t_Cb

        f3_kc = {}
        f3_last_mm = {}
        t_f3_p0 = {}

        def emit_f3_tp(tp):
            mm = None
            for kc in range(NKF):
                slot = kc % 2
                wl = P.op(Q if slot == 0 else G, "dma_start",
                          out=wdb[slot][:], in_=wdT_sc[kc],
                          dma=f"dwd{slot}",
                          waits=[wd_free[slot]] if wd_free[slot] else prep_sp)
                for tcin in range(2):
                    tc = tp * 2 + tcin
                    ps = (ps_w0, ps_w1)[tcin]
                    if kc == 0:
                        wt = [wl, t_hqT[(NKF - 1, tc)],
                              st["pw"][0], st["pw"][1], st.get("hslv1"),
                              psw_free[tcin] if (ti == 0 and tp == 0) else
                              (t_f3_p0[tcin] if tp == 1 else prev.get("psw_free%d" % tcin))]
                    else:
                        wt = [wl]
                    for nh in range(2):
                        mm = P.op(T, "matmul", ps[:, nh * 512:(nh + 1) * 512],
                                  hqT_v[:, kc, tc * 128:(tc + 1) * 128],
                                  wdb[slot][:, nh * 512:(nh + 1) * 512],
                                  start=(kc == 0), stop=(kc == NKF - 1),
                                  waits=[w for w in wt if w] if nh == 0 else ())
                    f3_last_mm[(tp, tcin)] = mm
                if tp == 1:
                    f3_kc[kc] = mm
                wd_free[slot] = mm

        emit_f3_tp(0)

        abufs = [argb, dAb]
        iters = [(dc, s_) for dc in range(NDC) for s_ in range(SEQ)]
        t_y = {}
        t_du = {}

        def emit_args(k):
            dc, s_ = iters[k]
            buf = abufs[k % 2]
            w = [t_dt[dc], t_negA if (ti == 0 and k == 0) else None,
                 sc_prev.get("tree%d" % (k % 2)),
                 sc_prev.get("exp%d" % (k % 2)),
                 sc_prev.get("ldt%d" % (k % 2))]
            if ti == 0 and k == 0:
                w.append(t_xq)
            if ti == 0 and k <= 1:
                w.append(t_hsl_last)
            if k <= 1:
                w.append(prev.get("SCANR_free"))
            te = None
            for n in range(DST):
                te = P.op(S, "activation", out=buf[:, n, :],
                          in_=dtb_v[:, dc, s_ * 256:(s_ + 1) * 256],
                          func=AF.Exp, scale=col(A_sc, dc, n),
                          waits=w if n == 0 else ())
            sc_prev["exp%d" % (k % 2)] = te
            sc_prev["argt%d" % (k % 2)] = te
            return te

        t_exp_k = {}
        t_exp_k[0] = emit_args(0)
        for k, (dc, s_) in enumerate(iters):
            buf = abufs[k % 2]
            if s_ == 0:
                t_du[dc] = P.op(V, "tensor_tensor", out=dtusl,
                                in0=dtb_v[:, dc, :], in1=ucb[:, dc, :],
                                op=ALU.mult,
                                waits=[t_dt[dc], t_conv[dc],
                                       sc_prev.get("dbu"),
                                       t_hsl_last if dc == 0 else None])
            tdbu = P.op(V, "tensor_tensor", out=dBub[:],
                        in0=dtusl[:, s_ * 256:(s_ + 1) * 256]
                        .unsqueeze(1).broadcast_to([128, DST, 256]),
                        in1=Bb_v[:, :, s_ * 256:(s_ + 1) * 256],
                        op=ALU.mult,
                        waits=[t_du[dc], t_Bb, sc_prev.get("hC"),
                               sc_prev.get("scan")])
            sc_prev["dbu"] = tdbu
            tsd = P.op(V, "tensor_tensor", out=sdt[:],
                       in0=buf[:, :, 0], in1=hcar[:, dc, :, s_],
                       op=ALU.mult,
                       waits=[t_exp_k[k], t_hc if ti == 0 else None,
                              sc_prev.get("carry")])
            tsd2 = P.op(V, "tensor_tensor", out=dBub[:, :, 0],
                        in0=dBub[:, :, 0], in1=sdt[:], op=ALU.add,
                        waits=[tsd, tdbu])
            tz = P.op(V, "memset", buf[:, :, 0:1], 0.0, waits=[tsd])
            sc_prev["ldt%d" % (k % 2)] = tz
            if k + 1 < len(iters):
                t_exp_k[k + 1] = emit_args(k + 1)
            tscan = P.op(V, "tensor_tensor_scan",
                         out=hb_v.rearrange("p n t -> p (n t)"),
                         data0=buf.rearrange("p n t -> p (n t)"),
                         data1=dBub.rearrange("p n t -> p (n t)"),
                         initial=0.0, op0=ALU.mult, op1=ALU.add,
                         waits=[tsd2, tz, sc_prev.get("hC"),
                                sc_prev.get("carry")])
            sc_prev["ldt%d" % (k % 2)] = tscan
            tcar = P.op(V, "tensor_copy", out=hcar[:, dc, :, s_],
                        in_=hb_v[:, :, 255], waits=[tscan])
            thc = P.op(V, "tensor_tensor", out=dBub[:], in0=hb_v[:],
                       in1=Cb_v[:, :, s_ * 256:(s_ + 1) * 256],
                       op=ALU.mult, waits=[tscan, t_Cb])
            if k == 0:
                for tcin in range(2):
                    ps_ = (ps_w0, ps_w1)[tcin]
                    t_f3_p0[tcin] = P.op(V, "tensor_tensor",
                                         out=acc[:, tcin, :],
                                         in0=acc[:, tcin, :], in1=ps_[:],
                                         op=ALU.add,
                                         waits=[f3_last_mm[(0, tcin)], t_acc0])
                emit_f3_tp(1)
            b8 = buf[:, 0:8, :]
            b4 = buf[:, 8:12, :]
            b2 = buf[:, 12:14, :]
            tt8_ = P.op(V, "tensor_tensor", out=b8, in0=dBub[:, 0:8, :],
                        in1=dBub[:, 8:16, :], op=ALU.add,
                        waits=[thc, tscan])
            tt4_ = P.op(V, "tensor_tensor", out=b4, in0=b8[:, 0:4, :],
                        in1=b8[:, 4:8, :], op=ALU.add, waits=[tt8_])
            tt2_ = P.op(V, "tensor_tensor", out=b2, in0=b4[:, 0:2, :],
                        in1=b4[:, 2:4, :], op=ALU.add, waits=[tt4_])
            ty = P.op(V, "tensor_tensor",
                      out=yb_v[:, dc, s_ * 256:(s_ + 1) * 256],
                      in0=b2[:, 0, :], in1=b2[:, 1, :], op=ALU.add,
                      waits=[tt2_, f3_kc[dc]])
            t_y[dc] = ty
            sc_prev["tree%d" % (k % 2)] = ty
            sc_prev["scan"] = tscan
            sc_prev["carry"] = tcar
            sc_prev["hC"] = thc
            sc_prev["tree"] = ty
            sc_prev["exp"] = t_exp_k[k]
        t_f3 = {(0, 0): t_f3_p0[0], (0, 1): t_f3_p0[1]}
        for tcin in range(2):
            tc = 2 + tcin
            ps = (ps_w0, ps_w1)[tcin]
            ev = P.op(V, "tensor_tensor", out=acc[:, tc, :],
                      in0=acc[:, tc, :], in1=ps[:], op=ALU.add,
                      waits=[f3_last_mm[(1, tcin)], t_acc0])
            t_f3[(1, tcin)] = ev

        t_yg = {}
        for dc in range(NDC):
            g1 = P.op(V, "scalar_tensor_tensor", out=ygb_v[:, dc, :],
                      in0=ucb[:, dc, :], scalar=col(D_sc, dc),
                      in1=yb_v[:, dc, :], op0=ALU.mult, op1=ALU.add,
                      waits=[t_y[dc]] + ([sc_prev["exp"], sc_prev["tree"],
                                          prev.get("SCB_m4")] if dc == 0 else []))
            g2 = P.op(V, "tensor_tensor", out=ygb_v[:, dc, :],
                      in0=ygb_v[:, dc, :], in1=szb[:, dc, :], op=ALU.mult,
                      waits=[g1, t_sz[dc]])
            t_yg[dc] = g2

        m4_last_mm = {}
        mm = None
        for dc in range(NDC):
            slot = dc % 2
            wl = P.op(Q if slot == 0 else G, "dma_start",
                      out=wob[slot][:], in_=woutT_sc[dc],
                      dma=("dwoq0" if slot == 0 else "dwo1"),
                      waits=[wo_free[slot]] if wo_free[slot] else prep_sp)
            for tcin in range(2):
                tc = tcin
                ps = (ps_w0, ps_w1)[tcin]
                if dc == 0:
                    wt = [wl, t_yg[0], t_f3[(1, tcin)]]
                else:
                    wt = [wl, t_yg[dc]]
                for nh in range(2):
                    mm = P.op(T, "matmul", ps[:, nh * 512:(nh + 1) * 512],
                              ygb_v[:, dc, tc * 128:(tc + 1) * 128],
                              wob[slot][:, nh * 512:(nh + 1) * 512],
                              start=(dc == 0), stop=(dc == NDC - 1),
                              waits=wt if nh == 0 else ())
                m4_last_mm[(0, tcin)] = mm
            wo_free[slot] = mm
        pending_m4 = {"mm": dict(m4_last_mm), "ti": ti, "yg": t_yg[NDC - 1]}

        prev["DtR_free"] = sc_prev["argt%d" % ((len(iters) - 1) % 2)]
        prev["DtR_free2"] = t_xq
        prev["BbR_free"] = sc_prev["dbu"]
        prev["BbR_free2"] = t_xq
        prev["CbR_free"] = sc_prev["hC"]
        prev["SCANR_free"] = wo_free[0]
        prev["SCB_m4"] = None
        prev["hqTR_free"] = t_yg[NDC - 1]
        prev["hqTR_free2"] = None
        prev["szb_free"] = t_yg[NDC - 1]
        prev["ucb_free"] = t_yg[NDC - 1]
        prev["bn_free"] = t_mv
        prev["ln_free"] = t_rat
        prev["uhead_free"] = prev["uhead_free"]

    if pending_m4 is not None:
        pi = pending_m4["ti"]
        m4all = []
        for tcin in range(2):
            ps = (ps_w0, ps_w1)[tcin]
            ev = P.op(V, "tensor_tensor", out=acc[:, tcin, :],
                      in0=acc[:, tcin, :], in1=ps[:], op=ALU.add,
                      waits=[pending_m4["mm"][(0, tcin)]])
            m4all.append(ev)
        mm = None
        tp1mm = {}
        for dc in range(NDC):
            slot = dc % 2
            wl = P.op(Q if slot == 0 else G, "dma_start",
                      out=wob[slot][:], in_=woutT_sc[dc],
                      dma=("dwoq0" if slot == 0 else "dwo1"),
                      waits=[wo_free[slot]] if wo_free[slot] else prep_sp)
            for tcin in range(2):
                tc = 2 + tcin
                ps = (ps_w0, ps_w1)[tcin]
                if dc == 0:
                    wt = [wl, pending_m4["yg"], m4all[tcin]]
                else:
                    wt = [wl]
                for nh in range(2):
                    mm = P.op(T, "matmul", ps[:, nh * 512:(nh + 1) * 512],
                              ygb_v[:, dc, tc * 128:(tc + 1) * 128],
                              wob[slot][:, nh * 512:(nh + 1) * 512],
                              start=(dc == 0), stop=(dc == NDC - 1),
                              waits=wt if nh == 0 else ())
                tp1mm[tcin] = mm
            wo_free[slot] = mm
        for tcin in range(2):
            tc = 2 + tcin
            ps = (ps_w0, ps_w1)[tcin]
            ev = P.op(V, "tensor_tensor", out=acc[:, tc, :],
                      in0=acc[:, tc, :], in1=ps[:], op=ALU.add,
                      waits=[tp1mm[tcin]])
            m4all.append(ev)
        for s_ in range(SEQ):
            P.op(G, "dma_start",
                 out=out_t.rearrange("s (a p) d -> s p a d", p=128)
                 [s_][:, 2 * pi:2 * pi + 2, :],
                 in_=acc[:, 2 * s_:2 * s_ + 2, :],
                 dma="dso", waits=m4all if s_ == 0 else ())
        pending_m4 = None

    P.op(G, "engine_nop", waits=[("dso", P.count.get("dso", 0))])

    for name in sorted(P.semnames):
        P.sems[name] = stack.enter_context(nc.semaphore(name))
    block = stack.enter_context(nc.Block())
    P.emit(block)
    stack.close()
    return nc


def _consts_array():
    c = np.zeros((128, 8), np.float32)
    c[:, 0] = 1.0
    c[:, 1] = EPS
    c[:, 2] = 1.0 + EPS
    c[:, 3] = EPS * EPS
    c[:, 4] = -0.5
    return c


def _sel_array():
    s = np.zeros((16, 16 * 128), ml_dtypes.bfloat16)
    for n in range(16):
        s[n, n * 128:(n + 1) * 128] = 1
    return s


def _base_inputs(inputs):
    return {
        "ln_g": inputs["L0_attn_ln_g"], "ln_b": inputs["L0_attn_ln_b"],
        "w_in": inputs["L0_attn_W_in"], "conv_w": inputs["L0_attn_conv_w"],
        "conv_b": inputs["L0_attn_conv_b"], "w_xp": inputs["L0_attn_W_xp"],
        "w_dt": inputs["L0_attn_W_dt"], "b_dt": inputs["L0_attn_b_dt"],
        "a_log": inputs["L0_attn_A_log"], "d_vec": inputs["L0_attn_D"],
        "w_out": inputs["L0_attn_W_out"], "wg": inputs["L0_ffn_w_gate"],
        "wu": inputs["L0_ffn_w_up"], "wd": inputs["L0_ffn_w_down"],
        "eye_b": np.eye(128, dtype=ml_dtypes.bfloat16),
        "ones1f": np.ones((1, 128), np.float32),
        "sel": _sel_array(),
        "csts": _consts_array(),
    }


_NC_CACHE = {}


def kernel(**inputs):
    if "prog" not in _NC_CACHE:
        _NC_CACHE["prog"] = build_program()
    nc = _NC_CACHE["prog"]
    x = np.ascontiguousarray(inputs["L0_attn_x"])
    base = _base_inputs(inputs)
    in_maps = []
    for c in range(NCORES):
        m = dict(base)
        m["x"] = x[c * SEQ:(c + 1) * SEQ]
        in_maps.append(m)
    res = run_bass_kernel_spmd(nc, in_maps, core_ids=list(range(NCORES)))
    out = np.concatenate([r["out"] for r in res.results], axis=0)
    return out.astype(np.float32)


# version 14
# speedup vs baseline: 1.3738x; 1.0257x over previous
import sys
sys.path.insert(0, "/opt/trn_rl_repo")
import numpy as np
import ml_dtypes

import concourse.bass as bass
import concourse.mybir as mybir
from concourse.bass_utils import run_bass_kernel_spmd

F32 = mybir.dt.float32
BF16 = mybir.dt.bfloat16
AF = mybir.ActivationFunctionType
ALU = mybir.AluOpType
AX = mybir.AxisListType

B, L, D = 16, 2048, 1024
NCORES = 8
SEQ = 2
DI = 2048
DST = 16
DFF = 2816
TC = 256
TOK = SEQ * TC
NT = L // TC
NDC = DI // 128
NKD = D // 128
NKF = DFF // 128
EPS = 1e-5
MID = [0.25, 0.75, 1.25, 1.75, 2.5, 3.5, 5.0]
MIDSQ = [m * m for m in MID]
MIDW = [0.5, 0.5, 0.5, 0.5, 1.0, 1.0, 2.0]

CFG_NT = NT


class Plan:
    ENGS = ("v", "s", "t", "g", "q")

    def __init__(self):
        self.ops = {e: [] for e in self.ENGS}
        self.count = {}
        self.semnames = set("S" + e for e in self.ENGS)
        self.sems = {}

    def dma_sem(self, name):
        self.semnames.add(name)

    def op(self, eng, method, *args, waits=(), dma=None, **kw):
        import traceback
        fr = traceback.extract_stack(limit=4)
        tbs = ";".join(f"{f.lineno}" for f in fr[:-1])
        w = tuple(t for t in waits if t is not None)
        if dma is not None:
            self.count[dma] = self.count.get(dma, 0) + 16
            tok = (dma, self.count[dma])
        else:
            s = "S" + eng
            self.count[s] = self.count.get(s, 0) + 1
            tok = (s, self.count[s])
        self.ops[eng].append((method, args, kw, w, tok, tbs))
        return tok

    def emit(self, block):
        plan = self
        secs = {"v": block.vector, "s": block.scalar, "t": block.tensor,
                "g": block.gpsimd, "q": block.sync}

        def make_body(e):
            own = "S" + e

            def body(engine):
                waited = {}
                for (method, args, kw, waits, tok, src) in plan.ops[e]:
                    need = {}
                    for (sname, val) in waits:
                        if val > need.get(sname, -1):
                            need[sname] = val
                    for sname in sorted(need):
                        val = need[sname]
                        if waited.get(sname, -1) >= val:
                            continue
                        engine.wait_ge(plan.sems[sname], val)
                        waited[sname] = val
                    try:
                        instr = getattr(engine, method)(*args, **kw)
                    except Exception as ex:
                        raise RuntimeError(f"emit failed: {method} planned at {src}: {ex}") from ex
                    sname, _ = tok
                    instr.then_inc(plan.sems[sname], 1 if sname == own else 16)
            return body

        for e in self.ENGS:
            if self.ops[e]:
                secs[e](make_body(e))


def col(ap, *idx):
    sl = tuple([slice(None)] + [slice(i, i + 1) for i in idx])
    return ap[sl]


def cdiv(a, b):
    return (a + b - 1) // b


def build_program():
    nc = bass.Bass("TRN2", target_bir_lowering=False)
    P = Plan()
    V, S, T, G, Q = "v", "s", "t", "g", "q"

    def din(name, shape, dt=F32):
        return nc.dram_tensor(name, shape, dt, kind="ExternalInput").ap()

    x_in = din("x", [SEQ, L, D])
    ln_g = din("ln_g", [D]); ln_b = din("ln_b", [D])
    w_in = din("w_in", [2 * DI, D])
    conv_w = din("conv_w", [DI, 4]); conv_b = din("conv_b", [DI])
    w_xp = din("w_xp", [96, DI])
    w_dt = din("w_dt", [DI, 64]); b_dt = din("b_dt", [DI])
    a_log = din("a_log", [DI, DST]); d_vec = din("d_vec", [DI])
    w_out = din("w_out", [D, DI])
    wg_in = din("wg", [DFF, D]); wu_in = din("wu", [DFF, D])
    wd_in = din("wd", [D, DFF])
    eye_b = din("eye_b", [128, 128], BF16)
    ones1f = din("ones1f", [1, 128], F32)
    sel_in = din("sel", [16, 16 * 128], BF16)
    csts = din("csts", [128, 8], F32)
    C_ONE, C_EPS, C_LNS, C_LNB, C_NEGH = 0, 1, 2, 3, 4

    out_t = nc.dram_tensor("out", [SEQ, L, D], F32, kind="ExternalOutput").ap()

    def scr(name, shape, dt=BF16):
        return nc.dram_tensor(name, shape, dt, kind="Internal").ap()

    winT_sc = scr("winT_sc", [32, 128, NKD, 128])
    woutT_sc = scr("woutT_sc", [NDC, 128, D])
    wgT_sc = scr("wgT_sc", [NKD, 128, DFF])
    wuT_sc = scr("wuT_sc", [NKD, 128, DFF])
    wdT_sc = scr("wdT_sc", [NKF, 128, D])

    from contextlib import ExitStack
    stack = ExitStack()

    def sb(name, shape, dt=F32):
        return stack.enter_context(nc.sbuf_tensor(name, shape, dt))

    def psum(name, shape, dt=F32):
        return stack.enter_context(nc.psum_tensor(name, shape, dt))

    cst = sb("cst", [128, 8])
    eyeb = sb("eyeb", [128, 128], BF16)
    onesf = sb("onesf", [1, 128])
    selb = sb("selb", [16, 16 * 128], BF16)
    A_sc = sb("A_sc", [128, NDC, DST])
    A_scb = sb("A_scb", [128, NDC, DST], BF16)
    D_sc = sb("D_sc", [128, NDC])
    bdt_sc = sb("bdt_sc", [128, NDC])
    cvb_sc = sb("cvb_sc", [128, NDC])
    cvw_sc = sb("cvw_sc", [128, NDC, 4])
    g_bc = sb("g_bc", [128, D])
    b_bc = sb("b_bc", [128, D])
    wxpT = sb("wxpT", [128, NDC, 96], BF16)
    wdtT = sb("wdtT", [64, DI], BF16)
    hcar = sb("hcar", [128, NDC, DST, SEQ])
    ctail = sb("ctail", [128, NDC, SEQ, 3])
    bnb = sb("bnb", [128, 4, 12])
    mv = sb("mv", [128, 4, 2])
    rr = sb("rr", [128, 4])
    rho = sb("rho", [128, 4])
    rat = sb("rat", [128, 4])
    lnt = sb("lnt", [128, 4])
    qm = sb("qm", [128, 64])
    qs = sb("qs", [128, 64])
    qr = sb("qr", [128, 64])
    qsb = sb("qsb", [128, 64], BF16)
    sdt = sb("sdt", [128, DST])
    uhead = sb("uhead", [128, NDC, SEQ, 8])
    uheadb = None
    dl_b = sb("dl_b", [64, TOK], BF16)
    bcrB = sb("bcrB", [16, TOK], BF16)
    bcrC = sb("bcrC", [16, TOK], BF16)

    acc = sb("acc", [128, 4, D])
    szb = sb("szb", [128, NDC, TOK], BF16)
    ucb = sb("ucb", [128, NDC, TOK], BF16)

    SCANR = sb("SCANR", [128, 8192])
    BbR = sb("BbR", [128, 8192], BF16)
    CbR = sb("CbR", [128, 8192], BF16)
    DtR = sb("DtR", [128, 8192], BF16)
    hqTR = sb("hqTR", [128, 11264], BF16)

    winb = [sb(f"winb{i}", [128, NKD, 128], BF16) for i in range(2)]
    wgb = sb("wgb", [128, NKD, 512], BF16)
    wub = sb("wub", [128, NKD, 512], BF16)
    wdb = [sb(f"wdb{i}", [128, D], BF16) for i in range(2)]
    wob = [sb(f"wob{i}", [128, D], BF16) for i in range(2)]

    SCB = SCANR[:].bitcast(BF16)
    u_v = SCANR[:].rearrange("p (c t) -> p c t", c=NDC)
    pp32 = [SCANR[:, 0:4096], SCANR[:, 4096:8192]]
    qv_v = SCANR[:, 0:1024]
    qa2_v = SCANR[:, 1024:2048]
    qABCD = [SCB[:, 4096 + i * 1024: 4096 + (i + 1) * 1024] for i in range(4)]
    argb = SCB[:, 0:4096].rearrange("p (n t) -> p n t", n=DST)
    dAb = SCB[:, 4096:8192].rearrange("p (n t) -> p n t", n=DST)
    dBub = SCB[:, 8192:12288].rearrange("p (n t) -> p n t", n=DST)
    hb_v = SCB[:, 12288:16384].rearrange("p (n t) -> p n t", n=DST)
    tr8 = SCB[:, 0:2048].rearrange("p (n t) -> p n t", n=8)
    tr4 = SCB[:, 2048:3072].rearrange("p (n t) -> p n t", n=4)
    tr2 = SCB[:, 3072:3584].rearrange("p (n t) -> p n t", n=2)
    ygb_v = SCB[:, 0:8192].rearrange("p (c t) -> p c t", c=NDC)

    BbF = BbR[:].bitcast(F32)
    xqb_v = BbR[:, 0:4096].rearrange("p (j d) -> p j d", j=4)
    xn2b_v = BbR[:, 4096:8192].rearrange("p (j d) -> p j d", j=4)
    ca_lo = BbF[:].rearrange("p (c t) -> p c t", c=8)
    Bb_v = BbR[:].rearrange("p (n t) -> p n t", n=DST)
    ppb = [BbR[:, 0:4096], BbR[:, 4096:8192]]
    hslp = [BbF[:, 0:512], BbF[:, 512:1024]]
    hsl = hslp[0]
    hqlp = [BbR[:, 2048:2560], BbR[:, 2560:3072]]

    CbF = CbR[:].bitcast(F32)
    xn_v = CbF[:].rearrange("p (j d) -> p j d", j=4)
    xqT_v = CbR[:, 0:4096].rearrange("p (k t) -> p k t", k=NKD)
    Cb_v = CbR[:].rearrange("p (n t) -> p n t", n=DST)
    ppq = [CbR[:, 0:4096], CbR[:, 4096:8192]]

    DtF = DtR[:].bitcast(F32)
    xt_v = DtF[:].rearrange("p (j d) -> p j d", j=4)
    ca_hi = DtF[:].rearrange("p (c t) -> p c t", c=8)
    dtb_v = DtR[:].rearrange("p (c t) -> p c t", c=NDC)

    hqF = hqTR[:].bitcast(F32)
    xn2T_v = hqTR[:, 0:4096].rearrange("p (k t) -> p k t", k=NKD)
    hqT_v = hqTR[:].rearrange("p (k t) -> p k t", k=NKF)
    yb_v = hqTR[:, 0:8192].rearrange("p (c t) -> p c t", c=NDC)
    dtusl = uhead[:].rearrange("p a b c -> p (a b c)").bitcast(BF16)[:, 0:512]
    pqv = hqF[:, 0:1024]
    pqa2 = hqF[:, 1024:2048]
    pqABCD = [hqTR[:, 4096 + i * 1024: 4096 + (i + 1) * 1024] for i in range(4)]
    x1qv = hqF[:, 2048:2560]
    x1qa2 = hqF[:, 2560:3072]
    x1ABCD = [hqTR[:, 6144 + i * 512: 6144 + (i + 1) * 512] for i in range(4)]
    p1s = hqF[:, 2048:3072]
    p2s = hqF[:, 3072:4096]

    ps_a0 = psum("ps_a0", [128, 512]); ps_a1 = psum("ps_a1", [128, 512])
    ps_w0 = psum("ps_w0", [128, 1024]); ps_w1 = psum("ps_w1", [128, 1024])
    ps_t0 = psum("ps_t0", [128, 1024], BF16)
    ps_t1 = psum("ps_t1", [128, 1024], BF16)
    ps_ts = [ps_t0, ps_t1]

    for s in ["dld0", "dld1", "dsp0", "dsp1", "dx", "dso", "dwin0", "dwin1",
              "dwg", "dwu", "dwd0", "dwd1", "dwo0", "dwo1", "dwoq0",
              "dxg"]:
        P.dma_sem(s)

    st = {"tp": 0, "tp_ev": {}, "paf": {0: None, 1: None}, "pw": {0: None, 1: None}, "pab": 0, "qprev": None}

    def transpose_group(blocks, evacs, wt):
        k = st["tp"]
        pst = ps_ts[k % 2]
        w = list(wt)
        if (k - 2) in st["tp_ev"]:
            w.append(st["tp_ev"][k - 2])
        mm = None
        for (j, src) in blocks:
            mm = P.op(T, "transpose", pst[:, j * 128:(j + 1) * 128], src,
                      eyeb[:], waits=(w + [ld_all]) if mm is None else ())
        ev = None
        for (dst, c0, c1) in evacs:
            ev = P.op(S, "copy", out=dst, in_=pst[:, c0:c1], waits=[mm])
        st["tp_ev"][k] = ev
        st["tp"] = k + 1
        return ev

    def quant(src_ap, dst_ap, nblk, wtoks, scratch, sq=False):
        qvv, qaa, (qa, qb_, qc, qd) = scratch
        n = nblk * 16
        mm_ = qm[:, 0:nblk]; ss = qs[:, 0:nblk]; rs = qr[:, 0:nblk]
        sbv = qsb[:, 0:nblk]
        vv = qvv[:, 0:n]; aa = qaa[:, 0:n]
        qa = qa[:, 0:n]; qb_ = qb_[:, 0:n]; qc = qc[:, 0:n]; qd = qd[:, 0:n]
        w0 = list(wtoks) + ([st["qprev"]] if st["qprev"] else [])
        src3 = src_ap.rearrange("p (b j) -> p b j", j=16)
        t_m = P.op(V, "tensor_reduce", out=mm_, in_=src3, axis=AX.X, op=ALU.max,
                   apply_absolute_value=True, waits=w0)
        t_s = P.op(V, "tensor_scalar", out=ss, in0=mm_, scalar1=1.0 / 6.0,
                   scalar2=1e-12, op0=ALU.mult, op1=ALU.max, waits=[t_m])
        t_r = P.op(V, "reciprocal", out=rs, in_=ss, waits=[t_s])
        t_sb = P.op(V, "tensor_copy", out=sbv, in_=ss, waits=[t_s])
        rb = rs.unsqueeze(2).broadcast_to([128, nblk, 16])
        t_v = P.op(V, "tensor_tensor", out=vv.rearrange("p (b j) -> p b j", j=16),
                   in0=src3, in1=rb, op=ALU.mult, waits=[t_r])
        if sq:
            t_a = P.op(V, "tensor_tensor", out=aa, in0=vv, in1=vv,
                       op=ALU.mult, waits=[t_v])
        else:
            t_a = P.op(S, "activation", out=aa, in_=vv, func=AF.Abs,
                       waits=[t_v])
        thr = MIDSQ if sq else MID
        def cmp(dst, i, w):
            return P.op(V, "tensor_scalar", out=dst, in0=aa, scalar1=thr[i],
                        scalar2=MIDW[i], op0=ALU.is_gt, op1=ALU.mult, waits=w)
        c0 = cmp(qa, 0, [t_a]); c1 = cmp(qb_, 1, [t_a])
        s01 = P.op(V, "tensor_tensor", out=qc, in0=qa, in1=qb_, op=ALU.add,
                   waits=[c0, c1])
        c2 = cmp(qa, 2, [s01]); c3 = cmp(qb_, 3, [s01])
        s23 = P.op(V, "tensor_tensor", out=qd, in0=qa, in1=qb_, op=ALU.add,
                   waits=[c2, c3])
        s03 = P.op(V, "tensor_tensor", out=qc, in0=qc, in1=qd, op=ALU.add,
                   waits=[s23])
        c4 = cmp(qa, 4, [s03]); c5 = cmp(qb_, 5, [s03])
        s45 = P.op(V, "tensor_tensor", out=qd, in0=qa, in1=qb_, op=ALU.add,
                   waits=[c4, c5])
        c6 = cmp(qa, 6, [s45])
        s46 = P.op(V, "tensor_tensor", out=qd, in0=qd, in1=qa, op=ALU.add,
                   waits=[c6])
        lvl = P.op(V, "tensor_tensor", out=qc, in0=qc, in1=qd, op=ALU.add,
                   waits=[s46])
        sg = P.op(V, "tensor_scalar", out=qb_, in0=vv, scalar1=0.0,
                  scalar2=None, op0=ALU.is_ge, waits=[lvl])
        sg2 = P.op(V, "tensor_scalar", out=qb_, in0=qb_, scalar1=2.0,
                   scalar2=-1.0, op0=ALU.mult, op1=ALU.add, waits=[sg])
        tw = P.op(V, "tensor_tensor", out=qa, in0=qc, in1=qb_, op=ALU.mult,
                  waits=[sg2])
        sbb = sbv.unsqueeze(2).broadcast_to([128, nblk, 16])
        tq = P.op(V, "tensor_tensor",
                  out=dst_ap.rearrange("p (b j) -> p b j", j=16),
                  in0=qa.rearrange("p (b j) -> p b j", j=16), in1=sbb,
                  op=ALU.mult, waits=[tw, t_sb])
        st["qprev"] = tq
        return tq

    MAIN_SCR = (qv_v.unsqueeze(0)[0:0] if False else None)
    main_scr = (SCANR[:, 0:1024], SCANR[:, 1024:2048], qABCD)
    prep_scr = (pqv, pqa2, pqABCD)
    p1_scr = (x1qv, x1qa2, x1ABCD)

    dl = lambda **kw: P.op(Q, "dma_start", **kw)
    dl(out=cst[:], in_=csts, dma="dld0")
    dl(out=eyeb[:], in_=eye_b, dma="dld0")
    dl(out=onesf[:], in_=ones1f, dma="dld0")
    dl(out=selb[:], in_=sel_in, dma="dld0")
    dl(out=A_sc[:], in_=a_log.rearrange("(c p) n -> p c n", p=128), dma="dld0")
    dl(out=D_sc[:], in_=d_vec.rearrange("(c p) -> p c", p=128), dma="dld0", allow_slow_non_contiguous=True)
    dl(out=bdt_sc[:], in_=b_dt.rearrange("(c p) -> p c", p=128), dma="dld0", allow_slow_non_contiguous=True)
    dl(out=cvb_sc[:], in_=conv_b.rearrange("(c p) -> p c", p=128), dma="dld0", allow_slow_non_contiguous=True)
    dl(out=cvw_sc[:], in_=conv_w.rearrange("(c p) k -> p c k", p=128),
       dma="dld0")
    dl(out=g_bc[0:1, :], in_=ln_g.unsqueeze(0), dma="dld0")
    ld_all = dl(out=b_bc[0:1, :], in_=ln_b.unsqueeze(0), dma="dld0")

    t_exp = P.op(S, "activation", out=A_sc[:], in_=A_sc[:], func=AF.Exp,
                 waits=[ld_all])
    t_negA = P.op(V, "tensor_scalar_mul", out=A_sc[:], in0=A_sc[:],
                  scalar1=-1.0, waits=[t_exp])
    t_negAb = P.op(V, "tensor_copy", out=A_scb[:], in_=A_sc[:], waits=[t_negA])

    brc = []
    for half in range(2):
        brc.append(P.op(T, "matmul", ps_w0[:, half * 512:(half + 1) * 512],
                        onesf[:], g_bc[0:1, half * 512:(half + 1) * 512],
                        start=True, stop=True, waits=[ld_all]))
        brc.append(P.op(T, "matmul", ps_w1[:, half * 512:(half + 1) * 512],
                        onesf[:], b_bc[0:1, half * 512:(half + 1) * 512],
                        start=True, stop=True, waits=[ld_all]))
    t_gbc = P.op(V, "tensor_copy", out=g_bc[:], in_=ps_w0[:], waits=[brc[-1]])
    t_bbc = P.op(V, "tensor_copy", out=b_bc[:], in_=ps_w1[:], waits=[brc[-1]])

    t_hc = P.op(V, "memset", hcar[:], 0.0)
    t_ct = P.op(V, "memset", ctail[:], 0.0)

    pst8 = {"lc": [None, None], "bc": [None, None], "qc": [None, None], "i": 0}

    def prep_slab(src_dram, n_kblk, quant_nblk, stores, part=128):
        i = pst8["i"] % 2
        n = n_kblk * 128
        tl = P.op(Q if i == 0 else G, "dma_start",
                  out=pp32[i][0:part, 0:n], in_=src_dram,
                  dma=f"dld{i}", waits=[pst8["lc"][i], ld_all])
        if quant_nblk:
            tc_ = None
            for q0 in range(0, n, 1024):
                q1 = min(q0 + 1024, n)
                tc_ = quant(pp32[i][0:part, q0:q1], ppb[i][0:part, q0:q1],
                            (q1 - q0) // 16, [tl, pst8["bc"][i]], prep_scr)
        else:
            tc_ = P.op(V, "tensor_copy", out=ppb[i][0:part, 0:n],
                       in_=pp32[i][0:part, 0:n], waits=[tl, pst8["bc"][i]])
        pst8["lc"][i] = tc_
        evs = []
        lastmm = None
        for g0 in range(0, n_kblk, 4):
            gsz = min(4, n_kblk - g0)
            k = st["tp"]
            pst = ps_ts[k % 2]
            w = [tc_, pst8["qc"][i]]
            if (k - 2) in st["tp_ev"]:
                w.append(st["tp_ev"][k - 2])
            mm = None
            for j in range(gsz):
                mm = P.op(T, "transpose", pst[:, j * 128:(j + 1) * 128],
                          ppb[i][0:part, (g0 + j) * 128:(g0 + j + 1) * 128],
                          eyeb[:],
                          waits=([t for t in w if t] + [ld_all])
                          if mm is None else ())
            ev = P.op(S, "copy",
                      out=ppq[i][:, g0 * 128:(g0 + gsz) * 128],
                      in_=pst[:, 0:gsz * 128], waits=[mm])
            st["tp_ev"][k] = ev
            st["tp"] = k + 1
            evs.append(ev)
            lastmm = mm
        pst8["bc"][i] = lastmm
        stok = None
        for (dst, c0, c1) in stores:
            src = ppq[i][:, c0:c1]
            if dst.ndim == 3:
                src = src.rearrange("p (k e) -> p k e", e=128)
            stok = P.op(Q, "dma_start", out=dst, in_=src,
                        dma=f"dsp{i}", waits=evs)
        pst8["qc"][i] = stok
        pst8["i"] += 1
        return stok

    for ec in range(32):
        prep_slab(w_in[ec * 128:(ec + 1) * 128, :], NKD, 0,
                  [(winT_sc[ec].rearrange("p c e -> p (c e)"), 0, NKD * 128)])
    for oc in range(8):
        prep_slab(w_out[oc * 128:(oc + 1) * 128, :], NDC, 0,
                  [(woutT_sc.rearrange("c p f -> p c f")[:, :, oc * 128:(oc + 1) * 128], 0, NDC * 128)])
    for (w_dram, dstsc) in ((wg_in, wgT_sc), (wu_in, wuT_sc)):
        for fc in range(NKF):
            prep_slab(w_dram[fc * 128:(fc + 1) * 128, :], NKD, 64,
                      [(dstsc.rearrange("k p f -> p k f")[:, :, fc * 128:(fc + 1) * 128], 0, NKD * 128)])
    for oc in range(8):
        prep_slab(wd_in[oc * 128:(oc + 1) * 128, :], NKF, NKF * 8,
                  [(wdT_sc.rearrange("k p f -> p k f")[:, :, oc * 128:(oc + 1) * 128], 0, NKF * 128)])

    i = pst8["i"] % 2
    tl = P.op(Q if i == 0 else G, "dma_start",
              out=pp32[i][0:96, 0:DI], in_=w_xp,
              dma=f"dld{i}", waits=[pst8["lc"][i]])
    tcc = P.op(V, "tensor_copy", out=ppb[i][0:96, 0:DI],
               in_=pp32[i][0:96, 0:DI], waits=[tl, pst8["bc"][i]])
    t_wxp = None
    for g0 in range(0, NDC, 4):
        k = st["tp"]; pst = ps_ts[k % 2]
        w = [tcc] + ([st["tp_ev"][k - 2]] if (k - 2) in st["tp_ev"] else [])
        mm = None
        for j in range(4):
            dc = g0 + j
            mm = P.op(T, "transpose", pst[:, j * 128:j * 128 + 96],
                      ppb[i][0:96, dc * 128:(dc + 1) * 128], eyeb[0:96, 0:96],
                      waits=(w + [ld_all]) if mm is None else ())
        for j in range(4):
            dc = g0 + j
            t_wxp = P.op(V, "tensor_copy", out=wxpT[:, dc, :],
                         in_=pst[:, j * 128:j * 128 + 96], waits=[mm])
        st["tp_ev"][k] = t_wxp
        st["tp"] = k + 1
    pst8["lc"][i] = tcc
    pst8["bc"][i] = mm
    pst8["i"] += 1

    i = pst8["i"] % 2
    tl = P.op(Q if i == 0 else G, "dma_start",
              out=pp32[i][:, 0:NDC * 64].rearrange("p (c r) -> p c r", c=NDC),
              in_=w_dt.rearrange("(c p) r -> p c r", p=128),
              dma=f"dld{i}", waits=[pst8["lc"][i]])
    tcc = P.op(V, "tensor_copy", out=ppb[i][:, 0:NDC * 64],
               in_=pp32[i][:, 0:NDC * 64], waits=[tl, pst8["bc"][i]])
    t_wdt = None
    for g0 in range(0, NDC, 4):
        k = st["tp"]; pst = ps_ts[k % 2]
        w = [tcc] + ([st["tp_ev"][k - 2]] if (k - 2) in st["tp_ev"] else [])
        mm = None
        for j in range(4):
            dc = g0 + j
            mm = P.op(T, "transpose", pst[0:64, j * 128:(j + 1) * 128],
                      ppb[i][:, dc * 64:(dc + 1) * 64], eyeb[:],
                      waits=(w + [ld_all]) if mm is None else ())
        t_wdt = P.op(V, "tensor_copy",
                     out=wdtT[:, g0 * 128:(g0 + 4) * 128],
                     in_=pst[0:64, 0:512], waits=[mm])
        st["tp_ev"][k] = t_wdt
        st["tp"] = k + 1
    pst8["lc"][i] = tcc
    pst8["bc"][i] = mm
    pst8["i"] += 1

    prep_sp = [("dsp0", P.count.get("dsp0", 0)), ("dsp1", P.count.get("dsp1", 0))]
    prep_v = ("Sv", P.count["Sv"])
    prep_s = ("Ss", P.count.get("Ss", 0))

    def emit_xload(ti_, xw):
        toks = []
        for s_ in range(SEQ):
            toks.append(P.op(Q if s_ == 0 else G, "dma_start",
                             out=xt_v[:, 2 * s_:2 * s_ + 2, :],
                             in_=x_in.rearrange("s (a p) d -> s p a d", p=128)
                             [s_][:, 2 * ti_:2 * ti_ + 2, :],
                             dma=("dx" if s_ == 0 else "dxg"),
                             waits=xw))
        return toks

    prev = {}
    prev["t_x"] = emit_xload(0, list(prep_sp))
    ffn_pa_free = [t_gbc, t_bbc]
    psw_free = [t_gbc, t_bbc]
    win_free = [None, None]
    wg_free = None; wu_free = None
    wd_free = [None, None]
    wo_free = [None, None]
    sc_prev = {}
    pending_m4 = None

    for ti in range(CFG_NT):
        t0 = ti * TC
        t_x = prev["t_x"][-1]
        t_x0 = prev["t_x"][0]

        bn = []
        for j in range(4):
            bn.append(P.op(V, "bn_stats", out=bnb[:, j, 0:6],
                           in_=xt_v[:, j, 0:512],
                           waits=[t_x, t_x0, prev.get("bn_free"),
                                  prep_v if ti == 0 else None,
                                  prep_s if ti == 0 else None]))
            bn.append(P.op(V, "bn_stats", out=bnb[:, j, 6:12],
                           in_=xt_v[:, j, 512:1024], waits=[]))
        t_mv = None
        for j in range(4):
            t_mv = P.op(V, "bn_aggr", out=mv[:, j, :], in_=bnb[:, j, :],
                        waits=[bn[-1]] if j == 0 else ())
        t_ln1 = P.op(S, "activation", out=lnt[:], in_=mv[:, :, 1], func=AF.Ln,
                     bias=col(cst, C_EPS),
                     waits=[t_mv, ld_all, prev.get("ln_free")])
        t_r = P.op(S, "activation", out=rr[:], in_=lnt[:], func=AF.Exp,
                   scale=col(cst, C_NEGH), waits=[t_ln1])
        t_ln2 = P.op(S, "activation", out=lnt[:], in_=mv[:, :, 1], func=AF.Ln,
                     scale=col(cst, C_LNS), bias=col(cst, C_LNB), waits=[t_r])
        t_rho = P.op(S, "activation", out=rho[:], in_=lnt[:], func=AF.Exp,
                     scale=col(cst, C_NEGH), waits=[t_ln2])
        t_rci = P.op(V, "reciprocal", out=lnt[:], in_=rr[:],
                     waits=[t_rho, t_mv])
        t_rat = P.op(V, "tensor_tensor", out=rat[:], in0=rho[:], in1=lnt[:],
                     op=ALU.mult, waits=[t_rci])

        t_xn = []
        t_xn2 = []
        for j in range(4):
            a = P.op(V, "tensor_scalar", out=xn_v[:, j, :], in0=xt_v[:, j, :],
                     scalar1=col(mv, j, 0), scalar2=col(rr, j),
                     op0=ALU.subtract, op1=ALU.mult,
                     waits=[t_r, prev.get("CbR_free"),
                            prep_s if (ti == 0 and j == 0) else None])
            t_xn.append(a)
            pass
            c_ = P.op(V, "tensor_scalar", out=p1s, in0=xn_v[:, j, :],
                      scalar1=col(rat, j), scalar2=None, op0=ALU.mult,
                      waits=[t_rat, a, prev.get("hqTR_free") if (ti and j == 0) else None])
            d_ = P.op(V, "tensor_tensor", out=p2s, in0=p1s, in1=g_bc[:],
                      op=ALU.mult, waits=[c_, t_gbc])
            e_ = P.op(V, "tensor_tensor", out=xn2b_v[:, j, :], in0=p2s,
                      in1=b_bc[:], op=ALU.add,
                      waits=[d_, t_bbc, prev.get("BbR_free") if j == 0 else None])
            t_xn2.append(e_)


        if pending_m4 is not None:
            t_m4p = {}
            for tcin in range(2):
                ps = (ps_w0, ps_w1)[tcin]
                ev = P.op(V, "tensor_tensor", out=acc[:, tcin, :],
                          in0=acc[:, tcin, :], in1=ps[:], op=ALU.add,
                          waits=[pending_m4["mm"][(0, tcin)]])
                t_m4p[(0, tcin)] = ev
            mm = None
            tp1mm = {}
            for dc in range(NDC):
                slot = dc % 2
                wl = P.op(G, "dma_start", out=wob[slot][:], in_=woutT_sc[dc],
                          dma=f"dwo{slot}",
                          waits=[wo_free[slot]] if wo_free[slot] else prep_sp)
                for tcin in range(2):
                    tc = 2 + tcin
                    ps = (ps_w0, ps_w1)[tcin]
                    if dc == 0:
                        wt = [wl, pending_m4["yg"], t_m4p[(0, tcin)]]
                    else:
                        wt = [wl]
                    for nh in range(2):
                        mm = P.op(T, "matmul", ps[:, nh * 512:(nh + 1) * 512],
                                  ygb_v[:, dc, tc * 128:(tc + 1) * 128],
                                  wob[slot][:, nh * 512:(nh + 1) * 512],
                                  start=(dc == 0), stop=(dc == NDC - 1),
                                  waits=wt if nh == 0 else ())
                    tp1mm[tcin] = mm
                wo_free[slot] = mm
            pending_m4["evp0"] = t_m4p
            pending_m4["tp1mm"] = tp1mm
            prev["SCANR_free"] = mm
            prev["ygb_free"] = mm

        t_xn2T = None
        for kc in range(NKD):
            t_xn2T = transpose_group(
                [(j, xn2b_v[:, j, kc * 128:(kc + 1) * 128]) for j in range(4)],
                [(xn2T_v[:, kc, :], 0, 512)],
                [t_xn2[-1], prev.get("hqTR_free") if kc == 0 else None])
        t_u = {}
        t_sz = {}
        ec_order = list(range(16, 32)) + list(range(16))
        for eci, ec in enumerate(ec_order):
            slot = eci % 2
            wl = P.op(Q, "dma_start",
                      out=winb[slot][:],
                      in_=winT_sc[ec].rearrange("p c e -> p (c e)"),
                      dma=f"dwin{slot}",
                      waits=[win_free[slot]] if win_free[slot] else prep_sp)
            bk = eci % 2
            ps = (ps_a0, ps_a1)[bk]
            wt = [wl, t_xn2T,
                  ffn_pa_free[bk] if (ti == 0 and eci < 2) else None,
                  st["paf"][bk]]
            mm = None
            for kc in range(NKD):
                mm = P.op(T, "matmul", ps[:], winb[slot][:, kc, :],
                          xn2T_v[:, kc, :], start=(kc == 0),
                          stop=(kc == NKD - 1), waits=wt if kc == 0 else ())
            if ec < 16:
                ev = P.op(S, "copy", out=u_v[:, ec, :], in_=ps[:],
                          waits=[mm, prev.get("SCANR_free")])
                t_u[ec] = ev
            else:
                ev = P.op(S, "activation", out=szb[:, ec - 16, :], in_=ps[:],
                          func=AF.Silu, waits=[mm, prev.get("szb_free")])
                t_sz[ec - 16] = ev
            st["paf"][bk] = ev
            win_free[slot] = mm

        t_xq = None
        for j in range(4):
            for hh in range(2):
                t_xq = quant(xn_v[:, j, hh * 512:(hh + 1) * 512],
                             xqb_v[:, j, hh * 512:(hh + 1) * 512], 32,
                             [t_xn[j],
                              t_xn2[-1] if (j == 0 and hh == 0) else None,
                              prev.get("hqTR_free") if (j == 0 and hh == 0) else None],
                             p1_scr, sq=True)

        if pending_m4 is not None:
            pi = pending_m4["ti"]
            t_m4p = pending_m4["evp0"]
            evs = [t_m4p[(0, 0)], t_m4p[(0, 1)]]
            for tcin in range(2):
                tc = 2 + tcin
                ps = (ps_w0, ps_w1)[tcin]
                ev = P.op(V, "tensor_tensor", out=acc[:, tc, :],
                          in0=acc[:, tc, :], in1=ps[:], op=ALU.add,
                          waits=[pending_m4["tp1mm"][tcin]])
                evs.append(ev)
                prev["psw_free%d" % tcin] = ev
                st["pw"][tcin] = ev
            for s_ in range(SEQ):
                P.op(G, "dma_start",
                     out=out_t.rearrange("s (a p) d -> s p a d", p=128)
                     [s_][:, 2 * pi:2 * pi + 2, :],
                     in_=acc[:, 2 * s_:2 * s_ + 2, :],
                     dma="dso", waits=evs if s_ == 0 else ())
            prev["acc_free"] = ("dso", P.count["dso"])
            pending_m4 = None

        t_acc0 = None
        for j in range(4):
            t_acc0 = P.op(V, "tensor_tensor", out=acc[:, j, :],
                          in0=xt_v[:, j, :], in1=xn_v[:, j, :], op=ALU.add,
                          waits=[t_xn[j], prev.get("acc_free") if j == 0 else None])

        t_xqT = None
        for kc in range(NKD):
            t_xqT = transpose_group(
                [(j, xqb_v[:, j, kc * 128:(kc + 1) * 128]) for j in range(4)],
                [(xqT_v[:, kc, :], 0, 512)],
                [t_xq, t_acc0 if kc == 0 else None,
                 t_xn[3] if kc == 0 else None])

        u4 = u_v.rearrange("p c (s t) -> p c s t", s=SEQ)
        th1 = P.op(V, "tensor_copy", out=uhead[:, :, :, 0:3], in_=ctail[:],
                   waits=[t_ct if ti == 0 else prev.get("ctail_w"),
                          prev.get("uhead_free")] + [t_u[ec] for ec in range(16)])
        th2 = P.op(V, "tensor_copy", out=uhead[:, :, :, 3:8],
                   in_=u4[:, :, :, 0:5], waits=[th1])
        t_tailw = P.op(V, "tensor_copy", out=ctail[:],
                       in_=u4[:, :, :, 253:256], waits=[th2])
        prev["ctail_w"] = t_tailw
        t_conv = {}
        for dc in range(NDC):
            ca = (ca_lo if dc < 8 else ca_hi)[:, dc % 8, :] \
                .rearrange("p (s t) -> p s t", s=SEQ)
            w0 = [t_tailw]
            if dc == 0:
                w0 += [prev.get("BbR_free2"), t_xn2[-1], t_xqT]
            if dc == 8:
                w0 += [prev.get("DtR_free2"), t_xn[-1]]
            base = None
            for k in range(4):
                srcm = u4[:, dc, :, max(0, 2 + k - 2): 253 + k]
            for k in range(4):
                srcm = u4[:, dc, :, 2 + k: 253 + k]
                if k == 0:
                    base = P.op(V, "tensor_scalar", out=ca[:, :, 5:256],
                                in0=srcm, scalar1=col(cvw_sc, dc, 0),
                                scalar2=None, op0=ALU.mult, waits=w0)
                else:
                    base = P.op(V, "scalar_tensor_tensor", out=ca[:, :, 5:256],
                                in0=srcm, scalar=col(cvw_sc, dc, k),
                                in1=ca[:, :, 5:256], op0=ALU.mult, op1=ALU.add,
                                waits=[base])
            for k in range(4):
                srch = uhead[:, dc, :, k:k + 5]
                if k == 0:
                    hh = P.op(V, "tensor_scalar", out=ca[:, :, 0:5], in0=srch,
                              scalar1=col(cvw_sc, dc, 0), scalar2=None,
                              op0=ALU.mult, waits=[base])
                else:
                    hh = P.op(V, "scalar_tensor_tensor", out=ca[:, :, 0:5],
                              in0=srch, scalar=col(cvw_sc, dc, k),
                              in1=ca[:, :, 0:5], op0=ALU.mult, op1=ALU.add,
                              waits=[hh])
            sl = P.op(S, "activation", out=ucb[:, dc, :],
                      in_=ca.rearrange("p s t -> p (s t)"), func=AF.Silu,
                      bias=col(cvb_sc, dc),
                      waits=[hh, prev.get("ucb_free") if dc == 0 else None])
            t_conv[dc] = sl
        t_convlast = t_conv[NDC - 1]
        prev["uhead_free"] = hh

        wt = [t_convlast, t_wxp, st["paf"][0]]
        mm = None
        for dc in range(NDC):
            mm = P.op(T, "matmul", ps_a0[0:64, :], wxpT[:, dc, 0:64],
                      ucb[:, dc, :], start=(dc == 0), stop=(dc == NDC - 1),
                      waits=wt if dc == 0 else ())
        t_dl = P.op(V, "tensor_copy", out=dl_b[:], in_=ps_a0[0:64, :],
                    waits=[mm])
        st["paf"][0] = t_dl
        t_bc2 = {}
        for half, dst in ((0, bcrB), (1, bcrC)):
            bk = (half + 1) % 2
            ps = (ps_a0, ps_a1)[bk]
            wt = [t_convlast, st["paf"][bk],
                  ffn_pa_free[bk] if ti == 0 else None]
            mm = None
            for dc in range(NDC):
                mm = P.op(T, "matmul", ps[0:16, :],
                          wxpT[:, dc, 64 + half * 16:80 + half * 16],
                          ucb[:, dc, :], start=(dc == 0), stop=(dc == NDC - 1),
                          waits=wt if dc == 0 else ())
            t_bc2[half] = P.op(V, "tensor_copy", out=dst[:], in_=ps[0:16, :],
                               waits=[mm])
            st["paf"][bk] = t_bc2[half]

        m3t = SCANR[:, 7680:8192]
        t_dt = {}
        m3e1 = {}

        def emit_m3(dc):
            half = dc % 2
            ps = ps_w1[:, half * 512:(half + 1) * 512]
            wt = [t_dl, t_wdt,
                  m3e1[dc - 2] if dc >= 2 else st["pw"][1]]
            mm = P.op(T, "matmul", ps, wdtT[:, dc * 128:(dc + 1) * 128],
                      dl_b[:], start=True, stop=True, waits=wt)
            e1 = P.op(S, "activation", out=m3t, in_=ps, func=AF.Exp,
                      bias=col(bdt_sc, dc),
                      waits=[mm, t_convlast if dc == 0 else t_dt[dc - 1]])
            e2 = P.op(S, "activation", out=dtb_v[:, dc, :], in_=m3t,
                      func=AF.Ln, bias=col(cst, C_ONE), waits=[e1])
            m3e1[dc] = e1
            t_dt[dc] = e2

        t_hqT = {}
        last_f2 = None
        kff = 0
        pend_tp = None

        def emit_pend_tp():
            (p_hql, p_par, p_f0, p_tc, p_nkb, p_tq, p_kff) = pend_tp
            ev = transpose_group(
                [(j, p_hql[:, j * 128:(j + 1) * 128]) for j in range(p_nkb)],
                [(hqT_v[:, (p_f0 // 128) + j, p_tc * 128:(p_tc + 1) * 128],
                  j * 128, (j + 1) * 128) for j in range(p_nkb)],
                [p_tq, t_xn2T if p_kff == 0 else None,
                 prev.get("hqTR_free2") if p_kff == 0 else None])
            st["hqlt%d" % p_par] = st["tp_ev"][st["tp"] - 1]
            for j in range(p_nkb):
                t_hqT[((p_f0 // 128) + j, p_tc)] = ev

        for fsi in range(6):
            f0 = fsi * 512
            f1 = min(f0 + 512, DFF)
            fn = f1 - f0
            wlg = P.op(Q, "dma_start", out=wgb[:, :, 0:fn],
                       in_=wgT_sc.rearrange("k p f -> p k f")[:, :, f0:f1],
                       dma="dwg",
                       waits=[wg_free] if wg_free else list(prep_sp))
            wlu = P.op(Q, "dma_start", out=wub[:, :, 0:fn],
                       in_=wuT_sc.rearrange("k p f -> p k f")[:, :, f0:f1],
                       dma="dwu",
                       waits=[wu_free] if wu_free else list(prep_sp))
            for tc in range(4):
                if kff < NDC:
                    emit_m3(kff)
                par = kff % 2
                gps = (ps_a0[:, 0:fn], ps_w0[:, 0:fn])[par]
                ups = (ps_a1[:, 0:fn], ps_w0[:, 512:512 + fn])[par]
                gkey = ("paf", 0) if par == 0 else ("pw", 0)
                ukey = ("paf", 1) if par == 0 else ("pw", 1)
                g_mm = None
                wt0 = [wlg, t_xqT, st[gkey[0]][gkey[1]],
                       psw_free[0] if (ti == 0 and kff < 2) else None]
                for kc in range(NKD):
                    g_mm = P.op(T, "matmul", gps,
                                xqT_v[:, kc, tc * 128:(tc + 1) * 128],
                                wgb[:, kc, 0:fn], start=(kc == 0),
                                stop=(kc == NKD - 1),
                                waits=wt0 if kc == 0 else ())
                u_mm = None
                wt1 = [wlu, st[ukey[0]][ukey[1]]]
                for kc in range(NKD):
                    u_mm = P.op(T, "matmul", ups,
                                xqT_v[:, kc, tc * 128:(tc + 1) * 128],
                                wub[:, kc, 0:fn], start=(kc == 0),
                                stop=(kc == NKD - 1),
                                waits=wt1 if kc == 0 else ())
                hsl_k = hslp[par]
                hql_k = hqlp[par]
                sg = P.op(S, "activation", out=hsl_k[:, 0:fn], in_=gps,
                          func=AF.Silu,
                          waits=[g_mm, st.get("hslv%d" % par),
                                 t_conv[kff] if kff < 2 else None])
                hm = P.op(V, "tensor_tensor", out=hsl_k[:, 0:fn],
                          in0=hsl_k[:, 0:fn], in1=ups, op=ALU.mult,
                          waits=[sg, u_mm])
                st[gkey[0]][gkey[1]] = hm
                st[ukey[0]][ukey[1]] = hm
                tq = quant(hsl_k[:, 0:fn], hql_k[:, 0:fn], fn // 16,
                           [hm, st.get("hqlt%d" % par),
                            prev.get("SCANR_free") if (ti and kff == 0) else None,
                            t_convlast if kff == 0 else None],
                           main_scr)
                st["hslv%d" % par] = tq
                nkb = fn // 128
                if pend_tp is not None:
                    emit_pend_tp()
                pend_tp = (hql_k, par, f0, tc, nkb, tq, kff)
                last_f2 = u_mm
                kff += 1
            wg_free = g_mm
            wu_free = u_mm
        for dc in range(kff, NDC):
            emit_m3(dc)
        emit_pend_tp()
        pend_tp = None
        st["pw"][1] = m3e1[NDC - 1]
        t_hsl_last = st["qprev"]
        prev["hsl_free"] = t_hsl_last

        t_Bb = None
        t_Cb = None
        for n in range(2 * DST):
            bk = n % 2
            ps = (ps_a0, ps_a1)[bk]
            wt = [t_bc2[0], t_bc2[1], st["paf"][bk]]
            if n == 0:
                wt += [t_xn2[-1], t_xq, st["qprev"]]
            if n == DST:
                wt += [last_f2]
            src = bcrB if n < DST else bcrC
            mm = P.op(T, "matmul", ps[:], selb[:, (n % DST) * 128:(n % DST + 1) * 128],
                      src[:], start=True, stop=True, waits=wt)
            if n < DST:
                t_Bb = P.op(V, "tensor_copy", out=Bb_v[:, n, :], in_=ps[:],
                            waits=[mm])
                st["paf"][bk] = t_Bb
            else:
                t_Cb = P.op(V, "tensor_copy", out=Cb_v[:, n - DST, :],
                            in_=ps[:], waits=[mm])
                st["paf"][bk] = t_Cb

        f3_kc = {}
        f3_last_mm = {}
        t_f3_p0 = {}

        def emit_f3_tp(tp):
            mm = None
            for kc in range(NKF):
                slot = kc % 2
                wl = P.op(Q if slot == 0 else G, "dma_start",
                          out=wdb[slot][:], in_=wdT_sc[kc],
                          dma=f"dwd{slot}",
                          waits=[wd_free[slot]] if wd_free[slot] else prep_sp)
                for tcin in range(2):
                    tc = tp * 2 + tcin
                    ps = (ps_w0, ps_w1)[tcin]
                    if kc == 0:
                        wt = [wl, t_hqT[(NKF - 1, tc)],
                              st["pw"][0], st["pw"][1], st.get("hslv1"),
                              psw_free[tcin] if (ti == 0 and tp == 0) else
                              (t_f3_p0[tcin] if tp == 1 else prev.get("psw_free%d" % tcin))]
                    else:
                        wt = [wl]
                    for nh in range(2):
                        mm = P.op(T, "matmul", ps[:, nh * 512:(nh + 1) * 512],
                                  hqT_v[:, kc, tc * 128:(tc + 1) * 128],
                                  wdb[slot][:, nh * 512:(nh + 1) * 512],
                                  start=(kc == 0), stop=(kc == NKF - 1),
                                  waits=[w for w in wt if w] if nh == 0 else ())
                    f3_last_mm[(tp, tcin)] = mm
                if tp == 1:
                    f3_kc[kc] = mm
                wd_free[slot] = mm

        emit_f3_tp(0)

        abufs = [argb, dAb]
        iters = [(dc, s_) for dc in range(NDC) for s_ in range(SEQ)]
        t_y = {}
        t_du = {}

        def emit_args(k):
            dc, s_ = iters[k]
            buf = abufs[k % 2]
            w = [t_dt[dc], t_negA if (ti == 0 and k == 0) else None,
                 sc_prev.get("tree%d" % (k % 2)),
                 sc_prev.get("exp%d" % (k % 2)),
                 sc_prev.get("ldt%d" % (k % 2))]
            if ti == 0 and k == 0:
                w.append(t_xq)
            if ti == 0 and k <= 1:
                w.append(t_hsl_last)
            if k <= 1:
                w.append(prev.get("SCANR_free"))
            te = None
            for n in range(DST):
                te = P.op(S, "activation", out=buf[:, n, :],
                          in_=dtb_v[:, dc, s_ * 256:(s_ + 1) * 256],
                          func=AF.Exp, scale=col(A_sc, dc, n),
                          waits=w if n == 0 else ())
            sc_prev["exp%d" % (k % 2)] = te
            sc_prev["argt%d" % (k % 2)] = te
            return te

        t_exp_k = {}
        t_exp_k[0] = emit_args(0)
        for k, (dc, s_) in enumerate(iters):
            buf = abufs[k % 2]
            if s_ == 0:
                t_du[dc] = P.op(V, "tensor_tensor", out=dtusl,
                                in0=dtb_v[:, dc, :], in1=ucb[:, dc, :],
                                op=ALU.mult,
                                waits=[t_dt[dc], t_conv[dc],
                                       sc_prev.get("dbu"),
                                       t_hsl_last if dc == 0 else None])
            tdbu = P.op(V, "tensor_tensor", out=dBub[:],
                        in0=dtusl[:, s_ * 256:(s_ + 1) * 256]
                        .unsqueeze(1).broadcast_to([128, DST, 256]),
                        in1=Bb_v[:, :, s_ * 256:(s_ + 1) * 256],
                        op=ALU.mult,
                        waits=[t_du[dc], t_Bb, sc_prev.get("hC"),
                               sc_prev.get("scan")])
            sc_prev["dbu"] = tdbu
            tsd = P.op(V, "tensor_tensor", out=sdt[:],
                       in0=buf[:, :, 0], in1=hcar[:, dc, :, s_],
                       op=ALU.mult,
                       waits=[t_exp_k[k], t_hc if ti == 0 else None,
                              sc_prev.get("carry")])
            tsd2 = P.op(V, "tensor_tensor", out=dBub[:, :, 0],
                        in0=dBub[:, :, 0], in1=sdt[:], op=ALU.add,
                        waits=[tsd, tdbu])
            tz = P.op(V, "memset", buf[:, :, 0:1], 0.0, waits=[tsd])
            sc_prev["ldt%d" % (k % 2)] = tz
            if k + 1 < len(iters):
                t_exp_k[k + 1] = emit_args(k + 1)
            tscan = P.op(V, "tensor_tensor_scan",
                         out=hb_v.rearrange("p n t -> p (n t)"),
                         data0=buf.rearrange("p n t -> p (n t)"),
                         data1=dBub.rearrange("p n t -> p (n t)"),
                         initial=0.0, op0=ALU.mult, op1=ALU.add,
                         waits=[tsd2, tz, sc_prev.get("hC"),
                                sc_prev.get("carry")])
            sc_prev["ldt%d" % (k % 2)] = tscan
            tcar = P.op(V, "tensor_copy", out=hcar[:, dc, :, s_],
                        in_=hb_v[:, :, 255], waits=[tscan])
            thc = P.op(V, "tensor_tensor", out=dBub[:], in0=hb_v[:],
                       in1=Cb_v[:, :, s_ * 256:(s_ + 1) * 256],
                       op=ALU.mult, waits=[tscan, t_Cb])
            if k == 0:
                for tcin in range(2):
                    ps_ = (ps_w0, ps_w1)[tcin]
                    t_f3_p0[tcin] = P.op(V, "tensor_tensor",
                                         out=acc[:, tcin, :],
                                         in0=acc[:, tcin, :], in1=ps_[:],
                                         op=ALU.add,
                                         waits=[f3_last_mm[(0, tcin)], t_acc0])
                emit_f3_tp(1)
            b8 = buf[:, 0:8, :]
            b4 = buf[:, 8:12, :]
            b2 = buf[:, 12:14, :]
            tt8_ = P.op(V, "tensor_tensor", out=b8, in0=dBub[:, 0:8, :],
                        in1=dBub[:, 8:16, :], op=ALU.add,
                        waits=[thc, tscan])
            tt4_ = P.op(V, "tensor_tensor", out=b4, in0=b8[:, 0:4, :],
                        in1=b8[:, 4:8, :], op=ALU.add, waits=[tt8_])
            tt2_ = P.op(V, "tensor_tensor", out=b2, in0=b4[:, 0:2, :],
                        in1=b4[:, 2:4, :], op=ALU.add, waits=[tt4_])
            ty = P.op(V, "tensor_tensor",
                      out=yb_v[:, dc, s_ * 256:(s_ + 1) * 256],
                      in0=b2[:, 0, :], in1=b2[:, 1, :], op=ALU.add,
                      waits=[tt2_, f3_kc[dc]])
            t_y[dc] = ty
            sc_prev["tree%d" % (k % 2)] = ty
            sc_prev["scan"] = tscan
            sc_prev["carry"] = tcar
            sc_prev["hC"] = thc
            sc_prev["tree"] = ty
            sc_prev["exp"] = t_exp_k[k]
        t_f3 = {(0, 0): t_f3_p0[0], (0, 1): t_f3_p0[1]}
        for tcin in range(2):
            tc = 2 + tcin
            ps = (ps_w0, ps_w1)[tcin]
            ev = P.op(V, "tensor_tensor", out=acc[:, tc, :],
                      in0=acc[:, tc, :], in1=ps[:], op=ALU.add,
                      waits=[f3_last_mm[(1, tcin)], t_acc0])
            t_f3[(1, tcin)] = ev

        t_yg = {}
        for dc in range(NDC):
            g1 = P.op(V, "scalar_tensor_tensor", out=ygb_v[:, dc, :],
                      in0=ucb[:, dc, :], scalar=col(D_sc, dc),
                      in1=yb_v[:, dc, :], op0=ALU.mult, op1=ALU.add,
                      waits=[t_y[dc]] + ([sc_prev["exp"], sc_prev["tree"],
                                          prev.get("SCB_m4")] if dc == 0 else []))
            g2 = P.op(V, "tensor_tensor", out=ygb_v[:, dc, :],
                      in0=ygb_v[:, dc, :], in1=szb[:, dc, :], op=ALU.mult,
                      waits=[g1, t_sz[dc]])
            t_yg[dc] = g2

        if ti + 1 < CFG_NT:
            prev["t_x"] = emit_xload(
                ti + 1, [sc_prev["argt%d" % ((len(iters) - 1) % 2)],
                         sc_prev["dbu"]])

        m4_last_mm = {}
        mm = None
        for dc in range(NDC):
            slot = dc % 2
            wl = P.op(Q if slot == 0 else G, "dma_start",
                      out=wob[slot][:], in_=woutT_sc[dc],
                      dma=("dwoq0" if slot == 0 else "dwo1"),
                      waits=[wo_free[slot]] if wo_free[slot] else prep_sp)
            for tcin in range(2):
                tc = tcin
                ps = (ps_w0, ps_w1)[tcin]
                if dc == 0:
                    wt = [wl, t_yg[0], t_f3[(1, tcin)]]
                else:
                    wt = [wl, t_yg[dc]]
                for nh in range(2):
                    mm = P.op(T, "matmul", ps[:, nh * 512:(nh + 1) * 512],
                              ygb_v[:, dc, tc * 128:(tc + 1) * 128],
                              wob[slot][:, nh * 512:(nh + 1) * 512],
                              start=(dc == 0), stop=(dc == NDC - 1),
                              waits=wt if nh == 0 else ())
                m4_last_mm[(0, tcin)] = mm
            wo_free[slot] = mm
        pending_m4 = {"mm": dict(m4_last_mm), "ti": ti, "yg": t_yg[NDC - 1]}

        prev["DtR_free"] = sc_prev["argt%d" % ((len(iters) - 1) % 2)]
        prev["DtR_free2"] = t_xq
        prev["BbR_free"] = sc_prev["dbu"]
        prev["BbR_free2"] = t_xq
        prev["CbR_free"] = sc_prev["hC"]
        prev["SCANR_free"] = wo_free[0]
        prev["SCB_m4"] = None
        prev["hqTR_free"] = t_yg[NDC - 1]
        prev["hqTR_free2"] = None
        prev["szb_free"] = t_yg[NDC - 1]
        prev["ucb_free"] = t_yg[NDC - 1]
        prev["bn_free"] = t_mv
        prev["ln_free"] = t_rat
        prev["uhead_free"] = prev["uhead_free"]

    if pending_m4 is not None:
        pi = pending_m4["ti"]
        m4all = []
        for tcin in range(2):
            ps = (ps_w0, ps_w1)[tcin]
            ev = P.op(V, "tensor_tensor", out=acc[:, tcin, :],
                      in0=acc[:, tcin, :], in1=ps[:], op=ALU.add,
                      waits=[pending_m4["mm"][(0, tcin)]])
            m4all.append(ev)
        mm = None
        tp1mm = {}
        for dc in range(NDC):
            slot = dc % 2
            wl = P.op(Q if slot == 0 else G, "dma_start",
                      out=wob[slot][:], in_=woutT_sc[dc],
                      dma=("dwoq0" if slot == 0 else "dwo1"),
                      waits=[wo_free[slot]] if wo_free[slot] else prep_sp)
            for tcin in range(2):
                tc = 2 + tcin
                ps = (ps_w0, ps_w1)[tcin]
                if dc == 0:
                    wt = [wl, pending_m4["yg"], m4all[tcin]]
                else:
                    wt = [wl]
                for nh in range(2):
                    mm = P.op(T, "matmul", ps[:, nh * 512:(nh + 1) * 512],
                              ygb_v[:, dc, tc * 128:(tc + 1) * 128],
                              wob[slot][:, nh * 512:(nh + 1) * 512],
                              start=(dc == 0), stop=(dc == NDC - 1),
                              waits=wt if nh == 0 else ())
                tp1mm[tcin] = mm
            wo_free[slot] = mm
        for tcin in range(2):
            tc = 2 + tcin
            ps = (ps_w0, ps_w1)[tcin]
            ev = P.op(V, "tensor_tensor", out=acc[:, tc, :],
                      in0=acc[:, tc, :], in1=ps[:], op=ALU.add,
                      waits=[tp1mm[tcin]])
            m4all.append(ev)
        for s_ in range(SEQ):
            P.op(G, "dma_start",
                 out=out_t.rearrange("s (a p) d -> s p a d", p=128)
                 [s_][:, 2 * pi:2 * pi + 2, :],
                 in_=acc[:, 2 * s_:2 * s_ + 2, :],
                 dma="dso", waits=m4all if s_ == 0 else ())
        pending_m4 = None

    P.op(G, "engine_nop", waits=[("dso", P.count.get("dso", 0))])

    for name in sorted(P.semnames):
        P.sems[name] = stack.enter_context(nc.semaphore(name))
    block = stack.enter_context(nc.Block())
    P.emit(block)
    stack.close()
    return nc


def _consts_array():
    c = np.zeros((128, 8), np.float32)
    c[:, 0] = 1.0
    c[:, 1] = EPS
    c[:, 2] = 1.0 + EPS
    c[:, 3] = EPS * EPS
    c[:, 4] = -0.5
    return c


def _sel_array():
    s = np.zeros((16, 16 * 128), ml_dtypes.bfloat16)
    for n in range(16):
        s[n, n * 128:(n + 1) * 128] = 1
    return s


def _base_inputs(inputs):
    return {
        "ln_g": inputs["L0_attn_ln_g"], "ln_b": inputs["L0_attn_ln_b"],
        "w_in": inputs["L0_attn_W_in"], "conv_w": inputs["L0_attn_conv_w"],
        "conv_b": inputs["L0_attn_conv_b"], "w_xp": inputs["L0_attn_W_xp"],
        "w_dt": inputs["L0_attn_W_dt"], "b_dt": inputs["L0_attn_b_dt"],
        "a_log": inputs["L0_attn_A_log"], "d_vec": inputs["L0_attn_D"],
        "w_out": inputs["L0_attn_W_out"], "wg": inputs["L0_ffn_w_gate"],
        "wu": inputs["L0_ffn_w_up"], "wd": inputs["L0_ffn_w_down"],
        "eye_b": np.eye(128, dtype=ml_dtypes.bfloat16),
        "ones1f": np.ones((1, 128), np.float32),
        "sel": _sel_array(),
        "csts": _consts_array(),
    }


_NC_CACHE = {}


def kernel(**inputs):
    if "prog" not in _NC_CACHE:
        _NC_CACHE["prog"] = build_program()
    nc = _NC_CACHE["prog"]
    x = np.ascontiguousarray(inputs["L0_attn_x"])
    base = _base_inputs(inputs)
    in_maps = []
    for c in range(NCORES):
        m = dict(base)
        m["x"] = x[c * SEQ:(c + 1) * SEQ]
        in_maps.append(m)
    res = run_bass_kernel_spmd(nc, in_maps, core_ids=list(range(NCORES)))
    out = np.concatenate([r["out"] for r in res.results], axis=0)
    return out.astype(np.float32)
